# version 02
# baseline (speedup 1.0000x reference)
import contextlib
import numpy as np
import ml_dtypes
import concourse.bass as bass
import concourse.mybir as mybir
from concourse.bass_utils import run_bass_kernel_spmd

F32 = mybir.dt.float32
BF16 = mybir.dt.bfloat16
I32 = mybir.dt.int32
AF = mybir.ActivationFunctionType
ALU = mybir.AluOpType
AX = mybir.AxisListType.X
BB = ml_dtypes.bfloat16

B, S, D = 4, 4096, 1024
T = 2048
H, DH = 16, 64
NC = 8
FFN0 = 3072
QK_SCALE = 8.0
ENGS = ("tensor", "vector", "scalar", "gpsimd", "sync")


class Stage:
    __slots__ = ("name", "ops")

    def __init__(self, name):
        self.name = name
        self.ops = {e: [] for e in ENGS}

    def t(self, fn): self.ops["tensor"].append(("c", fn, 1))
    def v(self, fn): self.ops["vector"].append(("c", fn, 1))
    def s(self, fn): self.ops["scalar"].append(("c", fn, 1))
    def g(self, fn): self.ops["gpsimd"].append(("c", fn, 1))
    def dma(self, fn, n=1): self.ops["sync"].append(("d", fn, n))
    def gdma(self, fn, n=1): self.ops["gpsimd"].append(("d", fn, n))


class Prog:

    _PSMAP = {"PS1": "Z0", "PS2": "Z1", "PSM1": "Z2", "PSM2": "Z3",
              "PSM1E": "Z2", "PSM2E": "Z3", "PSZ0": "Z0", "PSZ1": "Z1",
              "PSZ2": "Z2", "PSZ3": "Z3", "PSN1": "N1", "PSN2": "N2",
              "PSN": "N1", "PSB": "B", "PSBE": "B", "PSO": "O", "PSOE": "O"}
    _PSHAPE = {"Z0": [128, 512], "Z1": [128, 512], "Z2": [128, 512],
               "Z3": [128, 512], "N1": [1, 512], "N2": [1, 512],
               "B": [64, 512], "O": [65, 512]}

    def __init__(self, name="prog"):
        self.nc = bass.Bass("TRN2", target_bir_lowering=False, debug=False,
                            detect_race_conditions=False)
        self.stack = contextlib.ExitStack()
        self.stages = []
        self.name = name
        self.prefix = ""
        self._off = 16512
        self._pspool = {}

    SBUF_BASE = 16512

    def begin_section(self, prefix):
        self.prefix = prefix
        self._off = self.SBUF_BASE

    def sbuf(self, name, shape, dtype):
        nbytes = int(np.prod(shape[1:])) * mybir.dt.size(dtype)
        off = (self._off + 31) & ~31
        self._off = off + nbytes
        h = self.nc.alloc_sbuf_tensor_at(self.prefix + name, shape, dtype,
                                         offset=off)
        return h

    def psum(self, name, shape, dtype=F32):
        key = self._PSMAP[name]
        if key not in self._pspool:
            self._pspool[key] = self.stack.enter_context(
                self.nc.psum_tensor("ps_" + key, self._PSHAPE[key], F32))
        t = self._pspool[key]
        if list(shape) == self._PSHAPE[key]:
            return t
        return t[0:shape[0], 0:shape[1]]

    def sem(self, name):
        return self.stack.enter_context(self.nc.semaphore(name))

    def stage(self, name=""):
        st = Stage(name or f"st{len(self.stages)}")
        self.stages.append(st)
        return st

    def finalize(self):
        nc = self.nc
        barC = self.sem("barC")
        barD = self.sem("barD")
        cb, db = [0], [0]
        for st in self.stages:
            c, d = cb[-1], db[-1]
            for e in ENGS:
                ops = st.ops[e]
                if not ops:
                    continue
                d += 16 * sum(n for k, _, n in ops if k == "d")
                if ops[-1][0] == "c":
                    c += 1
            cb.append(c)
            db.append(d)

        def run_engine(eng_name):
            def body(eng):
                lwc, lwd = 0, 0
                for k, st in enumerate(self.stages):
                    ops = st.ops[eng_name]
                    if not ops:
                        continue
                    if cb[k] > lwc:
                        eng.wait_ge(barC, cb[k])
                        lwc = cb[k]
                    if db[k] > lwd:
                        eng.wait_ge(barD, db[k])
                        lwd = db[k]
                    last_c = None
                    for kind, fn, n in ops:
                        r = fn(eng)
                        insts = r if isinstance(r, (list, tuple)) else [r]
                        insts = [i for i in insts if i is not None]
                        if kind == "d":
                            assert len(insts) >= n, (
                                st.name, eng_name, len(insts), n)
                            for i in insts:
                                i.then_inc(barD, 16)
                        elif insts:
                            last_c = insts[-1]
                    if ops[-1][0] == "c":
                        assert last_c is not None, (st.name, eng_name)
                        last_c.then_inc(barC, 1)
            return body

        with nc.Block() as blk:
            for e in ENGS:
                if any(st.ops[e] for st in self.stages):
                    getattr(blk, e)(run_engine(e))

            def final_wait(eng):
                eng.wait_ge(barC, cb[-1])
                eng.wait_ge(barD, db[-1])
            blk.vector(final_wait)
        self.stack.close()
        return nc


def cdiv(a, b):
    return (a + b - 1) // b


def emit_a(p, D):
    x_fm = D("x_fm", [1024, T], F32, "ExternalInput")
    wq_t = D("wq_t", [1024, 1024], F32, "ExternalInput")
    wk_t = D("wk_t", [1024, 1024], F32, "ExternalInput")
    wv_t = D("wv_t", [1024, 1024], F32, "ExternalInput")
    wg_t = D("wg_t", [1024, FFN0], I32, "ExternalInput")
    wu_t = D("wu_t", [1024, FFN0], I32, "ExternalInput")
    wd_t = D("wd_t", [FFN0, 1024], I32, "ExternalInput")
    cvec = D("cvec", [128, 4], F32, "ExternalInput")
    o_qaug = D("qaug", [16, 66, T], BF16, "ExternalOutput")
    o_kaug = D("kaug", [16, 66, T], BF16, "ExternalOutput")
    o_vt = D("vtok", [T, 16 * 65], BF16, "ExternalOutput")
    o_kn32 = D("kn32", [16, T], F32, "ExternalOutput")
    o_g = D("g", [1024, T], F32, "ExternalOutput")

    cv = p.sbuf("cv", [128, 4], F32)
    xn16 = p.sbuf("xn16", [128, 8 * T], BF16)
    arenaA = p.sbuf("arenaA", [128, 24 * 1024], BF16)
    wgc = p.sbuf("wgc", [128, 8 * 512], BF16)
    wuc = p.sbuf("wuc", [128, 8 * 512], BF16)
    wdbuf = p.sbuf("wdbuf", [128, 24 * 128], BF16)
    xc = p.sbuf("xc", [128, T], F32)
    rowsA = p.sbuf("rowsA", [65, T], F32)
    rowsB = p.sbuf("rowsB", [33, T], F32)
    ones128 = p.sbuf("ones128", [33, 128], F32)
    onesb = p.sbuf("onesb", [128, 1], BF16)
    sq16 = p.sbuf("sq16", [128, T], BF16)
    xc16 = p.sbuf("xc16", [128, T], BF16)
    sqq = p.sbuf("sqq", [64, 512], BF16)
    qa = p.sbuf("qa", [66, T], BF16)
    ka = p.sbuf("ka", [66, T], BF16)
    vtbuf = p.sbuf("vtbuf", [128, 16 * 65], BF16)
    knrow16 = p.sbuf("knrow16", [1, T], BF16)
    sg = p.sbuf("sg", [128, 512], F32)
    gout = p.sbuf("gout", [128, 512], F32)
    sums = rowsA[0:1, :]
    sqs = rowsA[32:33, :]
    knrow = rowsA[64:65, :]
    mean = rowsB[0:1, :]
    rstd = rowsB[32:33, :]

    PS1 = p.psum("PS1", [128, 512], F32)
    PS2 = p.psum("PS2", [128, 512], F32)
    PSN1 = p.psum("PSN1", [1, 512], F32)
    PSN2 = p.psum("PSN2", [1, 512], F32)
    PSM1 = p.psum("PSM1", [128, 512], F32)
    PSM2 = p.psum("PSM2", [128, 512], F32)

    def wq16(fc):
        return arenaA[:, fc * 1024:(fc + 1) * 1024]

    def wk16(fc):
        return arenaA[:, 8 * 1024 + fc * 1024:8 * 1024 + (fc + 1) * 1024]

    def wv16(fc):
        return arenaA[:, 16 * 1024 + fc * 1024:16 * 1024 + (fc + 1) * 1024]

    def hsl(hc, csl):
        base = hc * 1024
        return arenaA[:, base + csl.start:base + csl.stop]

    st = p.stage("consts")
    st.dma(lambda y: y.dma_start(out=cv[:], in_=cvec))

    st = p.stage("init")

    def init(v):
        v.memset(ones128[:], 1.0)
        v.memset(onesb[:], 1.0)
        v.memset(sums, 0.0)
        v.memset(sqs, 0.0)
        v.memset(qa[64:66, :], 1.0)
        return v.memset(ka[64:66, :], 1.0)
    st.v(init)

    st = p.stage("wload_attn")

    def wload(g):
        out = []
        for fc in range(8):
            out.append(g.dma_start(out=wq16(fc), in_=wq_t[fc * 128:(fc + 1) * 128, :]))
            out.append(g.dma_start(out=wk16(fc), in_=wk_t[fc * 128:(fc + 1) * 128, :]))
            out.append(g.dma_start(out=wv16(fc), in_=wv_t[fc * 128:(fc + 1) * 128, :]))
        return out
    st.gdma(wload, n=24)

    for fc in range(8):
        st = p.stage(f"lnl{fc}")
        st.dma(lambda y, fc=fc: y.dma_start(
            out=xc[:], in_=x_fm[fc * 128:(fc + 1) * 128, :]))
        st = p.stage(f"lnc{fc}")
        st.s(lambda s: s.activation(out=sq16[:], in_=xc[:], func=AF.Square))
        st.v(lambda v: v.tensor_copy(out=xc16[:], in_=xc[:]))
        for tc in range(4):
            tsl = slice(tc * 512, (tc + 1) * 512)
            st = p.stage(f"lnm{fc}_{tc}")

            def mm(t, tsl=tsl):
                t.matmul(PSN1[:, :], onesb[:, 0:1], xc16[:, tsl],
                         start=True, stop=True)
                return t.matmul(PSN2[:, :], onesb[:, 0:1], sq16[:, tsl],
                                start=True, stop=True)
            st.t(mm)
            st = p.stage(f"lne{fc}_{tc}")

            def ev(v, tsl=tsl):
                v.tensor_add(out=sums[:, tsl], in0=sums[:, tsl], in1=PSN1[:, :])
                return v.tensor_add(out=sqs[:, tsl], in0=sqs[:, tsl], in1=PSN2[:, :])
            st.v(ev)

    st = p.stage("lnfin")

    def fin(v):
        v.tensor_scalar_mul(out=mean, in0=sums, scalar1=1.0 / 1024)
        v.tensor_scalar_mul(out=sqs, in0=sqs, scalar1=1.0 / 1024)
        v.tensor_mul(out=rstd, in0=mean, in1=mean)
        return v.tensor_sub(out=sqs, in0=sqs, in1=rstd)
    st.v(fin)
    st = p.stage("lnsqrt")
    st.s(lambda s: s.activation(out=sqs, in_=sqs, func=AF.Sqrt,
                                bias=cv[0:1, 0:1]))
    st = p.stage("lnrec")
    st.v(lambda v: v.reciprocal(out=rstd, in_=sqs))

    for fc in range(8):
        st = p.stage(f"lal{fc}")
        st.dma(lambda y, fc=fc: y.dma_start(
            out=xc[:], in_=x_fm[fc * 128:(fc + 1) * 128, :]))
        for tc in range(4):
            tsl = slice(tc * 512, (tc + 1) * 512)
            st = p.stage(f"lab{fc}_{tc}")

            def bc(t, tsl=tsl):
                t.matmul(PSM1[:, :], ones128[0:1, :], mean[:, tsl],
                         start=True, stop=True)
                return t.matmul(PSM2[:, :], ones128[32:33, :], rstd[:, tsl],
                                start=True, stop=True)
            st.t(bc)
            st = p.stage(f"laa{fc}_{tc}")

            def ap_(v, fc=fc, tsl=tsl):
                v.tensor_sub(out=xc[:, tsl], in0=xc[:, tsl], in1=PSM1[:, :])
                return v.tensor_mul(
                    out=xn16[:, fc * T + tsl.start:fc * T + tsl.stop],
                    in0=xc[:, tsl], in1=PSM2[:, :])
            st.v(ap_)

    for h in range(H):
        hs = h * 64
        for tc in range(4):
            tsl = slice(tc * 512, (tc + 1) * 512)
            st = p.stage(f"qk{h}_{tc}")

            def mmqk(t, h=h, tsl=tsl):
                for fc in range(8):
                    xs = xn16[:, fc * T + tsl.start:fc * T + tsl.stop]
                    t.matmul(PS1[0:64, :], wq16(fc)[:, h * 64:h * 64 + 64], xs,
                             start=(fc == 0), stop=(fc == 7))
                last = None
                for fc in range(8):
                    xs = xn16[:, fc * T + tsl.start:fc * T + tsl.stop]
                    last = t.matmul(PS2[0:64, :], wk16(fc)[:, h * 64:h * 64 + 64],
                                    xs, start=(fc == 0), stop=(fc == 7))
                return last
            st.t(mmqk)
            st = p.stage(f"qke{h}_{tc}")

            def epi(s, tsl=tsl):
                s.activation(out=sqq[:, :], in_=PS1[0:64, :], func=AF.Square)
                s.activation(out=qa[0:64, tsl], in_=PS1[0:64, :],
                             func=AF.Identity, scale=-2.0)
                return s.activation(out=ka[0:64, tsl], in_=PS2[0:64, :],
                                    func=AF.Identity)
            st.s(epi)
            st = p.stage(f"qn{h}_{tc}")
            st.t(lambda t: t.matmul(PSN1[:, :], onesb[0:64, 0:1], sqq[:, :],
                                    start=True, stop=True))
            st = p.stage(f"qne{h}_{tc}")
            st.v(lambda v, tsl=tsl: v.tensor_copy(out=qa[64:65, tsl],
                                                  in_=PSN1[:, :]))
            st = p.stage(f"ksq{h}_{tc}")
            st.s(lambda s, tsl=tsl: s.activation(out=sqq[:, :],
                                                 in_=PS2[0:64, :],
                                                 func=AF.Square))
            st = p.stage(f"kn{h}_{tc}")
            st.t(lambda t: t.matmul(PSN2[:, :], onesb[0:64, 0:1], sqq[:, :],
                                    start=True, stop=True))
            st = p.stage(f"kne{h}_{tc}")

            def kne(v, tsl=tsl):
                v.tensor_copy(out=knrow[:, tsl], in_=PSN2[:, :])
                return v.tensor_copy(out=knrow16[0:1, tsl], in_=PSN2[:, :])
            st.v(kne)
        st = p.stage(f"hst{h}")

        def hstore(y, h=h):
            return [
                y.dma_start(out=o_qaug[h], in_=qa[:]),
                y.dma_start(out=o_kaug[h, 0:65, :], in_=ka[0:65, :]),
                y.dma_start(out=o_kaug[h, 65:66, :], in_=knrow16[:]),
                y.dma_start(out=o_kn32[h:h + 1, :], in_=knrow),
            ]
        st.dma(hstore, n=4)


    st = p.stage("vtones")
    st.v(lambda v: [v.memset(vtbuf[:], 0.0)][-1])
    st = p.stage("vtones2")

    def vto(v):
        last = None
        for hh in range(16):
            last = v.memset(vtbuf[:, hh * 65 + 64:hh * 65 + 65], 1.0)
        return last
    st.v(vto)
    for tt in range(16):
        for vc in range(2):
            st = p.stage(f"vt{tt}_{vc}")

            def vmm(t, tt=tt, vc=vc):
                last = None
                for fc in range(8):
                    last = t.matmul(
                        PS1[:, :],
                        xn16[:, fc * T + tt * 128:fc * T + (tt + 1) * 128],
                        wv16(fc)[:, vc * 512:(vc + 1) * 512],
                        start=(fc == 0), stop=(fc == 7))
                return last
            st.t(vmm)
            st = p.stage(f"vte{tt}_{vc}")

            def vev(v, vc=vc):
                last = None
                for hh in range(8):
                    h = vc * 8 + hh
                    last = v.tensor_copy(
                        out=vtbuf[:, h * 65:h * 65 + 64],
                        in_=PS1[:, hh * 64:(hh + 1) * 64])
                return last
            st.v(vev)
        st = p.stage(f"vts{tt}")
        st.dma(lambda y, tt=tt: y.dma_start(
            out=o_vt[tt * 128:(tt + 1) * 128, :], in_=vtbuf[:]))

    for th in range(2):
        for oc in range(6):
            st = p.stage(f"fw{th}_{oc}")

            def fwload(g, oc=oc):
                out = []
                for fc in range(8):
                    out.append(g.dma_start(
                        out=wgc[:, fc * 512:(fc + 1) * 512],
                        in_=wg_t[fc * 128:(fc + 1) * 128, oc * 512:(oc + 1) * 512]))
                    out.append(g.dma_start(
                        out=wuc[:, fc * 512:(fc + 1) * 512],
                        in_=wu_t[fc * 128:(fc + 1) * 128, oc * 512:(oc + 1) * 512]))
                return out
            st.gdma(fwload, n=16)
            for pc in range(4):
                for tc in range(2):
                    tsl = slice(th * 1024 + tc * 512, th * 1024 + tc * 512 + 512)
                    csl = slice(tc * 512, tc * 512 + 512)
                    st = p.stage(f"fg{th}_{oc}_{pc}_{tc}")

                    def mmg(t, pc=pc, tsl=tsl):
                        for fc in range(8):
                            xs = xn16[:, fc * T + tsl.start:fc * T + tsl.stop]
                            t.matmul(PS1[:, :],
                                     wgc[:, fc * 512 + pc * 128:fc * 512 + pc * 128 + 128],
                                     xs, start=(fc == 0), stop=(fc == 7))
                        last = None
                        for fc in range(8):
                            xs = xn16[:, fc * T + tsl.start:fc * T + tsl.stop]
                            last = t.matmul(
                                PS2[:, :],
                                wuc[:, fc * 512 + pc * 128:fc * 512 + pc * 128 + 128],
                                xs, start=(fc == 0), stop=(fc == 7))
                        return last
                    st.t(mmg)
                    st = p.stage(f"fs{th}_{oc}_{pc}_{tc}")
                    st.s(lambda s: s.activation(out=sg[:], in_=PS1[:, :],
                                                func=AF.Silu, scale=1.0 / 127))
                    st = p.stage(f"fh{th}_{oc}_{pc}_{tc}")
                    st.v(lambda v, oc=oc, pc=pc, csl=csl: v.scalar_tensor_tensor(
                        out=hsl(oc * 4 + pc, csl), in0=sg[:], scalar=1.0 / 127,
                        in1=PS2[:, :], op0=ALU.mult, op1=ALU.mult))

        for op_ in range(8):
            st = p.stage(f"wdl{th}_{op_}")

            def wdload(g, op_=op_):
                return [g.dma_start(
                    out=wdbuf[:, hc * 128:(hc + 1) * 128],
                    in_=wd_t[hc * 128:(hc + 1) * 128, op_ * 128:(op_ + 1) * 128])
                    for hc in range(24)]
            st.gdma(wdload, n=24)
            for tc in range(2):
                tsl = slice(th * 1024 + tc * 512, th * 1024 + tc * 512 + 512)
                csl = slice(tc * 512, tc * 512 + 512)
                st = p.stage(f"gd{th}_{op_}_{tc}")

                def mmd(t, csl=csl):
                    last = None
                    for hc in range(24):
                        last = t.matmul(PS1[:, :],
                                        wdbuf[:, hc * 128:(hc + 1) * 128],
                                        hsl(hc, csl), start=(hc == 0),
                                        stop=(hc == 23))
                    return last
                st.t(mmd)
                st = p.stage(f"ge{th}_{op_}_{tc}")
                st.s(lambda s: s.activation(out=gout[:], in_=PS1[:, :],
                                            func=AF.Silu, scale=1.0 / 127))
                st = p.stage(f"gst{th}_{op_}_{tc}")
                st.dma(lambda y, op_=op_, tsl=tsl: y.dma_start(
                    out=o_g[op_ * 128:(op_ + 1) * 128, tsl], in_=gout[:]))



def _t(a):
    return np.ascontiguousarray(a.T)


def stage_a(inputs):
    x = inputs["L0_attn_x"]
    cvec = np.zeros((128, 4), np.float32)
    cvec[:, 0] = 1e-5
    shared = {
        "wq_t": _t(inputs["L0_attn_Wq"]), "wk_t": _t(inputs["L0_attn_Wk"]),
        "wv_t": _t(inputs["L0_attn_Wv"]),
        "wg_t": _t(inputs["L0_ffn_w_gate"]), "wu_t": _t(inputs["L0_ffn_w_up"]),
        "wd_t": _t(inputs["L0_ffn_w_down"]), "cvec": cvec,
    }
    maps = []
    for c in range(NC):
        b, r = c // 2, c % 2
        m = dict(shared)
        m["x_fm"] = _t(x[b, r * T:(r + 1) * T, :])
        maps.append(m)
    return maps


def kernel(**inputs):
    raise NotImplementedError("under construction")


def emit_b(p, D, exact=False, nheads=8, nqc=8, trunc=None):
    assert not exact, "exact variant is build_launch_bx"
    """L0 hyperbolic attention middle (per core: 8 heads, all 4096 tokens).

    In : qaug_g [8,66,S] bf16 rows [-2q; qn; ones]
         kaug_g [8,66,S] bf16 rows [k; ones; kn]
         v_g    [8,S,65] bf16 (token-major, col 64 = 1.0)
         knt    [8,128,32] f32 (kn tiled: [p, kt] = kn[kt*128+p])
         qnt    [8,128,32] f32 (same for qn; used by exact path)
         scal   [1,2] f32 = [log_c, log_beta]
         masks  [128,4*512] f32 additive causal masks (0 / +1e30),
                block mi at cols [mi*512,(mi+1)*512): mask for kk-tile offset mi*128
         cvecb  [128,8] f32: c0=1e-8 c1=1.0 c2=1.5576e-8 c3=thr(4.0001)
    Out: o16 [8,64,S] bf16 (normalized attention out, feature-major)
         flag [1,1] f32 (#kk-partitions with min z below threshold; fast only)

    Fast path: P = exp(-beta/2 * ln(z+eps) + (beta*c/4)(knmean - kn)), exact
    for the ed>2 branch; detection of any z < 4 via running min -> flag.
    Exact path: full 3-branch dist, same dropped per-q factor class (global
    shift only), overwrite o16.
    """
    i_q = D("qag", [2 * 1056, T], BF16, "ExternalInput")
    i_k = D("kag", [2 * 1056, T], BF16, "ExternalInput")
    i_v = D("vag", [2 * T, 16 * 65], BF16, "ExternalInput")
    i_kn = D("knag", [32, T], F32, "ExternalInput")
    i_hsel = D("hsel", [1, 1], I32, "ExternalInput")
    i_id32 = D("id32", [32, 32], F32, "ExternalInput")
    i_scal = D("scal", [1, 2], F32, "ExternalInput")
    i_masks = D("masks", [128, 4 * 512], F32, "ExternalInput")
    i_cv = D("cvecb", [128, 8], F32, "ExternalInput")
    o_o = D("o16", [8, 64, S], BF16, "ExternalOutput")
    o_flag = D("flag", [1, 1], F32, "ExternalOutput")

    cv = p.sbuf("cv", [128, 8], F32)
    ka = p.sbuf("ka", [66, S], BF16)
    qa = p.sbuf("qa", [66, S], BF16)
    vsb = p.sbuf("vsb", [128, 32 * 65], BF16)
    masks = p.sbuf("maskssb", [128, 4 * 512], F32)
    ptile = p.sbuf("ptile", [128, 4 * 512], BF16)
    lz = p.sbuf("lz", [128, 4 * 512], F32)
    knt = p.sbuf("kntsb", [128, 32], F32)
    knrows = p.sbuf("knrows", [32, 128], F32)
    hselb = p.sbuf("hselb", [1, 1], I32)
    id32b = p.sbuf("id32b", [32, 32], F32)
    bias = p.sbuf("biassb", [128, 32], F32)
    dacc = p.sbuf("dacc", [128, 1], F32)
    rmin = p.sbuf("rmin", [128, 1], F32)
    sc = p.sbuf("sc", [33, 8], F32)
    scb = p.sbuf("scb", [128, 4], F32)
    onesb = p.sbuf("onesb2", [128, 1], BF16)
    ones128 = p.sbuf("ones128b", [1, 128], F32)
    floor16 = p.sbuf("floor16", [1, 66], BF16)
    fval = p.sbuf("fval", [1, 512], BF16)
    rrow = p.sbuf("rrow", [1, 512], F32)
    otile = p.sbuf("otile", [64, 512], F32)
    obuf = p.sbuf("obuf", [64, 512], BF16)
    m128 = p.sbuf("m128", [128, 1], F32)
    m16 = p.sbuf("m16", [128, 1], BF16)
    flagrow = p.sbuf("flagrow", [1, 1], F32)

    PSZ = [p.psum(f"PSZ{i}", [128, 512], F32) for i in range(4)]
    PSO = p.psum("PSO", [65, 512], F32)
    PSB = p.psum("PSB", [64, 512], F32)
    PSN = p.psum("PSN", [1, 512], F32)

    st = p.stage("b_consts")

    def cload(y):
        return [y.dma_start(out=cv[:], in_=i_cv),
                y.dma_start(out=sc[0:1, 0:2], in_=i_scal),
                y.dma_start(out=masks[:], in_=i_masks),
                y.dma_start(out=hselb[:], in_=i_hsel),
                y.dma_start(out=id32b[:], in_=i_id32)]
    st.dma(cload, n=5)
    st = p.stage("b_init")

    def binit(v):
        v.memset(onesb[:], 1.0)
        v.memset(ones128[:], 1.0)
        v.memset(dacc[:], 1e30)
        v.memset(floor16[:], 0.0)
        v.memset(fval[:], 1e-30)
        return v.memset(flagrow[:], 0.0)
    st.v(binit)
    st = p.stage("b_init2")
    st.v(lambda v: v.memset(floor16[0:1, 64:65], 1.0))

    st = p.stage("b_sp1")
    st.s(lambda s: s.activation(out=sc[0:1, 2:4], in_=sc[0:1, 0:2], func=AF.Abs))
    st = p.stage("b_sp2")
    st.s(lambda s: s.activation(out=sc[0:1, 2:4], in_=sc[0:1, 2:4], func=AF.Exp,
                                scale=-1.0))
    st = p.stage("b_sp3")
    st.s(lambda s: s.activation(out=sc[0:1, 2:4], in_=sc[0:1, 2:4], func=AF.Ln,
                                bias=cv[0:1, 1:2]))
    st = p.stage("b_sp3b")
    st.s(lambda s: s.activation(out=sc[0:1, 4:6], in_=sc[0:1, 0:2],
                                func=AF.Relu))
    st = p.stage("b_sp4")
    st.v(lambda v: v.tensor_add(out=sc[0:1, 4:6], in0=sc[0:1, 4:6],
                                in1=sc[0:1, 2:4]))
    st = p.stage("b_sp5")
    st.v(lambda v: v.tensor_scalar_add(out=sc[0:1, 5:6], in0=sc[0:1, 5:6],
                                       scalar1=0.5))
    st = p.stage("b_sp6")
    st.v(lambda v: v.tensor_scalar_mul(out=sc[0:1, 6:7], in0=sc[0:1, 5:6],
                                       scalar1=-0.5))
    st = p.stage("b_sp7")
    st.v(lambda v: v.tensor_mul(out=sc[0:1, 7:8], in0=sc[0:1, 4:5],
                                in1=sc[0:1, 5:6]))
    st = p.stage("b_sp8")
    st.v(lambda v: v.tensor_scalar_mul(out=sc[0:1, 7:8], in0=sc[0:1, 7:8],
                                       scalar1=-0.25))
    st = p.stage("b_spb")

    def spb(t):
        t.matmul(PSZ[0][:, 0:1], ones128[0:1, :], sc[0:1, 6:7],
                 start=True, stop=True)
        return t.matmul(PSZ[0][:, 1:2], ones128[0:1, :], sc[0:1, 7:8],
                        start=True, stop=True)
    st.t(spb)
    st = p.stage("b_spb2")

    def spb2(v):
        v.tensor_copy(out=scb[:, 0:1], in_=PSZ[0][:, 0:1])
        return v.tensor_copy(out=scb[:, 1:2], in_=PSZ[0][:, 1:2])
    st.v(spb2)

    for h in range(nheads):
        st = p.stage(f"bh{h}_load")

        def hload(y, h=h):
            insts = []

            def emit_half(hg):
                gh = hg * 8 + h
                for rh in range(2):
                    insts.append(y.dma_start(
                        out=ka[:, rh * T:(rh + 1) * T],
                        in_=i_k[rh * 1056 + gh * 66:rh * 1056 + (gh + 1) * 66, :]))
                    insts.append(y.dma_start(
                        out=qa[:, rh * T:(rh + 1) * T],
                        in_=i_q[rh * 1056 + gh * 66:rh * 1056 + (gh + 1) * 66, :]))
                    insts.append(y.dma_start(
                        out=vsb[:].rearrange("p (kt d) -> p kt d", d=65)[
                            :, rh * 16:(rh + 1) * 16, :],
                        in_=i_v[rh * T:(rh + 1) * T, :].rearrange(
                            "(kt p) c -> p kt c", p=128)[
                            :, :, gh * 65:(gh + 1) * 65]))
                    insts.append(y.dma_start(
                        out=knrows[rh * 16:(rh + 1) * 16, :],
                        in_=i_kn[rh * 16 + gh:rh * 16 + gh + 1, :].rearrange(
                            "one (a b) -> (one a) b", b=128)))
            with y.register() as rg:
                y.reg_load(rg, hselb[0:1, 0:1])
                with y.If_ne(rg, 0):
                    emit_half(1)
                with y.Else():
                    emit_half(0)
                y.end_ifs()
            return insts
        st.dma(hload, n=8)
        st = p.stage(f"bh{h}_kt")
        st.t(lambda t: t.transpose(PSZ[0][:, 0:32], knrows[:, :], id32b[:, :]))
        st = p.stage(f"bh{h}_kte")
        st.v(lambda v: v.tensor_copy(out=knt[:], in_=PSZ[0][:, 0:32]))
        st = p.stage(f"bh{h}_b1")
        st.v(lambda v: v.reduce_sum(out=rmin[:], in_=knt[:], axis=AX))
        st = p.stage(f"bh{h}_b2")
        st.v(lambda v: v.tensor_copy(out=m16[:], in_=rmin[:]))
        st = p.stage(f"bh{h}_b3")
        st.t(lambda t: t.matmul(PSN[0:1, 0:1], onesb[:, 0:1], m16[:, 0:1],
                                start=True, stop=True))
        st = p.stage(f"bh{h}_b4")
        st.v(lambda v: v.tensor_scalar_mul(out=sc[0:1, 1:2], in0=PSN[0:1, 0:1],
                                           scalar1=1.0 / 4096))
        st = p.stage(f"bh{h}_b5")
        st.t(lambda t: t.matmul(PSZ[0][:, 0:1], ones128[0:1, :], sc[0:1, 1:2],
                                start=True, stop=True))
        st = p.stage(f"bh{h}_b5b")
        st.v(lambda v: v.tensor_copy(out=m128[:, 0:1], in_=PSZ[0][:, 0:1]))
        st = p.stage(f"bh{h}_b6")
        st.v(lambda v: v.tensor_scalar(out=bias[:], in0=knt[:],
                                       scalar1=m128[:, 0:1],
                                       scalar2=scb[:, 1:2], op0=ALU.subtract,
                                       op1=ALU.mult))

        for qc in range(nqc):
            qsl = slice(qc * 512, (qc + 1) * 512)
            ngr = qc + 1
            for gidx in range(ngr):
                kts = [gidx * 4 + j for j in range(4)]
                st = p.stage(f"bh{h}_z{qc}_{gidx}")

                def zmm(t, h=h, qsl=qsl, kts=kts):
                    out = None
                    for j, kt in enumerate(kts):
                        ksl = slice(kt * 128, (kt + 1) * 128)
                        out = t.matmul(PSZ[j][:, :], ka[0:66, ksl],
                                       qa[0:66, qsl], start=True, stop=True)
                        if exact:
                            out = t.matmul(PSB[0:2, :] if False else PSZ[j][:, :],
                                           ka[64:66, ksl], qa[64:66, qsl],
                                           start=False, stop=True)
                    return out
                st.t(zmm)
                st = p.stage(f"bh{h}_e{qc}_{gidx}")

                def ew(v, h=h, qc=qc, kts=kts):
                    last = None
                    for j, kt in enumerate(kts):
                        v.tensor_reduce(out=rmin[:], in_=PSZ[j][:, :], axis=AX,
                                        op=ALU.min)
                        last = v.tensor_tensor(out=dacc[:], in0=dacc[:],
                                               in1=rmin[:], op=ALU.min)
                    return last
                st.v(ew)
                st = p.stage(f"bh{h}_ea{qc}_{gidx}")

                def ew_act(s, qc=qc, kts=kts):
                    last = None
                    for j, kt in enumerate(kts):
                        psl = slice(j * 512, (j + 1) * 512)
                        last = s.activation(out=lz[:, psl], in_=PSZ[j][:, :],
                                            func=AF.Ln, bias=cv[:, 0:1])
                    return last
                st.s(ew_act)
                if any(0 <= kt * 128 - qc * 512 < 512 for kt in kts):
                    st = p.stage(f"bh{h}_m{qc}_{gidx}")

                    def ewm(v, qc=qc, kts=kts):
                        last = None
                        for j, kt in enumerate(kts):
                            d = kt * 128 - qc * 512
                            if 0 <= d < 512:
                                mi = d // 128
                                psl = slice(j * 512, (j + 1) * 512)
                                last = v.tensor_add(
                                    out=lz[:, psl], in0=lz[:, psl],
                                    in1=masks[:, mi * 512:(mi + 1) * 512])
                        return last
                    st.v(ewm)
                st = p.stage(f"bh{h}_x{qc}_{gidx}")

                def ewx(s, kts=kts):
                    last = None
                    for j, kt in enumerate(kts):
                        psl = slice(j * 512, (j + 1) * 512)
                        last = s.activation(out=ptile[:, psl], in_=lz[:, psl],
                                            func=AF.Exp, scale=scb[:, 0:1],
                                            bias=bias[:, kt:kt + 1])
                    return last
                st.s(ewx)
                st = p.stage(f"bh{h}_pv{qc}_{gidx}")

                def pv(t, qc=qc, gidx=gidx, kts=kts):
                    last = None
                    for j, kt in enumerate(kts):
                        psl = slice(j * 512, (j + 1) * 512)
                        last = t.matmul(PSO[:, :],
                                        vsb[:, kt * 65:(kt + 1) * 65],
                                        ptile[:, psl],
                                        start=(gidx == 0 and j == 0), stop=False)
                    if gidx == qc:
                        last = t.matmul(PSO[:, :], floor16[0:1, 0:65],
                                        fval[0:1, :], start=False, stop=True)
                    return last
                st.t(pv)
            st = p.stage(f"bh{h}_n{qc}")
            st.v(lambda v: v.reciprocal(out=rrow[:], in_=PSO[64:65, :]))
            st = p.stage(f"bh{h}_ne{qc}")
            st.s(lambda s: s.activation(out=otile[:], in_=PSO[0:64, :],
                                        func=AF.Identity))
            st = p.stage(f"bh{h}_nb{qc}")
            st.t(lambda t: t.matmul(PSB[:, :], ones128[0:1, 0:64], rrow[0:1, :],
                                    start=True, stop=True))
            st = p.stage(f"bh{h}_nm{qc}")
            st.v(lambda v: v.tensor_mul(out=obuf[:], in0=otile[:], in1=PSB[:, :]))
            st = p.stage(f"bh{h}_st{qc}")
            st.dma(lambda y, h=h, qsl=qsl: y.dma_start(out=o_o[h][:, qsl],
                                                       in_=obuf[:]))

    st = p.stage("b_flag1")
    st.v(lambda v: v.tensor_scalar(out=m128[:], in0=dacc[:],
                                   scalar1=cv[:, 3:4], scalar2=0.0,
                                   op0=ALU.is_lt, op1=ALU.add))
    st = p.stage("b_flag2")
    st.v(lambda v: v.tensor_copy(out=m16[:], in_=m128[:]))
    st = p.stage("b_flag3")
    st.t(lambda t: t.matmul(PSN[0:1, 0:1], onesb[:, 0:1], m16[:, 0:1],
                            start=True, stop=True))
    st = p.stage("b_flag4")
    st.v(lambda v: v.tensor_copy(out=flagrow[:], in_=PSN[0:1, 0:1]))
    st = p.stage("b_flag5")
    st.dma(lambda y: y.dma_start(out=o_flag, in_=flagrow[:]))
    if trunc is not None:
        p.stages = p.stages[:trunc]


def _causal_masks_b():
    out = np.zeros((128, 4 * 512), np.float32)
    for mi in range(4):
        kk = np.arange(128)[:, None] + mi * 128
        q = np.arange(512)[None, :]
        out[:, mi * 512:(mi + 1) * 512] = np.where(kk > q, 1e30, 0.0).astype(np.float32)
    return out


def stage_b(inputs, a_res):
    masks = _causal_masks_b()
    cvb = np.zeros((128, 8), np.float32)
    cvb[:, 0] = 1e-8
    cvb[:, 1] = 1.0
    cvb[:, 3] = 4.0001
    scal = np.array([[float(inputs["L0_attn_log_c"]),
                      float(inputs["L0_attn_log_beta"])]], np.float32)
    id32 = np.eye(32, dtype=np.float32)
    maps = []
    for c in range(NC):
        b, hg = c // 2, c % 2
        ra, rb = a_res[2 * b], a_res[2 * b + 1]
        qag = np.concatenate([ra["qaug"].reshape(1056, T),
                              rb["qaug"].reshape(1056, T)], axis=0)
        kag = np.concatenate([ra["kaug"].reshape(1056, T),
                              rb["kaug"].reshape(1056, T)], axis=0)
        vag = np.concatenate([ra["vtok"], rb["vtok"]], axis=0)
        knag = np.concatenate([ra["kn32"], rb["kn32"]], axis=0)
        maps.append({
            "qag": np.ascontiguousarray(qag),
            "kag": np.ascontiguousarray(kag),
            "vag": np.ascontiguousarray(vag),
            "knag": np.ascontiguousarray(knag),
            "hsel": np.array([[hg]], np.int32), "id32": id32,
            "scal": scal, "masks": masks, "cvecb": cvb,
        })
    return maps


def emit_c(p, D):
    x_fm = D("x_fm", [1024, T], F32, "ExternalInput")
    i_oag = D("oag", [2 * 512, S], BF16, "ExternalInput")
    i_tsel = D("tsel", [1, 1], I32, "ExternalInput")
    wo_t = D("wo_t", [1024, 1024], F32, "ExternalInput")
    g_in = D("g_in", [1024, T], F32, "ExternalInput")
    w_qkv = D("w_qkv", [1024, 3 * 1024], F32, "ExternalInput")
    cosrep = D("cosrep", [128, T], F32, "ExternalInput")
    sinsgn = D("sinsgn", [128, T], F32, "ExternalInput")
    cvec = D("cvec", [128, 4], F32, "ExternalInput")
    o_x1 = D("x1", [1024, T], F32, "ExternalOutput")
    o_xn1 = D("xn1f", [1024, T], F32, "ExternalOutput")
    o_q = D("q1r", [16, 64, T], BF16, "ExternalOutput")
    o_k = D("k1r", [16, 64, T], BF16, "ExternalOutput")
    o_vt = D("v1tok", [T, 16 * 65], BF16, "ExternalOutput")

    cv = p.sbuf("cv", [128, 4], F32)
    x1 = p.sbuf("x1b", [128, 8 * T], F32)
    xn16 = p.sbuf("xn16", [128, 8 * T], BF16)
    osb = p.sbuf("osb", [128, 8 * T], BF16)
    tselb = p.sbuf("tselb", [1, 1], I32)
    vtbuf = p.sbuf("vtbufc", [128, 16 * 65], BF16)
    wbuf = p.sbuf("wbuf", [128, 8 * 1024], BF16)
    gch = p.sbuf("gch", [128, T], F32)
    cosb = p.sbuf("cosb", [128, T], F32)
    sinb = p.sbuf("sinb", [128, T], F32)
    rowsA = p.sbuf("rowsA", [65, T], F32)
    rowsB = p.sbuf("rowsB", [33, T], F32)
    ones128 = p.sbuf("ones128", [33, 128], F32)
    onesb = p.sbuf("onesb", [128, 1], BF16)
    sq16 = p.sbuf("sq16", [128, T], BF16)
    xc16 = p.sbuf("xc16", [128, T], BF16)
    tmp = p.sbuf("tmpc", [128, 512], F32)
    tmq = p.sbuf("tmqc", [128, 512], F32)
    qtile = p.sbuf("qtile", [128, T], BF16)
    sums = rowsA[0:1, :]
    sqs = rowsA[32:33, :]
    mean = rowsB[0:1, :]
    rstd = rowsB[32:33, :]

    PS1 = p.psum("PS1", [128, 512], F32)
    PSN1 = p.psum("PSN1", [1, 512], F32)
    PSN2 = p.psum("PSN2", [1, 512], F32)
    PSM1 = p.psum("PSM1", [128, 512], F32)
    PSM2 = p.psum("PSM2", [128, 512], F32)

    st = p.stage("c_consts")

    def cl(y):
        return [y.dma_start(out=cv[:], in_=cvec),
                y.dma_start(out=cosb[:], in_=cosrep),
                y.dma_start(out=sinb[:], in_=sinsgn),
                y.dma_start(out=tselb[:], in_=i_tsel)]
    st.dma(cl, n=4)
    st = p.stage("c_init")

    def ci(v):
        v.memset(ones128[:], 1.0)
        v.memset(onesb[:], 1.0)
        v.memset(sums, 0.0)
        return v.memset(sqs, 0.0)
    st.v(ci)
    st = p.stage("c_wload")

    def cw(g):
        out = []
        for fc in range(8):
            out.append(g.dma_start(out=wbuf[:, fc * 1024:(fc + 1) * 1024],
                                   in_=wo_t[fc * 128:(fc + 1) * 128, :]))
        with g.register() as rg:
            g.reg_load(rg, tselb[0:1, 0:1])
            with g.If_ne(rg, 0):
                for fc in range(8):
                    out.append(g.dma_start(
                        out=osb[:, fc * T:(fc + 1) * T],
                        in_=i_oag[fc * 128:(fc + 1) * 128, T:2 * T]))
            with g.Else():
                for fc in range(8):
                    out.append(g.dma_start(
                        out=osb[:, fc * T:(fc + 1) * T],
                        in_=i_oag[fc * 128:(fc + 1) * 128, 0:T]))
            g.end_ifs()
        return out
    st.gdma(cw, n=16)

    for op_ in range(8):
        st = p.stage(f"cx{op_}")

        def cx(y, op_=op_):
            return [y.dma_start(out=gch[:], in_=g_in[op_ * 128:(op_ + 1) * 128, :]),
                    y.dma_start(out=x1[:, op_ * T:(op_ + 1) * T],
                                in_=x_fm[op_ * 128:(op_ + 1) * 128, :])]
        st.dma(cx, n=2)
        for tc in range(4):
            tsl = slice(tc * 512, (tc + 1) * 512)
            st = p.stage(f"cs{op_}_{tc}")

            def mms(t, op_=op_, tsl=tsl):
                last = None
                for fc in range(8):
                    last = t.matmul(
                        PS1[:, :],
                        wbuf[:, fc * 1024 + op_ * 128:fc * 1024 + op_ * 128 + 128],
                        osb[:, fc * T + tsl.start:fc * T + tsl.stop],
                        start=(fc == 0), stop=(fc == 7))
                return last
            st.t(mms)
            st = p.stage(f"ce{op_}_{tc}")

            def ep(v, op_=op_, tsl=tsl):
                xs = x1[:, op_ * T + tsl.start:op_ * T + tsl.stop]
                v.tensor_mul(out=tmp[:], in0=gch[:, tsl], in1=PS1[:, :])
                return v.tensor_add(out=xs, in0=xs, in1=tmp[:])
            st.v(ep)
        st = p.stage(f"cxs{op_}")
        st.dma(lambda y, op_=op_: y.dma_start(
            out=o_x1[op_ * 128:(op_ + 1) * 128, :],
            in_=x1[:, op_ * T:(op_ + 1) * T]))
        st = p.stage(f"cq{op_}")

        def csq(s, op_=op_):
            return s.activation(out=sq16[:], in_=x1[:, op_ * T:(op_ + 1) * T],
                                func=AF.Square)
        st.s(csq)
        st.v(lambda v, op_=op_: v.tensor_copy(
            out=xc16[:], in_=x1[:, op_ * T:(op_ + 1) * T]))
        for tc in range(4):
            tsl = slice(tc * 512, (tc + 1) * 512)
            st = p.stage(f"cm{op_}_{tc}")

            def mm(t, tsl=tsl):
                t.matmul(PSN1[:, :], onesb[:, 0:1], xc16[:, tsl],
                         start=True, stop=True)
                return t.matmul(PSN2[:, :], onesb[:, 0:1], sq16[:, tsl],
                                start=True, stop=True)
            st.t(mm)
            st = p.stage(f"cme{op_}_{tc}")

            def ev(v, tsl=tsl):
                v.tensor_add(out=sums[:, tsl], in0=sums[:, tsl], in1=PSN1[:, :])
                return v.tensor_add(out=sqs[:, tsl], in0=sqs[:, tsl],
                                    in1=PSN2[:, :])
            st.v(ev)

    st = p.stage("cfin")

    def fin(v):
        v.tensor_scalar_mul(out=mean, in0=sums, scalar1=1.0 / 1024)
        v.tensor_scalar_mul(out=sqs, in0=sqs, scalar1=1.0 / 1024)
        v.tensor_mul(out=rstd, in0=mean, in1=mean)
        return v.tensor_sub(out=sqs, in0=sqs, in1=rstd)
    st.v(fin)
    st = p.stage("csqrt")
    st.s(lambda s: s.activation(out=sqs, in_=sqs, func=AF.Sqrt,
                                bias=cv[0:1, 0:1]))
    st = p.stage("crec")
    st.v(lambda v: v.reciprocal(out=rstd, in_=sqs))

    for fc in range(8):
        for tc in range(4):
            tsl = slice(tc * 512, (tc + 1) * 512)
            st = p.stage(f"cb{fc}_{tc}")

            def bc(t, tsl=tsl):
                t.matmul(PSM1[:, :], ones128[0:1, :], mean[:, tsl],
                         start=True, stop=True)
                return t.matmul(PSM2[:, :], ones128[32:33, :], rstd[:, tsl],
                                start=True, stop=True)
            st.t(bc)
            st = p.stage(f"ca{fc}_{tc}")

            def ap_(v, fc=fc, tsl=tsl):
                xs = x1[:, fc * T + tsl.start:fc * T + tsl.stop]
                v.tensor_sub(out=tmp[:], in0=xs, in1=PSM1[:, :])
                v.tensor_mul(out=tmp[:], in0=tmp[:], in1=PSM2[:, :])
                return v.tensor_copy(
                    out=xn16[:, fc * T + tsl.start:fc * T + tsl.stop],
                    in_=tmp[:])
            st.v(ap_)
            st = p.stage(f"cao{fc}_{tc}")
            st.dma(lambda y, fc=fc, tsl=tsl: y.dma_start(
                out=o_xn1[fc * 128:(fc + 1) * 128, tsl], in_=tmp[:]))

    for oc in range(16):
        part = oc // 8
        if oc % 8 == 0:
            st = p.stage(f"cwq{oc}")

            def cwq(g, oc=oc):
                return [g.dma_start(
                    out=wbuf[:, fc * 1024:(fc + 1) * 1024],
                    in_=w_qkv[fc * 128:(fc + 1) * 128,
                              (oc // 8) * 1024:(oc // 8 + 1) * 1024])
                    for fc in range(8)]
            st.gdma(cwq, n=8)
        for tc in range(4):
            tsl = slice(tc * 512, (tc + 1) * 512)
            st = p.stage(f"cp{oc}_{tc}")

            def mmp(t, oc=oc, tsl=tsl):
                last = None
                for fc in range(8):
                    last = t.matmul(
                        PS1[:, :],
                        wbuf[:, fc * 1024 + (oc % 8) * 128:
                             fc * 1024 + (oc % 8) * 128 + 128],
                        xn16[:, fc * T + tsl.start:fc * T + tsl.stop],
                        start=(fc == 0), stop=(fc == 7))
                return last
            st.t(mmp)
            if part < 2:
                st = p.stage(f"cr{oc}_{tc}")

                def rot(v, tsl=tsl):
                    v.tensor_copy(out=tmq[0:32, :], in_=PS1[32:64, :])
                    v.tensor_copy(out=tmq[32:64, :], in_=PS1[0:32, :])
                    v.tensor_copy(out=tmq[64:96, :], in_=PS1[96:128, :])
                    return v.tensor_copy(out=tmq[96:128, :], in_=PS1[64:96, :])
                st.v(rot)
                st = p.stage(f"cr2{oc}_{tc}")

                def rope(v, tsl=tsl):
                    v.tensor_mul(out=tmp[:], in0=cosb[:, tsl], in1=PS1[:, :])
                    v.tensor_mul(out=tmq[:, :], in0=tmq[:, :], in1=sinb[:, tsl])
                    return v.tensor_add(out=tmp[:], in0=tmp[:], in1=tmq[:, :])
                st.v(rope)
                st = p.stage(f"cr3{oc}_{tc}")
                st.v(lambda v, tsl=tsl: v.tensor_copy(out=qtile[:, tsl],
                                                      in_=tmp[:]))
        st = p.stage(f"cst{oc}")
        dst = (o_q, o_k)[part]
        h0 = (oc % 8) * 2

        def stq(y, dst=dst, h0=h0):
            return [y.dma_start(out=dst[h0], in_=qtile[0:64, :]),
                    y.dma_start(out=dst[h0 + 1], in_=qtile[64:128, :])]
        st.dma(stq, n=2)

    st = p.stage("cvw")
    st.gdma(lambda g: [g.dma_start(
        out=wbuf[:, fc * 1024:(fc + 1) * 1024],
        in_=w_qkv[fc * 128:(fc + 1) * 128, 2048:3072]) for fc in range(8)], n=8)
    st = p.stage("cvones")

    def cvo(v):
        v.memset(vtbuf[:], 0.0)
        last = None
        for hh in range(16):
            last = v.memset(vtbuf[:, hh * 65 + 64:hh * 65 + 65], 1.0)
        return last
    st.v(cvo)
    for tt in range(16):
        for vc in range(2):
            st = p.stage(f"cvt{tt}_{vc}")

            def cvmm(t, tt=tt, vc=vc):
                last = None
                for fc in range(8):
                    last = t.matmul(
                        PS1[:, :],
                        xn16[:, fc * T + tt * 128:fc * T + (tt + 1) * 128],
                        wbuf[:, fc * 1024 + vc * 512:fc * 1024 + (vc + 1) * 512],
                        start=(fc == 0), stop=(fc == 7))
                return last
            st.t(cvmm)
            st = p.stage(f"cve{tt}_{vc}")

            def cvev(v, vc=vc):
                last = None
                for hh in range(8):
                    h = vc * 8 + hh
                    last = v.tensor_copy(
                        out=vtbuf[:, h * 65:h * 65 + 64],
                        in_=PS1[:, hh * 64:(hh + 1) * 64])
                return last
            st.v(cvev)
        st = p.stage(f"cvs{tt}")
        st.dma(lambda y, tt=tt: y.dma_start(
            out=o_vt[tt * 128:(tt + 1) * 128, :], in_=vtbuf[:]))


def emit_d(p, D, nheads=8, nqc=8):
    i_q = D("qag1", [2 * 1024, T], BF16, "ExternalInput")
    i_k = D("kag1", [2 * 1024, T], BF16, "ExternalInput")
    i_v = D("vag1", [2 * T, 16 * 65], BF16, "ExternalInput")
    i_hsel = D("hseld", [1, 1], I32, "ExternalInput")
    i_masks = D("maskd", [128, 4 * 512], F32, "ExternalInput")
    o_o = D("o16", [8, 64, S], BF16, "ExternalOutput")

    ka = p.sbuf("ka", [64, S], BF16)
    qa = p.sbuf("qa", [64, S], BF16)
    vsb = p.sbuf("vsb", [128, 32 * 65], BF16)
    hselb = p.sbuf("hselbd", [1, 1], I32)
    masks = p.sbuf("maskssb", [128, 4 * 512], F32)
    ptile = p.sbuf("ptile", [128, 4 * 512], BF16)
    lzd = p.sbuf("lzd", [128, 512], F32)
    ones128 = p.sbuf("ones128", [1, 128], F32)
    floor16 = p.sbuf("floor16", [1, 66], BF16)
    fval = p.sbuf("fval", [1, 512], BF16)
    rrow = p.sbuf("rrow", [1, 512], F32)
    otile = p.sbuf("otile", [64, 512], F32)
    obuf = p.sbuf("obuf", [64, 512], BF16)

    PSZ = [p.psum(f"PSZ{i}", [128, 512], F32) for i in range(4)]
    PSO = p.psum("PSO", [65, 512], F32)
    PSB = p.psum("PSB", [64, 512], F32)

    st = p.stage("d_consts")
    st.dma(lambda y: [y.dma_start(out=masks[:], in_=i_masks),
                      y.dma_start(out=hselb[:], in_=i_hsel)], n=2)
    st = p.stage("d_init")

    def di(v):
        v.memset(ones128[:], 1.0)
        v.memset(floor16[:], 0.0)
        return v.memset(fval[:], 1e-30)
    st.v(di)
    st = p.stage("d_init2")
    st.v(lambda v: v.memset(floor16[0:1, 64:65], 1.0))

    for h in range(nheads):
        st = p.stage(f"dh{h}_load")

        def hload(y, h=h):
            insts = []

            def emit_half(hg):
                gh = hg * 8 + h
                for rh in range(2):
                    insts.append(y.dma_start(
                        out=ka[:, rh * T:(rh + 1) * T],
                        in_=i_k[rh * 1024 + gh * 64:rh * 1024 + (gh + 1) * 64, :]))
                    insts.append(y.dma_start(
                        out=qa[:, rh * T:(rh + 1) * T],
                        in_=i_q[rh * 1024 + gh * 64:rh * 1024 + (gh + 1) * 64, :]))
                    insts.append(y.dma_start(
                        out=vsb[:].rearrange("p (kt d) -> p kt d", d=65)[
                            :, rh * 16:(rh + 1) * 16, :],
                        in_=i_v[rh * T:(rh + 1) * T, :].rearrange(
                            "(kt p) c -> p kt c", p=128)[
                            :, :, gh * 65:(gh + 1) * 65]))
            with y.register() as rg:
                y.reg_load(rg, hselb[0:1, 0:1])
                with y.If_ne(rg, 0):
                    emit_half(1)
                with y.Else():
                    emit_half(0)
                y.end_ifs()
            return insts
        st.dma(hload, n=6)
        for qc in range(nqc):
            qsl = slice(qc * 512, (qc + 1) * 512)
            for gidx in range(qc + 1):
                kts = [gidx * 4 + j for j in range(4)]
                st = p.stage(f"dh{h}_z{qc}_{gidx}")

                def zmm(t, qsl=qsl, kts=kts):
                    out = None
                    for j, kt in enumerate(kts):
                        ksl = slice(kt * 128, (kt + 1) * 128)
                        out = t.matmul(PSZ[j][:, :], ka[0:64, ksl],
                                       qa[0:64, qsl], start=True, stop=True)
                    return out
                st.t(zmm)
                diag = [(j, kt) for j, kt in enumerate(kts)
                        if 0 <= kt * 128 - qc * 512 < 512]
                if diag:
                    for j, kt in diag:
                        st = p.stage(f"dh{h}_m{qc}_{gidx}_{j}")
                        mi = (kt * 128 - qc * 512) // 128

                        def ewm(v, j=j, mi=mi):
                            return v.tensor_add(
                                out=lzd[:], in0=PSZ[j][:, :],
                                in1=masks[:, mi * 512:(mi + 1) * 512])
                        st.v(ewm)
                        st = p.stage(f"dh{h}_mx{qc}_{gidx}_{j}")
                        st.s(lambda s, j=j: s.activation(
                            out=ptile[:, j * 512:(j + 1) * 512], in_=lzd[:],
                            func=AF.Exp, scale=0.125))
                ndiag = [(j, kt) for j, kt in enumerate(kts)
                         if not (0 <= kt * 128 - qc * 512 < 512)]
                if ndiag:
                    st = p.stage(f"dh{h}_x{qc}_{gidx}")

                    def ewx(s, ndiag=ndiag):
                        last = None
                        for j, kt in ndiag:
                            last = s.activation(
                                out=ptile[:, j * 512:(j + 1) * 512],
                                in_=PSZ[j][:, :], func=AF.Exp, scale=0.125)
                        return last
                    st.s(ewx)
                st = p.stage(f"dh{h}_pv{qc}_{gidx}")

                def pv(t, qc=qc, gidx=gidx, kts=kts):
                    last = None
                    for j, kt in enumerate(kts):
                        last = t.matmul(PSO[:, :],
                                        vsb[:, kt * 65:(kt + 1) * 65],
                                        ptile[:, j * 512:(j + 1) * 512],
                                        start=(gidx == 0 and j == 0),
                                        stop=False)
                    if gidx == qc:
                        last = t.matmul(PSO[:, :], floor16[0:1, 0:65],
                                        fval[0:1, :], start=False, stop=True)
                    return last
                st.t(pv)
            st = p.stage(f"dh{h}_n{qc}")
            st.v(lambda v: v.reciprocal(out=rrow[:], in_=PSO[64:65, :]))
            st = p.stage(f"dh{h}_ne{qc}")
            st.s(lambda s: s.activation(out=otile[:], in_=PSO[0:64, :],
                                        func=AF.Identity))
            st = p.stage(f"dh{h}_nb{qc}")
            st.t(lambda t: t.matmul(PSB[:, :], ones128[0:1, 0:64], rrow[0:1, :],
                                    start=True, stop=True))
            st = p.stage(f"dh{h}_nm{qc}")
            st.v(lambda v: v.tensor_mul(out=obuf[:], in0=otile[:], in1=PSB[:, :]))
            st = p.stage(f"dh{h}_st{qc}")
            st.dma(lambda y, h=h, qsl=qsl: y.dma_start(out=o_o[h][:, qsl],
                                                       in_=obuf[:]))


def _rope_tables_c(r):
    inv_freq = 1.0 / (10000.0 ** (np.arange(0, 64, 2, dtype=np.float32) / 64))
    t = np.arange(r * T, (r + 1) * T, dtype=np.float32)
    fr = t[:, None] * inv_freq[None, :]
    cos = np.cos(fr).T.astype(np.float32)
    sin = np.sin(fr).T.astype(np.float32)
    cosrep = np.tile(cos, (4, 1))
    sinsgn = np.concatenate([-sin, sin, -sin, sin], axis=0)
    return np.ascontiguousarray(cosrep), np.ascontiguousarray(sinsgn)


def stage_c(inputs, a_res, b_res):
    cvec = np.zeros((128, 4), np.float32)
    cvec[:, 0] = 1e-5
    wo_t = _t(inputs["L0_attn_Wo"])
    w_qkv = np.ascontiguousarray(inputs["L1_attn_w_qkv"])
    maps = []
    for c in range(NC):
        b, r = c // 2, c % 2
        tsl = slice(r * T, (r + 1) * T)
        oag = np.concatenate([b_res[2 * b]["o16"].reshape(512, S),
                              b_res[2 * b + 1]["o16"].reshape(512, S)], axis=0)
        cosrep, sinsgn = _rope_tables_c(r)
        maps.append({
            "x_fm": _t(inputs["L0_attn_x"][b, tsl, :]),
            "oag": np.ascontiguousarray(oag), "tsel": np.array([[r]], np.int32),
            "wo_t": wo_t, "g_in": a_res[c]["g"],
            "w_qkv": w_qkv, "cosrep": cosrep, "sinsgn": sinsgn, "cvec": cvec,
        })
    return maps


def _causal_masks_d():
    out = np.zeros((128, 4 * 512), np.float32)
    for mi in range(4):
        kk = np.arange(128)[:, None] + mi * 128
        q = np.arange(512)[None, :]
        out[:, mi * 512:(mi + 1) * 512] = np.where(kk > q, -1e30, 0.0)
    return out


def stage_d(c_res):
    maskd = _causal_masks_d()
    maps = []
    for c in range(NC):
        b, hg = c // 2, c % 2
        ra, rb = c_res[2 * b], c_res[2 * b + 1]
        qag1 = np.concatenate([ra["q1r"].reshape(1024, T),
                               rb["q1r"].reshape(1024, T)], axis=0)
        kag1 = np.concatenate([ra["k1r"].reshape(1024, T),
                               rb["k1r"].reshape(1024, T)], axis=0)
        vag1 = np.concatenate([ra["v1tok"], rb["v1tok"]], axis=0)
        maps.append({"qag1": np.ascontiguousarray(qag1),
                     "kag1": np.ascontiguousarray(kag1),
                     "vag1": np.ascontiguousarray(vag1),
                     "hseld": np.array([[hg]], np.int32), "maskd": maskd})
    return maps


EXT = 258 + T
NW = T // 256


def emit_e(p, D):
    i_y = D("y_ext", [1024, EXT], F32, "ExternalInput")
    i_wqkv = D("wqkv", [1024, 3072], F32, "ExternalInput")
    i_wo = D("wo", [1024, 1024], F32, "ExternalInput")
    i_w1a = D("w1a", [1024, 2730], F32, "ExternalInput")
    i_w1g = D("w1g", [1024, 2730], F32, "ExternalInput")
    i_w2 = D("w2", [2730, 1024], F32, "ExternalInput")
    i_b1a = D("b1a", [128, 22], F32, "ExternalInput")
    i_b1g = D("b1g", [128, 22], F32, "ExternalInput")
    i_b2 = D("b2c", [128, 8], F32, "ExternalInput")
    i_lng = D("lng", [128, 8], F32, "ExternalInput")
    i_lnb = D("lnb", [128, 8], F32, "ExternalInput")
    i_qsA = D("qsA", [64, EXT], F32, "ExternalInput")
    i_qsB = D("qsB", [64, EXT], F32, "ExternalInput")
    i_ksA = D("ksA", [64, EXT], F32, "ExternalInput")
    i_ksB = D("ksB", [64, EXT], F32, "ExternalInput")
    i_qsc = D("qscol", [64, 2], F32, "ExternalInput")
    i_ksc = D("kscol", [64, 2], F32, "ExternalInput")
    i_mw = D("maskw", [128, 2 * 1024], F32, "ExternalInput")
    i_zg = D("zgate", [1, 1], F32, "ExternalInput")
    i_cv = D("cvec", [128, 4], F32, "ExternalInput")
    o_y = D("ynew", [1024, T], F32, "ExternalOutput")

    ysb = p.sbuf("ysb", [128, 8 * EXT], F32)
    xs16 = p.sbuf("xs16", [128, 8 * EXT], BF16)
    oacc = p.sbuf("oaccE", [128, 4 * T], BF16)
    vsb = p.sbuf("vsbE", [128, 8 * 21 * 65], BF16)
    qk16 = p.sbuf("qk16", [128, 2 * EXT], BF16)
    wbuf = p.sbuf("wbufE", [128, 8 * 512], BF16)
    tabs = p.sbuf("tabsE", [64, 4 * EXT], BF16)
    masks = p.sbuf("masksE", [128, 2 * 1024], BF16)
    qhat = p.sbuf("qhatE", [128, 1024], F32)
    tmr = p.sbuf("tmrE", [64, 512], F32)
    tmp = p.sbuf("tmpE", [128, 512], F32)
    ptile = p.sbuf("ptileE", [128, 4 * 256], BF16)
    rows = p.sbuf("rowsE", [65, 528], F32)
    rowsB = p.sbuf("rowsBE", [33, 512], F32)
    cv = p.sbuf("cvE", [128, 4], F32)
    epsT = p.sbuf("epsTE", [65, 512], BF16)
    rows2 = p.sbuf("rows2E", [33, 512], F32)
    ones128 = p.sbuf("ones128E", [65, 128], F32)
    onesb = p.sbuf("onesbE", [128, 1], BF16)
    floor16 = p.sbuf("floor16E", [1, 66], BF16)
    fval = p.sbuf("fvalE", [1, 256], BF16)
    otile = p.sbuf("otileE", [64, 256], F32)
    obuf = p.sbuf("obufE", [64, 256], BF16)
    qsc = p.sbuf("qscE", [64, 2], F32)
    ksc = p.sbuf("kscE", [64, 2], F32)
    zg = p.sbuf("zgE", [1, 1], F32)
    bcols = p.sbuf("bcolsE", [128, 68], F32)
    obnd16 = p.sbuf("obnd16", [128, 8], BF16)
    sq16 = p.sbuf("sq16E", [128, 512], BF16)
    xt16 = p.sbuf("xt16E", [128, 512], BF16)
    hh21 = p.sbuf("hh21E", [64, 512], BF16)

    PS1 = p.psum("PS1", [128, 512], F32)
    PS2 = p.psum("PS2", [128, 512], F32)
    PSM1 = p.psum("PSM1", [128, 512], F32)
    PSM2 = p.psum("PSM2", [128, 512], F32)
    PSN1 = p.psum("PSN1", [1, 512], F32)
    PSN2 = p.psum("PSN2", [1, 512], F32)
    PSB = p.psum("PSBE", [64, 512], F32)
    PSO = p.psum("PSOE", [65, 256], F32)

    rrow = rows2[0:1, 0:256]
    b1a_c, b1g_c = bcols[:, 0:22], bcols[:, 22:44]
    b2_c, lng_c, lnb_c = bcols[:, 44:52], bcols[:, 52:60], bcols[:, 60:68]
    sums, sqs = rowsB[0:1, :], rowsB[32:33, :]
    meanr = rows2[0:1, :]
    msq = rows2[32:33, :]
    rstdr = rowsB[32:33, :]
    nrmr = rows[64:65, 0:512]

    PROJ_SLICES = [slice(1, 513), slice(513, 1025), slice(1025, 1537),
                   slice(1537, 2049), slice(2049, EXT)]
    VTILES = [(2 + 128 * k, 130 + 128 * k) for k in range(18)] + \
             [(1, 129), (129, 257), (257, 258)]

    def yv(fc, a, b):
        return ysb[:, fc * EXT + a:fc * EXT + b]

    def xv(fc, a, b, shifted=True):
        if shifted and fc >= 4:
            a, b = a - 1, b - 1
        return xs16[:, fc * EXT + a:fc * EXT + b]

    def vtile(hh, vt):
        base = (hh * 21 + vt) * 65
        return vsb[:, base:base + 65]

    def hhloc(i):
        if i < 16:
            return oacc[:, i * 512:(i + 1) * 512]
        if i < 21:
            return qk16[:, (i - 16) * 512:(i - 15) * 512]
        return hh21[:, :]

    def xn2loc(fc):
        return vsb[:, fc * 512:(fc + 1) * 512]

    st = p.stage("e_consts")

    def ecl(y):
        return [y.dma_start(out=cv[:], in_=i_cv),
                y.dma_start(out=qsc[:], in_=i_qsc),
                y.dma_start(out=ksc[:], in_=i_ksc),
                y.dma_start(out=zg[:], in_=i_zg),
                y.dma_start(out=bcols[:, 0:22], in_=i_b1a),
                y.dma_start(out=bcols[:, 22:44], in_=i_b1g),
                y.dma_start(out=bcols[:, 44:52], in_=i_b2),
                y.dma_start(out=bcols[:, 52:60], in_=i_lng),
                y.dma_start(out=bcols[:, 60:68], in_=i_lnb)]
    st.dma(ecl, n=9)
    st.gdma(lambda g: g.dma_start(out=masks[:], in_=i_mw), n=1)
    st = p.stage("e_init")

    def ei(v):
        v.memset(ones128[:], 1.0)
        v.memset(onesb[:], 1.0)
        v.memset(floor16[:], 0.0)
        v.memset(fval[:], 1e-30)
        return v.memset(epsT[64:65, :], 1e-12)
    st.v(ei)
    st = p.stage("e_init2")
    st.v(lambda v: v.memset(floor16[0:1, 64:65], 1.0))

    for ti, (src, sc_, col) in enumerate(
            ((i_qsA, 'q', 0), (i_qsB, 'q', 1), (i_ksA, 'k', 0), (i_ksB, 'k', 1))):
        st = p.stage(f"e_tl{ti}")
        st.dma(lambda y, src=src: y.dma_start(out=ysb[0:64, 0:EXT], in_=src))
        st = p.stage(f"e_tm{ti}")
        scap = qsc if sc_ == 'q' else ksc
        st.v(lambda v, ti=ti, scap=scap, col=col: v.tensor_scalar_mul(
            out=tabs[:, ti * EXT:(ti + 1) * EXT], in0=ysb[0:64, 0:EXT],
            scalar1=scap[:, col:col + 1]))

    for fc in range(8):
        st = p.stage(f"e_yl{fc}")
        st.dma(lambda y, fc=fc: y.dma_start(
            out=yv(fc, 0, EXT), in_=i_y[fc * 128:(fc + 1) * 128, :]))
        st = p.stage(f"e_yc{fc}")
        st.v(lambda v, fc=fc: v.tensor_copy(out=xs16[:, fc * EXT:(fc + 1) * EXT],
                                            in_=yv(fc, 0, EXT)))

    for hg in range(2):
        st = p.stage(f"e_vw{hg}")
        st.gdma(lambda g, hg=hg: g.dma_start(
            out=wbuf[:].rearrange("p (fc o) -> p fc o", o=512),
            in_=i_wqkv.rearrange("(fc p) o -> p fc o", p=128)[
                :, :, 2048 + hg * 512:2048 + (hg + 1) * 512]), n=1)
        for vt, (a, b) in enumerate(VTILES):
            n = b - a
            st = p.stage(f"e_vp{hg}_{vt}")

            def vmm(t, a=a, b=b, n=n):
                last = None
                for fc in range(8):
                    last = t.matmul(PS1[0:n, :], xv(fc, a, b),
                                    wbuf[:, fc * 512:(fc + 1) * 512],
                                    start=(fc == 0), stop=(fc == 7))
                return last
            st.t(vmm)
            st = p.stage(f"e_ve{hg}_{vt}")

            def vev(v, vt=vt, n=n):
                last = None
                for hh in range(8):
                    last = v.tensor_copy(out=vtile(hh, vt)[0:n, 0:64],
                                         in_=PS1[0:n, hh * 64:(hh + 1) * 64])
                return last
            st.v(vev)
        st = p.stage(f"e_vones{hg}")

        def von(v):
            last = None
            for hh in range(8):
                for vt in range(21):
                    last = v.memset(vtile(hh, vt)[:, 64:65], 1.0)
            return last
        st.v(von)

        for hp in range(4):
            st = p.stage(f"e_qw{hg}_{hp}")

            def qw(g, hg=hg, hp=hp):
                c0 = hg * 512 + hp * 128
                return [
                    g.dma_start(
                        out=wbuf[:].rearrange("p (fc o) -> p fc o", o=512)[
                            :, :, 0:128],
                        in_=i_wqkv.rearrange("(fc p) o -> p fc o", p=128)[
                            :, :, c0:c0 + 128]),
                    g.dma_start(
                        out=wbuf[:].rearrange("p (fc o) -> p fc o", o=512)[
                            :, :, 128:256],
                        in_=i_wqkv.rearrange("(fc p) o -> p fc o", p=128)[
                            :, :, 1024 + c0:1024 + c0 + 128]),
                ]
            st.gdma(qw, n=2)
            for ts_i, tsl in enumerate(PROJ_SLICES):
                n = tsl.stop - tsl.start
                st = p.stage(f"e_qp{hg}_{hp}_{ts_i}")

                def qmm(t, tsl=tsl, n=n):
                    for fc in range(8):
                        t.matmul(PS1[:, 0:n],
                                 wbuf[:, fc * 512:fc * 512 + 128],
                                 xv(fc, tsl.start, tsl.stop),
                                 start=(fc == 0), stop=(fc == 7))
                    last = None
                    for fc in range(8):
                        last = t.matmul(PS2[:, 0:n],
                                        wbuf[:, fc * 512 + 128:fc * 512 + 256],
                                        xv(fc, tsl.start, tsl.stop),
                                        start=(fc == 0), stop=(fc == 7))
                    return last
                st.t(qmm)
                for lh in range(2):
                    rs = lh * 64
                    for side, PS in (("q", PS1), ("k", PS2)):
                        sfx = f"{hg}_{hp}_{ts_i}_{lh}{side}"
                        st = p.stage(f"e_n1{sfx}")
                        st.s(lambda s, PS=PS, rs=rs, n=n: s.activation(
                            out=sq16[0:64, 0:n], in_=PS[rs:rs + 64, 0:n],
                            func=AF.Square))
                        st = p.stage(f"e_n2{sfx}")
                        st.t(lambda t, n=n: t.matmul(
                            PSN1[:, 0:n], onesb[0:64, 0:1], sq16[0:64, 0:n],
                            start=True, stop=True))
                        st = p.stage(f"e_n3{sfx}")
                        st.s(lambda s, n=n: s.activation(
                            out=nrmr[:, 0:n], in_=PSN1[:, 0:n], func=AF.Sqrt))
                        st = p.stage(f"e_n4{sfx}")

                        def n4(v, n=n):
                            v.tensor_max(out=nrmr[:, 0:n], in0=nrmr[:, 0:n],
                                         in1=epsT[64:65, 0:n])
                            return v.reciprocal(out=nrmr[:, 0:n],
                                                in_=nrmr[:, 0:n])
                        st.v(n4)
                        st = p.stage(f"e_n5{sfx}")
                        st.t(lambda t, n=n: t.matmul(
                            PSB[:, 0:n], ones128[64:65, 0:64], nrmr[:, 0:n],
                            start=True, stop=True))
                        st = p.stage(f"e_n6a{sfx}")
                        st.s(lambda s, PS=PS, rs=rs, n=n: s.activation(
                            out=qhat[0:64, 0:n], in_=PS[rs:rs + 64, 0:n],
                            func=AF.Identity))
                        st = p.stage(f"e_n6{sfx}")
                        st.v(lambda v, n=n: v.tensor_mul(
                            out=qhat[0:64, 0:n], in0=qhat[0:64, 0:n],
                            in1=PSB[:, 0:n]))
                        st = p.stage(f"e_r1{sfx}")

                        def r1(v, n=n):
                            v.tensor_copy(out=tmr[0:32, 0:n],
                                          in_=qhat[32:64, 0:n])
                            return v.tensor_copy(out=tmr[32:64, 0:n],
                                                 in_=qhat[0:32, 0:n])
                        st.v(r1)
                        tb = 0 if side == "q" else 2 * EXT
                        st = p.stage(f"e_r2{sfx}")

                        def r2(v, tsl=tsl, n=n, tb=tb):
                            v.tensor_mul(out=qhat[0:64, 0:n],
                                         in0=qhat[0:64, 0:n],
                                         in1=tabs[:, tb + tsl.start:tb + tsl.stop])
                            v.tensor_mul(
                                out=tmr[0:64, 0:n], in0=tmr[0:64, 0:n],
                                in1=tabs[:, tb + EXT + tsl.start:
                                         tb + EXT + tsl.stop])
                            return v.tensor_add(out=qhat[0:64, 0:n],
                                                in0=qhat[0:64, 0:n],
                                                in1=tmr[0:64, 0:n])
                        st.v(r2)
                        qkoff = 0 if side == "q" else EXT
                        st = p.stage(f"e_r3{sfx}")
                        st.v(lambda v, lh=lh, qkoff=qkoff, tsl=tsl, n=n:
                             v.tensor_copy(
                                 out=qk16[lh * 64:lh * 64 + 64,
                                          qkoff + tsl.start:qkoff + tsl.stop],
                                 in_=qhat[0:64, 0:n]))
            for lh in range(2):
                hh = hp * 2 + lh
                qrow = slice(lh * 64, lh * 64 + 64)
                PSL = [PS1, PS2, PSM1, PSM2]
                for w in range(NW):
                    qa_, qb_ = 258 + w * 256, 258 + (w + 1) * 256
                    ka_ = w * 256 + 2
                    mvar = 0 if w == 0 else 1024
                    sfx = f"{hg}_{hp}_{lh}_{w}"
                    st = p.stage(f"e_az{sfx}")

                    def az(t, qrow=qrow, qa_=qa_, qb_=qb_, ka_=ka_, PSL=PSL):
                        out = None
                        for j in range(4):
                            ks = ka_ + j * 128
                            out = t.matmul(PSL[j][:, 0:256],
                                           qk16[qrow, EXT + ks:EXT + ks + 128],
                                           qk16[qrow, qa_:qb_],
                                           start=True, stop=True)
                        return out
                    st.t(az)
                    st = p.stage(f"e_am{sfx}")

                    def am(v, mvar=mvar, PSL=PSL):
                        last = None
                        for j in range(4):
                            last = v.scalar_tensor_tensor(
                                out=qhat[:, j * 256:(j + 1) * 256],
                                in0=PSL[j][:, 0:256], scalar=QK_SCALE,
                                in1=masks[:, mvar + j * 256:mvar + (j + 1) * 256],
                                op0=ALU.mult, op1=ALU.add)
                        return last
                    st.v(am)
                    st = p.stage(f"e_ax{sfx}")

                    def ax(s):
                        last = None
                        for j in range(4):
                            last = s.activation(
                                out=ptile[:, j * 256:(j + 1) * 256],
                                in_=qhat[:, j * 256:(j + 1) * 256], func=AF.Exp)
                        return last
                    st.s(ax)
                    st = p.stage(f"e_apv{sfx}")

                    def apv(t, hh=hh, w=w):
                        for j in range(4):
                            t.matmul(PSO[:, :], vtile(hh, 2 * w + j),
                                     ptile[:, j * 256:(j + 1) * 256],
                                     start=(j == 0), stop=False)
                        return t.matmul(PSO[:, :], floor16[0:1, 0:65],
                                        fval[0:1, :], start=False, stop=True)
                    st.t(apv)
                    st = p.stage(f"e_an{sfx}")
                    st.v(lambda v: v.reciprocal(out=rrow[:], in_=PSO[64:65, :]))
                    st = p.stage(f"e_ane{sfx}")
                    st.s(lambda s: s.activation(out=otile[:], in_=PSO[0:64, :],
                                                func=AF.Identity))
                    st = p.stage(f"e_anb{sfx}")
                    st.t(lambda t: t.matmul(PSB[:, 0:256], ones128[0:1, 0:64],
                                            rrow[0:1, :], start=True, stop=True))
                    st = p.stage(f"e_anm{sfx}")

                    def anm(v, hh=hh, w=w):
                        v.tensor_mul(out=obuf[:], in0=otile[:], in1=PSB[:, 0:256])
                        oc_, ro = hh // 2, (hh % 2) * 64
                        return v.tensor_copy(
                            out=oacc[ro:ro + 64,
                                     oc_ * T + w * 256:oc_ * T + (w + 1) * 256],
                            in_=obuf[:])
                    st.v(anm)
                sfx = f"b{hg}_{hp}_{lh}"
                st = p.stage(f"e_bz{sfx}")

                def bz(t, qrow=qrow):
                    t.matmul(PS1[:, 0:1], qk16[qrow, EXT + 1:EXT + 129],
                             qk16[qrow, 257:258], start=True, stop=True)
                    t.matmul(PS2[:, 0:1], qk16[qrow, EXT + 129:EXT + 257],
                             qk16[qrow, 257:258], start=True, stop=True)
                    return t.matmul(PSM1[0:1, 0:1],
                                    qk16[qrow, EXT + 257:EXT + 258],
                                    qk16[qrow, 257:258], start=True, stop=True)
                st.t(bz)
                st = p.stage(f"e_bm{sfx}")

                def bm(v):
                    v.tensor_scalar_mul(out=qhat[:, 0:1], in0=PS1[:, 0:1],
                                        scalar1=QK_SCALE)
                    v.tensor_scalar_mul(out=qhat[:, 1:2], in0=PS2[:, 0:1],
                                        scalar1=QK_SCALE)
                    return v.tensor_scalar_mul(out=qhat[0:1, 2:3],
                                               in0=PSM1[0:1, 0:1],
                                               scalar1=QK_SCALE)
                st.v(bm)
                st = p.stage(f"e_bx{sfx}")
                st.s(lambda s: s.activation(out=ptile[:, 0:3],
                                            in_=qhat[:, 0:3], func=AF.Exp))
                st = p.stage(f"e_bpv{sfx}")

                def bpv(t, hh=hh):
                    t.matmul(PSO[:, 0:1], vtile(hh, 18), ptile[:, 0:1],
                             start=True, stop=False)
                    t.matmul(PSO[:, 0:1], vtile(hh, 19), ptile[:, 1:2],
                             start=False, stop=False)
                    t.matmul(PSO[:, 0:1], vtile(hh, 20)[0:1, :],
                             ptile[0:1, 2:3], start=False, stop=False)
                    return t.matmul(PSO[:, 0:1], floor16[0:1, 0:65],
                                    fval[0:1, 0:1], start=False, stop=True)
                st.t(bpv)
                st = p.stage(f"e_bn{sfx}")
                st.v(lambda v: v.reciprocal(out=rrow[:, 0:1],
                                            in_=PSO[64:65, 0:1]))
                st = p.stage(f"e_bne{sfx}")
                st.s(lambda s: s.activation(out=otile[:, 0:1],
                                            in_=PSO[0:64, 0:1],
                                            func=AF.Identity))
                st = p.stage(f"e_bnb{sfx}")
                st.t(lambda t: t.matmul(PSB[:, 0:1], ones128[0:1, 0:64],
                                        rrow[0:1, 0:1], start=True, stop=True))
                st = p.stage(f"e_bnm{sfx}")

                def bnm(v, hg=hg, hh=hh):
                    v.tensor_mul(out=obuf[:, 0:1], in0=otile[:, 0:1],
                                 in1=PSB[:, 0:1])
                    return v.tensor_copy(out=rows[0:64, hg * 8 + hh:
                                                  hg * 8 + hh + 1],
                                         in_=obuf[:, 0:1])
                st.v(bnm)

        st = p.stage(f"e_obc{hg}")

        def obc(v, hg=hg):
            last = None
            for fc in range(4):
                h0 = hg * 8 + 2 * fc
                last = v.tensor_copy(out=obnd16[0:64, fc:fc + 1],
                                     in_=rows[0:64, h0:h0 + 1])
                last = v.tensor_copy(out=obnd16[64:128, fc:fc + 1],
                                     in_=rows[0:64, h0 + 1:h0 + 2])
            return last
        st.v(obc)
        for op_ in range(8):
            st = p.stage(f"e_ow{hg}_{op_}")
            st.gdma(lambda g, hg=hg, op_=op_: g.dma_start(
                out=wbuf[:].rearrange("p (fc o) -> p fc o", o=512)[:, 0:4, 0:128],
                in_=i_wo.rearrange("(fc p) o -> p fc o", p=128)[
                    :, hg * 4:(hg + 1) * 4, op_ * 128:(op_ + 1) * 128]), n=1)
            for tc in range(4):
                tsl = slice(tc * 512, (tc + 1) * 512)
                st = p.stage(f"e_op{hg}_{op_}_{tc}")

                def omm(t, tsl=tsl):
                    last = None
                    for fc in range(4):
                        last = t.matmul(
                            PS1[:, :], wbuf[:, fc * 512:fc * 512 + 128],
                            oacc[:, fc * T + tsl.start:fc * T + tsl.stop],
                            start=(fc == 0), stop=(fc == 3))
                    return last
                st.t(omm)
                st = p.stage(f"e_oe{hg}_{op_}_{tc}")

                def oe(v, op_=op_, tsl=tsl):
                    ys = yv(op_, 258 + tsl.start, 258 + tsl.stop)
                    return v.tensor_add(out=ys, in0=ys, in1=PS1[:, :])
                st.v(oe)
            st = p.stage(f"e_obm{hg}_{op_}")

            def obm(t):
                last = None
                for fc in range(4):
                    last = t.matmul(PS2[:, 0:1],
                                    wbuf[:, fc * 512:fc * 512 + 128],
                                    obnd16[:, fc:fc + 1], start=(fc == 0),
                                    stop=(fc == 3))
                return last
            st.t(obm)
            st = p.stage(f"e_obe{hg}_{op_}")

            def obe(v, op_=op_):
                ys = yv(op_, 257, 258)
                return v.tensor_add(out=ys, in0=ys, in1=PS2[:, 0:1])
            st.v(obe)

    st = p.stage("e_zgb")
    st.t(lambda t: t.matmul(PSM2[:, 0:1], ones128[0:1, :], zg[0:1, 0:1],
                            start=True, stop=True))
    st = p.stage("e_zgc")
    st.v(lambda v: v.tensor_copy(out=qhat[:, 0:1], in_=PSM2[:, 0:1]))
    for op_ in range(8):
        st = p.stage(f"e_zg{op_}")
        st.v(lambda v, op_=op_: v.tensor_mul(out=yv(op_, 257, 258),
                                             in0=yv(op_, 257, 258),
                                             in1=qhat[:, 0:1]))

    for tq in range(4):
        ha = 258 + tq * 512
        st = p.stage(f"e_f0{tq}")

        def f0(v):
            v.memset(sums, 0.0)
            return v.memset(sqs, 0.0)
        st.v(f0)
        for fc in range(8):
            sh = 1 if fc >= 4 else 0
            st = p.stage(f"e_fs{tq}_{fc}")
            st.s(lambda s, fc=fc, ha=ha, sh=sh: s.activation(
                out=sq16[:, :], in_=yv(fc, ha - sh, ha - sh + 512),
                func=AF.Square))
            st.v(lambda v, fc=fc, ha=ha, sh=sh: v.tensor_copy(
                out=xt16[:, :], in_=yv(fc, ha - sh, ha - sh + 512)))
            st = p.stage(f"e_fm{tq}_{fc}")

            def fmm(t):
                t.matmul(PSN1[:, :], onesb[:, 0:1], xt16[:, :],
                         start=True, stop=True)
                return t.matmul(PSN2[:, :], onesb[:, 0:1], sq16[:, :],
                                start=True, stop=True)
            st.t(fmm)
            st = p.stage(f"e_fe{tq}_{fc}")

            def fev(v):
                v.tensor_add(out=sums, in0=sums, in1=PSN1[:, :])
                return v.tensor_add(out=sqs, in0=sqs, in1=PSN2[:, :])
            st.v(fev)
        st = p.stage(f"e_ff{tq}")

        def ffin(v):
            v.tensor_scalar_mul(out=meanr, in0=sums, scalar1=1.0 / 1024)
            v.tensor_scalar_mul(out=sqs, in0=sqs, scalar1=1.0 / 1024)
            v.tensor_mul(out=msq, in0=meanr, in1=meanr)
            return v.tensor_sub(out=sqs, in0=sqs, in1=msq)
        st.v(ffin)
        st = p.stage(f"e_fq{tq}")
        st.s(lambda s: s.activation(out=sqs, in_=sqs, func=AF.Sqrt,
                                    bias=cv[32:33, 0:1]))
        st = p.stage(f"e_fr{tq}")
        st.v(lambda v: v.reciprocal(out=rstdr, in_=sqs))
        st = p.stage(f"e_fb{tq}")

        def fbc(t):
            t.matmul(PSM1[:, :], ones128[0:1, :], meanr[:, :],
                     start=True, stop=True)
            return t.matmul(PSM2[:, :], ones128[32:33, :], rstdr[:, :],
                            start=True, stop=True)
        st.t(fbc)
        for fc in range(8):
            sh = 1 if fc >= 4 else 0
            st = p.stage(f"e_fa{tq}_{fc}")

            def fap(v, fc=fc, ha=ha, sh=sh):
                ys = yv(fc, ha - sh, ha - sh + 512)
                v.tensor_sub(out=tmp[:], in0=ys, in1=PSM1[:, :])
                v.tensor_mul(out=tmp[:], in0=tmp[:], in1=PSM2[:, :])
                return v.tensor_scalar(
                    out=xn2loc(fc)[:, :], in0=tmp[:],
                    scalar1=lng_c[:, fc:fc + 1], scalar2=lnb_c[:, fc:fc + 1],
                    op0=ALU.mult, op1=ALU.add)
            st.v(fap)
        for i in range(22):
            n = 128 if i < 21 else 42
            st = p.stage(f"e_w1l{tq}_{i}")

            def w1l(g, i=i, n=n):
                return [
                    g.dma_start(
                        out=wbuf[:].rearrange("p (fc o) -> p fc o", o=512)[
                            :, :, 0:n],
                        in_=i_w1a.rearrange("(fc p) o -> p fc o", p=128)[
                            :, :, i * 128:i * 128 + n]),
                    g.dma_start(
                        out=wbuf[:].rearrange("p (fc o) -> p fc o", o=512)[
                            :, :, 128:128 + n],
                        in_=i_w1g.rearrange("(fc p) o -> p fc o", p=128)[
                            :, :, i * 128:i * 128 + n]),
                ]
            st.gdma(w1l, n=2)
            st = p.stage(f"e_w1m{tq}_{i}")

            def w1m(t, i=i, n=n):
                for fc in range(8):
                    t.matmul(PS1[0:n, :], wbuf[:, fc * 512:fc * 512 + n],
                             xn2loc(fc)[:, :], start=(fc == 0), stop=(fc == 7))
                last = None
                for fc in range(8):
                    last = t.matmul(
                        PS2[0:n, :], wbuf[:, fc * 512 + 128:fc * 512 + 128 + n],
                        xn2loc(fc)[:, :], start=(fc == 0), stop=(fc == 7))
                return last
            st.t(w1m)
            st = p.stage(f"e_w1g{tq}_{i}")
            st.s(lambda s, i=i, n=n: s.activation(
                out=tmp[0:n, :], in_=PS2[0:n, :], func=AF.Gelu,
                bias=b1g_c[0:n, i:i + 1]))
            st = p.stage(f"e_w1h{tq}_{i}")

            def w1h(v, i=i, n=n):
                v.tensor_scalar(out=qhat[0:n, 0:512], in0=PS1[0:n, :],
                                scalar1=b1a_c[0:n, i:i + 1], scalar2=0.0,
                                op0=ALU.add, op1=ALU.add)
                return v.tensor_mul(out=hhloc(i)[0:n, 0:512],
                                    in0=qhat[0:n, 0:512], in1=tmp[0:n, :])
            st.v(w1h)
        for op_ in range(8):
            st = p.stage(f"e_w2l{tq}_{op_}")

            def w2l(g, op_=op_):
                return [
                    g.dma_start(
                        out=wbuf[:, 0:2688].rearrange("p (i o) -> p i o", o=128),
                        in_=i_w2[0:2688, :].rearrange("(i p) o -> p i o", p=128)[
                            :, :, op_ * 128:(op_ + 1) * 128]),
                    g.dma_start(
                        out=wbuf[0:42, 2688:2816],
                        in_=i_w2[2688:2730, op_ * 128:(op_ + 1) * 128]),
                ]
            st.gdma(w2l, n=2)
            st = p.stage(f"e_w2m{tq}_{op_}")

            def w2m(t):
                last = None
                for i in range(22):
                    n = 128 if i < 21 else 42
                    last = t.matmul(PS1[:, :], wbuf[0:n, i * 128:i * 128 + 128],
                                    hhloc(i)[0:n, 0:512], start=(i == 0),
                                    stop=(i == 21))
                return last
            st.t(w2m)
            st = p.stage(f"e_w2e{tq}_{op_}")

            def w2e(v, op_=op_, ha=ha):
                v.tensor_scalar(out=tmp[:], in0=PS1[:, :],
                                scalar1=b2_c[:, op_:op_ + 1], scalar2=0.0,
                                op0=ALU.add, op1=ALU.add)
                ys = yv(op_, ha, ha + 512)
                return v.tensor_add(out=ys, in0=ys, in1=tmp[:])
            st.v(w2e)
            st = p.stage(f"e_w2s{tq}_{op_}")
            st.dma(lambda y, op_=op_, tq=tq, ha=ha: y.dma_start(
                out=o_y[op_ * 128:(op_ + 1) * 128, tq * 512:(tq + 1) * 512],
                in_=yv(op_, ha, ha + 512)))


def _xpos_ext_tables(r):
    half = np.arange(0, 64, 2, dtype=np.float64)
    inv_freq = 1.0 / 10000.0 ** (half / 64)
    t = np.arange(r * T - 258, r * T + T, dtype=np.float64)
    freqs = t[:, None] * inv_freq[None, :]
    freqs = np.concatenate([freqs, freqs], -1)
    base_scale = (half + 0.4 * 64) / (1.4 * 64)
    power = (t - S // 2) / (256 // 2)
    scale = base_scale[None, :] ** power[:, None]
    scale = np.concatenate([scale, scale], -1)
    cos, sin = np.cos(freqs), np.sin(freqs)
    sgn = np.concatenate([-np.ones(32), np.ones(32)])[None, :]
    qsA = np.ascontiguousarray((cos * scale).T.astype(np.float32))
    qsB = np.ascontiguousarray((sgn * sin * scale).T.astype(np.float32))
    ksA = np.ascontiguousarray((cos / scale).T.astype(np.float32))
    ksB = np.ascontiguousarray((sgn * sin / scale).T.astype(np.float32))
    return qsA, qsB, ksA, ksB


def _maskw(r):
    kk = np.arange(128)[:, None]
    q = np.arange(256)[None, :]
    out = np.zeros((128, 2048), np.float32)
    for mi in range(4):
        key_rel = mi * 128 + kk - 256
        ok = (key_rel <= q) & (key_rel >= q - 256)
        normal = np.where(ok, 0.0, -1e30).astype(np.float32)
        w0 = normal.copy()
        if r == 0:
            w0 = np.where(key_rel < 0, -1e30, w0).astype(np.float32)
        out[:, 1024 + mi * 256:1024 + (mi + 1) * 256] = normal
        out[:, mi * 256:(mi + 1) * 256] = w0
    return out


def _colpack(vec, ncol):
    out = np.zeros((128, ncol), np.float32)
    v = np.asarray(vec, np.float32)
    for j in range(ncol):
        seg = v[j * 128:(j + 1) * 128]
        out[:len(seg), j] = seg
    return out


def stage_e(inputs, li, y_fm_per_core):
    cvec = np.zeros((128, 4), np.float32)
    cvec[:, 0] = 1e-5
    W1 = inputs["L1_ffn_W1"][li]
    b1 = inputs["L1_ffn_b1"][li]
    shared = {
        "wqkv": np.ascontiguousarray(inputs["L1_ffn_Wqkv"][li]),
        "wo": np.ascontiguousarray(inputs["L1_ffn_Wo"][li]),
        "w1a": np.ascontiguousarray(W1[:, 0:2730]),
        "w1g": np.ascontiguousarray(W1[:, 2730:]),
        "w2": np.ascontiguousarray(inputs["L1_ffn_W2"][li]),
        "b1a": _colpack(b1[0:2730], 22), "b1g": _colpack(b1[2730:], 22),
        "b2c": _colpack(inputs["L1_ffn_b2"][li], 8),
        "lng": _colpack(inputs["L1_ffn_ln_g"][li], 8),
        "lnb": _colpack(inputs["L1_ffn_ln_b"][li], 8),
        "cvec": cvec,
    }
    qs = np.asarray(inputs["L1_ffn_q_scale"][li], np.float32)
    ks = np.asarray(inputs["L1_ffn_k_scale"][li], np.float32)
    shared["qscol"] = np.stack([qs, np.concatenate([qs[32:], qs[:32]])], axis=1)
    shared["kscol"] = np.stack([ks, np.concatenate([ks[32:], ks[:32]])], axis=1)
    maps = []
    for c in range(NC):
        b, r = c // 2, c % 2
        m = dict(shared)
        qsA, qsB, ksA, ksB = _xpos_ext_tables(r)
        m.update(qsA=qsA, qsB=qsB, ksA=ksA, ksB=ksB, maskw=_maskw(r),
                 zgate=np.array([[0.0 if r == 0 else 1.0]], np.float32))
        yext = np.zeros((1024, EXT), np.float32)
        if r == 1:
            yext[:, 0:258] = y_fm_per_core[2 * b][:, T - 258:]
        yext[:, 258:] = y_fm_per_core[c]
        m["y_ext"] = yext
        maps.append(m)
    return maps


def emit_f(p, D):
    i_x1 = D("x1", [1024, T], F32, "ExternalInput")
    i_oag = D("oag1", [2 * 512, S], BF16, "ExternalInput")
    i_tsel = D("tself", [1, 1], I32, "ExternalInput")
    i_w = D("wout", [1024, 1024], F32, "ExternalInput")
    i_y6 = D("y6", [1024, T], F32, "ExternalInput")
    o_x = D("xout", [1024, T], F32, "ExternalOutput")

    osb = p.sbuf("osbF", [128, 8 * T], BF16)
    tselb = p.sbuf("tselbF", [1, 1], I32)
    wsb = p.sbuf("wsbF", [128, 8 * 1024], BF16)
    x1c = p.sbuf("x1cF", [128, T], F32)
    y6c = p.sbuf("y6cF", [128, T], F32)
    gg = p.sbuf("ggF", [128, 512], F32)
    tt = p.sbuf("ttF", [128, 512], F32)
    PS1 = p.psum("PS1", [128, 512], F32)

    st = p.stage("f_load")

    def fl(g):
        out = [g.dma_start(out=wsb[:].rearrange("p (fc o) -> p fc o", o=1024),
                           in_=i_w.rearrange("(fc p) o -> p fc o", p=128)),
               g.dma_start(out=tselb[:], in_=i_tsel)]
        with g.register() as rg:
            g.reg_load(rg, tselb[0:1, 0:1])
            with g.If_ne(rg, 0):
                for fc in range(8):
                    out.append(g.dma_start(
                        out=osb[:, fc * T:(fc + 1) * T],
                        in_=i_oag[fc * 128:(fc + 1) * 128, T:2 * T]))
            with g.Else():
                for fc in range(8):
                    out.append(g.dma_start(
                        out=osb[:, fc * T:(fc + 1) * T],
                        in_=i_oag[fc * 128:(fc + 1) * 128, 0:T]))
            g.end_ifs()
        return out
    st.gdma(fl, n=10)

    for op_ in range(8):
        st = p.stage(f"f_x{op_}")

        def fx(y, op_=op_):
            return [y.dma_start(out=x1c[:], in_=i_x1[op_ * 128:(op_ + 1) * 128, :]),
                    y.dma_start(out=y6c[:], in_=i_y6[op_ * 128:(op_ + 1) * 128, :])]
        st.dma(fx, n=2)
        for tc in range(4):
            tsl = slice(tc * 512, (tc + 1) * 512)
            st = p.stage(f"f_m{op_}_{tc}")

            def fm(t, op_=op_, tsl=tsl):
                last = None
                for fc in range(8):
                    last = t.matmul(
                        PS1[:, :],
                        wsb[:, fc * 1024 + op_ * 128:fc * 1024 + op_ * 128 + 128],
                        osb[:, fc * T + tsl.start:fc * T + tsl.stop],
                        start=(fc == 0), stop=(fc == 7))
                return last
            st.t(fm)
            st = p.stage(f"f_g{op_}_{tc}")
            st.s(lambda s, tsl=tsl: s.activation(out=gg[:], in_=y6c[:, tsl],
                                                 func=AF.Silu))
            st = p.stage(f"f_e{op_}_{tc}")

            def fe(v, tsl=tsl):
                v.tensor_mul(out=tt[:], in0=gg[:], in1=PS1[:, :])
                return v.tensor_add(out=tt[:], in0=tt[:], in1=x1c[:, tsl])
            st.v(fe)
            st = p.stage(f"f_s{op_}_{tc}")
            st.dma(lambda y, op_=op_, tsl=tsl: y.dma_start(
                out=o_x[op_ * 128:(op_ + 1) * 128, tsl], in_=tt[:]))


def stage_f(inputs, c_res, d_res, e_res):
    wout = np.ascontiguousarray(inputs["L1_attn_w_out"])
    maps = []
    for c in range(NC):
        b, r = c // 2, c % 2
        oag1 = np.concatenate([d_res[2 * b]["o16"].reshape(512, S),
                               d_res[2 * b + 1]["o16"].reshape(512, S)], axis=0)
        maps.append({
            "x1": c_res[c]["x1"], "oag1": np.ascontiguousarray(oag1),
            "tself": np.array([[r]], np.int32),
            "wout": wout, "y6": e_res[c]["ynew"],
        })
    return maps


_NC_CACHE = {}


def _get(name, builder):
    if name not in _NC_CACHE:
        _NC_CACHE[name] = builder()
    return _NC_CACHE[name]


def kernel(**inputs):
    inputs = {k: np.asarray(v) for k, v in inputs.items()}
    cores = list(range(NC))
    resA = run_bass_kernel_spmd(_get("A", build_launch_a), stage_a(inputs),
                                core_ids=cores)
    resB = run_bass_kernel_spmd(_get("B", build_launch_b),
                                stage_b(inputs, resA.results), core_ids=cores)
    if any(float(r["flag"][0, 0]) > 0 for r in resB.results):
        raise NotImplementedError(
            "L0 hyperbolic attention hit a non-asymptotic branch; exact "
            "fixup path required")
    resC = run_bass_kernel_spmd(_get("C", build_launch_c),
                                stage_c(inputs, resA.results, resB.results),
                                core_ids=cores)
    resD = run_bass_kernel_spmd(_get("D", build_launch_d),
                                stage_d(resC.results), core_ids=cores)
    y_fm = [resC.results[c]["xn1f"] for c in range(NC)]
    ncE = _get("E", build_launch_e)
    for li in range(6):
        resE = run_bass_kernel_spmd(ncE, stage_e(inputs, li, y_fm),
                                    core_ids=cores)
        y_fm = [resE.results[c]["ynew"] for c in range(NC)]
    resF = run_bass_kernel_spmd(_get("F", build_launch_f),
                                stage_f(inputs, resC.results, resD.results,
                                        {c: {"ynew": y_fm[c]} for c in range(NC)}),
                                core_ids=cores)
    out = np.zeros((B, S, D), np.float32)
    for c in range(NC):
        b, r = c // 2, c % 2
        out[b, r * T:(r + 1) * T, :] = resF.results[c]["xout"].T
    return out


def _mk_builder(emit, name, **kw):
    def build():
        p = Prog(name)
        p.begin_section("")

        def D(nm, shape, dt, kind):
            return p.nc.dram_tensor(nm, shape, dt, kind=kind).ap()
        emit(p, D, **kw)
        return p.finalize()
    return build


build_launch_a = _mk_builder(emit_a, "A")
build_launch_b = _mk_builder(emit_b, "B")
build_launch_c = _mk_builder(emit_c, "C")
build_launch_d = _mk_builder(emit_d, "D")
build_launch_e = _mk_builder(emit_e, "E")
build_launch_f = _mk_builder(emit_f, "F")


# version 04
# speedup vs baseline: 1.1407x; 1.1407x over previous
import contextlib
import numpy as np
import ml_dtypes
import concourse.bass as bass
import concourse.mybir as mybir
from concourse.bass_utils import run_bass_kernel_spmd

F32 = mybir.dt.float32
BF16 = mybir.dt.bfloat16
I32 = mybir.dt.int32
AF = mybir.ActivationFunctionType
ALU = mybir.AluOpType
AX = mybir.AxisListType.X
BB = ml_dtypes.bfloat16

B, S, D = 4, 4096, 1024
T = 2048
H, DH = 16, 64
NC = 8
FFN0 = 3072
QK_SCALE = 8.0
ENGS = ("tensor", "vector", "scalar", "gpsimd", "sync")


class Stage:
    __slots__ = ("name", "ops")

    def __init__(self, name):
        self.name = name
        self.ops = {e: [] for e in ENGS}

    def t(self, fn): self.ops["tensor"].append(("c", fn, 1))
    def v(self, fn): self.ops["vector"].append(("c", fn, 1))
    def s(self, fn): self.ops["scalar"].append(("c", fn, 1))
    def g(self, fn): self.ops["gpsimd"].append(("c", fn, 1))
    def dma(self, fn, n=1): self.ops["sync"].append(("d", fn, n))
    def gdma(self, fn, n=1): self.ops["gpsimd"].append(("d", fn, n))


class Prog:

    _PSMAP = {"PS1": "Z0", "PS2": "Z1", "PSM1": "Z2", "PSM2": "Z3",
              "PSM1E": "Z2", "PSM2E": "Z3", "PSZ0": "Z0", "PSZ1": "Z1",
              "PSZ2": "Z2", "PSZ3": "Z3", "PSN1": "N1", "PSN2": "N2",
              "PSN": "N1", "PSB": "B", "PSBE": "B", "PSO": "O", "PSOE": "O"}
    _PSHAPE = {"Z0": [128, 512], "Z1": [128, 512], "Z2": [128, 512],
               "Z3": [128, 512], "N1": [1, 512], "N2": [1, 512],
               "B": [64, 512], "O": [65, 512]}

    def __init__(self, name="prog"):
        self.nc = bass.Bass("TRN2", target_bir_lowering=False, debug=False,
                            detect_race_conditions=False)
        self.stack = contextlib.ExitStack()
        self.stages = []
        self.name = name
        self.prefix = ""
        self._off = 16512
        self._pspool = {}

    SBUF_BASE = 16512

    def begin_section(self, prefix):
        self.prefix = prefix
        self._off = self.SBUF_BASE

    def sbuf(self, name, shape, dtype):
        nbytes = int(np.prod(shape[1:])) * mybir.dt.size(dtype)
        off = (self._off + 31) & ~31
        self._off = off + nbytes
        h = self.nc.alloc_sbuf_tensor_at(self.prefix + name, shape, dtype,
                                         offset=off)
        return h

    def psum(self, name, shape, dtype=F32):
        key = self._PSMAP[name]
        if key not in self._pspool:
            self._pspool[key] = self.stack.enter_context(
                self.nc.psum_tensor("ps_" + key, self._PSHAPE[key], F32))
        t = self._pspool[key]
        if list(shape) == self._PSHAPE[key]:
            return t
        return t[0:shape[0], 0:shape[1]]

    def sem(self, name):
        return self.stack.enter_context(self.nc.semaphore(name))

    def stage(self, name=""):
        st = Stage(name or f"st{len(self.stages)}")
        self.stages.append(st)
        return st

    def finalize(self):
        nc = self.nc
        barC = self.sem("barC")
        barD = self.sem("barD")
        cb, db = [0], [0]
        for st in self.stages:
            c, d = cb[-1], db[-1]
            for e in ENGS:
                ops = st.ops[e]
                if not ops:
                    continue
                d += 16 * sum(n for k, _, n in ops if k == "d")
                c += sum(n for k, _, n in ops if k == "cc")
                if ops[-1][0] == "c":
                    c += 1
            cb.append(c)
            db.append(d)

        def run_engine(eng_name):
            def body(eng):
                lwc, lwd = 0, 0
                for k, st in enumerate(self.stages):
                    ops = st.ops[eng_name]
                    if not ops:
                        continue
                    if cb[k] > lwc:
                        eng.wait_ge(barC, cb[k])
                        lwc = cb[k]
                    if db[k] > lwd:
                        eng.wait_ge(barD, db[k])
                        lwd = db[k]
                    last_c = None
                    for kind, fn, n in ops:
                        r = fn(eng)
                        insts = r if isinstance(r, (list, tuple)) else [r]
                        insts = [i for i in insts if i is not None]
                        if kind == "d":
                            assert len(insts) >= n, (
                                st.name, eng_name, len(insts), n)
                            for i in insts:
                                i.then_inc(barD, 16)
                        elif kind == "cc":
                            for i in insts:
                                i.then_inc(barC)
                        elif insts:
                            last_c = insts[-1]
                    if ops[-1][0] == "c":
                        assert last_c is not None, (st.name, eng_name)
                        last_c.then_inc(barC, 1)
            return body

        with nc.Block() as blk:
            for e in ENGS:
                if any(st.ops[e] for st in self.stages):
                    getattr(blk, e)(run_engine(e))

            def final_wait(eng):
                eng.wait_ge(barC, cb[-1])
                eng.wait_ge(barD, db[-1])
            blk.vector(final_wait)
        self.stack.close()
        return nc


def cdiv(a, b):
    return (a + b - 1) // b


def emit_a(p, D):
    x_fm = D("x_fm", [1024, T], F32, "ExternalInput")
    wq_t = D("wq_t", [1024, 1024], F32, "ExternalInput")
    wk_t = D("wk_t", [1024, 1024], F32, "ExternalInput")
    wv_t = D("wv_t", [1024, 1024], F32, "ExternalInput")
    wg_t = D("wg_t", [1024, FFN0], I32, "ExternalInput")
    wu_t = D("wu_t", [1024, FFN0], I32, "ExternalInput")
    wd_t = D("wd_t", [FFN0, 1024], I32, "ExternalInput")
    cvec = D("cvec", [128, 4], F32, "ExternalInput")
    o_qaug = D("qaug", [16, 66, T], BF16, "ExternalOutput")
    o_kaug = D("kaug", [16, 66, T], BF16, "ExternalOutput")
    o_vt = D("vtok", [T, 16 * 65], BF16, "ExternalOutput")
    o_kn32 = D("kn32", [16, T], F32, "ExternalOutput")
    o_g = D("g", [1024, T], F32, "ExternalOutput")

    cv = p.sbuf("cv", [128, 4], F32)
    xn16 = p.sbuf("xn16", [128, 8 * T], BF16)
    arenaA = p.sbuf("arenaA", [128, 24 * 1024], BF16)
    wgc = p.sbuf("wgc", [128, 8 * 512], BF16)
    wuc = p.sbuf("wuc", [128, 8 * 512], BF16)
    wdbuf = p.sbuf("wdbuf", [128, 24 * 128], BF16)
    xc = p.sbuf("xc", [128, T], F32)
    rowsA = p.sbuf("rowsA", [65, T], F32)
    rowsB = p.sbuf("rowsB", [33, T], F32)
    ones128 = p.sbuf("ones128", [33, 128], F32)
    onesb = p.sbuf("onesb", [128, 1], BF16)
    sq16 = p.sbuf("sq16", [128, T], BF16)
    xc16 = p.sbuf("xc16", [128, T], BF16)
    sqq = p.sbuf("sqq", [64, 512], BF16)
    qa = p.sbuf("qa", [66, T], BF16)
    ka = p.sbuf("ka", [66, T], BF16)
    vtbuf = p.sbuf("vtbuf", [128, 16 * 65], BF16)
    knrow16 = p.sbuf("knrow16", [1, T], BF16)
    sg = p.sbuf("sg", [128, 512], F32)
    gout = p.sbuf("gout", [128, 512], F32)
    sums = rowsA[0:1, :]
    sqs = rowsA[32:33, :]
    knrow = rowsA[64:65, :]
    mean = rowsB[0:1, :]
    rstd = rowsB[32:33, :]

    PS1 = p.psum("PS1", [128, 512], F32)
    PS2 = p.psum("PS2", [128, 512], F32)
    PSN1 = p.psum("PSN1", [1, 512], F32)
    PSN2 = p.psum("PSN2", [1, 512], F32)
    PSM1 = p.psum("PSM1", [128, 512], F32)
    PSM2 = p.psum("PSM2", [128, 512], F32)

    def wq16(fc):
        return arenaA[:, fc * 1024:(fc + 1) * 1024]

    def wk16(fc):
        return arenaA[:, 8 * 1024 + fc * 1024:8 * 1024 + (fc + 1) * 1024]

    def wv16(fc):
        return arenaA[:, 16 * 1024 + fc * 1024:16 * 1024 + (fc + 1) * 1024]

    def hsl(hc, csl):
        base = hc * 1024
        return arenaA[:, base + csl.start:base + csl.stop]

    st = p.stage("consts")
    st.dma(lambda y: y.dma_start(out=cv[:], in_=cvec))

    st = p.stage("init")

    def init(v):
        v.memset(ones128[:], 1.0)
        v.memset(onesb[:], 1.0)
        v.memset(sums, 0.0)
        v.memset(sqs, 0.0)
        v.memset(qa[64:66, :], 1.0)
        return v.memset(ka[64:66, :], 1.0)
    st.v(init)

    st = p.stage("wload_attn")

    def wload(g):
        out = []
        for fc in range(8):
            out.append(g.dma_start(out=wq16(fc), in_=wq_t[fc * 128:(fc + 1) * 128, :]))
            out.append(g.dma_start(out=wk16(fc), in_=wk_t[fc * 128:(fc + 1) * 128, :]))
            out.append(g.dma_start(out=wv16(fc), in_=wv_t[fc * 128:(fc + 1) * 128, :]))
        return out
    st.gdma(wload, n=24)

    for fc in range(8):
        st = p.stage(f"lnl{fc}")
        st.dma(lambda y, fc=fc: y.dma_start(
            out=xc[:], in_=x_fm[fc * 128:(fc + 1) * 128, :]))
        st = p.stage(f"lnc{fc}")
        st.s(lambda s: s.activation(out=sq16[:], in_=xc[:], func=AF.Square))
        st.v(lambda v: v.tensor_copy(out=xc16[:], in_=xc[:]))
        for tc in range(4):
            tsl = slice(tc * 512, (tc + 1) * 512)
            st = p.stage(f"lnm{fc}_{tc}")

            def mm(t, tsl=tsl):
                t.matmul(PSN1[:, :], onesb[:, 0:1], xc16[:, tsl],
                         start=True, stop=True)
                return t.matmul(PSN2[:, :], onesb[:, 0:1], sq16[:, tsl],
                                start=True, stop=True)
            st.t(mm)
            st = p.stage(f"lne{fc}_{tc}")

            def ev(v, tsl=tsl):
                v.tensor_add(out=sums[:, tsl], in0=sums[:, tsl], in1=PSN1[:, :])
                return v.tensor_add(out=sqs[:, tsl], in0=sqs[:, tsl], in1=PSN2[:, :])
            st.v(ev)

    st = p.stage("lnfin")

    def fin(v):
        v.tensor_scalar_mul(out=mean, in0=sums, scalar1=1.0 / 1024)
        v.tensor_scalar_mul(out=sqs, in0=sqs, scalar1=1.0 / 1024)
        v.tensor_mul(out=rstd, in0=mean, in1=mean)
        return v.tensor_sub(out=sqs, in0=sqs, in1=rstd)
    st.v(fin)
    st = p.stage("lnsqrt")
    st.s(lambda s: s.activation(out=sqs, in_=sqs, func=AF.Sqrt,
                                bias=cv[0:1, 0:1]))
    st = p.stage("lnrec")
    st.v(lambda v: v.reciprocal(out=rstd, in_=sqs))

    for fc in range(8):
        st = p.stage(f"lal{fc}")
        st.dma(lambda y, fc=fc: y.dma_start(
            out=xc[:], in_=x_fm[fc * 128:(fc + 1) * 128, :]))
        for tc in range(4):
            tsl = slice(tc * 512, (tc + 1) * 512)
            st = p.stage(f"lab{fc}_{tc}")

            def bc(t, tsl=tsl):
                t.matmul(PSM1[:, :], ones128[0:1, :], mean[:, tsl],
                         start=True, stop=True)
                return t.matmul(PSM2[:, :], ones128[32:33, :], rstd[:, tsl],
                                start=True, stop=True)
            st.t(bc)
            st = p.stage(f"laa{fc}_{tc}")

            def ap_(v, fc=fc, tsl=tsl):
                v.tensor_sub(out=xc[:, tsl], in0=xc[:, tsl], in1=PSM1[:, :])
                return v.tensor_mul(
                    out=xn16[:, fc * T + tsl.start:fc * T + tsl.stop],
                    in0=xc[:, tsl], in1=PSM2[:, :])
            st.v(ap_)

    for h in range(H):
        hs = h * 64
        for tc in range(4):
            tsl = slice(tc * 512, (tc + 1) * 512)
            st = p.stage(f"qk{h}_{tc}")

            def mmqk(t, h=h, tsl=tsl):
                for fc in range(8):
                    xs = xn16[:, fc * T + tsl.start:fc * T + tsl.stop]
                    t.matmul(PS1[0:64, :], wq16(fc)[:, h * 64:h * 64 + 64], xs,
                             start=(fc == 0), stop=(fc == 7))
                last = None
                for fc in range(8):
                    xs = xn16[:, fc * T + tsl.start:fc * T + tsl.stop]
                    last = t.matmul(PS2[0:64, :], wk16(fc)[:, h * 64:h * 64 + 64],
                                    xs, start=(fc == 0), stop=(fc == 7))
                return last
            st.t(mmqk)
            st = p.stage(f"qke{h}_{tc}")

            def epi(s, tsl=tsl):
                s.activation(out=sqq[:, :], in_=PS1[0:64, :], func=AF.Square)
                s.activation(out=qa[0:64, tsl], in_=PS1[0:64, :],
                             func=AF.Identity, scale=-2.0)
                return s.activation(out=ka[0:64, tsl], in_=PS2[0:64, :],
                                    func=AF.Identity)
            st.s(epi)
            st = p.stage(f"qn{h}_{tc}")
            st.t(lambda t: t.matmul(PSN1[:, :], onesb[0:64, 0:1], sqq[:, :],
                                    start=True, stop=True))
            st = p.stage(f"qne{h}_{tc}")
            st.v(lambda v, tsl=tsl: v.tensor_copy(out=qa[64:65, tsl],
                                                  in_=PSN1[:, :]))
            st = p.stage(f"ksq{h}_{tc}")
            st.s(lambda s, tsl=tsl: s.activation(out=sqq[:, :],
                                                 in_=PS2[0:64, :],
                                                 func=AF.Square))
            st = p.stage(f"kn{h}_{tc}")
            st.t(lambda t: t.matmul(PSN2[:, :], onesb[0:64, 0:1], sqq[:, :],
                                    start=True, stop=True))
            st = p.stage(f"kne{h}_{tc}")

            def kne(v, tsl=tsl):
                v.tensor_copy(out=knrow[:, tsl], in_=PSN2[:, :])
                return v.tensor_copy(out=knrow16[0:1, tsl], in_=PSN2[:, :])
            st.v(kne)
        st = p.stage(f"hst{h}")

        def hstore(y, h=h):
            return [
                y.dma_start(out=o_qaug[h], in_=qa[:]),
                y.dma_start(out=o_kaug[h, 0:65, :], in_=ka[0:65, :]),
                y.dma_start(out=o_kaug[h, 65:66, :], in_=knrow16[:]),
                y.dma_start(out=o_kn32[h:h + 1, :], in_=knrow),
            ]
        st.dma(hstore, n=4)


    st = p.stage("vtones")
    st.v(lambda v: [v.memset(vtbuf[:], 0.0)][-1])
    st = p.stage("vtones2")

    def vto(v):
        last = None
        for hh in range(16):
            last = v.memset(vtbuf[:, hh * 65 + 64:hh * 65 + 65], 1.0)
        return last
    st.v(vto)
    for tt in range(16):
        for vc in range(2):
            st = p.stage(f"vt{tt}_{vc}")

            def vmm(t, tt=tt, vc=vc):
                last = None
                for fc in range(8):
                    last = t.matmul(
                        PS1[:, :],
                        xn16[:, fc * T + tt * 128:fc * T + (tt + 1) * 128],
                        wv16(fc)[:, vc * 512:(vc + 1) * 512],
                        start=(fc == 0), stop=(fc == 7))
                return last
            st.t(vmm)
            st = p.stage(f"vte{tt}_{vc}")

            def vev(v, vc=vc):
                last = None
                for hh in range(8):
                    h = vc * 8 + hh
                    last = v.tensor_copy(
                        out=vtbuf[:, h * 65:h * 65 + 64],
                        in_=PS1[:, hh * 64:(hh + 1) * 64])
                return last
            st.v(vev)
        st = p.stage(f"vts{tt}")
        st.dma(lambda y, tt=tt: y.dma_start(
            out=o_vt[tt * 128:(tt + 1) * 128, :], in_=vtbuf[:]))

    for th in range(2):
        for oc in range(6):
            st = p.stage(f"fw{th}_{oc}")

            def fwload(g, oc=oc):
                out = []
                for fc in range(8):
                    out.append(g.dma_start(
                        out=wgc[:, fc * 512:(fc + 1) * 512],
                        in_=wg_t[fc * 128:(fc + 1) * 128, oc * 512:(oc + 1) * 512]))
                    out.append(g.dma_start(
                        out=wuc[:, fc * 512:(fc + 1) * 512],
                        in_=wu_t[fc * 128:(fc + 1) * 128, oc * 512:(oc + 1) * 512]))
                return out
            st.gdma(fwload, n=16)
            for pc in range(4):
                for tc in range(2):
                    tsl = slice(th * 1024 + tc * 512, th * 1024 + tc * 512 + 512)
                    csl = slice(tc * 512, tc * 512 + 512)
                    st = p.stage(f"fg{th}_{oc}_{pc}_{tc}")

                    def mmg(t, pc=pc, tsl=tsl):
                        for fc in range(8):
                            xs = xn16[:, fc * T + tsl.start:fc * T + tsl.stop]
                            t.matmul(PS1[:, :],
                                     wgc[:, fc * 512 + pc * 128:fc * 512 + pc * 128 + 128],
                                     xs, start=(fc == 0), stop=(fc == 7))
                        last = None
                        for fc in range(8):
                            xs = xn16[:, fc * T + tsl.start:fc * T + tsl.stop]
                            last = t.matmul(
                                PS2[:, :],
                                wuc[:, fc * 512 + pc * 128:fc * 512 + pc * 128 + 128],
                                xs, start=(fc == 0), stop=(fc == 7))
                        return last
                    st.t(mmg)
                    st = p.stage(f"fs{th}_{oc}_{pc}_{tc}")
                    st.s(lambda s: s.activation(out=sg[:], in_=PS1[:, :],
                                                func=AF.Silu, scale=1.0 / 127))
                    st = p.stage(f"fh{th}_{oc}_{pc}_{tc}")
                    st.v(lambda v, oc=oc, pc=pc, csl=csl: v.scalar_tensor_tensor(
                        out=hsl(oc * 4 + pc, csl), in0=sg[:], scalar=1.0 / 127,
                        in1=PS2[:, :], op0=ALU.mult, op1=ALU.mult))

        for op_ in range(8):
            st = p.stage(f"wdl{th}_{op_}")

            def wdload(g, op_=op_):
                return [g.dma_start(
                    out=wdbuf[:, hc * 128:(hc + 1) * 128],
                    in_=wd_t[hc * 128:(hc + 1) * 128, op_ * 128:(op_ + 1) * 128])
                    for hc in range(24)]
            st.gdma(wdload, n=24)
            for tc in range(2):
                tsl = slice(th * 1024 + tc * 512, th * 1024 + tc * 512 + 512)
                csl = slice(tc * 512, tc * 512 + 512)
                st = p.stage(f"gd{th}_{op_}_{tc}")

                def mmd(t, csl=csl):
                    last = None
                    for hc in range(24):
                        last = t.matmul(PS1[:, :],
                                        wdbuf[:, hc * 128:(hc + 1) * 128],
                                        hsl(hc, csl), start=(hc == 0),
                                        stop=(hc == 23))
                    return last
                st.t(mmd)
                st = p.stage(f"ge{th}_{op_}_{tc}")
                st.s(lambda s: s.activation(out=gout[:], in_=PS1[:, :],
                                            func=AF.Silu, scale=1.0 / 127))
                st = p.stage(f"gst{th}_{op_}_{tc}")
                st.dma(lambda y, op_=op_, tsl=tsl: y.dma_start(
                    out=o_g[op_ * 128:(op_ + 1) * 128, tsl], in_=gout[:]))



def _t(a):
    return np.ascontiguousarray(a.T)


def stage_a(inputs):
    x = inputs["L0_attn_x"]
    cvec = np.zeros((128, 4), np.float32)
    cvec[:, 0] = 1e-5
    shared = {
        "wq_t": _t(inputs["L0_attn_Wq"]), "wk_t": _t(inputs["L0_attn_Wk"]),
        "wv_t": _t(inputs["L0_attn_Wv"]),
        "wg_t": _t(inputs["L0_ffn_w_gate"]), "wu_t": _t(inputs["L0_ffn_w_up"]),
        "wd_t": _t(inputs["L0_ffn_w_down"]), "cvec": cvec,
    }
    maps = []
    for c in range(NC):
        b, r = c // 2, c % 2
        m = dict(shared)
        m["x_fm"] = _t(x[b, r * T:(r + 1) * T, :])
        maps.append(m)
    return maps


def kernel(**inputs):
    raise NotImplementedError("under construction")


def emit_b(p, D, exact=False, nheads=8, nqc=8, trunc=None):
    assert not exact, "exact variant is build_launch_bx"
    """L0 hyperbolic attention middle (per core: 8 heads, all 4096 tokens).

    In : qaug_g [8,66,S] bf16 rows [-2q; qn; ones]
         kaug_g [8,66,S] bf16 rows [k; ones; kn]
         v_g    [8,S,65] bf16 (token-major, col 64 = 1.0)
         knt    [8,128,32] f32 (kn tiled: [p, kt] = kn[kt*128+p])
         qnt    [8,128,32] f32 (same for qn; used by exact path)
         scal   [1,2] f32 = [log_c, log_beta]
         masks  [128,4*512] f32 additive causal masks (0 / +1e30),
                block mi at cols [mi*512,(mi+1)*512): mask for kk-tile offset mi*128
         cvecb  [128,8] f32: c0=1e-8 c1=1.0 c2=1.5576e-8 c3=thr(4.0001)
    Out: o16 [8,64,S] bf16 (normalized attention out, feature-major)
         flag [1,1] f32 (#kk-partitions with min z below threshold; fast only)

    Fast path: P = exp(-beta/2 * ln(z+eps) + (beta*c/4)(knmean - kn)), exact
    for the ed>2 branch; detection of any z < 4 via running min -> flag.
    Exact path: full 3-branch dist, same dropped per-q factor class (global
    shift only), overwrite o16.
    """
    i_q = D("qag", [2 * 1056, T], BF16, "ExternalInput")
    i_k = D("kag", [2 * 1056, T], BF16, "ExternalInput")
    i_v = D("vag", [2 * T, 16 * 65], BF16, "ExternalInput")
    i_kn = D("knag", [32, T], F32, "ExternalInput")
    i_hsel = D("hsel", [1, 1], I32, "ExternalInput")
    i_id32 = D("id32", [32, 32], F32, "ExternalInput")
    i_scal = D("scal", [1, 2], F32, "ExternalInput")
    i_masks = D("masks", [128, 4 * 512], F32, "ExternalInput")
    i_cv = D("cvecb", [128, 8], F32, "ExternalInput")
    o_o = D("o16", [8, 64, S], BF16, "ExternalOutput")
    o_flag = D("flag", [1, 1], F32, "ExternalOutput")

    cv = p.sbuf("cv", [128, 8], F32)
    ka = p.sbuf("ka", [66, S], BF16)
    qa = p.sbuf("qa", [66, S], BF16)
    vsb = p.sbuf("vsb", [128, 32 * 65], BF16)
    masks = p.sbuf("maskssb", [128, 4 * 512], F32)
    ptile = p.sbuf("ptile", [128, 4 * 512], BF16)
    lz = p.sbuf("lz", [128, 4 * 512], F32)
    knt = p.sbuf("kntsb", [128, 32], F32)
    knrows = p.sbuf("knrows", [32, 128], F32)
    hselb = p.sbuf("hselb", [1, 1], I32)
    id32b = p.sbuf("id32b", [32, 32], F32)
    bias = p.sbuf("biassb", [128, 32], F32)
    dacc = p.sbuf("dacc", [128, 1], F32)
    rmin = p.sbuf("rmin", [128, 1], F32)
    sc = p.sbuf("sc", [33, 8], F32)
    scb = p.sbuf("scb", [128, 4], F32)
    onesb = p.sbuf("onesb2", [128, 1], BF16)
    ones128 = p.sbuf("ones128b", [1, 128], F32)
    floor16 = p.sbuf("floor16", [1, 66], BF16)
    fval = p.sbuf("fval", [1, 512], BF16)
    rrow = p.sbuf("rrow", [1, 512], F32)
    otile = p.sbuf("otile", [64, 512], F32)
    obuf = p.sbuf("obuf", [64, 512], BF16)
    m128 = p.sbuf("m128", [128, 1], F32)
    m16 = p.sbuf("m16", [128, 1], BF16)
    flagrow = p.sbuf("flagrow", [1, 1], F32)

    PSZ = [p.psum(f"PSZ{i}", [128, 512], F32) for i in range(4)]
    PSO = p.psum("PSO", [65, 512], F32)
    PSB = p.psum("PSB", [64, 512], F32)
    PSN = p.psum("PSN", [1, 512], F32)

    st = p.stage("b_consts")

    def cload(y):
        return [y.dma_start(out=cv[:], in_=i_cv),
                y.dma_start(out=sc[0:1, 0:2], in_=i_scal),
                y.dma_start(out=masks[:], in_=i_masks),
                y.dma_start(out=hselb[:], in_=i_hsel),
                y.dma_start(out=id32b[:], in_=i_id32)]
    st.dma(cload, n=5)
    st = p.stage("b_init")

    def binit(v):
        v.memset(onesb[:], 1.0)
        v.memset(ones128[:], 1.0)
        v.memset(dacc[:], 1e30)
        v.memset(floor16[:], 0.0)
        v.memset(fval[:], 1e-30)
        return v.memset(flagrow[:], 0.0)
    st.v(binit)
    st = p.stage("b_init2")
    st.v(lambda v: v.memset(floor16[0:1, 64:65], 1.0))

    st = p.stage("b_sp1")
    st.s(lambda s: s.activation(out=sc[0:1, 2:4], in_=sc[0:1, 0:2], func=AF.Abs))
    st = p.stage("b_sp2")
    st.s(lambda s: s.activation(out=sc[0:1, 2:4], in_=sc[0:1, 2:4], func=AF.Exp,
                                scale=-1.0))
    st = p.stage("b_sp3")
    st.s(lambda s: s.activation(out=sc[0:1, 2:4], in_=sc[0:1, 2:4], func=AF.Ln,
                                bias=cv[0:1, 1:2]))
    st = p.stage("b_sp3b")
    st.s(lambda s: s.activation(out=sc[0:1, 4:6], in_=sc[0:1, 0:2],
                                func=AF.Relu))
    st = p.stage("b_sp4")
    st.v(lambda v: v.tensor_add(out=sc[0:1, 4:6], in0=sc[0:1, 4:6],
                                in1=sc[0:1, 2:4]))
    st = p.stage("b_sp5")
    st.v(lambda v: v.tensor_scalar_add(out=sc[0:1, 5:6], in0=sc[0:1, 5:6],
                                       scalar1=0.5))
    st = p.stage("b_sp6")
    st.v(lambda v: v.tensor_scalar_mul(out=sc[0:1, 6:7], in0=sc[0:1, 5:6],
                                       scalar1=-0.5))
    st = p.stage("b_sp7")
    st.v(lambda v: v.tensor_mul(out=sc[0:1, 7:8], in0=sc[0:1, 4:5],
                                in1=sc[0:1, 5:6]))
    st = p.stage("b_sp8")
    st.v(lambda v: v.tensor_scalar_mul(out=sc[0:1, 7:8], in0=sc[0:1, 7:8],
                                       scalar1=-0.25))
    st = p.stage("b_spb")

    def spb(t):
        t.matmul(PSZ[0][:, 0:1], ones128[0:1, :], sc[0:1, 6:7],
                 start=True, stop=True)
        return t.matmul(PSZ[0][:, 1:2], ones128[0:1, :], sc[0:1, 7:8],
                        start=True, stop=True)
    st.t(spb)
    st = p.stage("b_spb2")

    def spb2(v):
        v.tensor_copy(out=scb[:, 0:1], in_=PSZ[0][:, 0:1])
        return v.tensor_copy(out=scb[:, 1:2], in_=PSZ[0][:, 1:2])
    st.v(spb2)

    for h in range(nheads):
        st = p.stage(f"bh{h}_load")

        def hload(y, h=h):
            insts = []

            def emit_half(hg):
                gh = hg * 8 + h
                for rh in range(2):
                    insts.append(y.dma_start(
                        out=ka[:, rh * T:(rh + 1) * T],
                        in_=i_k[rh * 1056 + gh * 66:rh * 1056 + (gh + 1) * 66, :]))
                    insts.append(y.dma_start(
                        out=qa[:, rh * T:(rh + 1) * T],
                        in_=i_q[rh * 1056 + gh * 66:rh * 1056 + (gh + 1) * 66, :]))
                    insts.append(y.dma_start(
                        out=vsb[:].rearrange("p (kt d) -> p kt d", d=65)[
                            :, rh * 16:(rh + 1) * 16, :],
                        in_=i_v[rh * T:(rh + 1) * T, :].rearrange(
                            "(kt p) c -> p kt c", p=128)[
                            :, :, gh * 65:(gh + 1) * 65]))
                    insts.append(y.dma_start(
                        out=knrows[rh * 16:(rh + 1) * 16, :],
                        in_=i_kn[rh * 16 + gh:rh * 16 + gh + 1, :].rearrange(
                            "one (a b) -> (one a) b", b=128)))
            with y.register() as rg:
                y.reg_load(rg, hselb[0:1, 0:1])
                with y.If_ne(rg, 0):
                    emit_half(1)
                with y.Else():
                    emit_half(0)
                y.end_ifs()
            return insts
        st.dma(hload, n=8)
        st = p.stage(f"bh{h}_kt")
        st.t(lambda t: t.transpose(PSZ[0][:, 0:32], knrows[:, :], id32b[:, :]))
        st = p.stage(f"bh{h}_kte")
        st.v(lambda v: v.tensor_copy(out=knt[:], in_=PSZ[0][:, 0:32]))
        st = p.stage(f"bh{h}_b1")
        st.v(lambda v: v.reduce_sum(out=rmin[:], in_=knt[:], axis=AX))
        st = p.stage(f"bh{h}_b2")
        st.v(lambda v: v.tensor_copy(out=m16[:], in_=rmin[:]))
        st = p.stage(f"bh{h}_b3")
        st.t(lambda t: t.matmul(PSN[0:1, 0:1], onesb[:, 0:1], m16[:, 0:1],
                                start=True, stop=True))
        st = p.stage(f"bh{h}_b4")
        st.v(lambda v: v.tensor_scalar_mul(out=sc[0:1, 1:2], in0=PSN[0:1, 0:1],
                                           scalar1=1.0 / 4096))
        st = p.stage(f"bh{h}_b5")
        st.t(lambda t: t.matmul(PSZ[0][:, 0:1], ones128[0:1, :], sc[0:1, 1:2],
                                start=True, stop=True))
        st = p.stage(f"bh{h}_b5b")
        st.v(lambda v: v.tensor_copy(out=m128[:, 0:1], in_=PSZ[0][:, 0:1]))
        st = p.stage(f"bh{h}_b6")
        st.v(lambda v: v.tensor_scalar(out=bias[:], in0=knt[:],
                                       scalar1=m128[:, 0:1],
                                       scalar2=scb[:, 1:2], op0=ALU.subtract,
                                       op1=ALU.mult))

        for qc in range(nqc):
            qsl = slice(qc * 512, (qc + 1) * 512)
            ngr = qc + 1
            for gidx in range(ngr):
                kts = [gidx * 4 + j for j in range(4)]
                st = p.stage(f"bh{h}_z{qc}_{gidx}")

                def zmm(t, h=h, qsl=qsl, kts=kts):
                    out = None
                    for j, kt in enumerate(kts):
                        ksl = slice(kt * 128, (kt + 1) * 128)
                        out = t.matmul(PSZ[j][:, :], ka[0:66, ksl],
                                       qa[0:66, qsl], start=True, stop=True)
                        if exact:
                            out = t.matmul(PSB[0:2, :] if False else PSZ[j][:, :],
                                           ka[64:66, ksl], qa[64:66, qsl],
                                           start=False, stop=True)
                    return out
                st.t(zmm)
                st = p.stage(f"bh{h}_e{qc}_{gidx}")

                def ew(v, h=h, qc=qc, kts=kts):
                    last = None
                    for j, kt in enumerate(kts):
                        v.tensor_reduce(out=rmin[:], in_=PSZ[j][:, :], axis=AX,
                                        op=ALU.min)
                        last = v.tensor_tensor(out=dacc[:], in0=dacc[:],
                                               in1=rmin[:], op=ALU.min)
                    return last
                st.v(ew)
                st = p.stage(f"bh{h}_ea{qc}_{gidx}")

                def ew_act(s, qc=qc, kts=kts):
                    last = None
                    for j, kt in enumerate(kts):
                        psl = slice(j * 512, (j + 1) * 512)
                        last = s.activation(out=lz[:, psl], in_=PSZ[j][:, :],
                                            func=AF.Ln, bias=cv[:, 0:1])
                    return last
                st.s(ew_act)
                if any(0 <= kt * 128 - qc * 512 < 512 for kt in kts):
                    st = p.stage(f"bh{h}_m{qc}_{gidx}")

                    def ewm(v, qc=qc, kts=kts):
                        last = None
                        for j, kt in enumerate(kts):
                            d = kt * 128 - qc * 512
                            if 0 <= d < 512:
                                mi = d // 128
                                psl = slice(j * 512, (j + 1) * 512)
                                last = v.tensor_add(
                                    out=lz[:, psl], in0=lz[:, psl],
                                    in1=masks[:, mi * 512:(mi + 1) * 512])
                        return last
                    st.v(ewm)
                st = p.stage(f"bh{h}_x{qc}_{gidx}")

                def ewx(s, kts=kts):
                    last = None
                    for j, kt in enumerate(kts):
                        psl = slice(j * 512, (j + 1) * 512)
                        last = s.activation(out=ptile[:, psl], in_=lz[:, psl],
                                            func=AF.Exp, scale=scb[:, 0:1],
                                            bias=bias[:, kt:kt + 1])
                    return last
                st.s(ewx)
                st = p.stage(f"bh{h}_pv{qc}_{gidx}")

                def pv(t, qc=qc, gidx=gidx, kts=kts):
                    last = None
                    for j, kt in enumerate(kts):
                        psl = slice(j * 512, (j + 1) * 512)
                        last = t.matmul(PSO[:, :],
                                        vsb[:, kt * 65:(kt + 1) * 65],
                                        ptile[:, psl],
                                        start=(gidx == 0 and j == 0), stop=False)
                    if gidx == qc:
                        last = t.matmul(PSO[:, :], floor16[0:1, 0:65],
                                        fval[0:1, :], start=False, stop=True)
                    return last
                st.t(pv)
            st = p.stage(f"bh{h}_n{qc}")
            st.v(lambda v: v.reciprocal(out=rrow[:], in_=PSO[64:65, :]))
            st = p.stage(f"bh{h}_ne{qc}")
            st.s(lambda s: s.activation(out=otile[:], in_=PSO[0:64, :],
                                        func=AF.Identity))
            st = p.stage(f"bh{h}_nb{qc}")
            st.t(lambda t: t.matmul(PSB[:, :], ones128[0:1, 0:64], rrow[0:1, :],
                                    start=True, stop=True))
            st = p.stage(f"bh{h}_nm{qc}")
            st.v(lambda v: v.tensor_mul(out=obuf[:], in0=otile[:], in1=PSB[:, :]))
            st = p.stage(f"bh{h}_st{qc}")
            st.dma(lambda y, h=h, qsl=qsl: y.dma_start(out=o_o[h][:, qsl],
                                                       in_=obuf[:]))

    st = p.stage("b_flag1")
    st.v(lambda v: v.tensor_scalar(out=m128[:], in0=dacc[:],
                                   scalar1=cv[:, 3:4], scalar2=0.0,
                                   op0=ALU.is_lt, op1=ALU.add))
    st = p.stage("b_flag2")
    st.v(lambda v: v.tensor_copy(out=m16[:], in_=m128[:]))
    st = p.stage("b_flag3")
    st.t(lambda t: t.matmul(PSN[0:1, 0:1], onesb[:, 0:1], m16[:, 0:1],
                            start=True, stop=True))
    st = p.stage("b_flag4")
    st.v(lambda v: v.tensor_copy(out=flagrow[:], in_=PSN[0:1, 0:1]))
    st = p.stage("b_flag5")
    st.dma(lambda y: y.dma_start(out=o_flag, in_=flagrow[:]))
    if trunc is not None:
        p.stages = p.stages[:trunc]


def _causal_masks_b():
    out = np.zeros((128, 4 * 512), np.float32)
    for mi in range(4):
        kk = np.arange(128)[:, None] + mi * 128
        q = np.arange(512)[None, :]
        out[:, mi * 512:(mi + 1) * 512] = np.where(kk > q, 1e30, 0.0).astype(np.float32)
    return out


def stage_b(inputs, a_res):
    masks = _causal_masks_b()
    cvb = np.zeros((128, 8), np.float32)
    cvb[:, 0] = 1e-8
    cvb[:, 1] = 1.0
    cvb[:, 3] = 4.0001
    scal = np.array([[float(inputs["L0_attn_log_c"]),
                      float(inputs["L0_attn_log_beta"])]], np.float32)
    id32 = np.eye(32, dtype=np.float32)
    maps = []
    for c in range(NC):
        b, hg = c // 2, c % 2
        ra, rb = a_res[2 * b], a_res[2 * b + 1]
        qag = np.concatenate([ra["qaug"].reshape(1056, T),
                              rb["qaug"].reshape(1056, T)], axis=0)
        kag = np.concatenate([ra["kaug"].reshape(1056, T),
                              rb["kaug"].reshape(1056, T)], axis=0)
        vag = np.concatenate([ra["vtok"], rb["vtok"]], axis=0)
        knag = np.concatenate([ra["kn32"], rb["kn32"]], axis=0)
        maps.append({
            "qag": np.ascontiguousarray(qag),
            "kag": np.ascontiguousarray(kag),
            "vag": np.ascontiguousarray(vag),
            "knag": np.ascontiguousarray(knag),
            "hsel": np.array([[hg]], np.int32), "id32": id32,
            "scal": scal, "masks": masks, "cvecb": cvb,
        })
    return maps


def emit_c(p, D):
    x_fm = D("x_fm", [1024, T], F32, "ExternalInput")
    i_oag = D("oag", [2 * 512, S], BF16, "ExternalInput")
    i_tsel = D("tsel", [1, 1], I32, "ExternalInput")
    wo_t = D("wo_t", [1024, 1024], F32, "ExternalInput")
    g_in = D("g_in", [1024, T], F32, "ExternalInput")
    w_qkv = D("w_qkv", [1024, 3 * 1024], F32, "ExternalInput")
    cosrep = D("cosrep", [128, T], F32, "ExternalInput")
    sinsgn = D("sinsgn", [128, T], F32, "ExternalInput")
    cvec = D("cvec", [128, 4], F32, "ExternalInput")
    o_x1 = D("x1", [1024, T], F32, "ExternalOutput")
    o_xn1 = D("xn1f", [1024, T], F32, "ExternalOutput")
    o_xt = D("xn1tail", [1024, 258], F32, "ExternalOutput")
    o_q = D("q1r", [16, 64, T], BF16, "ExternalOutput")
    o_k = D("k1r", [16, 64, T], BF16, "ExternalOutput")
    o_vt = D("v1tok", [T, 16 * 65], BF16, "ExternalOutput")

    cv = p.sbuf("cv", [128, 4], F32)
    x1 = p.sbuf("x1b", [128, 8 * T], F32)
    xn16 = p.sbuf("xn16", [128, 8 * T], BF16)
    osb = p.sbuf("osb", [128, 8 * T], BF16)
    tselb = p.sbuf("tselb", [1, 1], I32)
    vtbuf = p.sbuf("vtbufc", [128, 16 * 65], BF16)
    wbuf = p.sbuf("wbuf", [128, 8 * 1024], BF16)
    gch = p.sbuf("gch", [128, T], F32)
    cosb = p.sbuf("cosb", [128, T], F32)
    sinb = p.sbuf("sinb", [128, T], F32)
    rowsA = p.sbuf("rowsA", [65, T], F32)
    rowsB = p.sbuf("rowsB", [33, T], F32)
    ones128 = p.sbuf("ones128", [33, 128], F32)
    onesb = p.sbuf("onesb", [128, 1], BF16)
    sq16 = p.sbuf("sq16", [128, T], BF16)
    xc16 = p.sbuf("xc16", [128, T], BF16)
    tmp = p.sbuf("tmpc", [128, 512], F32)
    tmq = p.sbuf("tmqc", [128, 512], F32)
    qtile = p.sbuf("qtile", [128, T], BF16)
    sums = rowsA[0:1, :]
    sqs = rowsA[32:33, :]
    mean = rowsB[0:1, :]
    rstd = rowsB[32:33, :]

    PS1 = p.psum("PS1", [128, 512], F32)
    PSN1 = p.psum("PSN1", [1, 512], F32)
    PSN2 = p.psum("PSN2", [1, 512], F32)
    PSM1 = p.psum("PSM1", [128, 512], F32)
    PSM2 = p.psum("PSM2", [128, 512], F32)

    st = p.stage("c_consts")

    def cl(y):
        return [y.dma_start(out=cv[:], in_=cvec),
                y.dma_start(out=cosb[:], in_=cosrep),
                y.dma_start(out=sinb[:], in_=sinsgn),
                y.dma_start(out=tselb[:], in_=i_tsel)]
    st.dma(cl, n=4)
    st = p.stage("c_init")

    def ci(v):
        v.memset(ones128[:], 1.0)
        v.memset(onesb[:], 1.0)
        v.memset(sums, 0.0)
        return v.memset(sqs, 0.0)
    st.v(ci)
    st = p.stage("c_wload")

    def cw(g):
        out = []
        for fc in range(8):
            out.append(g.dma_start(out=wbuf[:, fc * 1024:(fc + 1) * 1024],
                                   in_=wo_t[fc * 128:(fc + 1) * 128, :]))
        with g.register() as rg:
            g.reg_load(rg, tselb[0:1, 0:1])
            with g.If_ne(rg, 0):
                for fc in range(8):
                    out.append(g.dma_start(
                        out=osb[:, fc * T:(fc + 1) * T],
                        in_=i_oag[fc * 128:(fc + 1) * 128, T:2 * T]))
            with g.Else():
                for fc in range(8):
                    out.append(g.dma_start(
                        out=osb[:, fc * T:(fc + 1) * T],
                        in_=i_oag[fc * 128:(fc + 1) * 128, 0:T]))
            g.end_ifs()
        return out
    st.gdma(cw, n=16)

    for op_ in range(8):
        st = p.stage(f"cx{op_}")

        def cx(y, op_=op_):
            return [y.dma_start(out=gch[:], in_=g_in[op_ * 128:(op_ + 1) * 128, :]),
                    y.dma_start(out=x1[:, op_ * T:(op_ + 1) * T],
                                in_=x_fm[op_ * 128:(op_ + 1) * 128, :])]
        st.dma(cx, n=2)
        for tc in range(4):
            tsl = slice(tc * 512, (tc + 1) * 512)
            st = p.stage(f"cs{op_}_{tc}")

            def mms(t, op_=op_, tsl=tsl):
                last = None
                for fc in range(8):
                    last = t.matmul(
                        PS1[:, :],
                        wbuf[:, fc * 1024 + op_ * 128:fc * 1024 + op_ * 128 + 128],
                        osb[:, fc * T + tsl.start:fc * T + tsl.stop],
                        start=(fc == 0), stop=(fc == 7))
                return last
            st.t(mms)
            st = p.stage(f"ce{op_}_{tc}")

            def ep(v, op_=op_, tsl=tsl):
                xs = x1[:, op_ * T + tsl.start:op_ * T + tsl.stop]
                v.tensor_mul(out=tmp[:], in0=gch[:, tsl], in1=PS1[:, :])
                return v.tensor_add(out=xs, in0=xs, in1=tmp[:])
            st.v(ep)
        st = p.stage(f"cxs{op_}")
        st.dma(lambda y, op_=op_: y.dma_start(
            out=o_x1[op_ * 128:(op_ + 1) * 128, :],
            in_=x1[:, op_ * T:(op_ + 1) * T]))
        st = p.stage(f"cq{op_}")

        def csq(s, op_=op_):
            return s.activation(out=sq16[:], in_=x1[:, op_ * T:(op_ + 1) * T],
                                func=AF.Square)
        st.s(csq)
        st.v(lambda v, op_=op_: v.tensor_copy(
            out=xc16[:], in_=x1[:, op_ * T:(op_ + 1) * T]))
        for tc in range(4):
            tsl = slice(tc * 512, (tc + 1) * 512)
            st = p.stage(f"cm{op_}_{tc}")

            def mm(t, tsl=tsl):
                t.matmul(PSN1[:, :], onesb[:, 0:1], xc16[:, tsl],
                         start=True, stop=True)
                return t.matmul(PSN2[:, :], onesb[:, 0:1], sq16[:, tsl],
                                start=True, stop=True)
            st.t(mm)
            st = p.stage(f"cme{op_}_{tc}")

            def ev(v, tsl=tsl):
                v.tensor_add(out=sums[:, tsl], in0=sums[:, tsl], in1=PSN1[:, :])
                return v.tensor_add(out=sqs[:, tsl], in0=sqs[:, tsl],
                                    in1=PSN2[:, :])
            st.v(ev)

    st = p.stage("cfin")

    def fin(v):
        v.tensor_scalar_mul(out=mean, in0=sums, scalar1=1.0 / 1024)
        v.tensor_scalar_mul(out=sqs, in0=sqs, scalar1=1.0 / 1024)
        v.tensor_mul(out=rstd, in0=mean, in1=mean)
        return v.tensor_sub(out=sqs, in0=sqs, in1=rstd)
    st.v(fin)
    st = p.stage("csqrt")
    st.s(lambda s: s.activation(out=sqs, in_=sqs, func=AF.Sqrt,
                                bias=cv[0:1, 0:1]))
    st = p.stage("crec")
    st.v(lambda v: v.reciprocal(out=rstd, in_=sqs))

    for fc in range(8):
        for tc in range(4):
            tsl = slice(tc * 512, (tc + 1) * 512)
            st = p.stage(f"cb{fc}_{tc}")

            def bc(t, tsl=tsl):
                t.matmul(PSM1[:, :], ones128[0:1, :], mean[:, tsl],
                         start=True, stop=True)
                return t.matmul(PSM2[:, :], ones128[32:33, :], rstd[:, tsl],
                                start=True, stop=True)
            st.t(bc)
            st = p.stage(f"ca{fc}_{tc}")

            def ap_(v, fc=fc, tsl=tsl):
                xs = x1[:, fc * T + tsl.start:fc * T + tsl.stop]
                v.tensor_sub(out=tmp[:], in0=xs, in1=PSM1[:, :])
                v.tensor_mul(out=tmp[:], in0=tmp[:], in1=PSM2[:, :])
                return v.tensor_copy(
                    out=xn16[:, fc * T + tsl.start:fc * T + tsl.stop],
                    in_=tmp[:])
            st.v(ap_)
            st = p.stage(f"cao{fc}_{tc}")
            st.dma(lambda y, fc=fc, tsl=tsl: y.dma_start(
                out=o_xn1[fc * 128:(fc + 1) * 128, tsl], in_=tmp[:]))

    for fc in range(8):
        st = p.stage(f"ctl{fc}")
        st.dma(lambda y, fc=fc: y.dma_start(
            out=gch[:, 0:258], in_=o_xn1[fc * 128:(fc + 1) * 128, T - 258:T]))
        st = p.stage(f"cts{fc}")
        st.dma(lambda y, fc=fc: y.dma_start(
            out=o_xt[fc * 128:(fc + 1) * 128, :], in_=gch[:, 0:258]))

    for oc in range(16):
        part = oc // 8
        if oc % 8 == 0:
            st = p.stage(f"cwq{oc}")

            def cwq(g, oc=oc):
                return [g.dma_start(
                    out=wbuf[:, fc * 1024:(fc + 1) * 1024],
                    in_=w_qkv[fc * 128:(fc + 1) * 128,
                              (oc // 8) * 1024:(oc // 8 + 1) * 1024])
                    for fc in range(8)]
            st.gdma(cwq, n=8)
        for tc in range(4):
            tsl = slice(tc * 512, (tc + 1) * 512)
            st = p.stage(f"cp{oc}_{tc}")

            def mmp(t, oc=oc, tsl=tsl):
                last = None
                for fc in range(8):
                    last = t.matmul(
                        PS1[:, :],
                        wbuf[:, fc * 1024 + (oc % 8) * 128:
                             fc * 1024 + (oc % 8) * 128 + 128],
                        xn16[:, fc * T + tsl.start:fc * T + tsl.stop],
                        start=(fc == 0), stop=(fc == 7))
                return last
            st.t(mmp)
            if part < 2:
                st = p.stage(f"cr{oc}_{tc}")

                def rot(v, tsl=tsl):
                    v.tensor_copy(out=tmq[0:32, :], in_=PS1[32:64, :])
                    v.tensor_copy(out=tmq[32:64, :], in_=PS1[0:32, :])
                    v.tensor_copy(out=tmq[64:96, :], in_=PS1[96:128, :])
                    return v.tensor_copy(out=tmq[96:128, :], in_=PS1[64:96, :])
                st.v(rot)
                st = p.stage(f"cr2{oc}_{tc}")

                def rope(v, tsl=tsl):
                    v.tensor_mul(out=tmp[:], in0=cosb[:, tsl], in1=PS1[:, :])
                    v.tensor_mul(out=tmq[:, :], in0=tmq[:, :], in1=sinb[:, tsl])
                    return v.tensor_add(out=tmp[:], in0=tmp[:], in1=tmq[:, :])
                st.v(rope)
                st = p.stage(f"cr3{oc}_{tc}")
                st.v(lambda v, tsl=tsl: v.tensor_copy(out=qtile[:, tsl],
                                                      in_=tmp[:]))
        st = p.stage(f"cst{oc}")
        dst = (o_q, o_k)[part]
        h0 = (oc % 8) * 2

        def stq(y, dst=dst, h0=h0):
            return [y.dma_start(out=dst[h0], in_=qtile[0:64, :]),
                    y.dma_start(out=dst[h0 + 1], in_=qtile[64:128, :])]
        st.dma(stq, n=2)

    st = p.stage("cvw")
    st.gdma(lambda g: [g.dma_start(
        out=wbuf[:, fc * 1024:(fc + 1) * 1024],
        in_=w_qkv[fc * 128:(fc + 1) * 128, 2048:3072]) for fc in range(8)], n=8)
    st = p.stage("cvones")

    def cvo(v):
        v.memset(vtbuf[:], 0.0)
        last = None
        for hh in range(16):
            last = v.memset(vtbuf[:, hh * 65 + 64:hh * 65 + 65], 1.0)
        return last
    st.v(cvo)
    for tt in range(16):
        for vc in range(2):
            st = p.stage(f"cvt{tt}_{vc}")

            def cvmm(t, tt=tt, vc=vc):
                last = None
                for fc in range(8):
                    last = t.matmul(
                        PS1[:, :],
                        xn16[:, fc * T + tt * 128:fc * T + (tt + 1) * 128],
                        wbuf[:, fc * 1024 + vc * 512:fc * 1024 + (vc + 1) * 512],
                        start=(fc == 0), stop=(fc == 7))
                return last
            st.t(cvmm)
            st = p.stage(f"cve{tt}_{vc}")

            def cvev(v, vc=vc):
                last = None
                for hh in range(8):
                    h = vc * 8 + hh
                    last = v.tensor_copy(
                        out=vtbuf[:, h * 65:h * 65 + 64],
                        in_=PS1[:, hh * 64:(hh + 1) * 64])
                return last
            st.v(cvev)
        st = p.stage(f"cvs{tt}")
        st.dma(lambda y, tt=tt: y.dma_start(
            out=o_vt[tt * 128:(tt + 1) * 128, :], in_=vtbuf[:]))


def emit_d(p, D, nheads=8, nqc=8):
    i_q = D("qag1", [2 * 1024, T], BF16, "ExternalInput")
    i_k = D("kag1", [2 * 1024, T], BF16, "ExternalInput")
    i_v = D("vag1", [2 * T, 16 * 65], BF16, "ExternalInput")
    i_hsel = D("hseld", [1, 1], I32, "ExternalInput")
    i_masks = D("maskd", [128, 4 * 512], F32, "ExternalInput")
    o_o = D("o16", [8, 64, S], BF16, "ExternalOutput")

    ka = p.sbuf("ka", [64, S], BF16)
    qa = p.sbuf("qa", [64, S], BF16)
    vsb = p.sbuf("vsb", [128, 32 * 65], BF16)
    hselb = p.sbuf("hselbd", [1, 1], I32)
    masks = p.sbuf("maskssb", [128, 4 * 512], F32)
    ptile = p.sbuf("ptile", [128, 4 * 512], BF16)
    lzd = p.sbuf("lzd", [128, 512], F32)
    ones128 = p.sbuf("ones128", [1, 128], F32)
    floor16 = p.sbuf("floor16", [1, 66], BF16)
    fval = p.sbuf("fval", [1, 512], BF16)
    rrow = p.sbuf("rrow", [1, 512], F32)
    otile = p.sbuf("otile", [64, 512], F32)
    obuf = p.sbuf("obuf", [64, 512], BF16)

    PSZ = [p.psum(f"PSZ{i}", [128, 512], F32) for i in range(4)]
    PSO = p.psum("PSO", [65, 512], F32)
    PSB = p.psum("PSB", [64, 512], F32)

    st = p.stage("d_consts")
    st.dma(lambda y: [y.dma_start(out=masks[:], in_=i_masks),
                      y.dma_start(out=hselb[:], in_=i_hsel)], n=2)
    st = p.stage("d_init")

    def di(v):
        v.memset(ones128[:], 1.0)
        v.memset(floor16[:], 0.0)
        return v.memset(fval[:], 1e-30)
    st.v(di)
    st = p.stage("d_init2")
    st.v(lambda v: v.memset(floor16[0:1, 64:65], 1.0))

    for h in range(nheads):
        st = p.stage(f"dh{h}_load")

        def hload(y, h=h):
            insts = []

            def emit_half(hg):
                gh = hg * 8 + h
                for rh in range(2):
                    insts.append(y.dma_start(
                        out=ka[:, rh * T:(rh + 1) * T],
                        in_=i_k[rh * 1024 + gh * 64:rh * 1024 + (gh + 1) * 64, :]))
                    insts.append(y.dma_start(
                        out=qa[:, rh * T:(rh + 1) * T],
                        in_=i_q[rh * 1024 + gh * 64:rh * 1024 + (gh + 1) * 64, :]))
                    insts.append(y.dma_start(
                        out=vsb[:].rearrange("p (kt d) -> p kt d", d=65)[
                            :, rh * 16:(rh + 1) * 16, :],
                        in_=i_v[rh * T:(rh + 1) * T, :].rearrange(
                            "(kt p) c -> p kt c", p=128)[
                            :, :, gh * 65:(gh + 1) * 65]))
            with y.register() as rg:
                y.reg_load(rg, hselb[0:1, 0:1])
                with y.If_ne(rg, 0):
                    emit_half(1)
                with y.Else():
                    emit_half(0)
                y.end_ifs()
            return insts
        st.dma(hload, n=6)
        for qc in range(nqc):
            qsl = slice(qc * 512, (qc + 1) * 512)
            for gidx in range(qc + 1):
                kts = [gidx * 4 + j for j in range(4)]
                st = p.stage(f"dh{h}_z{qc}_{gidx}")

                def zmm(t, qsl=qsl, kts=kts):
                    out = None
                    for j, kt in enumerate(kts):
                        ksl = slice(kt * 128, (kt + 1) * 128)
                        out = t.matmul(PSZ[j][:, :], ka[0:64, ksl],
                                       qa[0:64, qsl], start=True, stop=True)
                    return out
                st.t(zmm)
                diag = [(j, kt) for j, kt in enumerate(kts)
                        if 0 <= kt * 128 - qc * 512 < 512]
                if diag:
                    for j, kt in diag:
                        st = p.stage(f"dh{h}_m{qc}_{gidx}_{j}")
                        mi = (kt * 128 - qc * 512) // 128

                        def ewm(v, j=j, mi=mi):
                            return v.tensor_add(
                                out=lzd[:], in0=PSZ[j][:, :],
                                in1=masks[:, mi * 512:(mi + 1) * 512])
                        st.v(ewm)
                        st = p.stage(f"dh{h}_mx{qc}_{gidx}_{j}")
                        st.s(lambda s, j=j: s.activation(
                            out=ptile[:, j * 512:(j + 1) * 512], in_=lzd[:],
                            func=AF.Exp, scale=0.125))
                ndiag = [(j, kt) for j, kt in enumerate(kts)
                         if not (0 <= kt * 128 - qc * 512 < 512)]
                if ndiag:
                    st = p.stage(f"dh{h}_x{qc}_{gidx}")

                    def ewx(s, ndiag=ndiag):
                        last = None
                        for j, kt in ndiag:
                            last = s.activation(
                                out=ptile[:, j * 512:(j + 1) * 512],
                                in_=PSZ[j][:, :], func=AF.Exp, scale=0.125)
                        return last
                    st.s(ewx)
                st = p.stage(f"dh{h}_pv{qc}_{gidx}")

                def pv(t, qc=qc, gidx=gidx, kts=kts):
                    last = None
                    for j, kt in enumerate(kts):
                        last = t.matmul(PSO[:, :],
                                        vsb[:, kt * 65:(kt + 1) * 65],
                                        ptile[:, j * 512:(j + 1) * 512],
                                        start=(gidx == 0 and j == 0),
                                        stop=False)
                    if gidx == qc:
                        last = t.matmul(PSO[:, :], floor16[0:1, 0:65],
                                        fval[0:1, :], start=False, stop=True)
                    return last
                st.t(pv)
            st = p.stage(f"dh{h}_n{qc}")
            st.v(lambda v: v.reciprocal(out=rrow[:], in_=PSO[64:65, :]))
            st = p.stage(f"dh{h}_ne{qc}")
            st.s(lambda s: s.activation(out=otile[:], in_=PSO[0:64, :],
                                        func=AF.Identity))
            st = p.stage(f"dh{h}_nb{qc}")
            st.t(lambda t: t.matmul(PSB[:, :], ones128[0:1, 0:64], rrow[0:1, :],
                                    start=True, stop=True))
            st = p.stage(f"dh{h}_nm{qc}")
            st.v(lambda v: v.tensor_mul(out=obuf[:], in0=otile[:], in1=PSB[:, :]))
            st = p.stage(f"dh{h}_st{qc}")
            st.dma(lambda y, h=h, qsl=qsl: y.dma_start(out=o_o[h][:, qsl],
                                                       in_=obuf[:]))


def _rope_tables_c(r):
    inv_freq = 1.0 / (10000.0 ** (np.arange(0, 64, 2, dtype=np.float32) / 64))
    t = np.arange(r * T, (r + 1) * T, dtype=np.float32)
    fr = t[:, None] * inv_freq[None, :]
    cos = np.cos(fr).T.astype(np.float32)
    sin = np.sin(fr).T.astype(np.float32)
    cosrep = np.tile(cos, (4, 1))
    sinsgn = np.concatenate([-sin, sin, -sin, sin], axis=0)
    return np.ascontiguousarray(cosrep), np.ascontiguousarray(sinsgn)


def stage_c(inputs, a_res, b_res):
    cvec = np.zeros((128, 4), np.float32)
    cvec[:, 0] = 1e-5
    wo_t = _t(inputs["L0_attn_Wo"])
    w_qkv = np.ascontiguousarray(inputs["L1_attn_w_qkv"])
    maps = []
    for c in range(NC):
        b, r = c // 2, c % 2
        tsl = slice(r * T, (r + 1) * T)
        oag = np.concatenate([b_res[2 * b]["o16"].reshape(512, S),
                              b_res[2 * b + 1]["o16"].reshape(512, S)], axis=0)
        cosrep, sinsgn = _rope_tables_c(r)
        maps.append({
            "x_fm": _t(inputs["L0_attn_x"][b, tsl, :]),
            "oag": np.ascontiguousarray(oag), "tsel": np.array([[r]], np.int32),
            "wo_t": wo_t, "g_in": a_res[c]["g"],
            "w_qkv": w_qkv, "cosrep": cosrep, "sinsgn": sinsgn, "cvec": cvec,
        })
    return maps


def _causal_masks_d():
    out = np.zeros((128, 4 * 512), np.float32)
    for mi in range(4):
        kk = np.arange(128)[:, None] + mi * 128
        q = np.arange(512)[None, :]
        out[:, mi * 512:(mi + 1) * 512] = np.where(kk > q, -1e30, 0.0)
    return out


def stage_d(c_res):
    maskd = _causal_masks_d()
    maps = []
    for c in range(NC):
        b, hg = c // 2, c % 2
        ra, rb = c_res[2 * b], c_res[2 * b + 1]
        qag1 = np.concatenate([ra["q1r"].reshape(1024, T),
                               rb["q1r"].reshape(1024, T)], axis=0)
        kag1 = np.concatenate([ra["k1r"].reshape(1024, T),
                               rb["k1r"].reshape(1024, T)], axis=0)
        vag1 = np.concatenate([ra["v1tok"], rb["v1tok"]], axis=0)
        maps.append({"qag1": np.ascontiguousarray(qag1),
                     "kag1": np.ascontiguousarray(kag1),
                     "vag1": np.ascontiguousarray(vag1),
                     "hseld": np.array([[hg]], np.int32), "maskd": maskd})
    return maps


EXT = 258 + T
NW = T // 256


def emit_e(p, D):
    i_yo = D("y_own", [1024, T], F32, "ExternalInput")
    i_yh = D("y_halo", [2 * 1024, 258], F32, "ExternalInput")
    i_wqkv = D("wqkv", [1024, 3072], F32, "ExternalInput")
    i_wo = D("wo", [1024, 1024], F32, "ExternalInput")
    i_w1a = D("w1a", [1024, 2730], F32, "ExternalInput")
    i_w1g = D("w1g", [1024, 2730], F32, "ExternalInput")
    i_w2 = D("w2", [2730, 1024], F32, "ExternalInput")
    i_b1a = D("b1a", [128, 22], F32, "ExternalInput")
    i_b1g = D("b1g", [128, 22], F32, "ExternalInput")
    i_b2 = D("b2c", [128, 8], F32, "ExternalInput")
    i_lng = D("lng", [128, 8], F32, "ExternalInput")
    i_lnb = D("lnb", [128, 8], F32, "ExternalInput")
    i_qsA = D("qsA", [64, EXT], F32, "ExternalInput")
    i_qsB = D("qsB", [64, EXT], F32, "ExternalInput")
    i_ksA = D("ksA", [64, EXT], F32, "ExternalInput")
    i_ksB = D("ksB", [64, EXT], F32, "ExternalInput")
    i_qsc = D("qscol", [64, 2], F32, "ExternalInput")
    i_ksc = D("kscol", [64, 2], F32, "ExternalInput")
    i_mw = D("maskw", [128, 2 * 1024], F32, "ExternalInput")
    i_zg = D("zgate", [1, 1], F32, "ExternalInput")
    i_cv = D("cvec", [128, 4], F32, "ExternalInput")
    o_y = D("ynew", [1024, T], F32, "ExternalOutput")
    o_yt = D("ytail", [1024, 258], F32, "ExternalOutput")

    ysb = p.sbuf("ysb", [128, 8 * EXT], F32)
    xs16 = p.sbuf("xs16", [128, 8 * EXT], BF16)
    oacc = p.sbuf("oaccE", [128, 4 * T], BF16)
    vsb = p.sbuf("vsbE", [128, 8 * 21 * 65], BF16)
    qk16 = p.sbuf("qk16", [128, 2 * EXT], BF16)
    wbuf = p.sbuf("wbufE", [128, 8 * 512], BF16)
    tabs = p.sbuf("tabsE", [64, 4 * EXT], BF16)
    masks = p.sbuf("masksE", [128, 2 * 1024], BF16)
    qhat = p.sbuf("qhatE", [128, 1024], F32)
    tmr = p.sbuf("tmrE", [64, 512], F32)
    tmp = p.sbuf("tmpE", [128, 512], F32)
    ptile = p.sbuf("ptileE", [128, 4 * 256], BF16)
    rows = p.sbuf("rowsE", [65, 528], F32)
    rowsB = p.sbuf("rowsBE", [33, 512], F32)
    cv = p.sbuf("cvE", [128, 4], F32)
    epsT = p.sbuf("epsTE", [65, 512], BF16)
    rows2 = p.sbuf("rows2E", [33, 512], F32)
    ones128 = p.sbuf("ones128E", [65, 128], F32)
    onesb = p.sbuf("onesbE", [128, 1], BF16)
    floor16 = p.sbuf("floor16E", [1, 66], BF16)
    fval = p.sbuf("fvalE", [1, 256], BF16)
    otile = p.sbuf("otileE", [64, 256], F32)
    obuf = p.sbuf("obufE", [64, 256], BF16)
    qsc = p.sbuf("qscE", [64, 2], F32)
    ksc = p.sbuf("kscE", [64, 2], F32)
    zg = p.sbuf("zgE", [1, 1], F32)
    bcols = p.sbuf("bcolsE", [128, 68], F32)
    obnd16 = p.sbuf("obnd16", [128, 8], BF16)
    sq16 = p.sbuf("sq16E", [128, 512], BF16)
    xt16 = p.sbuf("xt16E", [128, 512], BF16)
    hh21 = p.sbuf("hh21E", [64, 512], BF16)

    PS1 = p.psum("PS1", [128, 512], F32)
    PS2 = p.psum("PS2", [128, 512], F32)
    PSM1 = p.psum("PSM1", [128, 512], F32)
    PSM2 = p.psum("PSM2", [128, 512], F32)
    PSN1 = p.psum("PSN1", [1, 512], F32)
    PSN2 = p.psum("PSN2", [1, 512], F32)
    PSB = p.psum("PSBE", [64, 512], F32)
    PSO = p.psum("PSOE", [65, 256], F32)

    rrow = rows2[0:1, 0:256]
    b1a_c, b1g_c = bcols[:, 0:22], bcols[:, 22:44]
    b2_c, lng_c, lnb_c = bcols[:, 44:52], bcols[:, 52:60], bcols[:, 60:68]
    sums, sqs = rowsB[0:1, :], rowsB[32:33, :]
    meanr = rows2[0:1, :]
    msq = rows2[32:33, :]
    rstdr = rowsB[32:33, :]
    nrmr = rows[64:65, 0:512]

    PROJ_SLICES = [slice(1, 513), slice(513, 1025), slice(1025, 1537),
                   slice(1537, 2049), slice(2049, EXT)]
    VTILES = [(2 + 128 * k, 130 + 128 * k) for k in range(18)] + \
             [(1, 129), (129, 257), (257, 258)]

    def yv(fc, a, b):
        return ysb[:, fc * EXT + a:fc * EXT + b]

    def xv(fc, a, b, shifted=True):
        if shifted and fc >= 4:
            a, b = a - 1, b - 1
        return xs16[:, fc * EXT + a:fc * EXT + b]

    def vtile(hh, vt):
        base = (hh * 21 + vt) * 65
        return vsb[:, base:base + 65]

    def hhloc(i):
        if i < 16:
            return oacc[:, i * 512:(i + 1) * 512]
        if i < 21:
            return qk16[:, (i - 16) * 512:(i - 15) * 512]
        return hh21[:, :]

    def xn2loc(fc):
        return vsb[:, fc * 512:(fc + 1) * 512]

    st = p.stage("e_consts")

    def ecl(y):
        return [y.dma_start(out=cv[:], in_=i_cv),
                y.dma_start(out=qsc[:], in_=i_qsc),
                y.dma_start(out=ksc[:], in_=i_ksc),
                y.dma_start(out=zg[:], in_=i_zg),
                y.dma_start(out=bcols[:, 0:22], in_=i_b1a),
                y.dma_start(out=bcols[:, 22:44], in_=i_b1g),
                y.dma_start(out=bcols[:, 44:52], in_=i_b2),
                y.dma_start(out=bcols[:, 52:60], in_=i_lng),
                y.dma_start(out=bcols[:, 60:68], in_=i_lnb)]
    st.dma(ecl, n=9)
    st.gdma(lambda g: g.dma_start(out=masks[:], in_=i_mw), n=1)
    st = p.stage("e_init")

    def ei(v):
        v.memset(ones128[:], 1.0)
        v.memset(onesb[:], 1.0)
        v.memset(floor16[:], 0.0)
        v.memset(fval[:], 1e-30)
        return v.memset(epsT[64:65, :], 1e-12)
    st.v(ei)
    st = p.stage("e_init2")
    st.v(lambda v: v.memset(floor16[0:1, 64:65], 1.0))
    st = p.stage("e_zg0")
    st.t(lambda t: t.matmul(PSM2[:, 0:1], ones128[0:1, :], zg[0:1, 0:1],
                            start=True, stop=True))
    st = p.stage("e_zg0c")
    st.v(lambda v: v.tensor_copy(out=qhat[:, 0:1], in_=PSM2[:, 0:1]))

    for ti, (src, sc_, col) in enumerate(
            ((i_qsA, 'q', 0), (i_qsB, 'q', 1), (i_ksA, 'k', 0), (i_ksB, 'k', 1))):
        st = p.stage(f"e_tl{ti}")
        st.dma(lambda y, src=src: y.dma_start(out=ysb[0:64, 0:EXT], in_=src))
        st = p.stage(f"e_tm{ti}")
        scap = qsc if sc_ == 'q' else ksc
        st.v(lambda v, ti=ti, scap=scap, col=col: v.tensor_scalar_mul(
            out=tabs[:, ti * EXT:(ti + 1) * EXT], in0=ysb[0:64, 0:EXT],
            scalar1=scap[:, col:col + 1]))

    for fc in range(8):
        st = p.stage(f"e_yl{fc}")

        def yl(y, fc=fc):
            return [y.dma_start(out=yv(fc, 0, 258),
                                in_=i_yh[fc * 128:(fc + 1) * 128, :]),
                    y.dma_start(out=yv(fc, 258, EXT),
                                in_=i_yo[fc * 128:(fc + 1) * 128, :])]
        st.dma(yl, n=2)
        st = p.stage(f"e_yg{fc}")
        st.v(lambda v, fc=fc: v.tensor_scalar_mul(out=yv(fc, 0, 258),
                                                  in0=yv(fc, 0, 258),
                                                  scalar1=qhat[:, 0:1]))
        st = p.stage(f"e_yc{fc}")
        st.v(lambda v, fc=fc: v.tensor_copy(out=xs16[:, fc * EXT:(fc + 1) * EXT],
                                            in_=yv(fc, 0, EXT)))

    for hg in range(2):
        st = p.stage(f"e_vw{hg}")
        st.gdma(lambda g, hg=hg: g.dma_start(
            out=wbuf[:].rearrange("p (fc o) -> p fc o", o=512),
            in_=i_wqkv.rearrange("(fc p) o -> p fc o", p=128)[
                :, :, 2048 + hg * 512:2048 + (hg + 1) * 512]), n=1)
        for vt, (a, b) in enumerate(VTILES):
            n = b - a
            st = p.stage(f"e_vp{hg}_{vt}")

            def vmm(t, a=a, b=b, n=n):
                last = None
                for fc in range(8):
                    last = t.matmul(PS1[0:n, :], xv(fc, a, b),
                                    wbuf[:, fc * 512:(fc + 1) * 512],
                                    start=(fc == 0), stop=(fc == 7))
                return last
            st.t(vmm)
            st = p.stage(f"e_ve{hg}_{vt}")

            def vev(v, vt=vt, n=n):
                last = None
                for hh in range(8):
                    last = v.tensor_copy(out=vtile(hh, vt)[0:n, 0:64],
                                         in_=PS1[0:n, hh * 64:(hh + 1) * 64])
                return last
            st.v(vev)
        st = p.stage(f"e_vones{hg}")

        def von(v):
            last = None
            for hh in range(8):
                for vt in range(21):
                    last = v.memset(vtile(hh, vt)[:, 64:65], 1.0)
            return last
        st.v(von)

        for hp in range(4):
            st = p.stage(f"e_qw{hg}_{hp}")

            def qw(g, hg=hg, hp=hp):
                c0 = hg * 512 + hp * 128
                return [
                    g.dma_start(
                        out=wbuf[:].rearrange("p (fc o) -> p fc o", o=512)[
                            :, :, 0:128],
                        in_=i_wqkv.rearrange("(fc p) o -> p fc o", p=128)[
                            :, :, c0:c0 + 128]),
                    g.dma_start(
                        out=wbuf[:].rearrange("p (fc o) -> p fc o", o=512)[
                            :, :, 128:256],
                        in_=i_wqkv.rearrange("(fc p) o -> p fc o", p=128)[
                            :, :, 1024 + c0:1024 + c0 + 128]),
                ]
            st.gdma(qw, n=2)
            for ts_i, tsl in enumerate(PROJ_SLICES):
                n = tsl.stop - tsl.start
                st = p.stage(f"e_qp{hg}_{hp}_{ts_i}")

                def qmm(t, tsl=tsl, n=n):
                    for fc in range(8):
                        t.matmul(PS1[:, 0:n],
                                 wbuf[:, fc * 512:fc * 512 + 128],
                                 xv(fc, tsl.start, tsl.stop),
                                 start=(fc == 0), stop=(fc == 7))
                    last = None
                    for fc in range(8):
                        last = t.matmul(PS2[:, 0:n],
                                        wbuf[:, fc * 512 + 128:fc * 512 + 256],
                                        xv(fc, tsl.start, tsl.stop),
                                        start=(fc == 0), stop=(fc == 7))
                    return last
                st.t(qmm)
                for lh in range(2):
                    rs = lh * 64
                    for side, PS in (("q", PS1), ("k", PS2)):
                        sfx = f"{hg}_{hp}_{ts_i}_{lh}{side}"
                        st = p.stage(f"e_n1{sfx}")
                        st.s(lambda s, PS=PS, rs=rs, n=n: s.activation(
                            out=sq16[0:64, 0:n], in_=PS[rs:rs + 64, 0:n],
                            func=AF.Square))
                        st = p.stage(f"e_n2{sfx}")
                        st.t(lambda t, n=n: t.matmul(
                            PSN1[:, 0:n], onesb[0:64, 0:1], sq16[0:64, 0:n],
                            start=True, stop=True))
                        st = p.stage(f"e_n3{sfx}")
                        st.s(lambda s, n=n: s.activation(
                            out=nrmr[:, 0:n], in_=PSN1[:, 0:n], func=AF.Sqrt))
                        st = p.stage(f"e_n4{sfx}")

                        def n4(v, n=n):
                            v.tensor_max(out=nrmr[:, 0:n], in0=nrmr[:, 0:n],
                                         in1=epsT[64:65, 0:n])
                            return v.reciprocal(out=nrmr[:, 0:n],
                                                in_=nrmr[:, 0:n])
                        st.v(n4)
                        st = p.stage(f"e_n5{sfx}")
                        st.t(lambda t, n=n: t.matmul(
                            PSB[:, 0:n], ones128[64:65, 0:64], nrmr[:, 0:n],
                            start=True, stop=True))
                        st = p.stage(f"e_n6a{sfx}")
                        st.s(lambda s, PS=PS, rs=rs, n=n: s.activation(
                            out=qhat[0:64, 0:n], in_=PS[rs:rs + 64, 0:n],
                            func=AF.Identity))
                        st = p.stage(f"e_n6{sfx}")
                        st.v(lambda v, n=n: v.tensor_mul(
                            out=qhat[0:64, 0:n], in0=qhat[0:64, 0:n],
                            in1=PSB[:, 0:n]))
                        st = p.stage(f"e_r1{sfx}")

                        def r1(v, n=n):
                            v.tensor_copy(out=tmr[0:32, 0:n],
                                          in_=qhat[32:64, 0:n])
                            return v.tensor_copy(out=tmr[32:64, 0:n],
                                                 in_=qhat[0:32, 0:n])
                        st.v(r1)
                        tb = 0 if side == "q" else 2 * EXT
                        st = p.stage(f"e_r2{sfx}")

                        def r2(v, tsl=tsl, n=n, tb=tb):
                            v.tensor_mul(out=qhat[0:64, 0:n],
                                         in0=qhat[0:64, 0:n],
                                         in1=tabs[:, tb + tsl.start:tb + tsl.stop])
                            v.tensor_mul(
                                out=tmr[0:64, 0:n], in0=tmr[0:64, 0:n],
                                in1=tabs[:, tb + EXT + tsl.start:
                                         tb + EXT + tsl.stop])
                            return v.tensor_add(out=qhat[0:64, 0:n],
                                                in0=qhat[0:64, 0:n],
                                                in1=tmr[0:64, 0:n])
                        st.v(r2)
                        qkoff = 0 if side == "q" else EXT
                        st = p.stage(f"e_r3{sfx}")
                        st.v(lambda v, lh=lh, qkoff=qkoff, tsl=tsl, n=n:
                             v.tensor_copy(
                                 out=qk16[lh * 64:lh * 64 + 64,
                                          qkoff + tsl.start:qkoff + tsl.stop],
                                 in_=qhat[0:64, 0:n]))
            for lh in range(2):
                hh = hp * 2 + lh
                qrow = slice(lh * 64, lh * 64 + 64)
                PSL = [PS1, PS2, PSM1, PSM2]
                for w in range(NW):
                    qa_, qb_ = 258 + w * 256, 258 + (w + 1) * 256
                    ka_ = w * 256 + 2
                    mvar = 0 if w == 0 else 1024
                    sfx = f"{hg}_{hp}_{lh}_{w}"
                    st = p.stage(f"e_az{sfx}")

                    def az(t, qrow=qrow, qa_=qa_, qb_=qb_, ka_=ka_, PSL=PSL):
                        out = None
                        for j in range(4):
                            ks = ka_ + j * 128
                            out = t.matmul(PSL[j][:, 0:256],
                                           qk16[qrow, EXT + ks:EXT + ks + 128],
                                           qk16[qrow, qa_:qb_],
                                           start=True, stop=True)
                        return out
                    st.t(az)
                    st = p.stage(f"e_am{sfx}")

                    def am(v, mvar=mvar, PSL=PSL):
                        last = None
                        for j in range(4):
                            last = v.scalar_tensor_tensor(
                                out=qhat[:, j * 256:(j + 1) * 256],
                                in0=PSL[j][:, 0:256], scalar=QK_SCALE,
                                in1=masks[:, mvar + j * 256:mvar + (j + 1) * 256],
                                op0=ALU.mult, op1=ALU.add)
                        return last
                    st.v(am)
                    st = p.stage(f"e_ax{sfx}")

                    def ax(s):
                        last = None
                        for j in range(4):
                            last = s.activation(
                                out=ptile[:, j * 256:(j + 1) * 256],
                                in_=qhat[:, j * 256:(j + 1) * 256], func=AF.Exp)
                        return last
                    st.s(ax)
                    st = p.stage(f"e_apv{sfx}")

                    def apv(t, hh=hh, w=w):
                        for j in range(4):
                            t.matmul(PSO[:, :], vtile(hh, 2 * w + j),
                                     ptile[:, j * 256:(j + 1) * 256],
                                     start=(j == 0), stop=False)
                        return t.matmul(PSO[:, :], floor16[0:1, 0:65],
                                        fval[0:1, :], start=False, stop=True)
                    st.t(apv)
                    st = p.stage(f"e_an{sfx}")
                    st.v(lambda v: v.reciprocal(out=rrow[:], in_=PSO[64:65, :]))
                    st = p.stage(f"e_ane{sfx}")
                    st.s(lambda s: s.activation(out=otile[:], in_=PSO[0:64, :],
                                                func=AF.Identity))
                    st = p.stage(f"e_anb{sfx}")
                    st.t(lambda t: t.matmul(PSB[:, 0:256], ones128[0:1, 0:64],
                                            rrow[0:1, :], start=True, stop=True))
                    st = p.stage(f"e_anm{sfx}")

                    def anm(v, hh=hh, w=w):
                        v.tensor_mul(out=obuf[:], in0=otile[:], in1=PSB[:, 0:256])
                        oc_, ro = hh // 2, (hh % 2) * 64
                        return v.tensor_copy(
                            out=oacc[ro:ro + 64,
                                     oc_ * T + w * 256:oc_ * T + (w + 1) * 256],
                            in_=obuf[:])
                    st.v(anm)
                sfx = f"b{hg}_{hp}_{lh}"
                st = p.stage(f"e_bz{sfx}")

                def bz(t, qrow=qrow):
                    t.matmul(PS1[:, 0:1], qk16[qrow, EXT + 1:EXT + 129],
                             qk16[qrow, 257:258], start=True, stop=True)
                    t.matmul(PS2[:, 0:1], qk16[qrow, EXT + 129:EXT + 257],
                             qk16[qrow, 257:258], start=True, stop=True)
                    return t.matmul(PSM1[0:1, 0:1],
                                    qk16[qrow, EXT + 257:EXT + 258],
                                    qk16[qrow, 257:258], start=True, stop=True)
                st.t(bz)
                st = p.stage(f"e_bm{sfx}")

                def bm(v):
                    v.tensor_scalar_mul(out=qhat[:, 0:1], in0=PS1[:, 0:1],
                                        scalar1=QK_SCALE)
                    v.tensor_scalar_mul(out=qhat[:, 1:2], in0=PS2[:, 0:1],
                                        scalar1=QK_SCALE)
                    return v.tensor_scalar_mul(out=qhat[0:1, 2:3],
                                               in0=PSM1[0:1, 0:1],
                                               scalar1=QK_SCALE)
                st.v(bm)
                st = p.stage(f"e_bx{sfx}")
                st.s(lambda s: s.activation(out=ptile[:, 0:3],
                                            in_=qhat[:, 0:3], func=AF.Exp))
                st = p.stage(f"e_bpv{sfx}")

                def bpv(t, hh=hh):
                    t.matmul(PSO[:, 0:1], vtile(hh, 18), ptile[:, 0:1],
                             start=True, stop=False)
                    t.matmul(PSO[:, 0:1], vtile(hh, 19), ptile[:, 1:2],
                             start=False, stop=False)
                    t.matmul(PSO[:, 0:1], vtile(hh, 20)[0:1, :],
                             ptile[0:1, 2:3], start=False, stop=False)
                    return t.matmul(PSO[:, 0:1], floor16[0:1, 0:65],
                                    fval[0:1, 0:1], start=False, stop=True)
                st.t(bpv)
                st = p.stage(f"e_bn{sfx}")
                st.v(lambda v: v.reciprocal(out=rrow[:, 0:1],
                                            in_=PSO[64:65, 0:1]))
                st = p.stage(f"e_bne{sfx}")
                st.s(lambda s: s.activation(out=otile[:, 0:1],
                                            in_=PSO[0:64, 0:1],
                                            func=AF.Identity))
                st = p.stage(f"e_bnb{sfx}")
                st.t(lambda t: t.matmul(PSB[:, 0:1], ones128[0:1, 0:64],
                                        rrow[0:1, 0:1], start=True, stop=True))
                st = p.stage(f"e_bnm{sfx}")

                def bnm(v, hg=hg, hh=hh):
                    v.tensor_mul(out=obuf[:, 0:1], in0=otile[:, 0:1],
                                 in1=PSB[:, 0:1])
                    return v.tensor_copy(out=rows[0:64, hg * 8 + hh:
                                                  hg * 8 + hh + 1],
                                         in_=obuf[:, 0:1])
                st.v(bnm)

        st = p.stage(f"e_obc{hg}")

        def obc(v, hg=hg):
            last = None
            for fc in range(4):
                h0 = hg * 8 + 2 * fc
                last = v.tensor_copy(out=obnd16[0:64, fc:fc + 1],
                                     in_=rows[0:64, h0:h0 + 1])
                last = v.tensor_copy(out=obnd16[64:128, fc:fc + 1],
                                     in_=rows[0:64, h0 + 1:h0 + 2])
            return last
        st.v(obc)
        for op_ in range(8):
            st = p.stage(f"e_ow{hg}_{op_}")
            st.gdma(lambda g, hg=hg, op_=op_: g.dma_start(
                out=wbuf[:].rearrange("p (fc o) -> p fc o", o=512)[:, 0:4, 0:128],
                in_=i_wo.rearrange("(fc p) o -> p fc o", p=128)[
                    :, hg * 4:(hg + 1) * 4, op_ * 128:(op_ + 1) * 128]), n=1)
            for tc in range(4):
                tsl = slice(tc * 512, (tc + 1) * 512)
                st = p.stage(f"e_op{hg}_{op_}_{tc}")

                def omm(t, tsl=tsl):
                    last = None
                    for fc in range(4):
                        last = t.matmul(
                            PS1[:, :], wbuf[:, fc * 512:fc * 512 + 128],
                            oacc[:, fc * T + tsl.start:fc * T + tsl.stop],
                            start=(fc == 0), stop=(fc == 3))
                    return last
                st.t(omm)
                st = p.stage(f"e_oe{hg}_{op_}_{tc}")

                def oe(v, op_=op_, tsl=tsl):
                    ys = yv(op_, 258 + tsl.start, 258 + tsl.stop)
                    return v.tensor_add(out=ys, in0=ys, in1=PS1[:, :])
                st.v(oe)
            st = p.stage(f"e_obm{hg}_{op_}")

            def obm(t):
                last = None
                for fc in range(4):
                    last = t.matmul(PS2[:, 0:1],
                                    wbuf[:, fc * 512:fc * 512 + 128],
                                    obnd16[:, fc:fc + 1], start=(fc == 0),
                                    stop=(fc == 3))
                return last
            st.t(obm)
            st = p.stage(f"e_obe{hg}_{op_}")

            def obe(v, op_=op_):
                ys = yv(op_, 257, 258)
                return v.tensor_add(out=ys, in0=ys, in1=PS2[:, 0:1])
            st.v(obe)

    st = p.stage("e_zgb")
    st.t(lambda t: t.matmul(PSM2[:, 0:1], ones128[0:1, :], zg[0:1, 0:1],
                            start=True, stop=True))
    st = p.stage("e_zgc")
    st.v(lambda v: v.tensor_copy(out=qhat[:, 0:1], in_=PSM2[:, 0:1]))
    for op_ in range(8):
        st = p.stage(f"e_zg{op_}")
        st.v(lambda v, op_=op_: v.tensor_mul(out=yv(op_, 257, 258),
                                             in0=yv(op_, 257, 258),
                                             in1=qhat[:, 0:1]))

    for tq in range(4):
        ha = 258 + tq * 512
        st = p.stage(f"e_f0{tq}")

        def f0(v):
            v.memset(sums, 0.0)
            return v.memset(sqs, 0.0)
        st.v(f0)
        for fc in range(8):
            sh = 1 if fc >= 4 else 0
            st = p.stage(f"e_fs{tq}_{fc}")
            st.s(lambda s, fc=fc, ha=ha, sh=sh: s.activation(
                out=sq16[:, :], in_=yv(fc, ha - sh, ha - sh + 512),
                func=AF.Square))
            st.v(lambda v, fc=fc, ha=ha, sh=sh: v.tensor_copy(
                out=xt16[:, :], in_=yv(fc, ha - sh, ha - sh + 512)))
            st = p.stage(f"e_fm{tq}_{fc}")

            def fmm(t):
                t.matmul(PSN1[:, :], onesb[:, 0:1], xt16[:, :],
                         start=True, stop=True)
                return t.matmul(PSN2[:, :], onesb[:, 0:1], sq16[:, :],
                                start=True, stop=True)
            st.t(fmm)
            st = p.stage(f"e_fe{tq}_{fc}")

            def fev(v):
                v.tensor_add(out=sums, in0=sums, in1=PSN1[:, :])
                return v.tensor_add(out=sqs, in0=sqs, in1=PSN2[:, :])
            st.v(fev)
        st = p.stage(f"e_ff{tq}")

        def ffin(v):
            v.tensor_scalar_mul(out=meanr, in0=sums, scalar1=1.0 / 1024)
            v.tensor_scalar_mul(out=sqs, in0=sqs, scalar1=1.0 / 1024)
            v.tensor_mul(out=msq, in0=meanr, in1=meanr)
            return v.tensor_sub(out=sqs, in0=sqs, in1=msq)
        st.v(ffin)
        st = p.stage(f"e_fq{tq}")
        st.s(lambda s: s.activation(out=sqs, in_=sqs, func=AF.Sqrt,
                                    bias=cv[32:33, 0:1]))
        st = p.stage(f"e_fr{tq}")
        st.v(lambda v: v.reciprocal(out=rstdr, in_=sqs))
        st = p.stage(f"e_fb{tq}")

        def fbc(t):
            t.matmul(PSM1[:, :], ones128[0:1, :], meanr[:, :],
                     start=True, stop=True)
            return t.matmul(PSM2[:, :], ones128[32:33, :], rstdr[:, :],
                            start=True, stop=True)
        st.t(fbc)
        for fc in range(8):
            sh = 1 if fc >= 4 else 0
            st = p.stage(f"e_fa{tq}_{fc}")

            def fap(v, fc=fc, ha=ha, sh=sh):
                ys = yv(fc, ha - sh, ha - sh + 512)
                v.tensor_sub(out=tmp[:], in0=ys, in1=PSM1[:, :])
                v.tensor_mul(out=tmp[:], in0=tmp[:], in1=PSM2[:, :])
                return v.tensor_scalar(
                    out=xn2loc(fc)[:, :], in0=tmp[:],
                    scalar1=lng_c[:, fc:fc + 1], scalar2=lnb_c[:, fc:fc + 1],
                    op0=ALU.mult, op1=ALU.add)
            st.v(fap)
        for i in range(22):
            n = 128 if i < 21 else 42
            st = p.stage(f"e_w1l{tq}_{i}")

            def w1l(g, i=i, n=n):
                return [
                    g.dma_start(
                        out=wbuf[:].rearrange("p (fc o) -> p fc o", o=512)[
                            :, :, 0:n],
                        in_=i_w1a.rearrange("(fc p) o -> p fc o", p=128)[
                            :, :, i * 128:i * 128 + n]),
                    g.dma_start(
                        out=wbuf[:].rearrange("p (fc o) -> p fc o", o=512)[
                            :, :, 128:128 + n],
                        in_=i_w1g.rearrange("(fc p) o -> p fc o", p=128)[
                            :, :, i * 128:i * 128 + n]),
                ]
            st.gdma(w1l, n=2)
            st = p.stage(f"e_w1m{tq}_{i}")

            def w1m(t, i=i, n=n):
                for fc in range(8):
                    t.matmul(PS1[0:n, :], wbuf[:, fc * 512:fc * 512 + n],
                             xn2loc(fc)[:, :], start=(fc == 0), stop=(fc == 7))
                last = None
                for fc in range(8):
                    last = t.matmul(
                        PS2[0:n, :], wbuf[:, fc * 512 + 128:fc * 512 + 128 + n],
                        xn2loc(fc)[:, :], start=(fc == 0), stop=(fc == 7))
                return last
            st.t(w1m)
            st = p.stage(f"e_w1g{tq}_{i}")
            st.s(lambda s, i=i, n=n: s.activation(
                out=tmp[0:n, :], in_=PS2[0:n, :], func=AF.Gelu,
                bias=b1g_c[0:n, i:i + 1]))
            st = p.stage(f"e_w1h{tq}_{i}")

            def w1h(v, i=i, n=n):
                v.tensor_scalar(out=qhat[0:n, 0:512], in0=PS1[0:n, :],
                                scalar1=b1a_c[0:n, i:i + 1], scalar2=0.0,
                                op0=ALU.add, op1=ALU.add)
                return v.tensor_mul(out=hhloc(i)[0:n, 0:512],
                                    in0=qhat[0:n, 0:512], in1=tmp[0:n, :])
            st.v(w1h)
        for op_ in range(8):
            st = p.stage(f"e_w2l{tq}_{op_}")

            def w2l(g, op_=op_):
                return [
                    g.dma_start(
                        out=wbuf[:, 0:2688].rearrange("p (i o) -> p i o", o=128),
                        in_=i_w2[0:2688, :].rearrange("(i p) o -> p i o", p=128)[
                            :, :, op_ * 128:(op_ + 1) * 128]),
                    g.dma_start(
                        out=wbuf[0:42, 2688:2816],
                        in_=i_w2[2688:2730, op_ * 128:(op_ + 1) * 128]),
                ]
            st.gdma(w2l, n=2)
            st = p.stage(f"e_w2m{tq}_{op_}")

            def w2m(t):
                last = None
                for i in range(22):
                    n = 128 if i < 21 else 42
                    last = t.matmul(PS1[:, :], wbuf[0:n, i * 128:i * 128 + 128],
                                    hhloc(i)[0:n, 0:512], start=(i == 0),
                                    stop=(i == 21))
                return last
            st.t(w2m)
            st = p.stage(f"e_w2e{tq}_{op_}")

            def w2e(v, op_=op_, ha=ha):
                v.tensor_scalar(out=tmp[:], in0=PS1[:, :],
                                scalar1=b2_c[:, op_:op_ + 1], scalar2=0.0,
                                op0=ALU.add, op1=ALU.add)
                ys = yv(op_, ha, ha + 512)
                return v.tensor_add(out=ys, in0=ys, in1=tmp[:])
            st.v(w2e)
            st = p.stage(f"e_w2s{tq}_{op_}")
            st.dma(lambda y, op_=op_, tq=tq, ha=ha: y.dma_start(
                out=o_y[op_ * 128:(op_ + 1) * 128, tq * 512:(tq + 1) * 512],
                in_=yv(op_, ha, ha + 512)))
    for op_ in range(8):
        st = p.stage(f"e_tail{op_}")
        st.dma(lambda y, op_=op_: y.dma_start(
            out=o_yt[op_ * 128:(op_ + 1) * 128, :],
            in_=yv(op_, EXT - 258, EXT)))


def _xpos_ext_tables(r):
    half = np.arange(0, 64, 2, dtype=np.float64)
    inv_freq = 1.0 / 10000.0 ** (half / 64)
    t = np.arange(r * T - 258, r * T + T, dtype=np.float64)
    freqs = t[:, None] * inv_freq[None, :]
    freqs = np.concatenate([freqs, freqs], -1)
    base_scale = (half + 0.4 * 64) / (1.4 * 64)
    power = (t - S // 2) / (256 // 2)
    scale = base_scale[None, :] ** power[:, None]
    scale = np.concatenate([scale, scale], -1)
    cos, sin = np.cos(freqs), np.sin(freqs)
    sgn = np.concatenate([-np.ones(32), np.ones(32)])[None, :]
    qsA = np.ascontiguousarray((cos * scale).T.astype(np.float32))
    qsB = np.ascontiguousarray((sgn * sin * scale).T.astype(np.float32))
    ksA = np.ascontiguousarray((cos / scale).T.astype(np.float32))
    ksB = np.ascontiguousarray((sgn * sin / scale).T.astype(np.float32))
    return qsA, qsB, ksA, ksB


def _maskw(r):
    kk = np.arange(128)[:, None]
    q = np.arange(256)[None, :]
    out = np.zeros((128, 2048), np.float32)
    for mi in range(4):
        key_rel = mi * 128 + kk - 256
        ok = (key_rel <= q) & (key_rel >= q - 256)
        normal = np.where(ok, 0.0, -1e30).astype(np.float32)
        w0 = normal.copy()
        if r == 0:
            w0 = np.where(key_rel < 0, -1e30, w0).astype(np.float32)
        out[:, 1024 + mi * 256:1024 + (mi + 1) * 256] = normal
        out[:, mi * 256:(mi + 1) * 256] = w0
    return out


def _colpack(vec, ncol):
    out = np.zeros((128, ncol), np.float32)
    v = np.asarray(vec, np.float32)
    for j in range(ncol):
        seg = v[j * 128:(j + 1) * 128]
        out[:len(seg), j] = seg
    return out


def stage_e(inputs, li, y_fm_per_core):
    cvec = np.zeros((128, 4), np.float32)
    cvec[:, 0] = 1e-5
    W1 = inputs["L1_ffn_W1"][li]
    b1 = inputs["L1_ffn_b1"][li]
    shared = {
        "wqkv": np.ascontiguousarray(inputs["L1_ffn_Wqkv"][li]),
        "wo": np.ascontiguousarray(inputs["L1_ffn_Wo"][li]),
        "w1a": np.ascontiguousarray(W1[:, 0:2730]),
        "w1g": np.ascontiguousarray(W1[:, 2730:]),
        "w2": np.ascontiguousarray(inputs["L1_ffn_W2"][li]),
        "b1a": _colpack(b1[0:2730], 22), "b1g": _colpack(b1[2730:], 22),
        "b2c": _colpack(inputs["L1_ffn_b2"][li], 8),
        "lng": _colpack(inputs["L1_ffn_ln_g"][li], 8),
        "lnb": _colpack(inputs["L1_ffn_ln_b"][li], 8),
        "cvec": cvec,
    }
    qs = np.asarray(inputs["L1_ffn_q_scale"][li], np.float32)
    ks = np.asarray(inputs["L1_ffn_k_scale"][li], np.float32)
    shared["qscol"] = np.stack([qs, np.concatenate([qs[32:], qs[:32]])], axis=1)
    shared["kscol"] = np.stack([ks, np.concatenate([ks[32:], ks[:32]])], axis=1)
    maps = []
    for c in range(NC):
        b, r = c // 2, c % 2
        m = dict(shared)
        qsA, qsB, ksA, ksB = _xpos_ext_tables(r)
        m.update(qsA=qsA, qsB=qsB, ksA=ksA, ksB=ksB, maskw=_maskw(r),
                 zgate=np.array([[0.0 if r == 0 else 1.0]], np.float32))
        m["y_own"] = np.ascontiguousarray(y_fm_per_core[c])
        yh = np.zeros((2 * 1024, 258), np.float32)
        yh[0:1024] = y_fm_per_core[2 * b][:, T - 258:]
        yh[1024:] = y_fm_per_core[2 * b + 1][:, T - 258:]
        m["y_halo"] = yh
        maps.append(m)
    return maps


def emit_f(p, D):
    i_x1 = D("x1", [1024, T], F32, "ExternalInput")
    i_oag = D("oag1", [2 * 512, S], BF16, "ExternalInput")
    i_tsel = D("tself", [1, 1], I32, "ExternalInput")
    i_w = D("wout", [1024, 1024], F32, "ExternalInput")
    i_y6 = D("y6", [1024, T], F32, "ExternalInput")
    o_x = D("xout", [1024, T], F32, "ExternalOutput")

    osb = p.sbuf("osbF", [128, 8 * T], BF16)
    tselb = p.sbuf("tselbF", [1, 1], I32)
    wsb = p.sbuf("wsbF", [128, 8 * 1024], BF16)
    x1c = p.sbuf("x1cF", [128, T], F32)
    y6c = p.sbuf("y6cF", [128, T], F32)
    gg = p.sbuf("ggF", [128, 512], F32)
    tt = p.sbuf("ttF", [128, 512], F32)
    PS1 = p.psum("PS1", [128, 512], F32)

    st = p.stage("f_load")

    def fl(g):
        out = [g.dma_start(out=wsb[:].rearrange("p (fc o) -> p fc o", o=1024),
                           in_=i_w.rearrange("(fc p) o -> p fc o", p=128)),
               g.dma_start(out=tselb[:], in_=i_tsel)]
        with g.register() as rg:
            g.reg_load(rg, tselb[0:1, 0:1])
            with g.If_ne(rg, 0):
                for fc in range(8):
                    out.append(g.dma_start(
                        out=osb[:, fc * T:(fc + 1) * T],
                        in_=i_oag[fc * 128:(fc + 1) * 128, T:2 * T]))
            with g.Else():
                for fc in range(8):
                    out.append(g.dma_start(
                        out=osb[:, fc * T:(fc + 1) * T],
                        in_=i_oag[fc * 128:(fc + 1) * 128, 0:T]))
            g.end_ifs()
        return out
    st.gdma(fl, n=10)

    for op_ in range(8):
        st = p.stage(f"f_x{op_}")

        def fx(y, op_=op_):
            return [y.dma_start(out=x1c[:], in_=i_x1[op_ * 128:(op_ + 1) * 128, :]),
                    y.dma_start(out=y6c[:], in_=i_y6[op_ * 128:(op_ + 1) * 128, :])]
        st.dma(fx, n=2)
        for tc in range(4):
            tsl = slice(tc * 512, (tc + 1) * 512)
            st = p.stage(f"f_m{op_}_{tc}")

            def fm(t, op_=op_, tsl=tsl):
                last = None
                for fc in range(8):
                    last = t.matmul(
                        PS1[:, :],
                        wsb[:, fc * 1024 + op_ * 128:fc * 1024 + op_ * 128 + 128],
                        osb[:, fc * T + tsl.start:fc * T + tsl.stop],
                        start=(fc == 0), stop=(fc == 7))
                return last
            st.t(fm)
            st = p.stage(f"f_g{op_}_{tc}")
            st.s(lambda s, tsl=tsl: s.activation(out=gg[:], in_=y6c[:, tsl],
                                                 func=AF.Silu))
            st = p.stage(f"f_e{op_}_{tc}")

            def fe(v, tsl=tsl):
                v.tensor_mul(out=tt[:], in0=gg[:], in1=PS1[:, :])
                return v.tensor_add(out=tt[:], in0=tt[:], in1=x1c[:, tsl])
            st.v(fe)
            st = p.stage(f"f_s{op_}_{tc}")
            st.dma(lambda y, op_=op_, tsl=tsl: y.dma_start(
                out=o_x[op_ * 128:(op_ + 1) * 128, tsl], in_=tt[:]))


def stage_f(inputs, c_res, d_res, e_res):
    wout = np.ascontiguousarray(inputs["L1_attn_w_out"])
    maps = []
    for c in range(NC):
        b, r = c // 2, c % 2
        oag1 = np.concatenate([d_res[2 * b]["o16"].reshape(512, S),
                               d_res[2 * b + 1]["o16"].reshape(512, S)], axis=0)
        maps.append({
            "x1": c_res[c]["x1"], "oag1": np.ascontiguousarray(oag1),
            "tself": np.array([[r]], np.int32),
            "wout": wout, "y6": e_res[c]["ynew"],
        })
    return maps


_NC_CACHE = {}


def _get(name, builder):
    if name not in _NC_CACHE:
        _NC_CACHE[name] = builder()
    return _NC_CACHE[name]


def kernel_unfused(**inputs):
    inputs = {k: np.asarray(v) for k, v in inputs.items()}
    cores = list(range(NC))
    resA = run_bass_kernel_spmd(_get("A", build_launch_a), stage_a(inputs),
                                core_ids=cores)
    resB = run_bass_kernel_spmd(_get("B", build_launch_b),
                                stage_b(inputs, resA.results), core_ids=cores)
    if any(float(r["flag"][0, 0]) > 0 for r in resB.results):
        raise NotImplementedError(
            "L0 hyperbolic attention hit a non-asymptotic branch; exact "
            "fixup path required")
    resC = run_bass_kernel_spmd(_get("C", build_launch_c),
                                stage_c(inputs, resA.results, resB.results),
                                core_ids=cores)
    resD = run_bass_kernel_spmd(_get("D", build_launch_d),
                                stage_d(resC.results), core_ids=cores)
    y_fm = [resC.results[c]["xn1f"] for c in range(NC)]
    ncE = _get("E", build_launch_e)
    for li in range(6):
        resE = run_bass_kernel_spmd(ncE, stage_e(inputs, li, y_fm),
                                    core_ids=cores)
        y_fm = [resE.results[c]["ynew"] for c in range(NC)]
    resF = run_bass_kernel_spmd(_get("F", build_launch_f),
                                stage_f(inputs, resC.results, resD.results,
                                        {c: {"ynew": y_fm[c]} for c in range(NC)}),
                                core_ids=cores)
    out = np.zeros((B, S, D), np.float32)
    for c in range(NC):
        b, r = c // 2, c % 2
        out[b, r * T:(r + 1) * T, :] = resF.results[c]["xout"].T
    return out


def _mk_builder(emit, name, **kw):
    def build():
        p = Prog(name)
        p.begin_section("")

        def D(nm, shape, dt, kind):
            return p.nc.dram_tensor(nm, shape, dt, kind=kind).ap()
        emit(p, D, **kw)
        return p.finalize()
    return build


build_launch_a = _mk_builder(emit_a, "A")
build_launch_b = _mk_builder(emit_b, "B")
build_launch_c = _mk_builder(emit_c, "C")
build_launch_d = _mk_builder(emit_d, "D")
build_launch_e = _mk_builder(emit_e, "E")
build_launch_f = _mk_builder(emit_f, "F")


PAIRS = [[0, 1], [2, 3], [4, 5], [6, 7]]

_E_PER_LI = {"wqkv", "wo", "w1a", "w1g", "w2", "b1a", "b1g", "b2c", "lng",
             "lnb", "qscol", "kscol", "y_own", "y_halo", "ynew", "ytail"}


def build_fused():
    p = Prog("FU")
    nc = p.nc
    tensors = {}

    def getD(prefix, internal_names, rename=None):
        rename = rename or {}

        def D(nm, shape, dt, kind):
            key = rename.get(nm, prefix + nm)
            if key in tensors:
                return tensors[key]
            k = "Internal" if nm in internal_names else kind
            if k == "ExternalOutput" and nm not in internal_names:
                ap = nc.dram_tensor(key, shape, dt, kind="ExternalOutput").ap()
            else:
                ap = nc.dram_tensor(key, shape, dt, kind=k).ap()
            tensors[key] = ap
            return ap
        return D

    def ag(name, in_ap, out_ap):
        st = p.stage("ag_" + name)
        st.ops["gpsimd"].append(("cc", lambda g, i=in_ap, o=out_ap:
                                 g.collective_compute(
                                     "AllGather", ALU.bypass,
                                     replica_groups=PAIRS,
                                     ins=[i.opt()], outs=[o.opt()]), 1))

    def mk(key, shape, dt):
        ap = nc.dram_tensor(key, shape, dt, kind="Internal").ap()
        tensors[key] = ap
        return ap

    p.begin_section("A_")
    emit_a(p, getD("A_", {"qaug", "kaug", "vtok", "kn32", "g"}))
    qag = mk("qag", [2 * 1056, T], BF16)
    kag = mk("kag", [2 * 1056, T], BF16)
    vag = mk("vag", [2 * T, 16 * 65], BF16)
    knag = mk("knag", [32, T], F32)
    ag("qaug", tensors["A_qaug"].rearrange("h r t -> (h r) t"), qag)
    ag("kaug", tensors["A_kaug"].rearrange("h r t -> (h r) t"), kag)
    ag("vtok", tensors["A_vtok"], vag)
    ag("kn32", tensors["A_kn32"], knag)
    p.begin_section("B_")
    emit_b(p, getD("B_", {"o16"},
                   rename={"qag": "qag", "kag": "kag", "vag": "vag",
                           "knag": "knag"}))
    oag = mk("oag", [2 * 512, S], BF16)
    ag("o16", tensors["B_o16"].rearrange("h d s -> (h d) s"), oag)
    p.begin_section("C_")
    emit_c(p, getD("C_", {"x1", "xn1f", "xn1tail", "q1r", "k1r", "v1tok"},
                   rename={"oag": "oag", "g_in": "A_g"}))
    qag1 = mk("qag1", [2 * 1024, T], BF16)
    kag1 = mk("kag1", [2 * 1024, T], BF16)
    vag1 = mk("vag1", [2 * T, 16 * 65], BF16)
    yh0 = mk("yh0", [2 * 1024, 258], F32)
    ag("q1r", tensors["C_q1r"].rearrange("h d t -> (h d) t"), qag1)
    ag("k1r", tensors["C_k1r"].rearrange("h d t -> (h d) t"), kag1)
    ag("v1tok", tensors["C_v1tok"], vag1)
    ag("yh0", tensors["C_xn1tail"], yh0)
    p.begin_section("D_")
    emit_d(p, getD("D_", {"o16"},
                   rename={"qag1": "qag1", "kag1": "kag1", "vag1": "vag1"}))
    oag1 = mk("oag1", [2 * 512, S], BF16)
    ag("o16d", tensors["D_o16"].rearrange("h d s -> (h d) s"), oag1)
    yprev = tensors["C_xn1f"]
    yhalo = yh0
    for li in range(6):
        pre = f"E{li}_"

        def D_e(nm, shape, dt, kind, li=li, yprev=yprev, yhalo=yhalo):
            if nm == "y_own":
                return yprev
            if nm == "y_halo":
                return yhalo
            key = (f"E{li}_" + nm) if nm in _E_PER_LI else ("E_" + nm)
            if key in tensors:
                return tensors[key]
            k = "Internal" if nm in ("ynew", "ytail") else kind
            ap = nc.dram_tensor(key, shape, dt, kind=k).ap()
            tensors[key] = ap
            return ap
        p.begin_section(pre)
        emit_e(p, D_e)
        ynew = tensors[f"E{li}_ynew"]
        if li < 5:
            yh = mk(f"yh{li + 1}", [2 * 1024, 258], F32)
            ag(f"yh{li + 1}", tensors[f"E{li}_ytail"], yh)
            yhalo = yh
        yprev = ynew
    p.begin_section("F_")
    emit_f(p, getD("F_", set(),
                   rename={"oag1": "oag1", "x1": "C_x1", "y6": "E5_ynew"}))
    return p.finalize()


def stage_fused(inputs):
    cvec4 = np.zeros((128, 4), np.float32)
    cvec4[:, 0] = 1e-5
    cvb = np.zeros((128, 8), np.float32)
    cvb[:, 0] = 1e-8
    cvb[:, 1] = 1.0
    cvb[:, 3] = 4.0001
    scal = np.array([[float(inputs["L0_attn_log_c"]),
                      float(inputs["L0_attn_log_beta"])]], np.float32)
    id32 = np.eye(32, dtype=np.float32)
    masksb = _causal_masks_b()
    maskd = _causal_masks_d()
    wo_t = _t(inputs["L0_attn_Wo"])
    w_qkv = np.ascontiguousarray(inputs["L1_attn_w_qkv"])
    wout = np.ascontiguousarray(inputs["L1_attn_w_out"])
    shared_a = {
        "A_wq_t": _t(inputs["L0_attn_Wq"]), "A_wk_t": _t(inputs["L0_attn_Wk"]),
        "A_wv_t": _t(inputs["L0_attn_Wv"]),
        "A_wg_t": _t(inputs["L0_ffn_w_gate"]),
        "A_wu_t": _t(inputs["L0_ffn_w_up"]),
        "A_wd_t": _t(inputs["L0_ffn_w_down"]), "A_cvec": cvec4,
    }
    e_shared = {}
    e_perli = {}
    for li in range(6):
        m = stage_e(inputs, li, [np.zeros((1024, T), np.float32)] * NC)
        sm = m[0]
        for nm in ("wqkv", "wo", "w1a", "w1g", "w2", "b1a", "b1g", "b2c",
                   "lng", "lnb", "qscol", "kscol"):
            e_perli[f"E{li}_{nm}"] = sm[nm]
    maps = []
    for c in range(NC):
        b, r = c // 2, c % 2
        tsl = slice(r * T, (r + 1) * T)
        m = dict(shared_a)
        m["A_x_fm"] = _t(inputs["L0_attn_x"][b, tsl, :])
        m.update({"B_hsel": np.array([[r]], np.int32), "B_id32": id32,
                  "B_scal": scal, "B_masks": masksb, "B_cvecb": cvb})
        m.update({"C_x_fm": m["A_x_fm"], "C_tsel": np.array([[r]], np.int32),
                  "C_wo_t": wo_t, "C_w_qkv": w_qkv, "C_cvec": cvec4})
        cosrep, sinsgn = _rope_tables_c(r)
        m["C_cosrep"] = cosrep
        m["C_sinsgn"] = sinsgn
        m.update({"D_hseld": np.array([[r]], np.int32), "D_maskd": maskd})
        qsA, qsB, ksA, ksB = _xpos_ext_tables(r)
        m.update({"E_qsA": qsA, "E_qsB": qsB, "E_ksA": ksA, "E_ksB": ksB,
                  "E_maskw": _maskw(r),
                  "E_zgate": np.array([[0.0 if r == 0 else 1.0]], np.float32),
                  "E_cvec": cvec4})
        m.update(e_perli)
        m.update({"F_wout": wout, "F_tself": np.array([[r]], np.int32)})
        maps.append(m)
    return maps


def kernel_fused(**inputs):
    inputs = {k: np.asarray(v) for k, v in inputs.items()}
    nc = _get("FU", build_fused)
    res = run_bass_kernel_spmd(nc, stage_fused(inputs),
                               core_ids=list(range(NC)))
    if any(float(r["B_flag"][0, 0]) > 0 for r in res.results):
        raise NotImplementedError(
            "L0 hyperbolic attention hit a non-asymptotic branch; exact "
            "fixup path required")
    out = np.zeros((B, S, D), np.float32)
    for c in range(NC):
        b, r = c // 2, c % 2
        out[b, r * T:(r + 1) * T, :] = res.results[c]["F_xout"].T
    return out


kernel = kernel_fused


# version 05
# speedup vs baseline: 1.2855x; 1.1269x over previous
import contextlib
import numpy as np
import ml_dtypes
import concourse.bass as bass
import concourse.mybir as mybir
from concourse.bass_utils import run_bass_kernel_spmd

F32 = mybir.dt.float32
BF16 = mybir.dt.bfloat16
I32 = mybir.dt.int32
AF = mybir.ActivationFunctionType
ALU = mybir.AluOpType
AX = mybir.AxisListType.X
BB = ml_dtypes.bfloat16

B, S, D = 4, 4096, 1024
T = 2048
H, DH = 16, 64
NC = 8
FFN0 = 3072
QK_SCALE = 8.0
ENGS = ("tensor", "vector", "scalar", "gpsimd", "sync")


class Stage:
    __slots__ = ("name", "ops")

    def __init__(self, name):
        self.name = name
        self.ops = {e: [] for e in ENGS}

    def t(self, fn): self.ops["tensor"].append(("c", fn, 1))
    def v(self, fn): self.ops["vector"].append(("c", fn, 1))
    def s(self, fn): self.ops["scalar"].append(("c", fn, 1))
    def g(self, fn): self.ops["gpsimd"].append(("c", fn, 1))
    def dma(self, fn, n=1): self.ops["sync"].append(("d", fn, n))
    def gdma(self, fn, n=1): self.ops["gpsimd"].append(("d", fn, n))


class Prog:

    _PSMAP = {"PS1": "Z0", "PS2": "Z1", "PSM1": "Z2", "PSM2": "Z3",
              "PSM1E": "Z2", "PSM2E": "Z3", "PSZ0": "Z0", "PSZ1": "Z1",
              "PSZ2": "Z2", "PSZ3": "Z3", "PSN1": "N1", "PSN2": "N2",
              "PSN": "N1", "PSB": "B", "PSBE": "B", "PSO": "O", "PSOE": "O"}
    _PSHAPE = {"Z0": [128, 512], "Z1": [128, 512], "Z2": [128, 512],
               "Z3": [128, 512], "N1": [1, 512], "N2": [1, 512],
               "B": [64, 512], "O": [65, 512]}

    def __init__(self, name="prog"):
        self.nc = bass.Bass("TRN2", target_bir_lowering=False, debug=False,
                            detect_race_conditions=False)
        self.stack = contextlib.ExitStack()
        self.stages = []
        self.name = name
        self.prefix = ""
        self._off = 16512
        self._pspool = {}

    SBUF_BASE = 16512

    def begin_section(self, prefix):
        self.prefix = prefix
        self._off = self.SBUF_BASE

    def sbuf(self, name, shape, dtype):
        nbytes = int(np.prod(shape[1:])) * mybir.dt.size(dtype)
        off = (self._off + 31) & ~31
        self._off = off + nbytes
        h = self.nc.alloc_sbuf_tensor_at(self.prefix + name, shape, dtype,
                                         offset=off)
        return h

    def psum(self, name, shape, dtype=F32):
        key = self._PSMAP[name]
        if key not in self._pspool:
            self._pspool[key] = self.stack.enter_context(
                self.nc.psum_tensor("ps_" + key, self._PSHAPE[key], F32))
        t = self._pspool[key]
        if list(shape) == self._PSHAPE[key]:
            return t
        return t[0:shape[0], 0:shape[1]]

    def sem(self, name):
        return self.stack.enter_context(self.nc.semaphore(name))

    def stage(self, name=""):
        st = Stage(name or f"st{len(self.stages)}")
        self.stages.append(st)
        return st

    def finalize(self):
        nc = self.nc
        barC = self.sem("barC")
        barD = self.sem("barD")
        cb, db = [0], [0]
        for st in self.stages:
            c, d = cb[-1], db[-1]
            for e in ENGS:
                ops = st.ops[e]
                if not ops:
                    continue
                d += 16 * sum(n for k, _, n in ops if k == "d")
                c += sum(n for k, _, n in ops if k == "cc")
                if ops[-1][0] == "c":
                    c += 1
            cb.append(c)
            db.append(d)

        def run_engine(eng_name):
            def body(eng):
                lwc, lwd = 0, 0
                for k, st in enumerate(self.stages):
                    ops = st.ops[eng_name]
                    if not ops:
                        continue
                    if cb[k] > lwc:
                        eng.wait_ge(barC, cb[k])
                        lwc = cb[k]
                    if db[k] > lwd:
                        eng.wait_ge(barD, db[k])
                        lwd = db[k]
                    last_c = None
                    for kind, fn, n in ops:
                        r = fn(eng)
                        insts = r if isinstance(r, (list, tuple)) else [r]
                        insts = [i for i in insts if i is not None]
                        if kind == "d":
                            assert len(insts) >= n, (
                                st.name, eng_name, len(insts), n)
                            for i in insts:
                                i.then_inc(barD, 16)
                        elif kind == "cc":
                            for i in insts:
                                i.then_inc(barC)
                        elif insts:
                            last_c = insts[-1]
                    if ops[-1][0] == "c":
                        assert last_c is not None, (st.name, eng_name)
                        last_c.then_inc(barC, 1)
            return body

        with nc.Block() as blk:
            for e in ENGS:
                if any(st.ops[e] for st in self.stages):
                    getattr(blk, e)(run_engine(e))

            def final_wait(eng):
                eng.wait_ge(barC, cb[-1])
                eng.wait_ge(barD, db[-1])
            blk.vector(final_wait)
        self.stack.close()
        return nc


def cdiv(a, b):
    return (a + b - 1) // b


def emit_a(p, D):
    x_fm = D("x_fm", [1024, T], F32, "ExternalInput")
    wq_t = D("wq_t", [1024, 1024], F32, "ExternalInput")
    wk_t = D("wk_t", [1024, 1024], F32, "ExternalInput")
    wv_t = D("wv_t", [1024, 1024], F32, "ExternalInput")
    wg_t = D("wg_t", [1024, FFN0], I32, "ExternalInput")
    wu_t = D("wu_t", [1024, FFN0], I32, "ExternalInput")
    wd_t = D("wd_t", [FFN0, 1024], I32, "ExternalInput")
    cvec = D("cvec", [128, 4], F32, "ExternalInput")
    o_qaug = D("qaug", [16, 66, T], BF16, "ExternalOutput")
    o_kaug = D("kaug", [16, 66, T], BF16, "ExternalOutput")
    o_vt = D("vtok", [T, 16 * 65], BF16, "ExternalOutput")
    o_kn32 = D("kn32", [16, T], F32, "ExternalOutput")
    o_g = D("g", [1024, T], F32, "ExternalOutput")

    cv = p.sbuf("cv", [128, 4], F32)
    xn16 = p.sbuf("xn16", [128, 8 * T], BF16)
    arenaA = p.sbuf("arenaA", [128, 24 * 1024], BF16)
    wgc = p.sbuf("wgc", [128, 8 * 512], BF16)
    wuc = p.sbuf("wuc", [128, 8 * 512], BF16)
    wdbuf = p.sbuf("wdbuf", [128, 24 * 128], BF16)
    xc = p.sbuf("xc", [128, T], F32)
    rowsA = p.sbuf("rowsA", [65, T], F32)
    rowsB = p.sbuf("rowsB", [33, T], F32)
    ones128 = p.sbuf("ones128", [33, 128], F32)
    onesb = p.sbuf("onesb", [128, 1], BF16)
    sq16 = p.sbuf("sq16", [128, T], BF16)
    xc16 = p.sbuf("xc16", [128, T], BF16)
    sqq = p.sbuf("sqq", [64, 512], BF16)
    qa = p.sbuf("qa", [66, T], BF16)
    ka = p.sbuf("ka", [66, T], BF16)
    vtbuf = p.sbuf("vtbuf", [128, 16 * 65], BF16)
    knrow16 = p.sbuf("knrow16", [1, T], BF16)
    sg = p.sbuf("sg", [128, 512], F32)
    gout = p.sbuf("gout", [128, 512], F32)
    sums = rowsA[0:1, :]
    sqs = rowsA[32:33, :]
    knrow = rowsA[64:65, :]
    mean = rowsB[0:1, :]
    rstd = rowsB[32:33, :]

    PS1 = p.psum("PS1", [128, 512], F32)
    PS2 = p.psum("PS2", [128, 512], F32)
    PSN1 = p.psum("PSN1", [1, 512], F32)
    PSN2 = p.psum("PSN2", [1, 512], F32)
    PSM1 = p.psum("PSM1", [128, 512], F32)
    PSM2 = p.psum("PSM2", [128, 512], F32)

    def wq16(fc):
        return arenaA[:, fc * 1024:(fc + 1) * 1024]

    def wk16(fc):
        return arenaA[:, 8 * 1024 + fc * 1024:8 * 1024 + (fc + 1) * 1024]

    def wv16(fc):
        return arenaA[:, 16 * 1024 + fc * 1024:16 * 1024 + (fc + 1) * 1024]

    def hsl(hc, csl):
        base = hc * 1024
        return arenaA[:, base + csl.start:base + csl.stop]

    st = p.stage("consts")
    st.dma(lambda y: y.dma_start(out=cv[:], in_=cvec))

    st = p.stage("init")

    def init(v):
        v.memset(ones128[:], 1.0)
        v.memset(onesb[:], 1.0)
        v.memset(sums, 0.0)
        v.memset(sqs, 0.0)
        v.memset(qa[64:66, :], 1.0)
        return v.memset(ka[64:66, :], 1.0)
    st.v(init)

    st = p.stage("wload_attn")

    def wload(g):
        out = []
        for fc in range(8):
            out.append(g.dma_start(out=wq16(fc), in_=wq_t[fc * 128:(fc + 1) * 128, :]))
            out.append(g.dma_start(out=wk16(fc), in_=wk_t[fc * 128:(fc + 1) * 128, :]))
            out.append(g.dma_start(out=wv16(fc), in_=wv_t[fc * 128:(fc + 1) * 128, :]))
        return out
    st.gdma(wload, n=24)

    for fc in range(8):
        st = p.stage(f"lnl{fc}")
        st.dma(lambda y, fc=fc: y.dma_start(
            out=xc[:], in_=x_fm[fc * 128:(fc + 1) * 128, :]))
        st = p.stage(f"lnc{fc}")
        st.s(lambda s: s.activation(out=sq16[:], in_=xc[:], func=AF.Square))
        st.v(lambda v: v.tensor_copy(out=xc16[:], in_=xc[:]))
        for tc in range(4):
            tsl = slice(tc * 512, (tc + 1) * 512)
            st = p.stage(f"lnm{fc}_{tc}")

            def mm(t, tsl=tsl):
                t.matmul(PSN1[:, :], onesb[:, 0:1], xc16[:, tsl],
                         start=True, stop=True)
                return t.matmul(PSN2[:, :], onesb[:, 0:1], sq16[:, tsl],
                                start=True, stop=True)
            st.t(mm)
            st = p.stage(f"lne{fc}_{tc}")

            def ev(v, tsl=tsl):
                v.tensor_add(out=sums[:, tsl], in0=sums[:, tsl], in1=PSN1[:, :])
                return v.tensor_add(out=sqs[:, tsl], in0=sqs[:, tsl], in1=PSN2[:, :])
            st.v(ev)

    st = p.stage("lnfin")

    def fin(v):
        v.tensor_scalar_mul(out=mean, in0=sums, scalar1=1.0 / 1024)
        v.tensor_scalar_mul(out=sqs, in0=sqs, scalar1=1.0 / 1024)
        v.tensor_mul(out=rstd, in0=mean, in1=mean)
        return v.tensor_sub(out=sqs, in0=sqs, in1=rstd)
    st.v(fin)
    st = p.stage("lnsqrt")
    st.s(lambda s: s.activation(out=sqs, in_=sqs, func=AF.Sqrt,
                                bias=cv[0:1, 0:1]))
    st = p.stage("lnrec")
    st.v(lambda v: v.reciprocal(out=rstd, in_=sqs))

    for fc in range(8):
        st = p.stage(f"lal{fc}")
        st.dma(lambda y, fc=fc: y.dma_start(
            out=xc[:], in_=x_fm[fc * 128:(fc + 1) * 128, :]))
        for tc in range(4):
            tsl = slice(tc * 512, (tc + 1) * 512)
            st = p.stage(f"lab{fc}_{tc}")

            def bc(t, tsl=tsl):
                t.matmul(PSM1[:, :], ones128[0:1, :], mean[:, tsl],
                         start=True, stop=True)
                return t.matmul(PSM2[:, :], ones128[32:33, :], rstd[:, tsl],
                                start=True, stop=True)
            st.t(bc)
            st = p.stage(f"laa{fc}_{tc}")

            def ap_(v, fc=fc, tsl=tsl):
                v.tensor_sub(out=xc[:, tsl], in0=xc[:, tsl], in1=PSM1[:, :])
                return v.tensor_mul(
                    out=xn16[:, fc * T + tsl.start:fc * T + tsl.stop],
                    in0=xc[:, tsl], in1=PSM2[:, :])
            st.v(ap_)

    for h in range(H):
        hs = h * 64
        for tc in range(4):
            tsl = slice(tc * 512, (tc + 1) * 512)
            st = p.stage(f"qk{h}_{tc}")

            def mmqk(t, h=h, tsl=tsl):
                for fc in range(8):
                    xs = xn16[:, fc * T + tsl.start:fc * T + tsl.stop]
                    t.matmul(PS1[0:64, :], wq16(fc)[:, h * 64:h * 64 + 64], xs,
                             start=(fc == 0), stop=(fc == 7))
                last = None
                for fc in range(8):
                    xs = xn16[:, fc * T + tsl.start:fc * T + tsl.stop]
                    last = t.matmul(PS2[0:64, :], wk16(fc)[:, h * 64:h * 64 + 64],
                                    xs, start=(fc == 0), stop=(fc == 7))
                return last
            st.t(mmqk)
            st = p.stage(f"qke{h}_{tc}")

            def epi(s, tsl=tsl):
                s.activation(out=sqq[:, :], in_=PS1[0:64, :], func=AF.Square)
                s.activation(out=qa[0:64, tsl], in_=PS1[0:64, :],
                             func=AF.Identity, scale=-2.0)
                return s.activation(out=ka[0:64, tsl], in_=PS2[0:64, :],
                                    func=AF.Identity)
            st.s(epi)
            st = p.stage(f"qn{h}_{tc}")
            st.t(lambda t: t.matmul(PSN1[:, :], onesb[0:64, 0:1], sqq[:, :],
                                    start=True, stop=True))
            st = p.stage(f"qne{h}_{tc}")
            st.v(lambda v, tsl=tsl: v.tensor_copy(out=qa[64:65, tsl],
                                                  in_=PSN1[:, :]))
            st = p.stage(f"ksq{h}_{tc}")
            st.s(lambda s, tsl=tsl: s.activation(out=sqq[:, :],
                                                 in_=PS2[0:64, :],
                                                 func=AF.Square))
            st = p.stage(f"kn{h}_{tc}")
            st.t(lambda t: t.matmul(PSN2[:, :], onesb[0:64, 0:1], sqq[:, :],
                                    start=True, stop=True))
            st = p.stage(f"kne{h}_{tc}")

            def kne(v, tsl=tsl):
                v.tensor_copy(out=knrow[:, tsl], in_=PSN2[:, :])
                return v.tensor_copy(out=knrow16[0:1, tsl], in_=PSN2[:, :])
            st.v(kne)
        st = p.stage(f"hst{h}")

        def hstore(y, h=h):
            return [
                y.dma_start(out=o_qaug[h], in_=qa[:]),
                y.dma_start(out=o_kaug[h, 0:65, :], in_=ka[0:65, :]),
                y.dma_start(out=o_kaug[h, 65:66, :], in_=knrow16[:]),
                y.dma_start(out=o_kn32[h:h + 1, :], in_=knrow),
            ]
        st.dma(hstore, n=4)


    st = p.stage("vtones")
    st.v(lambda v: [v.memset(vtbuf[:], 0.0)][-1])
    st = p.stage("vtones2")

    def vto(v):
        last = None
        for hh in range(16):
            last = v.memset(vtbuf[:, hh * 65 + 64:hh * 65 + 65], 1.0)
        return last
    st.v(vto)
    for tt in range(16):
        for vc in range(2):
            st = p.stage(f"vt{tt}_{vc}")

            def vmm(t, tt=tt, vc=vc):
                last = None
                for fc in range(8):
                    last = t.matmul(
                        PS1[:, :],
                        xn16[:, fc * T + tt * 128:fc * T + (tt + 1) * 128],
                        wv16(fc)[:, vc * 512:(vc + 1) * 512],
                        start=(fc == 0), stop=(fc == 7))
                return last
            st.t(vmm)
            st = p.stage(f"vte{tt}_{vc}")

            def vev(v, vc=vc):
                last = None
                for hh in range(8):
                    h = vc * 8 + hh
                    last = v.tensor_copy(
                        out=vtbuf[:, h * 65:h * 65 + 64],
                        in_=PS1[:, hh * 64:(hh + 1) * 64])
                return last
            st.v(vev)
        st = p.stage(f"vts{tt}")
        st.dma(lambda y, tt=tt: y.dma_start(
            out=o_vt[tt * 128:(tt + 1) * 128, :], in_=vtbuf[:]))

    for th in range(2):
        for oc in range(6):
            st = p.stage(f"fw{th}_{oc}")

            def fwload(g, oc=oc):
                out = []
                for fc in range(8):
                    out.append(g.dma_start(
                        out=wgc[:, fc * 512:(fc + 1) * 512],
                        in_=wg_t[fc * 128:(fc + 1) * 128, oc * 512:(oc + 1) * 512]))
                    out.append(g.dma_start(
                        out=wuc[:, fc * 512:(fc + 1) * 512],
                        in_=wu_t[fc * 128:(fc + 1) * 128, oc * 512:(oc + 1) * 512]))
                return out
            st.gdma(fwload, n=16)
            for pc in range(4):
                for tc in range(2):
                    tsl = slice(th * 1024 + tc * 512, th * 1024 + tc * 512 + 512)
                    csl = slice(tc * 512, tc * 512 + 512)
                    st = p.stage(f"fg{th}_{oc}_{pc}_{tc}")

                    def mmg(t, pc=pc, tsl=tsl):
                        for fc in range(8):
                            xs = xn16[:, fc * T + tsl.start:fc * T + tsl.stop]
                            t.matmul(PS1[:, :],
                                     wgc[:, fc * 512 + pc * 128:fc * 512 + pc * 128 + 128],
                                     xs, start=(fc == 0), stop=(fc == 7))
                        last = None
                        for fc in range(8):
                            xs = xn16[:, fc * T + tsl.start:fc * T + tsl.stop]
                            last = t.matmul(
                                PS2[:, :],
                                wuc[:, fc * 512 + pc * 128:fc * 512 + pc * 128 + 128],
                                xs, start=(fc == 0), stop=(fc == 7))
                        return last
                    st.t(mmg)
                    st = p.stage(f"fs{th}_{oc}_{pc}_{tc}")
                    st.s(lambda s: s.activation(out=sg[:], in_=PS1[:, :],
                                                func=AF.Silu, scale=1.0 / 127))
                    st = p.stage(f"fh{th}_{oc}_{pc}_{tc}")
                    st.v(lambda v, oc=oc, pc=pc, csl=csl: v.scalar_tensor_tensor(
                        out=hsl(oc * 4 + pc, csl), in0=sg[:], scalar=1.0 / 127,
                        in1=PS2[:, :], op0=ALU.mult, op1=ALU.mult))

        for op_ in range(8):
            st = p.stage(f"wdl{th}_{op_}")

            def wdload(g, op_=op_):
                return [g.dma_start(
                    out=wdbuf[:, hc * 128:(hc + 1) * 128],
                    in_=wd_t[hc * 128:(hc + 1) * 128, op_ * 128:(op_ + 1) * 128])
                    for hc in range(24)]
            st.gdma(wdload, n=24)
            for tc in range(2):
                tsl = slice(th * 1024 + tc * 512, th * 1024 + tc * 512 + 512)
                csl = slice(tc * 512, tc * 512 + 512)
                st = p.stage(f"gd{th}_{op_}_{tc}")

                def mmd(t, csl=csl):
                    last = None
                    for hc in range(24):
                        last = t.matmul(PS1[:, :],
                                        wdbuf[:, hc * 128:(hc + 1) * 128],
                                        hsl(hc, csl), start=(hc == 0),
                                        stop=(hc == 23))
                    return last
                st.t(mmd)
                st = p.stage(f"ge{th}_{op_}_{tc}")
                st.s(lambda s: s.activation(out=gout[:], in_=PS1[:, :],
                                            func=AF.Silu, scale=1.0 / 127))
                st = p.stage(f"gst{th}_{op_}_{tc}")
                st.dma(lambda y, op_=op_, tsl=tsl: y.dma_start(
                    out=o_g[op_ * 128:(op_ + 1) * 128, tsl], in_=gout[:]))



def _t(a):
    return np.ascontiguousarray(a.T)


def stage_a(inputs):
    x = inputs["L0_attn_x"]
    cvec = np.zeros((128, 4), np.float32)
    cvec[:, 0] = 1e-5
    shared = {
        "wq_t": _t(inputs["L0_attn_Wq"]), "wk_t": _t(inputs["L0_attn_Wk"]),
        "wv_t": _t(inputs["L0_attn_Wv"]),
        "wg_t": _t(inputs["L0_ffn_w_gate"]), "wu_t": _t(inputs["L0_ffn_w_up"]),
        "wd_t": _t(inputs["L0_ffn_w_down"]), "cvec": cvec,
    }
    maps = []
    for c in range(NC):
        b, r = c // 2, c % 2
        m = dict(shared)
        m["x_fm"] = _t(x[b, r * T:(r + 1) * T, :])
        maps.append(m)
    return maps


def kernel(**inputs):
    raise NotImplementedError("under construction")


def emit_b(p, D, exact=False, nheads=8, nqc=8, trunc=None):
    assert not exact, "exact variant is build_launch_bx"
    """L0 hyperbolic attention middle (per core: 8 heads, all 4096 tokens).

    In : qaug_g [8,66,S] bf16 rows [-2q; qn; ones]
         kaug_g [8,66,S] bf16 rows [k; ones; kn]
         v_g    [8,S,65] bf16 (token-major, col 64 = 1.0)
         knt    [8,128,32] f32 (kn tiled: [p, kt] = kn[kt*128+p])
         qnt    [8,128,32] f32 (same for qn; used by exact path)
         scal   [1,2] f32 = [log_c, log_beta]
         masks  [128,4*512] f32 additive causal masks (0 / +1e30),
                block mi at cols [mi*512,(mi+1)*512): mask for kk-tile offset mi*128
         cvecb  [128,8] f32: c0=1e-8 c1=1.0 c2=1.5576e-8 c3=thr(4.0001)
    Out: o16 [8,64,S] bf16 (normalized attention out, feature-major)
         flag [1,1] f32 (#kk-partitions with min z below threshold; fast only)

    Fast path: P = exp(-beta/2 * ln(z+eps) + (beta*c/4)(knmean - kn)), exact
    for the ed>2 branch; detection of any z < 4 via running min -> flag.
    Exact path: full 3-branch dist, same dropped per-q factor class (global
    shift only), overwrite o16.
    """
    i_q = D("qag", [2 * 1056, T], BF16, "ExternalInput")
    i_k = D("kag", [2 * 1056, T], BF16, "ExternalInput")
    i_v = D("vag", [2 * T, 16 * 65], BF16, "ExternalInput")
    i_kn = D("knag", [32, T], F32, "ExternalInput")
    i_hsel = D("hsel", [1, 1], I32, "ExternalInput")
    i_id32 = D("id32", [32, 32], F32, "ExternalInput")
    i_scal = D("scal", [1, 2], F32, "ExternalInput")
    i_masks = D("masks", [128, 4 * 512], F32, "ExternalInput")
    i_cv = D("cvecb", [128, 8], F32, "ExternalInput")
    o_o = D("o16", [8, 64, S], BF16, "ExternalOutput")
    o_flag = D("flag", [1, 1], F32, "ExternalOutput")

    cv = p.sbuf("cv", [128, 8], F32)
    ka = p.sbuf("ka", [66, S], BF16)
    qa = p.sbuf("qa", [66, S], BF16)
    vsb = p.sbuf("vsb", [128, 32 * 65], BF16)
    masks = p.sbuf("maskssb", [128, 4 * 512], F32)
    ptile = p.sbuf("ptile", [128, 4 * 512], BF16)
    lz = p.sbuf("lz", [128, 4 * 512], F32)
    knt = p.sbuf("kntsb", [128, 32], F32)
    knrows = p.sbuf("knrows", [32, 128], F32)
    hselb = p.sbuf("hselb", [1, 1], I32)
    id32b = p.sbuf("id32b", [32, 32], F32)
    bias = p.sbuf("biassb", [128, 32], F32)
    dacc = p.sbuf("dacc", [128, 1], F32)
    rmin = p.sbuf("rmin", [128, 1], F32)
    sc = p.sbuf("sc", [33, 8], F32)
    scb = p.sbuf("scb", [128, 4], F32)
    onesb = p.sbuf("onesb2", [128, 1], BF16)
    ones128 = p.sbuf("ones128b", [1, 128], F32)
    floor16 = p.sbuf("floor16", [1, 66], BF16)
    fval = p.sbuf("fval", [1, 512], BF16)
    rrow = p.sbuf("rrow", [1, 512], F32)
    otile = p.sbuf("otile", [64, 512], F32)
    obuf = p.sbuf("obuf", [64, 512], BF16)
    m128 = p.sbuf("m128", [128, 1], F32)
    m16 = p.sbuf("m16", [128, 1], BF16)
    flagrow = p.sbuf("flagrow", [1, 1], F32)

    PSZ = [p.psum(f"PSZ{i}", [128, 512], F32) for i in range(4)]
    PSO = p.psum("PSO", [65, 512], F32)
    PSB = p.psum("PSB", [64, 512], F32)
    PSN = p.psum("PSN", [1, 512], F32)

    st = p.stage("b_consts")

    def cload(y):
        return [y.dma_start(out=cv[:], in_=i_cv),
                y.dma_start(out=sc[0:1, 0:2], in_=i_scal),
                y.dma_start(out=masks[:], in_=i_masks),
                y.dma_start(out=hselb[:], in_=i_hsel),
                y.dma_start(out=id32b[:], in_=i_id32)]
    st.dma(cload, n=5)
    st = p.stage("b_init")

    def binit(v):
        v.memset(onesb[:], 1.0)
        v.memset(ones128[:], 1.0)
        v.memset(dacc[:], 1e30)
        v.memset(floor16[:], 0.0)
        v.memset(fval[:], 1e-30)
        return v.memset(flagrow[:], 0.0)
    st.v(binit)
    st = p.stage("b_init2")
    st.v(lambda v: v.memset(floor16[0:1, 64:65], 1.0))

    st = p.stage("b_sp1")
    st.s(lambda s: s.activation(out=sc[0:1, 2:4], in_=sc[0:1, 0:2], func=AF.Abs))
    st = p.stage("b_sp2")
    st.s(lambda s: s.activation(out=sc[0:1, 2:4], in_=sc[0:1, 2:4], func=AF.Exp,
                                scale=-1.0))
    st = p.stage("b_sp3")
    st.s(lambda s: s.activation(out=sc[0:1, 2:4], in_=sc[0:1, 2:4], func=AF.Ln,
                                bias=cv[0:1, 1:2]))
    st = p.stage("b_sp3b")
    st.s(lambda s: s.activation(out=sc[0:1, 4:6], in_=sc[0:1, 0:2],
                                func=AF.Relu))
    st = p.stage("b_sp4")
    st.v(lambda v: v.tensor_add(out=sc[0:1, 4:6], in0=sc[0:1, 4:6],
                                in1=sc[0:1, 2:4]))
    st = p.stage("b_sp5")
    st.v(lambda v: v.tensor_scalar_add(out=sc[0:1, 5:6], in0=sc[0:1, 5:6],
                                       scalar1=0.5))
    st = p.stage("b_sp6")
    st.v(lambda v: v.tensor_scalar_mul(out=sc[0:1, 6:7], in0=sc[0:1, 5:6],
                                       scalar1=-0.5))
    st = p.stage("b_sp7")
    st.v(lambda v: v.tensor_mul(out=sc[0:1, 7:8], in0=sc[0:1, 4:5],
                                in1=sc[0:1, 5:6]))
    st = p.stage("b_sp8")
    st.v(lambda v: v.tensor_scalar_mul(out=sc[0:1, 7:8], in0=sc[0:1, 7:8],
                                       scalar1=-0.25))
    st = p.stage("b_spb")

    def spb(t):
        t.matmul(PSZ[0][:, 0:1], ones128[0:1, :], sc[0:1, 6:7],
                 start=True, stop=True)
        return t.matmul(PSZ[0][:, 1:2], ones128[0:1, :], sc[0:1, 7:8],
                        start=True, stop=True)
    st.t(spb)
    st = p.stage("b_spb2")

    def spb2(v):
        v.tensor_copy(out=scb[:, 0:1], in_=PSZ[0][:, 0:1])
        return v.tensor_copy(out=scb[:, 1:2], in_=PSZ[0][:, 1:2])
    st.v(spb2)

    for h in range(nheads):
        st = p.stage(f"bh{h}_load")

        def hload(y, h=h):
            insts = []

            def emit_half(hg):
                gh = hg * 8 + h
                for rh in range(2):
                    insts.append(y.dma_start(
                        out=ka[:, rh * T:(rh + 1) * T],
                        in_=i_k[rh * 1056 + gh * 66:rh * 1056 + (gh + 1) * 66, :]))
                    insts.append(y.dma_start(
                        out=qa[:, rh * T:(rh + 1) * T],
                        in_=i_q[rh * 1056 + gh * 66:rh * 1056 + (gh + 1) * 66, :]))
                    insts.append(y.dma_start(
                        out=vsb[:].rearrange("p (kt d) -> p kt d", d=65)[
                            :, rh * 16:(rh + 1) * 16, :],
                        in_=i_v[rh * T:(rh + 1) * T, :].rearrange(
                            "(kt p) c -> p kt c", p=128)[
                            :, :, gh * 65:(gh + 1) * 65]))
                    insts.append(y.dma_start(
                        out=knrows[rh * 16:(rh + 1) * 16, :],
                        in_=i_kn[rh * 16 + gh:rh * 16 + gh + 1, :].rearrange(
                            "one (a b) -> (one a) b", b=128)))
            with y.register() as rg:
                y.reg_load(rg, hselb[0:1, 0:1])
                with y.If_ne(rg, 0):
                    emit_half(1)
                with y.Else():
                    emit_half(0)
                y.end_ifs()
            return insts
        st.dma(hload, n=8)
        st = p.stage(f"bh{h}_kt")
        st.t(lambda t: t.transpose(PSZ[0][:, 0:32], knrows[:, :], id32b[:, :]))
        st = p.stage(f"bh{h}_kte")
        st.v(lambda v: v.tensor_copy(out=knt[:], in_=PSZ[0][:, 0:32]))
        st = p.stage(f"bh{h}_b1")
        st.v(lambda v: v.reduce_sum(out=rmin[:], in_=knt[:], axis=AX))
        st = p.stage(f"bh{h}_b2")
        st.v(lambda v: v.tensor_copy(out=m16[:], in_=rmin[:]))
        st = p.stage(f"bh{h}_b3")
        st.t(lambda t: t.matmul(PSN[0:1, 0:1], onesb[:, 0:1], m16[:, 0:1],
                                start=True, stop=True))
        st = p.stage(f"bh{h}_b4")
        st.v(lambda v: v.tensor_scalar_mul(out=sc[0:1, 1:2], in0=PSN[0:1, 0:1],
                                           scalar1=1.0 / 4096))
        st = p.stage(f"bh{h}_b5")
        st.t(lambda t: t.matmul(PSZ[0][:, 0:1], ones128[0:1, :], sc[0:1, 1:2],
                                start=True, stop=True))
        st = p.stage(f"bh{h}_b5b")
        st.v(lambda v: v.tensor_copy(out=m128[:, 0:1], in_=PSZ[0][:, 0:1]))
        st = p.stage(f"bh{h}_b6")
        st.v(lambda v: v.tensor_scalar(out=bias[:], in0=knt[:],
                                       scalar1=m128[:, 0:1],
                                       scalar2=scb[:, 1:2], op0=ALU.subtract,
                                       op1=ALU.mult))

        for qc in range(nqc):
            qsl = slice(qc * 512, (qc + 1) * 512)
            ngr = qc + 1
            def zmm(t, kts, qsl=qsl):
                out = None
                for j, kt in enumerate(kts):
                    ksl = slice(kt * 128, (kt + 1) * 128)
                    out = t.matmul(PSZ[j][:, :], ka[0:66, ksl],
                                   qa[0:66, qsl], start=True, stop=True)
                    if exact:
                        out = t.matmul(PSZ[j][:, :],
                                       ka[64:66, ksl], qa[64:66, qsl],
                                       start=False, stop=True)
                return out

            st = p.stage(f"bh{h}_z{qc}")
            st.t(lambda t, zmm=zmm: zmm(t, [0, 1, 2, 3]))
            for gidx in range(ngr):
                kts = [gidx * 4 + j for j in range(4)]
                st = p.stage(f"bh{h}_e{qc}_{gidx}")

                def ew(v, h=h, qc=qc, kts=kts):
                    last = None
                    for j, kt in enumerate(kts):
                        v.tensor_reduce(out=rmin[:], in_=PSZ[j][:, :], axis=AX,
                                        op=ALU.min)
                        last = v.tensor_tensor(out=dacc[:], in0=dacc[:],
                                               in1=rmin[:], op=ALU.min)
                    return last
                st.v(ew)
                st = p.stage(f"bh{h}_ea{qc}_{gidx}")

                def ew_act(s, qc=qc, kts=kts):
                    last = None
                    for j, kt in enumerate(kts):
                        psl = slice(j * 512, (j + 1) * 512)
                        last = s.activation(out=lz[:, psl], in_=PSZ[j][:, :],
                                            func=AF.Ln, bias=cv[:, 0:1])
                    return last
                st.s(ew_act)
                if any(0 <= kt * 128 - qc * 512 < 512 for kt in kts):
                    st = p.stage(f"bh{h}_m{qc}_{gidx}")

                    def ewm(v, qc=qc, kts=kts):
                        last = None
                        for j, kt in enumerate(kts):
                            d = kt * 128 - qc * 512
                            if 0 <= d < 512:
                                mi = d // 128
                                psl = slice(j * 512, (j + 1) * 512)
                                last = v.tensor_add(
                                    out=lz[:, psl], in0=lz[:, psl],
                                    in1=masks[:, mi * 512:(mi + 1) * 512])
                        return last
                    st.v(ewm)
                st = p.stage(f"bh{h}_x{qc}_{gidx}")

                def ewx(s, kts=kts):
                    last = None
                    for j, kt in enumerate(kts):
                        psl = slice(j * 512, (j + 1) * 512)
                        last = s.activation(out=ptile[:, psl], in_=lz[:, psl],
                                            func=AF.Exp, scale=scb[:, 0:1],
                                            bias=bias[:, kt:kt + 1])
                    return last
                st.s(ewx)
                st = p.stage(f"bh{h}_pv{qc}_{gidx}")

                def pv(t, qc=qc, gidx=gidx, kts=kts, zmm=zmm):
                    last = None
                    for j, kt in enumerate(kts):
                        psl = slice(j * 512, (j + 1) * 512)
                        last = t.matmul(PSO[:, :],
                                        vsb[:, kt * 65:(kt + 1) * 65],
                                        ptile[:, psl],
                                        start=(gidx == 0 and j == 0), stop=False)
                    if gidx == qc:
                        last = t.matmul(PSO[:, :], floor16[0:1, 0:65],
                                        fval[0:1, :], start=False, stop=True)
                    else:
                        last = zmm(t, [(gidx + 1) * 4 + j for j in range(4)])
                    return last
                st.t(pv)
            st = p.stage(f"bh{h}_n{qc}")
            st.v(lambda v: v.reciprocal(out=rrow[:], in_=PSO[64:65, :]))
            st = p.stage(f"bh{h}_ne{qc}")
            st.s(lambda s: s.activation(out=otile[:], in_=PSO[0:64, :],
                                        func=AF.Identity))
            st = p.stage(f"bh{h}_nb{qc}")
            st.t(lambda t: t.matmul(PSB[:, :], ones128[0:1, 0:64], rrow[0:1, :],
                                    start=True, stop=True))
            st = p.stage(f"bh{h}_nm{qc}")
            st.v(lambda v: v.tensor_mul(out=obuf[:], in0=otile[:], in1=PSB[:, :]))
            st = p.stage(f"bh{h}_st{qc}")
            st.dma(lambda y, h=h, qsl=qsl: y.dma_start(out=o_o[h][:, qsl],
                                                       in_=obuf[:]))

    st = p.stage("b_flag1")
    st.v(lambda v: v.tensor_scalar(out=m128[:], in0=dacc[:],
                                   scalar1=cv[:, 3:4], scalar2=0.0,
                                   op0=ALU.is_lt, op1=ALU.add))
    st = p.stage("b_flag2")
    st.v(lambda v: v.tensor_copy(out=m16[:], in_=m128[:]))
    st = p.stage("b_flag3")
    st.t(lambda t: t.matmul(PSN[0:1, 0:1], onesb[:, 0:1], m16[:, 0:1],
                            start=True, stop=True))
    st = p.stage("b_flag4")
    st.v(lambda v: v.tensor_copy(out=flagrow[:], in_=PSN[0:1, 0:1]))
    st = p.stage("b_flag5")
    st.dma(lambda y: y.dma_start(out=o_flag, in_=flagrow[:]))
    if trunc is not None:
        p.stages = p.stages[:trunc]


def _causal_masks_b():
    out = np.zeros((128, 4 * 512), np.float32)
    for mi in range(4):
        kk = np.arange(128)[:, None] + mi * 128
        q = np.arange(512)[None, :]
        out[:, mi * 512:(mi + 1) * 512] = np.where(kk > q, 1e30, 0.0).astype(np.float32)
    return out


def stage_b(inputs, a_res):
    masks = _causal_masks_b()
    cvb = np.zeros((128, 8), np.float32)
    cvb[:, 0] = 1e-8
    cvb[:, 1] = 1.0
    cvb[:, 3] = 4.0001
    scal = np.array([[float(inputs["L0_attn_log_c"]),
                      float(inputs["L0_attn_log_beta"])]], np.float32)
    id32 = np.eye(32, dtype=np.float32)
    maps = []
    for c in range(NC):
        b, hg = c // 2, c % 2
        ra, rb = a_res[2 * b], a_res[2 * b + 1]
        qag = np.concatenate([ra["qaug"].reshape(1056, T),
                              rb["qaug"].reshape(1056, T)], axis=0)
        kag = np.concatenate([ra["kaug"].reshape(1056, T),
                              rb["kaug"].reshape(1056, T)], axis=0)
        vag = np.concatenate([ra["vtok"], rb["vtok"]], axis=0)
        knag = np.concatenate([ra["kn32"], rb["kn32"]], axis=0)
        maps.append({
            "qag": np.ascontiguousarray(qag),
            "kag": np.ascontiguousarray(kag),
            "vag": np.ascontiguousarray(vag),
            "knag": np.ascontiguousarray(knag),
            "hsel": np.array([[hg]], np.int32), "id32": id32,
            "scal": scal, "masks": masks, "cvecb": cvb,
        })
    return maps


def emit_c(p, D):
    x_fm = D("x_fm", [1024, T], F32, "ExternalInput")
    i_oag = D("oag", [2 * 512, S], BF16, "ExternalInput")
    i_tsel = D("tsel", [1, 1], I32, "ExternalInput")
    wo_t = D("wo_t", [1024, 1024], F32, "ExternalInput")
    g_in = D("g_in", [1024, T], F32, "ExternalInput")
    w_qkv = D("w_qkv", [1024, 3 * 1024], F32, "ExternalInput")
    cosrep = D("cosrep", [128, T], F32, "ExternalInput")
    sinsgn = D("sinsgn", [128, T], F32, "ExternalInput")
    cvec = D("cvec", [128, 4], F32, "ExternalInput")
    o_x1 = D("x1", [1024, T], F32, "ExternalOutput")
    o_xn1 = D("xn1f", [1024, T], F32, "ExternalOutput")
    o_xt = D("xn1tail", [1024, 258], F32, "ExternalOutput")
    o_q = D("q1r", [16, 64, T], BF16, "ExternalOutput")
    o_k = D("k1r", [16, 64, T], BF16, "ExternalOutput")
    o_vt = D("v1tok", [T, 16 * 65], BF16, "ExternalOutput")

    cv = p.sbuf("cv", [128, 4], F32)
    x1 = p.sbuf("x1b", [128, 8 * T], F32)
    xn16 = p.sbuf("xn16", [128, 8 * T], BF16)
    osb = p.sbuf("osb", [128, 8 * T], BF16)
    tselb = p.sbuf("tselb", [1, 1], I32)
    vtbuf = p.sbuf("vtbufc", [128, 16 * 65], BF16)
    wbuf = p.sbuf("wbuf", [128, 8 * 1024], BF16)
    gch = p.sbuf("gch", [128, T], F32)
    cosb = p.sbuf("cosb", [128, T], F32)
    sinb = p.sbuf("sinb", [128, T], F32)
    rowsA = p.sbuf("rowsA", [65, T], F32)
    rowsB = p.sbuf("rowsB", [33, T], F32)
    ones128 = p.sbuf("ones128", [33, 128], F32)
    onesb = p.sbuf("onesb", [128, 1], BF16)
    sq16 = p.sbuf("sq16", [128, T], BF16)
    xc16 = p.sbuf("xc16", [128, T], BF16)
    tmp = p.sbuf("tmpc", [128, 512], F32)
    tmq = p.sbuf("tmqc", [128, 512], F32)
    qtile = p.sbuf("qtile", [128, T], BF16)
    sums = rowsA[0:1, :]
    sqs = rowsA[32:33, :]
    mean = rowsB[0:1, :]
    rstd = rowsB[32:33, :]

    PS1 = p.psum("PS1", [128, 512], F32)
    PSN1 = p.psum("PSN1", [1, 512], F32)
    PSN2 = p.psum("PSN2", [1, 512], F32)
    PSM1 = p.psum("PSM1", [128, 512], F32)
    PSM2 = p.psum("PSM2", [128, 512], F32)

    st = p.stage("c_consts")

    def cl(y):
        return [y.dma_start(out=cv[:], in_=cvec),
                y.dma_start(out=cosb[:], in_=cosrep),
                y.dma_start(out=sinb[:], in_=sinsgn),
                y.dma_start(out=tselb[:], in_=i_tsel)]
    st.dma(cl, n=4)
    st = p.stage("c_init")

    def ci(v):
        v.memset(ones128[:], 1.0)
        v.memset(onesb[:], 1.0)
        v.memset(sums, 0.0)
        return v.memset(sqs, 0.0)
    st.v(ci)
    st = p.stage("c_wload")

    def cw(g):
        out = []
        for fc in range(8):
            out.append(g.dma_start(out=wbuf[:, fc * 1024:(fc + 1) * 1024],
                                   in_=wo_t[fc * 128:(fc + 1) * 128, :]))
        with g.register() as rg:
            g.reg_load(rg, tselb[0:1, 0:1])
            with g.If_ne(rg, 0):
                for fc in range(8):
                    out.append(g.dma_start(
                        out=osb[:, fc * T:(fc + 1) * T],
                        in_=i_oag[fc * 128:(fc + 1) * 128, T:2 * T]))
            with g.Else():
                for fc in range(8):
                    out.append(g.dma_start(
                        out=osb[:, fc * T:(fc + 1) * T],
                        in_=i_oag[fc * 128:(fc + 1) * 128, 0:T]))
            g.end_ifs()
        return out
    st.gdma(cw, n=16)

    for op_ in range(8):
        st = p.stage(f"cx{op_}")

        def cx(y, op_=op_):
            return [y.dma_start(out=gch[:], in_=g_in[op_ * 128:(op_ + 1) * 128, :]),
                    y.dma_start(out=x1[:, op_ * T:(op_ + 1) * T],
                                in_=x_fm[op_ * 128:(op_ + 1) * 128, :])]
        st.dma(cx, n=2)
        for tc in range(4):
            tsl = slice(tc * 512, (tc + 1) * 512)
            st = p.stage(f"cs{op_}_{tc}")

            def mms(t, op_=op_, tsl=tsl):
                last = None
                for fc in range(8):
                    last = t.matmul(
                        PS1[:, :],
                        wbuf[:, fc * 1024 + op_ * 128:fc * 1024 + op_ * 128 + 128],
                        osb[:, fc * T + tsl.start:fc * T + tsl.stop],
                        start=(fc == 0), stop=(fc == 7))
                return last
            st.t(mms)
            st = p.stage(f"ce{op_}_{tc}")

            def ep(v, op_=op_, tsl=tsl):
                xs = x1[:, op_ * T + tsl.start:op_ * T + tsl.stop]
                v.tensor_mul(out=tmp[:], in0=gch[:, tsl], in1=PS1[:, :])
                return v.tensor_add(out=xs, in0=xs, in1=tmp[:])
            st.v(ep)
        st = p.stage(f"cxs{op_}")
        st.dma(lambda y, op_=op_: y.dma_start(
            out=o_x1[op_ * 128:(op_ + 1) * 128, :],
            in_=x1[:, op_ * T:(op_ + 1) * T]))
        st = p.stage(f"cq{op_}")

        def csq(s, op_=op_):
            return s.activation(out=sq16[:], in_=x1[:, op_ * T:(op_ + 1) * T],
                                func=AF.Square)
        st.s(csq)
        st.v(lambda v, op_=op_: v.tensor_copy(
            out=xc16[:], in_=x1[:, op_ * T:(op_ + 1) * T]))
        for tc in range(4):
            tsl = slice(tc * 512, (tc + 1) * 512)
            st = p.stage(f"cm{op_}_{tc}")

            def mm(t, tsl=tsl):
                t.matmul(PSN1[:, :], onesb[:, 0:1], xc16[:, tsl],
                         start=True, stop=True)
                return t.matmul(PSN2[:, :], onesb[:, 0:1], sq16[:, tsl],
                                start=True, stop=True)
            st.t(mm)
            st = p.stage(f"cme{op_}_{tc}")

            def ev(v, tsl=tsl):
                v.tensor_add(out=sums[:, tsl], in0=sums[:, tsl], in1=PSN1[:, :])
                return v.tensor_add(out=sqs[:, tsl], in0=sqs[:, tsl],
                                    in1=PSN2[:, :])
            st.v(ev)

    st = p.stage("cfin")

    def fin(v):
        v.tensor_scalar_mul(out=mean, in0=sums, scalar1=1.0 / 1024)
        v.tensor_scalar_mul(out=sqs, in0=sqs, scalar1=1.0 / 1024)
        v.tensor_mul(out=rstd, in0=mean, in1=mean)
        return v.tensor_sub(out=sqs, in0=sqs, in1=rstd)
    st.v(fin)
    st = p.stage("csqrt")
    st.s(lambda s: s.activation(out=sqs, in_=sqs, func=AF.Sqrt,
                                bias=cv[0:1, 0:1]))
    st = p.stage("crec")
    st.v(lambda v: v.reciprocal(out=rstd, in_=sqs))

    for fc in range(8):
        for tc in range(4):
            tsl = slice(tc * 512, (tc + 1) * 512)
            st = p.stage(f"cb{fc}_{tc}")

            def bc(t, tsl=tsl):
                t.matmul(PSM1[:, :], ones128[0:1, :], mean[:, tsl],
                         start=True, stop=True)
                return t.matmul(PSM2[:, :], ones128[32:33, :], rstd[:, tsl],
                                start=True, stop=True)
            st.t(bc)
            st = p.stage(f"ca{fc}_{tc}")

            def ap_(v, fc=fc, tsl=tsl):
                xs = x1[:, fc * T + tsl.start:fc * T + tsl.stop]
                v.tensor_sub(out=tmp[:], in0=xs, in1=PSM1[:, :])
                v.tensor_mul(out=tmp[:], in0=tmp[:], in1=PSM2[:, :])
                return v.tensor_copy(
                    out=xn16[:, fc * T + tsl.start:fc * T + tsl.stop],
                    in_=tmp[:])
            st.v(ap_)
            st = p.stage(f"cao{fc}_{tc}")
            st.dma(lambda y, fc=fc, tsl=tsl: y.dma_start(
                out=o_xn1[fc * 128:(fc + 1) * 128, tsl], in_=tmp[:]))

    for fc in range(8):
        st = p.stage(f"ctl{fc}")
        st.dma(lambda y, fc=fc: y.dma_start(
            out=gch[:, 0:258], in_=o_xn1[fc * 128:(fc + 1) * 128, T - 258:T]))
        st = p.stage(f"cts{fc}")
        st.dma(lambda y, fc=fc: y.dma_start(
            out=o_xt[fc * 128:(fc + 1) * 128, :], in_=gch[:, 0:258]))

    for oc in range(16):
        part = oc // 8
        if oc % 8 == 0:
            st = p.stage(f"cwq{oc}")

            def cwq(g, oc=oc):
                return [g.dma_start(
                    out=wbuf[:, fc * 1024:(fc + 1) * 1024],
                    in_=w_qkv[fc * 128:(fc + 1) * 128,
                              (oc // 8) * 1024:(oc // 8 + 1) * 1024])
                    for fc in range(8)]
            st.gdma(cwq, n=8)
        for tc in range(4):
            tsl = slice(tc * 512, (tc + 1) * 512)
            st = p.stage(f"cp{oc}_{tc}")

            def mmp(t, oc=oc, tsl=tsl):
                last = None
                for fc in range(8):
                    last = t.matmul(
                        PS1[:, :],
                        wbuf[:, fc * 1024 + (oc % 8) * 128:
                             fc * 1024 + (oc % 8) * 128 + 128],
                        xn16[:, fc * T + tsl.start:fc * T + tsl.stop],
                        start=(fc == 0), stop=(fc == 7))
                return last
            st.t(mmp)
            if part < 2:
                st = p.stage(f"cr{oc}_{tc}")

                def rot(v, tsl=tsl):
                    v.tensor_copy(out=tmq[0:32, :], in_=PS1[32:64, :])
                    v.tensor_copy(out=tmq[32:64, :], in_=PS1[0:32, :])
                    v.tensor_copy(out=tmq[64:96, :], in_=PS1[96:128, :])
                    return v.tensor_copy(out=tmq[96:128, :], in_=PS1[64:96, :])
                st.v(rot)
                st = p.stage(f"cr2{oc}_{tc}")

                def rope(v, tsl=tsl):
                    v.tensor_mul(out=tmp[:], in0=cosb[:, tsl], in1=PS1[:, :])
                    v.tensor_mul(out=tmq[:, :], in0=tmq[:, :], in1=sinb[:, tsl])
                    return v.tensor_add(out=tmp[:], in0=tmp[:], in1=tmq[:, :])
                st.v(rope)
                st = p.stage(f"cr3{oc}_{tc}")
                st.v(lambda v, tsl=tsl: v.tensor_copy(out=qtile[:, tsl],
                                                      in_=tmp[:]))
        st = p.stage(f"cst{oc}")
        dst = (o_q, o_k)[part]
        h0 = (oc % 8) * 2

        def stq(y, dst=dst, h0=h0):
            return [y.dma_start(out=dst[h0], in_=qtile[0:64, :]),
                    y.dma_start(out=dst[h0 + 1], in_=qtile[64:128, :])]
        st.dma(stq, n=2)

    st = p.stage("cvw")
    st.gdma(lambda g: [g.dma_start(
        out=wbuf[:, fc * 1024:(fc + 1) * 1024],
        in_=w_qkv[fc * 128:(fc + 1) * 128, 2048:3072]) for fc in range(8)], n=8)
    st = p.stage("cvones")

    def cvo(v):
        v.memset(vtbuf[:], 0.0)
        last = None
        for hh in range(16):
            last = v.memset(vtbuf[:, hh * 65 + 64:hh * 65 + 65], 1.0)
        return last
    st.v(cvo)
    for tt in range(16):
        for vc in range(2):
            st = p.stage(f"cvt{tt}_{vc}")

            def cvmm(t, tt=tt, vc=vc):
                last = None
                for fc in range(8):
                    last = t.matmul(
                        PS1[:, :],
                        xn16[:, fc * T + tt * 128:fc * T + (tt + 1) * 128],
                        wbuf[:, fc * 1024 + vc * 512:fc * 1024 + (vc + 1) * 512],
                        start=(fc == 0), stop=(fc == 7))
                return last
            st.t(cvmm)
            st = p.stage(f"cve{tt}_{vc}")

            def cvev(v, vc=vc):
                last = None
                for hh in range(8):
                    h = vc * 8 + hh
                    last = v.tensor_copy(
                        out=vtbuf[:, h * 65:h * 65 + 64],
                        in_=PS1[:, hh * 64:(hh + 1) * 64])
                return last
            st.v(cvev)
        st = p.stage(f"cvs{tt}")
        st.dma(lambda y, tt=tt: y.dma_start(
            out=o_vt[tt * 128:(tt + 1) * 128, :], in_=vtbuf[:]))


def emit_d(p, D, nheads=8, nqc=8):
    i_q = D("qag1", [2 * 1024, T], BF16, "ExternalInput")
    i_k = D("kag1", [2 * 1024, T], BF16, "ExternalInput")
    i_v = D("vag1", [2 * T, 16 * 65], BF16, "ExternalInput")
    i_hsel = D("hseld", [1, 1], I32, "ExternalInput")
    i_masks = D("maskd", [128, 4 * 512], F32, "ExternalInput")
    o_o = D("o16", [8, 64, S], BF16, "ExternalOutput")

    ka = p.sbuf("ka", [64, S], BF16)
    qa = p.sbuf("qa", [64, S], BF16)
    vsb = p.sbuf("vsb", [128, 32 * 65], BF16)
    hselb = p.sbuf("hselbd", [1, 1], I32)
    masks = p.sbuf("maskssb", [128, 4 * 512], F32)
    ptile = p.sbuf("ptile", [128, 4 * 512], BF16)
    lzd = p.sbuf("lzd", [128, 512], F32)
    ones128 = p.sbuf("ones128", [1, 128], F32)
    floor16 = p.sbuf("floor16", [1, 66], BF16)
    fval = p.sbuf("fval", [1, 512], BF16)
    rrow = p.sbuf("rrow", [1, 512], F32)
    otile = p.sbuf("otile", [64, 512], F32)
    obuf = p.sbuf("obuf", [64, 512], BF16)

    PSZ = [p.psum(f"PSZ{i}", [128, 512], F32) for i in range(4)]
    PSO = p.psum("PSO", [65, 512], F32)
    PSB = p.psum("PSB", [64, 512], F32)

    st = p.stage("d_consts")
    st.dma(lambda y: [y.dma_start(out=masks[:], in_=i_masks),
                      y.dma_start(out=hselb[:], in_=i_hsel)], n=2)
    st = p.stage("d_init")

    def di(v):
        v.memset(ones128[:], 1.0)
        v.memset(floor16[:], 0.0)
        return v.memset(fval[:], 1e-30)
    st.v(di)
    st = p.stage("d_init2")
    st.v(lambda v: v.memset(floor16[0:1, 64:65], 1.0))

    for h in range(nheads):
        st = p.stage(f"dh{h}_load")

        def hload(y, h=h):
            insts = []

            def emit_half(hg):
                gh = hg * 8 + h
                for rh in range(2):
                    insts.append(y.dma_start(
                        out=ka[:, rh * T:(rh + 1) * T],
                        in_=i_k[rh * 1024 + gh * 64:rh * 1024 + (gh + 1) * 64, :]))
                    insts.append(y.dma_start(
                        out=qa[:, rh * T:(rh + 1) * T],
                        in_=i_q[rh * 1024 + gh * 64:rh * 1024 + (gh + 1) * 64, :]))
                    insts.append(y.dma_start(
                        out=vsb[:].rearrange("p (kt d) -> p kt d", d=65)[
                            :, rh * 16:(rh + 1) * 16, :],
                        in_=i_v[rh * T:(rh + 1) * T, :].rearrange(
                            "(kt p) c -> p kt c", p=128)[
                            :, :, gh * 65:(gh + 1) * 65]))
            with y.register() as rg:
                y.reg_load(rg, hselb[0:1, 0:1])
                with y.If_ne(rg, 0):
                    emit_half(1)
                with y.Else():
                    emit_half(0)
                y.end_ifs()
            return insts
        st.dma(hload, n=6)
        for qc in range(nqc):
            qsl = slice(qc * 512, (qc + 1) * 512)
            def zmm(t, kts, qsl=qsl):
                out = None
                for j, kt in enumerate(kts):
                    ksl = slice(kt * 128, (kt + 1) * 128)
                    out = t.matmul(PSZ[j][:, :], ka[0:64, ksl],
                                   qa[0:64, qsl], start=True, stop=True)
                return out

            st = p.stage(f"dh{h}_z{qc}")
            st.t(lambda t, zmm=zmm: zmm(t, [0, 1, 2, 3]))
            for gidx in range(qc + 1):
                kts = [gidx * 4 + j for j in range(4)]
                diag = [(j, kt) for j, kt in enumerate(kts)
                        if 0 <= kt * 128 - qc * 512 < 512]
                if diag:
                    for j, kt in diag:
                        st = p.stage(f"dh{h}_m{qc}_{gidx}_{j}")
                        mi = (kt * 128 - qc * 512) // 128

                        def ewm(v, j=j, mi=mi):
                            return v.tensor_add(
                                out=lzd[:], in0=PSZ[j][:, :],
                                in1=masks[:, mi * 512:(mi + 1) * 512])
                        st.v(ewm)
                        st = p.stage(f"dh{h}_mx{qc}_{gidx}_{j}")
                        st.s(lambda s, j=j: s.activation(
                            out=ptile[:, j * 512:(j + 1) * 512], in_=lzd[:],
                            func=AF.Exp, scale=0.125))
                ndiag = [(j, kt) for j, kt in enumerate(kts)
                         if not (0 <= kt * 128 - qc * 512 < 512)]
                if ndiag:
                    st = p.stage(f"dh{h}_x{qc}_{gidx}")

                    def ewx(s, ndiag=ndiag):
                        last = None
                        for j, kt in ndiag:
                            last = s.activation(
                                out=ptile[:, j * 512:(j + 1) * 512],
                                in_=PSZ[j][:, :], func=AF.Exp, scale=0.125)
                        return last
                    st.s(ewx)
                st = p.stage(f"dh{h}_pv{qc}_{gidx}")

                def pv(t, qc=qc, gidx=gidx, kts=kts, zmm=zmm):
                    last = None
                    for j, kt in enumerate(kts):
                        last = t.matmul(PSO[:, :],
                                        vsb[:, kt * 65:(kt + 1) * 65],
                                        ptile[:, j * 512:(j + 1) * 512],
                                        start=(gidx == 0 and j == 0),
                                        stop=False)
                    if gidx == qc:
                        last = t.matmul(PSO[:, :], floor16[0:1, 0:65],
                                        fval[0:1, :], start=False, stop=True)
                    else:
                        last = zmm(t, [(gidx + 1) * 4 + j for j in range(4)])
                    return last
                st.t(pv)
            st = p.stage(f"dh{h}_n{qc}")
            st.v(lambda v: v.reciprocal(out=rrow[:], in_=PSO[64:65, :]))
            st = p.stage(f"dh{h}_ne{qc}")
            st.s(lambda s: s.activation(out=otile[:], in_=PSO[0:64, :],
                                        func=AF.Identity))
            st = p.stage(f"dh{h}_nb{qc}")
            st.t(lambda t: t.matmul(PSB[:, :], ones128[0:1, 0:64], rrow[0:1, :],
                                    start=True, stop=True))
            st = p.stage(f"dh{h}_nm{qc}")
            st.v(lambda v: v.tensor_mul(out=obuf[:], in0=otile[:], in1=PSB[:, :]))
            st = p.stage(f"dh{h}_st{qc}")
            st.dma(lambda y, h=h, qsl=qsl: y.dma_start(out=o_o[h][:, qsl],
                                                       in_=obuf[:]))


def _rope_tables_c(r):
    inv_freq = 1.0 / (10000.0 ** (np.arange(0, 64, 2, dtype=np.float32) / 64))
    t = np.arange(r * T, (r + 1) * T, dtype=np.float32)
    fr = t[:, None] * inv_freq[None, :]
    cos = np.cos(fr).T.astype(np.float32)
    sin = np.sin(fr).T.astype(np.float32)
    cosrep = np.tile(cos, (4, 1))
    sinsgn = np.concatenate([-sin, sin, -sin, sin], axis=0)
    return np.ascontiguousarray(cosrep), np.ascontiguousarray(sinsgn)


def stage_c(inputs, a_res, b_res):
    cvec = np.zeros((128, 4), np.float32)
    cvec[:, 0] = 1e-5
    wo_t = _t(inputs["L0_attn_Wo"])
    w_qkv = np.ascontiguousarray(inputs["L1_attn_w_qkv"])
    maps = []
    for c in range(NC):
        b, r = c // 2, c % 2
        tsl = slice(r * T, (r + 1) * T)
        oag = np.concatenate([b_res[2 * b]["o16"].reshape(512, S),
                              b_res[2 * b + 1]["o16"].reshape(512, S)], axis=0)
        cosrep, sinsgn = _rope_tables_c(r)
        maps.append({
            "x_fm": _t(inputs["L0_attn_x"][b, tsl, :]),
            "oag": np.ascontiguousarray(oag), "tsel": np.array([[r]], np.int32),
            "wo_t": wo_t, "g_in": a_res[c]["g"],
            "w_qkv": w_qkv, "cosrep": cosrep, "sinsgn": sinsgn, "cvec": cvec,
        })
    return maps


def _causal_masks_d():
    out = np.zeros((128, 4 * 512), np.float32)
    for mi in range(4):
        kk = np.arange(128)[:, None] + mi * 128
        q = np.arange(512)[None, :]
        out[:, mi * 512:(mi + 1) * 512] = np.where(kk > q, -1e30, 0.0)
    return out


def stage_d(c_res):
    maskd = _causal_masks_d()
    maps = []
    for c in range(NC):
        b, hg = c // 2, c % 2
        ra, rb = c_res[2 * b], c_res[2 * b + 1]
        qag1 = np.concatenate([ra["q1r"].reshape(1024, T),
                               rb["q1r"].reshape(1024, T)], axis=0)
        kag1 = np.concatenate([ra["k1r"].reshape(1024, T),
                               rb["k1r"].reshape(1024, T)], axis=0)
        vag1 = np.concatenate([ra["v1tok"], rb["v1tok"]], axis=0)
        maps.append({"qag1": np.ascontiguousarray(qag1),
                     "kag1": np.ascontiguousarray(kag1),
                     "vag1": np.ascontiguousarray(vag1),
                     "hseld": np.array([[hg]], np.int32), "maskd": maskd})
    return maps


EXT = 258 + T
NW = T // 256


def emit_e(p, D):
    i_yo = D("y_own", [1024, T], F32, "ExternalInput")
    i_yh = D("y_halo", [2 * 1024, 258], F32, "ExternalInput")
    i_wqkv = D("wqkv", [1024, 3072], F32, "ExternalInput")
    i_wo = D("wo", [1024, 1024], F32, "ExternalInput")
    i_w1a = D("w1a", [1024, 2730], F32, "ExternalInput")
    i_w1g = D("w1g", [1024, 2730], F32, "ExternalInput")
    i_w2 = D("w2", [2730, 1024], F32, "ExternalInput")
    i_b1a = D("b1a", [128, 22], F32, "ExternalInput")
    i_b1g = D("b1g", [128, 22], F32, "ExternalInput")
    i_b2 = D("b2c", [128, 8], F32, "ExternalInput")
    i_lng = D("lng", [128, 8], F32, "ExternalInput")
    i_lnb = D("lnb", [128, 8], F32, "ExternalInput")
    i_qsA = D("qsA", [64, EXT], F32, "ExternalInput")
    i_qsB = D("qsB", [64, EXT], F32, "ExternalInput")
    i_ksA = D("ksA", [64, EXT], F32, "ExternalInput")
    i_ksB = D("ksB", [64, EXT], F32, "ExternalInput")
    i_qsc = D("qscol", [64, 2], F32, "ExternalInput")
    i_ksc = D("kscol", [64, 2], F32, "ExternalInput")
    i_mw = D("maskw", [128, 2 * 1024], F32, "ExternalInput")
    i_zg = D("zgate", [1, 1], F32, "ExternalInput")
    i_cv = D("cvec", [128, 4], F32, "ExternalInput")
    o_y = D("ynew", [1024, T], F32, "ExternalOutput")
    o_yt = D("ytail", [1024, 258], F32, "ExternalOutput")

    ysb = p.sbuf("ysb", [128, 8 * EXT], F32)
    xs16 = p.sbuf("xs16", [128, 8 * EXT], BF16)
    oacc = p.sbuf("oaccE", [128, 4 * T], BF16)
    vsb = p.sbuf("vsbE", [128, 8 * 21 * 65], BF16)
    qk16 = p.sbuf("qk16", [128, 2 * EXT], BF16)
    wbuf = p.sbuf("wbufE", [128, 8 * 512], BF16)
    tabs = p.sbuf("tabsE", [64, 4 * EXT], BF16)
    masks = p.sbuf("masksE", [128, 2 * 1024], BF16)
    qhat = p.sbuf("qhatE", [128, 1024], F32)
    tmr = p.sbuf("tmrE", [64, 512], F32)
    tmp = p.sbuf("tmpE", [128, 512], F32)
    ptile = p.sbuf("ptileE", [128, 4 * 256], BF16)
    rows = p.sbuf("rowsE", [65, 528], F32)
    rowsB = p.sbuf("rowsBE", [33, 512], F32)
    cv = p.sbuf("cvE", [128, 4], F32)
    epsT = p.sbuf("epsTE", [65, 512], BF16)
    rows2 = p.sbuf("rows2E", [33, 512], F32)
    ones128 = p.sbuf("ones128E", [65, 128], F32)
    onesb = p.sbuf("onesbE", [128, 1], BF16)
    floor16 = p.sbuf("floor16E", [1, 66], BF16)
    fval = p.sbuf("fvalE", [1, 256], BF16)
    otile = p.sbuf("otileE", [64, 256], F32)
    obuf = p.sbuf("obufE", [64, 256], BF16)
    qsc = p.sbuf("qscE", [64, 2], F32)
    ksc = p.sbuf("kscE", [64, 2], F32)
    zg = p.sbuf("zgE", [1, 1], F32)
    bcols = p.sbuf("bcolsE", [128, 68], F32)
    obnd16 = p.sbuf("obnd16", [128, 8], BF16)
    sq16 = p.sbuf("sq16E", [128, 512], BF16)
    xt16 = p.sbuf("xt16E", [128, 512], BF16)
    hh21 = p.sbuf("hh21E", [64, 512], BF16)

    PS1 = p.psum("PS1", [128, 512], F32)
    PS2 = p.psum("PS2", [128, 512], F32)
    PSM1 = p.psum("PSM1", [128, 512], F32)
    PSM2 = p.psum("PSM2", [128, 512], F32)
    PSN1 = p.psum("PSN1", [1, 512], F32)
    PSN2 = p.psum("PSN2", [1, 512], F32)
    PSB = p.psum("PSBE", [64, 512], F32)
    PSO = p.psum("PSOE", [65, 256], F32)
    PSOfull = p.psum("PSOE", [65, 512], F32)

    rrow = rows2[0:1, 0:256]
    b1a_c, b1g_c = bcols[:, 0:22], bcols[:, 22:44]
    b2_c, lng_c, lnb_c = bcols[:, 44:52], bcols[:, 52:60], bcols[:, 60:68]
    sums, sqs = rowsB[0:1, :], rowsB[32:33, :]
    meanr = rows2[0:1, :]
    msq = rows2[32:33, :]
    rstdr = rowsB[32:33, :]
    nrmr = rows[64:65, 0:512]

    PROJ_SLICES = [slice(1, 513), slice(513, 1025), slice(1025, 1537),
                   slice(1537, 2049), slice(2049, EXT)]
    VTILES = [(2 + 128 * k, 130 + 128 * k) for k in range(18)] + \
             [(1, 129), (129, 257), (257, 258)]

    def yv(fc, a, b):
        return ysb[:, fc * EXT + a:fc * EXT + b]

    def xv(fc, a, b, shifted=True):
        if shifted and fc >= 4:
            a, b = a - 1, b - 1
        return xs16[:, fc * EXT + a:fc * EXT + b]

    def vtile(hh, vt):
        base = (hh * 21 + vt) * 65
        return vsb[:, base:base + 65]

    def hhloc(i):
        if i < 16:
            return oacc[:, i * 512:(i + 1) * 512]
        if i < 21:
            return qk16[:, (i - 16) * 512:(i - 15) * 512]
        return hh21[:, :]

    def xn2loc(fc):
        return vsb[:, fc * 512:(fc + 1) * 512]

    st = p.stage("e_consts")

    def ecl(y):
        return [y.dma_start(out=cv[:], in_=i_cv),
                y.dma_start(out=qsc[:], in_=i_qsc),
                y.dma_start(out=ksc[:], in_=i_ksc),
                y.dma_start(out=zg[:], in_=i_zg),
                y.dma_start(out=bcols[:, 0:22], in_=i_b1a),
                y.dma_start(out=bcols[:, 22:44], in_=i_b1g),
                y.dma_start(out=bcols[:, 44:52], in_=i_b2),
                y.dma_start(out=bcols[:, 52:60], in_=i_lng),
                y.dma_start(out=bcols[:, 60:68], in_=i_lnb)]
    st.dma(ecl, n=9)
    st.gdma(lambda g: g.dma_start(out=masks[:], in_=i_mw), n=1)
    st = p.stage("e_init")

    def ei(v):
        v.memset(ones128[:], 1.0)
        v.memset(onesb[:], 1.0)
        v.memset(floor16[:], 0.0)
        v.memset(fval[:], 1e-30)
        v.memset(epsT[0:1, :], 1e-12)
        v.memset(epsT[32:33, :], 1e-12)
        return v.memset(epsT[64:65, :], 1e-12)
    st.v(ei)
    st = p.stage("e_init2")
    st.v(lambda v: v.memset(floor16[0:1, 64:65], 1.0))
    st = p.stage("e_zg0")
    st.t(lambda t: t.matmul(PSM2[:, 0:1], ones128[0:1, :], zg[0:1, 0:1],
                            start=True, stop=True))
    st = p.stage("e_zg0c")
    st.v(lambda v: v.tensor_copy(out=qhat[:, 0:1], in_=PSM2[:, 0:1]))

    for ti, (src, sc_, col) in enumerate(
            ((i_qsA, 'q', 0), (i_qsB, 'q', 1), (i_ksA, 'k', 0), (i_ksB, 'k', 1))):
        st = p.stage(f"e_tl{ti}")
        st.dma(lambda y, src=src: y.dma_start(out=ysb[0:64, 0:EXT], in_=src))
        st = p.stage(f"e_tm{ti}")
        scap = qsc if sc_ == 'q' else ksc
        st.v(lambda v, ti=ti, scap=scap, col=col: v.tensor_scalar_mul(
            out=tabs[:, ti * EXT:(ti + 1) * EXT], in0=ysb[0:64, 0:EXT],
            scalar1=scap[:, col:col + 1]))

    for fc in range(8):
        st = p.stage(f"e_yl{fc}")

        def yl(y, fc=fc):
            return [y.dma_start(out=yv(fc, 0, 258),
                                in_=i_yh[fc * 128:(fc + 1) * 128, :]),
                    y.dma_start(out=yv(fc, 258, EXT),
                                in_=i_yo[fc * 128:(fc + 1) * 128, :])]
        st.dma(yl, n=2)
        st = p.stage(f"e_yg{fc}")
        st.v(lambda v, fc=fc: v.tensor_scalar_mul(out=yv(fc, 0, 258),
                                                  in0=yv(fc, 0, 258),
                                                  scalar1=qhat[:, 0:1]))
        st = p.stage(f"e_yc{fc}")
        st.v(lambda v, fc=fc: v.tensor_copy(out=xs16[:, fc * EXT:(fc + 1) * EXT],
                                            in_=yv(fc, 0, EXT)))

    for hg in range(2):
        st = p.stage(f"e_vw{hg}")
        st.gdma(lambda g, hg=hg: g.dma_start(
            out=wbuf[:].rearrange("p (fc o) -> p fc o", o=512),
            in_=i_wqkv.rearrange("(fc p) o -> p fc o", p=128)[
                :, :, 2048 + hg * 512:2048 + (hg + 1) * 512]), n=1)
        for vt, (a, b) in enumerate(VTILES):
            n = b - a
            st = p.stage(f"e_vp{hg}_{vt}")

            def vmm(t, a=a, b=b, n=n):
                last = None
                for fc in range(8):
                    last = t.matmul(PS1[0:n, :], xv(fc, a, b),
                                    wbuf[:, fc * 512:(fc + 1) * 512],
                                    start=(fc == 0), stop=(fc == 7))
                return last
            st.t(vmm)
            st = p.stage(f"e_ve{hg}_{vt}")

            def vev(v, vt=vt, n=n):
                last = None
                for hh in range(8):
                    last = v.tensor_copy(out=vtile(hh, vt)[0:n, 0:64],
                                         in_=PS1[0:n, hh * 64:(hh + 1) * 64])
                return last
            st.v(vev)
        st = p.stage(f"e_vones{hg}")

        def von(v):
            last = None
            for hh in range(8):
                for vt in range(21):
                    last = v.memset(vtile(hh, vt)[:, 64:65], 1.0)
            return last
        st.v(von)

        for hp in range(4):
            st = p.stage(f"e_qw{hg}_{hp}")

            def qw(g, hg=hg, hp=hp):
                c0 = hg * 512 + hp * 128
                return [
                    g.dma_start(
                        out=wbuf[:].rearrange("p (fc o) -> p fc o", o=512)[
                            :, :, 0:128],
                        in_=i_wqkv.rearrange("(fc p) o -> p fc o", p=128)[
                            :, :, c0:c0 + 128]),
                    g.dma_start(
                        out=wbuf[:].rearrange("p (fc o) -> p fc o", o=512)[
                            :, :, 128:256],
                        in_=i_wqkv.rearrange("(fc p) o -> p fc o", p=128)[
                            :, :, 1024 + c0:1024 + c0 + 128]),
                ]
            st.gdma(qw, n=2)
            for ts_i, tsl in enumerate(PROJ_SLICES):
                n = tsl.stop - tsl.start
                st = p.stage(f"e_qp{hg}_{hp}_{ts_i}")

                def qmm(t, tsl=tsl, n=n):
                    for fc in range(8):
                        t.matmul(PS1[:, 0:n],
                                 wbuf[:, fc * 512:fc * 512 + 128],
                                 xv(fc, tsl.start, tsl.stop),
                                 start=(fc == 0), stop=(fc == 7))
                    last = None
                    for fc in range(8):
                        last = t.matmul(PS2[:, 0:n],
                                        wbuf[:, fc * 512 + 128:fc * 512 + 256],
                                        xv(fc, tsl.start, tsl.stop),
                                        start=(fc == 0), stop=(fc == 7))
                    return last
                st.t(qmm)
                sfx = f"{hg}_{hp}_{ts_i}"
                SLAB = [(PS1, 0, sq16, 0), (PS2, 0, sq16, 64),
                        (PS1, 64, xt16, 0), (PS2, 64, xt16, 64)]
                NR = [rows[64:65, 0:512], rows2[0:1, 0:512],
                      rows[0:1, 16:528], rows2[32:33, 0:512]]
                NEPS = [epsT[64:65, :], epsT[0:1, :], epsT[0:1, :],
                        epsT[32:33, :]]
                NONE_ = [ones128[64:65, 0:64], ones128[0:1, 0:64],
                         ones128[0:1, 0:64], ones128[32:33, 0:64]]
                st = p.stage(f"e_mn1{sfx}")

                def mn1(sct, n=n):
                    last = None
                    for (PS, rs, sb, ro) in SLAB:
                        last = sct.activation(out=sb[ro:ro + 64, 0:n],
                                              in_=PS[rs:rs + 64, 0:n],
                                              func=AF.Square)
                    return last
                st.s(mn1)
                st = p.stage(f"e_mn2{sfx}")
                NPS = [PSN1, PSN2, PSOfull, PSB]

                def mn2(t, n=n):
                    last = None
                    for ci, (PS, rs, sb, ro) in enumerate(SLAB):
                        last = t.matmul(NPS[ci][0:1, 0:n],
                                        onesb[ro:ro + 64, 0:1],
                                        sb[ro:ro + 64, 0:n],
                                        start=True, stop=True)
                    return last
                st.t(mn2)
                st = p.stage(f"e_mn3{sfx}")

                def mn3(sct, n=n):
                    last = None
                    for ci in range(4):
                        last = sct.activation(out=NR[ci][:, 0:n],
                                              in_=NPS[ci][0:1, 0:n],
                                              func=AF.Sqrt)
                    return last
                st.s(mn3)
                st = p.stage(f"e_mn4{sfx}")

                def mn4(v, n=n):
                    last = None
                    for ci in range(4):
                        v.tensor_max(out=NR[ci][:, 0:n], in0=NR[ci][:, 0:n],
                                     in1=NEPS[ci][:, 0:n])
                        last = v.reciprocal(out=NR[ci][:, 0:n],
                                            in_=NR[ci][:, 0:n])
                    return last
                st.v(mn4)
                for lh in range(2):
                    rs = lh * 64
                    st = p.stage(f"e_mn5{sfx}_{lh}")

                    def mn5(t, lh=lh, n=n):
                        t.matmul(PSB[0:64, 0:n], NONE_[2 * lh], NR[2 * lh][:, 0:n],
                                 start=True, stop=True)
                        return t.matmul(PSM1[0:64, 0:n], NONE_[2 * lh + 1],
                                        NR[2 * lh + 1][:, 0:n],
                                        start=True, stop=True)
                    st.t(mn5)
                    st = p.stage(f"e_mn6a{sfx}_{lh}")

                    def mn6a(sct, rs=rs, n=n):
                        sct.activation(out=qhat[0:64, 0:n],
                                       in_=PS1[rs:rs + 64, 0:n],
                                       func=AF.Identity)
                        return sct.activation(out=tmp[0:64, 0:n],
                                              in_=PS2[rs:rs + 64, 0:n],
                                              func=AF.Identity)
                    st.s(mn6a)
                    st = p.stage(f"e_mn6{sfx}_{lh}")

                    def mn6(v, n=n):
                        v.tensor_mul(out=qhat[0:64, 0:n], in0=qhat[0:64, 0:n],
                                     in1=PSB[0:64, 0:n])
                        return v.tensor_mul(out=tmp[0:64, 0:n],
                                            in0=tmp[0:64, 0:n],
                                            in1=PSM1[0:64, 0:n])
                    st.v(mn6)
                    st = p.stage(f"e_mr1{sfx}_{lh}")

                    def mr1(v, n=n):
                        v.tensor_copy(out=sq16[0:32, 0:n], in_=qhat[32:64, 0:n])
                        v.tensor_copy(out=sq16[32:64, 0:n], in_=qhat[0:32, 0:n])
                        v.tensor_copy(out=xt16[0:32, 0:n], in_=tmp[32:64, 0:n])
                        return v.tensor_copy(out=xt16[32:64, 0:n],
                                             in_=tmp[0:32, 0:n])
                    st.v(mr1)
                    st = p.stage(f"e_mr2{sfx}_{lh}")

                    def mr2(v, tsl=tsl, n=n):
                        a, b = tsl.start, tsl.stop
                        v.tensor_mul(out=qhat[0:64, 0:n], in0=qhat[0:64, 0:n],
                                     in1=tabs[:, a:b])
                        v.tensor_mul(out=sq16[0:64, 0:n], in0=sq16[0:64, 0:n],
                                     in1=tabs[:, EXT + a:EXT + b])
                        v.tensor_add(out=qhat[0:64, 0:n], in0=qhat[0:64, 0:n],
                                     in1=sq16[0:64, 0:n])
                        v.tensor_mul(out=tmp[0:64, 0:n], in0=tmp[0:64, 0:n],
                                     in1=tabs[:, 2 * EXT + a:2 * EXT + b])
                        v.tensor_mul(out=xt16[0:64, 0:n], in0=xt16[0:64, 0:n],
                                     in1=tabs[:, 3 * EXT + a:3 * EXT + b])
                        return v.tensor_add(out=tmp[0:64, 0:n],
                                            in0=tmp[0:64, 0:n],
                                            in1=xt16[0:64, 0:n])
                    st.v(mr2)
                    st = p.stage(f"e_mr3{sfx}_{lh}")

                    def mr3(v, lh=lh, tsl=tsl, n=n):
                        v.tensor_copy(
                            out=qk16[lh * 64:lh * 64 + 64,
                                     tsl.start:tsl.stop],
                            in_=qhat[0:64, 0:n])
                        return v.tensor_copy(
                            out=qk16[lh * 64:lh * 64 + 64,
                                     EXT + tsl.start:EXT + tsl.stop],
                            in_=tmp[0:64, 0:n])
                    st.v(mr3)
            for lh in range(2):
                hh = hp * 2 + lh
                qrow = slice(lh * 64, lh * 64 + 64)
                PSL = [PS1, PS2, PSM1, PSM2]
                def az_w(t, w, qrow):
                    qa_, qb_ = 258 + w * 256, 258 + (w + 1) * 256
                    ka_ = w * 256 + 2
                    out = None
                    for j in range(4):
                        ks = ka_ + j * 128
                        out = t.matmul(PSL[j][:, 0:256],
                                       qk16[qrow, EXT + ks:EXT + ks + 128],
                                       qk16[qrow, qa_:qb_],
                                       start=True, stop=True)
                    return out
                st = p.stage(f"e_az{hg}_{hp}_{lh}")
                st.t(lambda t, qrow=qrow: az_w(t, 0, qrow))
                for w in range(NW):
                    mvar = 0 if w == 0 else 1024
                    sfx = f"{hg}_{hp}_{lh}_{w}"
                    st = p.stage(f"e_am{sfx}")

                    def am(v, mvar=mvar, PSL=PSL):
                        last = None
                        for j in range(4):
                            last = v.scalar_tensor_tensor(
                                out=qhat[:, j * 256:(j + 1) * 256],
                                in0=PSL[j][:, 0:256], scalar=QK_SCALE,
                                in1=masks[:, mvar + j * 256:mvar + (j + 1) * 256],
                                op0=ALU.mult, op1=ALU.add)
                        return last
                    st.v(am)
                    st = p.stage(f"e_ax{sfx}")

                    def ax(s):
                        last = None
                        for j in range(4):
                            last = s.activation(
                                out=ptile[:, j * 256:(j + 1) * 256],
                                in_=qhat[:, j * 256:(j + 1) * 256], func=AF.Exp)
                        return last
                    st.s(ax)
                    st = p.stage(f"e_apv{sfx}")

                    def apv(t, hh=hh, w=w):
                        for j in range(4):
                            t.matmul(PSO[:, :], vtile(hh, 2 * w + j),
                                     ptile[:, j * 256:(j + 1) * 256],
                                     start=(j == 0), stop=False)
                        return t.matmul(PSO[:, :], floor16[0:1, 0:65],
                                        fval[0:1, :], start=False, stop=True)
                    st.t(apv)
                    st = p.stage(f"e_an{sfx}")
                    st.v(lambda v: v.reciprocal(out=rrow[:], in_=PSO[64:65, :]))
                    st = p.stage(f"e_ane{sfx}")
                    st.s(lambda s: s.activation(out=otile[:], in_=PSO[0:64, :],
                                                func=AF.Identity))
                    st = p.stage(f"e_anb{sfx}")
                    st.t(lambda t: t.matmul(PSB[:, 0:256], ones128[0:1, 0:64],
                                            rrow[0:1, :], start=True, stop=True))
                    st = p.stage(f"e_anm{sfx}")

                    def anm(v, hh=hh, w=w):
                        v.tensor_mul(out=obuf[:], in0=otile[:], in1=PSB[:, 0:256])
                        oc_, ro = hh // 2, (hh % 2) * 64
                        return v.tensor_copy(
                            out=oacc[ro:ro + 64,
                                     oc_ * T + w * 256:oc_ * T + (w + 1) * 256],
                            in_=obuf[:])
                    st.v(anm)
                    if w + 1 < NW:
                        st.t(lambda t, w=w, qrow=qrow: az_w(t, w + 1, qrow))
                sfx = f"b{hg}_{hp}_{lh}"
                st = p.stage(f"e_bz{sfx}")

                def bz(t, qrow=qrow):
                    t.matmul(PS1[:, 0:1], qk16[qrow, EXT + 1:EXT + 129],
                             qk16[qrow, 257:258], start=True, stop=True)
                    t.matmul(PS2[:, 0:1], qk16[qrow, EXT + 129:EXT + 257],
                             qk16[qrow, 257:258], start=True, stop=True)
                    return t.matmul(PSM1[0:1, 0:1],
                                    qk16[qrow, EXT + 257:EXT + 258],
                                    qk16[qrow, 257:258], start=True, stop=True)
                st.t(bz)
                st = p.stage(f"e_bm{sfx}")

                def bm(v):
                    v.tensor_scalar_mul(out=qhat[:, 0:1], in0=PS1[:, 0:1],
                                        scalar1=QK_SCALE)
                    v.tensor_scalar_mul(out=qhat[:, 1:2], in0=PS2[:, 0:1],
                                        scalar1=QK_SCALE)
                    return v.tensor_scalar_mul(out=qhat[0:1, 2:3],
                                               in0=PSM1[0:1, 0:1],
                                               scalar1=QK_SCALE)
                st.v(bm)
                st = p.stage(f"e_bx{sfx}")
                st.s(lambda s: s.activation(out=ptile[:, 0:3],
                                            in_=qhat[:, 0:3], func=AF.Exp))
                st = p.stage(f"e_bpv{sfx}")

                def bpv(t, hh=hh):
                    t.matmul(PSO[:, 0:1], vtile(hh, 18), ptile[:, 0:1],
                             start=True, stop=False)
                    t.matmul(PSO[:, 0:1], vtile(hh, 19), ptile[:, 1:2],
                             start=False, stop=False)
                    t.matmul(PSO[:, 0:1], vtile(hh, 20)[0:1, :],
                             ptile[0:1, 2:3], start=False, stop=False)
                    return t.matmul(PSO[:, 0:1], floor16[0:1, 0:65],
                                    fval[0:1, 0:1], start=False, stop=True)
                st.t(bpv)
                st = p.stage(f"e_bn{sfx}")
                st.v(lambda v: v.reciprocal(out=rrow[:, 0:1],
                                            in_=PSO[64:65, 0:1]))
                st = p.stage(f"e_bne{sfx}")
                st.s(lambda s: s.activation(out=otile[:, 0:1],
                                            in_=PSO[0:64, 0:1],
                                            func=AF.Identity))
                st = p.stage(f"e_bnb{sfx}")
                st.t(lambda t: t.matmul(PSB[:, 0:1], ones128[0:1, 0:64],
                                        rrow[0:1, 0:1], start=True, stop=True))
                st = p.stage(f"e_bnm{sfx}")

                def bnm(v, hg=hg, hh=hh):
                    v.tensor_mul(out=obuf[:, 0:1], in0=otile[:, 0:1],
                                 in1=PSB[:, 0:1])
                    return v.tensor_copy(out=rows[0:64, hg * 8 + hh:
                                                  hg * 8 + hh + 1],
                                         in_=obuf[:, 0:1])
                st.v(bnm)

        st = p.stage(f"e_obc{hg}")

        def obc(v, hg=hg):
            last = None
            for fc in range(4):
                h0 = hg * 8 + 2 * fc
                last = v.tensor_copy(out=obnd16[0:64, fc:fc + 1],
                                     in_=rows[0:64, h0:h0 + 1])
                last = v.tensor_copy(out=obnd16[64:128, fc:fc + 1],
                                     in_=rows[0:64, h0 + 1:h0 + 2])
            return last
        st.v(obc)
        for op_ in range(8):
            st = p.stage(f"e_ow{hg}_{op_}")
            st.gdma(lambda g, hg=hg, op_=op_: g.dma_start(
                out=wbuf[:].rearrange("p (fc o) -> p fc o", o=512)[:, 0:4, 0:128],
                in_=i_wo.rearrange("(fc p) o -> p fc o", p=128)[
                    :, hg * 4:(hg + 1) * 4, op_ * 128:(op_ + 1) * 128]), n=1)
            for tc in range(4):
                tsl = slice(tc * 512, (tc + 1) * 512)
                st = p.stage(f"e_op{hg}_{op_}_{tc}")

                def omm(t, tsl=tsl):
                    last = None
                    for fc in range(4):
                        last = t.matmul(
                            PS1[:, :], wbuf[:, fc * 512:fc * 512 + 128],
                            oacc[:, fc * T + tsl.start:fc * T + tsl.stop],
                            start=(fc == 0), stop=(fc == 3))
                    return last
                st.t(omm)
                st = p.stage(f"e_oe{hg}_{op_}_{tc}")

                def oe(v, op_=op_, tsl=tsl):
                    ys = yv(op_, 258 + tsl.start, 258 + tsl.stop)
                    return v.tensor_add(out=ys, in0=ys, in1=PS1[:, :])
                st.v(oe)
            st = p.stage(f"e_obm{hg}_{op_}")

            def obm(t):
                last = None
                for fc in range(4):
                    last = t.matmul(PS2[:, 0:1],
                                    wbuf[:, fc * 512:fc * 512 + 128],
                                    obnd16[:, fc:fc + 1], start=(fc == 0),
                                    stop=(fc == 3))
                return last
            st.t(obm)
            st = p.stage(f"e_obe{hg}_{op_}")

            def obe(v, op_=op_):
                ys = yv(op_, 257, 258)
                return v.tensor_add(out=ys, in0=ys, in1=PS2[:, 0:1])
            st.v(obe)

    st = p.stage("e_zgb")
    st.t(lambda t: t.matmul(PSM2[:, 0:1], ones128[0:1, :], zg[0:1, 0:1],
                            start=True, stop=True))
    st = p.stage("e_zgc")
    st.v(lambda v: v.tensor_copy(out=qhat[:, 0:1], in_=PSM2[:, 0:1]))
    for op_ in range(8):
        st = p.stage(f"e_zg{op_}")
        st.v(lambda v, op_=op_: v.tensor_mul(out=yv(op_, 257, 258),
                                             in0=yv(op_, 257, 258),
                                             in1=qhat[:, 0:1]))

    for tq in range(4):
        ha = 258 + tq * 512
        st = p.stage(f"e_f0{tq}")

        def f0(v):
            v.memset(sums, 0.0)
            return v.memset(sqs, 0.0)
        st.v(f0)
        for fc in range(8):
            sh = 1 if fc >= 4 else 0
            st = p.stage(f"e_fs{tq}_{fc}")
            st.s(lambda s, fc=fc, ha=ha, sh=sh: s.activation(
                out=sq16[:, :], in_=yv(fc, ha - sh, ha - sh + 512),
                func=AF.Square))
            st.v(lambda v, fc=fc, ha=ha, sh=sh: v.tensor_copy(
                out=xt16[:, :], in_=yv(fc, ha - sh, ha - sh + 512)))
            st = p.stage(f"e_fm{tq}_{fc}")

            def fmm(t):
                t.matmul(PSN1[:, :], onesb[:, 0:1], xt16[:, :],
                         start=True, stop=True)
                return t.matmul(PSN2[:, :], onesb[:, 0:1], sq16[:, :],
                                start=True, stop=True)
            st.t(fmm)
            st = p.stage(f"e_fe{tq}_{fc}")

            def fev(v):
                v.tensor_add(out=sums, in0=sums, in1=PSN1[:, :])
                return v.tensor_add(out=sqs, in0=sqs, in1=PSN2[:, :])
            st.v(fev)
        st = p.stage(f"e_ff{tq}")

        def ffin(v):
            v.tensor_scalar_mul(out=meanr, in0=sums, scalar1=1.0 / 1024)
            v.tensor_scalar_mul(out=sqs, in0=sqs, scalar1=1.0 / 1024)
            v.tensor_mul(out=msq, in0=meanr, in1=meanr)
            return v.tensor_sub(out=sqs, in0=sqs, in1=msq)
        st.v(ffin)
        st = p.stage(f"e_fq{tq}")
        st.s(lambda s: s.activation(out=sqs, in_=sqs, func=AF.Sqrt,
                                    bias=cv[32:33, 0:1]))
        st = p.stage(f"e_fr{tq}")
        st.v(lambda v: v.reciprocal(out=rstdr, in_=sqs))
        st = p.stage(f"e_fb{tq}")

        def fbc(t):
            t.matmul(PSM1[:, :], ones128[0:1, :], meanr[:, :],
                     start=True, stop=True)
            return t.matmul(PSM2[:, :], ones128[32:33, :], rstdr[:, :],
                            start=True, stop=True)
        st.t(fbc)
        for fc in range(8):
            sh = 1 if fc >= 4 else 0
            st = p.stage(f"e_fa{tq}_{fc}")

            def fap(v, fc=fc, ha=ha, sh=sh):
                ys = yv(fc, ha - sh, ha - sh + 512)
                v.tensor_sub(out=tmp[:], in0=ys, in1=PSM1[:, :])
                v.tensor_mul(out=tmp[:], in0=tmp[:], in1=PSM2[:, :])
                return v.tensor_scalar(
                    out=xn2loc(fc)[:, :], in0=tmp[:],
                    scalar1=lng_c[:, fc:fc + 1], scalar2=lnb_c[:, fc:fc + 1],
                    op0=ALU.mult, op1=ALU.add)
            st.v(fap)
        def w1load(g, i, n):
            po = (i % 2) * 256
            return [
                g.dma_start(
                    out=wbuf[:].rearrange("p (fc o) -> p fc o", o=512)[
                        :, :, po:po + n],
                    in_=i_w1a.rearrange("(fc p) o -> p fc o", p=128)[
                        :, :, i * 128:i * 128 + n]),
                g.dma_start(
                    out=wbuf[:].rearrange("p (fc o) -> p fc o", o=512)[
                        :, :, po + 128:po + 128 + n],
                    in_=i_w1g.rearrange("(fc p) o -> p fc o", p=128)[
                        :, :, i * 128:i * 128 + n]),
            ]
        st = p.stage(f"e_w1l{tq}")
        st.gdma(lambda g: w1load(g, 0, 128), n=2)
        for i in range(22):
            n = 128 if i < 21 else 42
            po = (i % 2) * 256
            st = p.stage(f"e_w1m{tq}_{i}")
            if i < 21:
                n2 = 128 if i + 1 < 21 else 42
                st.gdma(lambda g, i=i, n2=n2: w1load(g, i + 1, n2), n=2)

            def w1m(t, i=i, n=n, po=po):
                for fc in range(8):
                    t.matmul(PS1[0:n, :],
                             wbuf[:, fc * 512 + po:fc * 512 + po + n],
                             xn2loc(fc)[:, :], start=(fc == 0), stop=(fc == 7))
                last = None
                for fc in range(8):
                    last = t.matmul(
                        PS2[0:n, :],
                        wbuf[:, fc * 512 + po + 128:fc * 512 + po + 128 + n],
                        xn2loc(fc)[:, :], start=(fc == 0), stop=(fc == 7))
                return last
            st.t(w1m)
            st = p.stage(f"e_w1g{tq}_{i}")
            st.s(lambda s, i=i, n=n: s.activation(
                out=tmp[0:n, :], in_=PS2[0:n, :], func=AF.Gelu,
                bias=b1g_c[0:n, i:i + 1]))
            st = p.stage(f"e_w1h{tq}_{i}")

            def w1h(v, i=i, n=n):
                v.tensor_scalar(out=qhat[0:n, 0:512], in0=PS1[0:n, :],
                                scalar1=b1a_c[0:n, i:i + 1], scalar2=0.0,
                                op0=ALU.add, op1=ALU.add)
                return v.tensor_mul(out=hhloc(i)[0:n, 0:512],
                                    in0=qhat[0:n, 0:512], in1=tmp[0:n, :])
            st.v(w1h)
        for op_ in range(8):
            st = p.stage(f"e_w2l{tq}_{op_}")

            def w2l(g, op_=op_):
                return [
                    g.dma_start(
                        out=wbuf[:, 0:2688].rearrange("p (i o) -> p i o", o=128),
                        in_=i_w2[0:2688, :].rearrange("(i p) o -> p i o", p=128)[
                            :, :, op_ * 128:(op_ + 1) * 128]),
                    g.dma_start(
                        out=wbuf[0:42, 2688:2816],
                        in_=i_w2[2688:2730, op_ * 128:(op_ + 1) * 128]),
                ]
            st.gdma(w2l, n=2)
            st = p.stage(f"e_w2m{tq}_{op_}")

            def w2m(t):
                last = None
                for i in range(22):
                    n = 128 if i < 21 else 42
                    last = t.matmul(PS1[:, :], wbuf[0:n, i * 128:i * 128 + 128],
                                    hhloc(i)[0:n, 0:512], start=(i == 0),
                                    stop=(i == 21))
                return last
            st.t(w2m)
            st = p.stage(f"e_w2e{tq}_{op_}")

            def w2e(v, op_=op_, ha=ha):
                v.tensor_scalar(out=tmp[:], in0=PS1[:, :],
                                scalar1=b2_c[:, op_:op_ + 1], scalar2=0.0,
                                op0=ALU.add, op1=ALU.add)
                ys = yv(op_, ha, ha + 512)
                return v.tensor_add(out=ys, in0=ys, in1=tmp[:])
            st.v(w2e)
            st = p.stage(f"e_w2s{tq}_{op_}")
            st.dma(lambda y, op_=op_, tq=tq, ha=ha: y.dma_start(
                out=o_y[op_ * 128:(op_ + 1) * 128, tq * 512:(tq + 1) * 512],
                in_=yv(op_, ha, ha + 512)))
    for op_ in range(8):
        st = p.stage(f"e_tail{op_}")
        st.dma(lambda y, op_=op_: y.dma_start(
            out=o_yt[op_ * 128:(op_ + 1) * 128, :],
            in_=yv(op_, EXT - 258, EXT)))


def _xpos_ext_tables(r):
    half = np.arange(0, 64, 2, dtype=np.float64)
    inv_freq = 1.0 / 10000.0 ** (half / 64)
    t = np.arange(r * T - 258, r * T + T, dtype=np.float64)
    freqs = t[:, None] * inv_freq[None, :]
    freqs = np.concatenate([freqs, freqs], -1)
    base_scale = (half + 0.4 * 64) / (1.4 * 64)
    power = (t - S // 2) / (256 // 2)
    scale = base_scale[None, :] ** power[:, None]
    scale = np.concatenate([scale, scale], -1)
    cos, sin = np.cos(freqs), np.sin(freqs)
    sgn = np.concatenate([-np.ones(32), np.ones(32)])[None, :]
    qsA = np.ascontiguousarray((cos * scale).T.astype(np.float32))
    qsB = np.ascontiguousarray((sgn * sin * scale).T.astype(np.float32))
    ksA = np.ascontiguousarray((cos / scale).T.astype(np.float32))
    ksB = np.ascontiguousarray((sgn * sin / scale).T.astype(np.float32))
    return qsA, qsB, ksA, ksB


def _maskw(r):
    kk = np.arange(128)[:, None]
    q = np.arange(256)[None, :]
    out = np.zeros((128, 2048), np.float32)
    for mi in range(4):
        key_rel = mi * 128 + kk - 256
        ok = (key_rel <= q) & (key_rel >= q - 256)
        normal = np.where(ok, 0.0, -1e30).astype(np.float32)
        w0 = normal.copy()
        if r == 0:
            w0 = np.where(key_rel < 0, -1e30, w0).astype(np.float32)
        out[:, 1024 + mi * 256:1024 + (mi + 1) * 256] = normal
        out[:, mi * 256:(mi + 1) * 256] = w0
    return out


def _colpack(vec, ncol):
    out = np.zeros((128, ncol), np.float32)
    v = np.asarray(vec, np.float32)
    for j in range(ncol):
        seg = v[j * 128:(j + 1) * 128]
        out[:len(seg), j] = seg
    return out


def stage_e(inputs, li, y_fm_per_core):
    cvec = np.zeros((128, 4), np.float32)
    cvec[:, 0] = 1e-5
    W1 = inputs["L1_ffn_W1"][li]
    b1 = inputs["L1_ffn_b1"][li]
    shared = {
        "wqkv": np.ascontiguousarray(inputs["L1_ffn_Wqkv"][li]),
        "wo": np.ascontiguousarray(inputs["L1_ffn_Wo"][li]),
        "w1a": np.ascontiguousarray(W1[:, 0:2730]),
        "w1g": np.ascontiguousarray(W1[:, 2730:]),
        "w2": np.ascontiguousarray(inputs["L1_ffn_W2"][li]),
        "b1a": _colpack(b1[0:2730], 22), "b1g": _colpack(b1[2730:], 22),
        "b2c": _colpack(inputs["L1_ffn_b2"][li], 8),
        "lng": _colpack(inputs["L1_ffn_ln_g"][li], 8),
        "lnb": _colpack(inputs["L1_ffn_ln_b"][li], 8),
        "cvec": cvec,
    }
    qs = np.asarray(inputs["L1_ffn_q_scale"][li], np.float32)
    ks = np.asarray(inputs["L1_ffn_k_scale"][li], np.float32)
    shared["qscol"] = np.stack([qs, np.concatenate([qs[32:], qs[:32]])], axis=1)
    shared["kscol"] = np.stack([ks, np.concatenate([ks[32:], ks[:32]])], axis=1)
    maps = []
    for c in range(NC):
        b, r = c // 2, c % 2
        m = dict(shared)
        qsA, qsB, ksA, ksB = _xpos_ext_tables(r)
        m.update(qsA=qsA, qsB=qsB, ksA=ksA, ksB=ksB, maskw=_maskw(r),
                 zgate=np.array([[0.0 if r == 0 else 1.0]], np.float32))
        m["y_own"] = np.ascontiguousarray(y_fm_per_core[c])
        yh = np.zeros((2 * 1024, 258), np.float32)
        yh[0:1024] = y_fm_per_core[2 * b][:, T - 258:]
        yh[1024:] = y_fm_per_core[2 * b + 1][:, T - 258:]
        m["y_halo"] = yh
        maps.append(m)
    return maps


def emit_f(p, D):
    i_x1 = D("x1", [1024, T], F32, "ExternalInput")
    i_oag = D("oag1", [2 * 512, S], BF16, "ExternalInput")
    i_tsel = D("tself", [1, 1], I32, "ExternalInput")
    i_w = D("wout", [1024, 1024], F32, "ExternalInput")
    i_y6 = D("y6", [1024, T], F32, "ExternalInput")
    o_x = D("xout", [1024, T], F32, "ExternalOutput")

    osb = p.sbuf("osbF", [128, 8 * T], BF16)
    tselb = p.sbuf("tselbF", [1, 1], I32)
    wsb = p.sbuf("wsbF", [128, 8 * 1024], BF16)
    x1c = p.sbuf("x1cF", [128, T], F32)
    y6c = p.sbuf("y6cF", [128, T], F32)
    gg = p.sbuf("ggF", [128, 512], F32)
    tt = p.sbuf("ttF", [128, 512], F32)
    PS1 = p.psum("PS1", [128, 512], F32)

    st = p.stage("f_load")

    def fl(g):
        out = [g.dma_start(out=wsb[:].rearrange("p (fc o) -> p fc o", o=1024),
                           in_=i_w.rearrange("(fc p) o -> p fc o", p=128)),
               g.dma_start(out=tselb[:], in_=i_tsel)]
        with g.register() as rg:
            g.reg_load(rg, tselb[0:1, 0:1])
            with g.If_ne(rg, 0):
                for fc in range(8):
                    out.append(g.dma_start(
                        out=osb[:, fc * T:(fc + 1) * T],
                        in_=i_oag[fc * 128:(fc + 1) * 128, T:2 * T]))
            with g.Else():
                for fc in range(8):
                    out.append(g.dma_start(
                        out=osb[:, fc * T:(fc + 1) * T],
                        in_=i_oag[fc * 128:(fc + 1) * 128, 0:T]))
            g.end_ifs()
        return out
    st.gdma(fl, n=10)

    for op_ in range(8):
        st = p.stage(f"f_x{op_}")

        def fx(y, op_=op_):
            return [y.dma_start(out=x1c[:], in_=i_x1[op_ * 128:(op_ + 1) * 128, :]),
                    y.dma_start(out=y6c[:], in_=i_y6[op_ * 128:(op_ + 1) * 128, :])]
        st.dma(fx, n=2)
        for tc in range(4):
            tsl = slice(tc * 512, (tc + 1) * 512)
            st = p.stage(f"f_m{op_}_{tc}")

            def fm(t, op_=op_, tsl=tsl):
                last = None
                for fc in range(8):
                    last = t.matmul(
                        PS1[:, :],
                        wsb[:, fc * 1024 + op_ * 128:fc * 1024 + op_ * 128 + 128],
                        osb[:, fc * T + tsl.start:fc * T + tsl.stop],
                        start=(fc == 0), stop=(fc == 7))
                return last
            st.t(fm)
            st = p.stage(f"f_g{op_}_{tc}")
            st.s(lambda s, tsl=tsl: s.activation(out=gg[:], in_=y6c[:, tsl],
                                                 func=AF.Silu))
            st = p.stage(f"f_e{op_}_{tc}")

            def fe(v, tsl=tsl):
                v.tensor_mul(out=tt[:], in0=gg[:], in1=PS1[:, :])
                return v.tensor_add(out=tt[:], in0=tt[:], in1=x1c[:, tsl])
            st.v(fe)
            st = p.stage(f"f_s{op_}_{tc}")
            st.dma(lambda y, op_=op_, tsl=tsl: y.dma_start(
                out=o_x[op_ * 128:(op_ + 1) * 128, tsl], in_=tt[:]))


def stage_f(inputs, c_res, d_res, e_res):
    wout = np.ascontiguousarray(inputs["L1_attn_w_out"])
    maps = []
    for c in range(NC):
        b, r = c // 2, c % 2
        oag1 = np.concatenate([d_res[2 * b]["o16"].reshape(512, S),
                               d_res[2 * b + 1]["o16"].reshape(512, S)], axis=0)
        maps.append({
            "x1": c_res[c]["x1"], "oag1": np.ascontiguousarray(oag1),
            "tself": np.array([[r]], np.int32),
            "wout": wout, "y6": e_res[c]["ynew"],
        })
    return maps


_NC_CACHE = {}


def _get(name, builder):
    if name not in _NC_CACHE:
        _NC_CACHE[name] = builder()
    return _NC_CACHE[name]


def kernel_unfused(**inputs):
    inputs = {k: np.asarray(v) for k, v in inputs.items()}
    cores = list(range(NC))
    resA = run_bass_kernel_spmd(_get("A", build_launch_a), stage_a(inputs),
                                core_ids=cores)
    resB = run_bass_kernel_spmd(_get("B", build_launch_b),
                                stage_b(inputs, resA.results), core_ids=cores)
    if any(float(r["flag"][0, 0]) > 0 for r in resB.results):
        raise NotImplementedError(
            "L0 hyperbolic attention hit a non-asymptotic branch; exact "
            "fixup path required")
    resC = run_bass_kernel_spmd(_get("C", build_launch_c),
                                stage_c(inputs, resA.results, resB.results),
                                core_ids=cores)
    resD = run_bass_kernel_spmd(_get("D", build_launch_d),
                                stage_d(resC.results), core_ids=cores)
    y_fm = [resC.results[c]["xn1f"] for c in range(NC)]
    ncE = _get("E", build_launch_e)
    for li in range(6):
        resE = run_bass_kernel_spmd(ncE, stage_e(inputs, li, y_fm),
                                    core_ids=cores)
        y_fm = [resE.results[c]["ynew"] for c in range(NC)]
    resF = run_bass_kernel_spmd(_get("F", build_launch_f),
                                stage_f(inputs, resC.results, resD.results,
                                        {c: {"ynew": y_fm[c]} for c in range(NC)}),
                                core_ids=cores)
    out = np.zeros((B, S, D), np.float32)
    for c in range(NC):
        b, r = c // 2, c % 2
        out[b, r * T:(r + 1) * T, :] = resF.results[c]["xout"].T
    return out


def _mk_builder(emit, name, **kw):
    def build():
        p = Prog(name)
        p.begin_section("")

        def D(nm, shape, dt, kind):
            return p.nc.dram_tensor(nm, shape, dt, kind=kind).ap()
        emit(p, D, **kw)
        return p.finalize()
    return build


build_launch_a = _mk_builder(emit_a, "A")
build_launch_b = _mk_builder(emit_b, "B")
build_launch_c = _mk_builder(emit_c, "C")
build_launch_d = _mk_builder(emit_d, "D")
build_launch_e = _mk_builder(emit_e, "E")
build_launch_f = _mk_builder(emit_f, "F")


PAIRS = [[0, 1], [2, 3], [4, 5], [6, 7]]

_E_PER_LI = {"wqkv", "wo", "w1a", "w1g", "w2", "b1a", "b1g", "b2c", "lng",
             "lnb", "qscol", "kscol", "y_own", "y_halo", "ynew", "ytail"}


def build_fused():
    p = Prog("FU")
    nc = p.nc
    tensors = {}

    def getD(prefix, internal_names, rename=None):
        rename = rename or {}

        def D(nm, shape, dt, kind):
            key = rename.get(nm, prefix + nm)
            if key in tensors:
                return tensors[key]
            k = "Internal" if nm in internal_names else kind
            if k == "ExternalOutput" and nm not in internal_names:
                ap = nc.dram_tensor(key, shape, dt, kind="ExternalOutput").ap()
            else:
                ap = nc.dram_tensor(key, shape, dt, kind=k).ap()
            tensors[key] = ap
            return ap
        return D

    def ag(name, in_ap, out_ap):
        st = p.stage("ag_" + name)
        st.ops["gpsimd"].append(("cc", lambda g, i=in_ap, o=out_ap:
                                 g.collective_compute(
                                     "AllGather", ALU.bypass,
                                     replica_groups=PAIRS,
                                     ins=[i.opt()], outs=[o.opt()]), 1))

    def mk(key, shape, dt):
        ap = nc.dram_tensor(key, shape, dt, kind="Internal").ap()
        tensors[key] = ap
        return ap

    p.begin_section("A_")
    emit_a(p, getD("A_", {"qaug", "kaug", "vtok", "kn32", "g"}))
    qag = mk("qag", [2 * 1056, T], BF16)
    kag = mk("kag", [2 * 1056, T], BF16)
    vag = mk("vag", [2 * T, 16 * 65], BF16)
    knag = mk("knag", [32, T], F32)
    ag("qaug", tensors["A_qaug"].rearrange("h r t -> (h r) t"), qag)
    ag("kaug", tensors["A_kaug"].rearrange("h r t -> (h r) t"), kag)
    ag("vtok", tensors["A_vtok"], vag)
    ag("kn32", tensors["A_kn32"], knag)
    p.begin_section("B_")
    emit_b(p, getD("B_", {"o16"},
                   rename={"qag": "qag", "kag": "kag", "vag": "vag",
                           "knag": "knag"}))
    oag = mk("oag", [2 * 512, S], BF16)
    ag("o16", tensors["B_o16"].rearrange("h d s -> (h d) s"), oag)
    p.begin_section("C_")
    emit_c(p, getD("C_", {"x1", "xn1f", "xn1tail", "q1r", "k1r", "v1tok"},
                   rename={"oag": "oag", "g_in": "A_g"}))
    qag1 = mk("qag1", [2 * 1024, T], BF16)
    kag1 = mk("kag1", [2 * 1024, T], BF16)
    vag1 = mk("vag1", [2 * T, 16 * 65], BF16)
    yh0 = mk("yh0", [2 * 1024, 258], F32)
    ag("q1r", tensors["C_q1r"].rearrange("h d t -> (h d) t"), qag1)
    ag("k1r", tensors["C_k1r"].rearrange("h d t -> (h d) t"), kag1)
    ag("v1tok", tensors["C_v1tok"], vag1)
    ag("yh0", tensors["C_xn1tail"], yh0)
    p.begin_section("D_")
    emit_d(p, getD("D_", {"o16"},
                   rename={"qag1": "qag1", "kag1": "kag1", "vag1": "vag1"}))
    oag1 = mk("oag1", [2 * 512, S], BF16)
    ag("o16d", tensors["D_o16"].rearrange("h d s -> (h d) s"), oag1)
    yprev = tensors["C_xn1f"]
    yhalo = yh0
    for li in range(6):
        pre = f"E{li}_"

        def D_e(nm, shape, dt, kind, li=li, yprev=yprev, yhalo=yhalo):
            if nm == "y_own":
                return yprev
            if nm == "y_halo":
                return yhalo
            key = (f"E{li}_" + nm) if nm in _E_PER_LI else ("E_" + nm)
            if key in tensors:
                return tensors[key]
            k = "Internal" if nm in ("ynew", "ytail") else kind
            ap = nc.dram_tensor(key, shape, dt, kind=k).ap()
            tensors[key] = ap
            return ap
        p.begin_section(pre)
        emit_e(p, D_e)
        ynew = tensors[f"E{li}_ynew"]
        if li < 5:
            yh = mk(f"yh{li + 1}", [2 * 1024, 258], F32)
            ag(f"yh{li + 1}", tensors[f"E{li}_ytail"], yh)
            yhalo = yh
        yprev = ynew
    p.begin_section("F_")
    emit_f(p, getD("F_", set(),
                   rename={"oag1": "oag1", "x1": "C_x1", "y6": "E5_ynew"}))
    return p.finalize()


def stage_fused(inputs):
    cvec4 = np.zeros((128, 4), np.float32)
    cvec4[:, 0] = 1e-5
    cvb = np.zeros((128, 8), np.float32)
    cvb[:, 0] = 1e-8
    cvb[:, 1] = 1.0
    cvb[:, 3] = 4.0001
    scal = np.array([[float(inputs["L0_attn_log_c"]),
                      float(inputs["L0_attn_log_beta"])]], np.float32)
    id32 = np.eye(32, dtype=np.float32)
    masksb = _causal_masks_b()
    maskd = _causal_masks_d()
    wo_t = _t(inputs["L0_attn_Wo"])
    w_qkv = np.ascontiguousarray(inputs["L1_attn_w_qkv"])
    wout = np.ascontiguousarray(inputs["L1_attn_w_out"])
    shared_a = {
        "A_wq_t": _t(inputs["L0_attn_Wq"]), "A_wk_t": _t(inputs["L0_attn_Wk"]),
        "A_wv_t": _t(inputs["L0_attn_Wv"]),
        "A_wg_t": _t(inputs["L0_ffn_w_gate"]),
        "A_wu_t": _t(inputs["L0_ffn_w_up"]),
        "A_wd_t": _t(inputs["L0_ffn_w_down"]), "A_cvec": cvec4,
    }
    e_shared = {}
    e_perli = {}
    for li in range(6):
        m = stage_e(inputs, li, [np.zeros((1024, T), np.float32)] * NC)
        sm = m[0]
        for nm in ("wqkv", "wo", "w1a", "w1g", "w2", "b1a", "b1g", "b2c",
                   "lng", "lnb", "qscol", "kscol"):
            e_perli[f"E{li}_{nm}"] = sm[nm]
    maps = []
    for c in range(NC):
        b, r = c // 2, c % 2
        tsl = slice(r * T, (r + 1) * T)
        m = dict(shared_a)
        m["A_x_fm"] = _t(inputs["L0_attn_x"][b, tsl, :])
        m.update({"B_hsel": np.array([[r]], np.int32), "B_id32": id32,
                  "B_scal": scal, "B_masks": masksb, "B_cvecb": cvb})
        m.update({"C_x_fm": m["A_x_fm"], "C_tsel": np.array([[r]], np.int32),
                  "C_wo_t": wo_t, "C_w_qkv": w_qkv, "C_cvec": cvec4})
        cosrep, sinsgn = _rope_tables_c(r)
        m["C_cosrep"] = cosrep
        m["C_sinsgn"] = sinsgn
        m.update({"D_hseld": np.array([[r]], np.int32), "D_maskd": maskd})
        qsA, qsB, ksA, ksB = _xpos_ext_tables(r)
        m.update({"E_qsA": qsA, "E_qsB": qsB, "E_ksA": ksA, "E_ksB": ksB,
                  "E_maskw": _maskw(r),
                  "E_zgate": np.array([[0.0 if r == 0 else 1.0]], np.float32),
                  "E_cvec": cvec4})
        m.update(e_perli)
        m.update({"F_wout": wout, "F_tself": np.array([[r]], np.int32)})
        maps.append(m)
    return maps


def kernel_fused(**inputs):
    inputs = {k: np.asarray(v) for k, v in inputs.items()}
    nc = _get("FU", build_fused)
    res = run_bass_kernel_spmd(nc, stage_fused(inputs),
                               core_ids=list(range(NC)))
    if any(float(r["B_flag"][0, 0]) > 0 for r in res.results):
        raise NotImplementedError(
            "L0 hyperbolic attention hit a non-asymptotic branch; exact "
            "fixup path required")
    out = np.zeros((B, S, D), np.float32)
    for c in range(NC):
        b, r = c // 2, c % 2
        out[b, r * T:(r + 1) * T, :] = res.results[c]["F_xout"].T
    return out


kernel = kernel_fused


# version 06 (final)
# speedup vs baseline: 1.3362x; 1.0395x over previous
import contextlib
import numpy as np
import ml_dtypes
import concourse.bass as bass
import concourse.mybir as mybir
from concourse.bass_utils import run_bass_kernel_spmd

F32 = mybir.dt.float32
BF16 = mybir.dt.bfloat16
I32 = mybir.dt.int32
AF = mybir.ActivationFunctionType
ALU = mybir.AluOpType
AX = mybir.AxisListType.X
BB = ml_dtypes.bfloat16

B, S, D = 4, 4096, 1024
T = 2048
H, DH = 16, 64
NC = 8
FFN0 = 3072
QK_SCALE = 8.0
ENGS = ("tensor", "vector", "scalar", "gpsimd", "sync")


class Stage:
    __slots__ = ("name", "ops")

    def __init__(self, name):
        self.name = name
        self.ops = {e: [] for e in ENGS}

    def t(self, fn): self.ops["tensor"].append(("c", fn, 1))
    def v(self, fn): self.ops["vector"].append(("c", fn, 1))
    def s(self, fn): self.ops["scalar"].append(("c", fn, 1))
    def g(self, fn): self.ops["gpsimd"].append(("c", fn, 1))
    def dma(self, fn, n=1): self.ops["sync"].append(("d", fn, n))
    def gdma(self, fn, n=1): self.ops["gpsimd"].append(("d", fn, n))


class Prog:

    _PSMAP = {"PS1": "Z0", "PS2": "Z1", "PSM1": "Z2", "PSM2": "Z3",
              "PSM1E": "Z2", "PSM2E": "Z3", "PSZ0": "Z0", "PSZ1": "Z1",
              "PSZ2": "Z2", "PSZ3": "Z3", "PSN1": "N1", "PSN2": "N2",
              "PSN": "N1", "PSB": "B", "PSBE": "B", "PSO": "O", "PSOE": "O"}
    _PSHAPE = {"Z0": [128, 512], "Z1": [128, 512], "Z2": [128, 512],
               "Z3": [128, 512], "N1": [1, 512], "N2": [1, 512],
               "B": [64, 512], "O": [65, 512]}

    def __init__(self, name="prog"):
        self.nc = bass.Bass("TRN2", target_bir_lowering=False, debug=False,
                            detect_race_conditions=False)
        self.stack = contextlib.ExitStack()
        self.stages = []
        self.name = name
        self.prefix = ""
        self._off = 16512
        self._pspool = {}

    SBUF_BASE = 16512

    def begin_section(self, prefix):
        self.prefix = prefix
        self._off = self.SBUF_BASE

    def sbuf(self, name, shape, dtype):
        nbytes = int(np.prod(shape[1:])) * mybir.dt.size(dtype)
        off = (self._off + 31) & ~31
        self._off = off + nbytes
        h = self.nc.alloc_sbuf_tensor_at(self.prefix + name, shape, dtype,
                                         offset=off)
        return h

    def psum(self, name, shape, dtype=F32):
        key = self._PSMAP[name]
        if key not in self._pspool:
            self._pspool[key] = self.stack.enter_context(
                self.nc.psum_tensor("ps_" + key, self._PSHAPE[key], F32))
        t = self._pspool[key]
        if list(shape) == self._PSHAPE[key]:
            return t
        return t[0:shape[0], 0:shape[1]]

    def sem(self, name):
        return self.stack.enter_context(self.nc.semaphore(name))

    def stage(self, name=""):
        st = Stage(name or f"st{len(self.stages)}")
        self.stages.append(st)
        return st

    def finalize(self):
        nc = self.nc
        barC = self.sem("barC")
        barD = self.sem("barD")
        cb, db = [0], [0]
        for st in self.stages:
            c, d = cb[-1], db[-1]
            for e in ENGS:
                ops = st.ops[e]
                if not ops:
                    continue
                d += 16 * sum(n for k, _, n in ops if k == "d")
                c += sum(n for k, _, n in ops if k == "cc")
                if ops[-1][0] == "c":
                    c += 1
            cb.append(c)
            db.append(d)

        def run_engine(eng_name):
            def body(eng):
                lwc, lwd = 0, 0
                for k, st in enumerate(self.stages):
                    ops = st.ops[eng_name]
                    if not ops:
                        continue
                    if cb[k] > lwc:
                        eng.wait_ge(barC, cb[k])
                        lwc = cb[k]
                    if db[k] > lwd:
                        eng.wait_ge(barD, db[k])
                        lwd = db[k]
                    last_c = None
                    for kind, fn, n in ops:
                        r = fn(eng)
                        insts = r if isinstance(r, (list, tuple)) else [r]
                        insts = [i for i in insts if i is not None]
                        if kind == "d":
                            assert len(insts) >= n, (
                                st.name, eng_name, len(insts), n)
                            for i in insts:
                                i.then_inc(barD, 16)
                        elif kind == "cc":
                            for i in insts:
                                i.then_inc(barC)
                        elif insts:
                            last_c = insts[-1]
                    if ops[-1][0] == "c":
                        assert last_c is not None, (st.name, eng_name)
                        last_c.then_inc(barC, 1)
            return body

        with nc.Block() as blk:
            for e in ENGS:
                if any(st.ops[e] for st in self.stages):
                    getattr(blk, e)(run_engine(e))

            def final_wait(eng):
                eng.wait_ge(barC, cb[-1])
                eng.wait_ge(barD, db[-1])
            blk.vector(final_wait)
        self.stack.close()
        return nc


def cdiv(a, b):
    return (a + b - 1) // b


def emit_a(p, D):
    x_fm = D("x_fm", [1024, T], F32, "ExternalInput")
    wq_t = D("wq_t", [1024, 1024], F32, "ExternalInput")
    wk_t = D("wk_t", [1024, 1024], F32, "ExternalInput")
    wv_t = D("wv_t", [1024, 1024], F32, "ExternalInput")
    wg_t = D("wg_t", [1024, FFN0], I32, "ExternalInput")
    wu_t = D("wu_t", [1024, FFN0], I32, "ExternalInput")
    wd_t = D("wd_t", [FFN0, 1024], I32, "ExternalInput")
    cvec = D("cvec", [128, 4], F32, "ExternalInput")
    o_qaug = D("qaug", [16, 66, T], BF16, "ExternalOutput")
    o_kaug = D("kaug", [16, 66, T], BF16, "ExternalOutput")
    o_vt = D("vtok", [T, 16 * 65], BF16, "ExternalOutput")
    o_kn32 = D("kn32", [16, T], F32, "ExternalOutput")
    o_g = D("g", [1024, T], F32, "ExternalOutput")

    cv = p.sbuf("cv", [128, 4], F32)
    xn16 = p.sbuf("xn16", [128, 8 * T], BF16)
    arenaA = p.sbuf("arenaA", [128, 24 * 1024], BF16)
    wgc = p.sbuf("wgc", [128, 8 * 512], BF16)
    wuc = p.sbuf("wuc", [128, 8 * 512], BF16)
    wdbuf = p.sbuf("wdbuf", [128, 24 * 128], BF16)
    xc = p.sbuf("xc", [128, T], F32)
    rowsA = p.sbuf("rowsA", [65, T], F32)
    rowsB = p.sbuf("rowsB", [33, T], F32)
    ones128 = p.sbuf("ones128", [33, 128], F32)
    onesb = p.sbuf("onesb", [128, 1], BF16)
    sq16 = p.sbuf("sq16", [128, T], BF16)
    xc16 = p.sbuf("xc16", [128, T], BF16)
    sqq = p.sbuf("sqq", [64, 512], BF16)
    qa = p.sbuf("qa", [66, T], BF16)
    ka = p.sbuf("ka", [66, T], BF16)
    vtbuf = p.sbuf("vtbuf", [128, 16 * 65], BF16)
    knrow16 = p.sbuf("knrow16", [1, T], BF16)
    sg = p.sbuf("sg", [128, 512], F32)
    gout = p.sbuf("gout", [128, 512], F32)
    sums = rowsA[0:1, :]
    sqs = rowsA[32:33, :]
    knrow = rowsA[64:65, :]
    mean = rowsB[0:1, :]
    rstd = rowsB[32:33, :]

    PS1 = p.psum("PS1", [128, 512], F32)
    PS2 = p.psum("PS2", [128, 512], F32)
    PSN1 = p.psum("PSN1", [1, 512], F32)
    PSN2 = p.psum("PSN2", [1, 512], F32)
    PSM1 = p.psum("PSM1", [128, 512], F32)
    PSM2 = p.psum("PSM2", [128, 512], F32)

    def wq16(fc):
        return arenaA[:, fc * 1024:(fc + 1) * 1024]

    def wk16(fc):
        return arenaA[:, 8 * 1024 + fc * 1024:8 * 1024 + (fc + 1) * 1024]

    def wv16(fc):
        return arenaA[:, 16 * 1024 + fc * 1024:16 * 1024 + (fc + 1) * 1024]

    def hsl(hc, csl):
        base = hc * 1024
        return arenaA[:, base + csl.start:base + csl.stop]

    st = p.stage("consts")
    st.dma(lambda y: y.dma_start(out=cv[:], in_=cvec))

    st = p.stage("init")

    def init(v):
        v.memset(ones128[:], 1.0)
        v.memset(onesb[:], 1.0)
        v.memset(sums, 0.0)
        v.memset(sqs, 0.0)
        v.memset(qa[64:66, :], 1.0)
        return v.memset(ka[64:66, :], 1.0)
    st.v(init)

    st = p.stage("wload_attn")

    def wload(g):
        out = []
        for fc in range(8):
            out.append(g.dma_start(out=wq16(fc), in_=wq_t[fc * 128:(fc + 1) * 128, :]))
            out.append(g.dma_start(out=wk16(fc), in_=wk_t[fc * 128:(fc + 1) * 128, :]))
            out.append(g.dma_start(out=wv16(fc), in_=wv_t[fc * 128:(fc + 1) * 128, :]))
        return out
    st.gdma(wload, n=24)

    for fc in range(8):
        st = p.stage(f"lnl{fc}")
        st.dma(lambda y, fc=fc: y.dma_start(
            out=xc[:], in_=x_fm[fc * 128:(fc + 1) * 128, :]))
        st = p.stage(f"lnc{fc}")
        st.s(lambda s: s.activation(out=sq16[:], in_=xc[:], func=AF.Square))
        st.v(lambda v: v.tensor_copy(out=xc16[:], in_=xc[:]))
        for tc in range(4):
            tsl = slice(tc * 512, (tc + 1) * 512)
            st = p.stage(f"lnm{fc}_{tc}")

            def mm(t, tsl=tsl):
                t.matmul(PSN1[:, :], onesb[:, 0:1], xc16[:, tsl],
                         start=True, stop=True)
                return t.matmul(PSN2[:, :], onesb[:, 0:1], sq16[:, tsl],
                                start=True, stop=True)
            st.t(mm)
            st = p.stage(f"lne{fc}_{tc}")

            def ev(v, tsl=tsl):
                v.tensor_add(out=sums[:, tsl], in0=sums[:, tsl], in1=PSN1[:, :])
                return v.tensor_add(out=sqs[:, tsl], in0=sqs[:, tsl], in1=PSN2[:, :])
            st.v(ev)

    st = p.stage("lnfin")

    def fin(v):
        v.tensor_scalar_mul(out=mean, in0=sums, scalar1=1.0 / 1024)
        v.tensor_scalar_mul(out=sqs, in0=sqs, scalar1=1.0 / 1024)
        v.tensor_mul(out=rstd, in0=mean, in1=mean)
        return v.tensor_sub(out=sqs, in0=sqs, in1=rstd)
    st.v(fin)
    st = p.stage("lnsqrt")
    st.s(lambda s: s.activation(out=sqs, in_=sqs, func=AF.Sqrt,
                                bias=cv[0:1, 0:1]))
    st = p.stage("lnrec")
    st.v(lambda v: v.reciprocal(out=rstd, in_=sqs))

    for fc in range(8):
        st = p.stage(f"lal{fc}")
        st.dma(lambda y, fc=fc: y.dma_start(
            out=xc[:], in_=x_fm[fc * 128:(fc + 1) * 128, :]))
        for tc in range(4):
            tsl = slice(tc * 512, (tc + 1) * 512)
            st = p.stage(f"lab{fc}_{tc}")

            def bc(t, tsl=tsl):
                t.matmul(PSM1[:, :], ones128[0:1, :], mean[:, tsl],
                         start=True, stop=True)
                return t.matmul(PSM2[:, :], ones128[32:33, :], rstd[:, tsl],
                                start=True, stop=True)
            st.t(bc)
            st = p.stage(f"laa{fc}_{tc}")

            def ap_(v, fc=fc, tsl=tsl):
                v.tensor_sub(out=xc[:, tsl], in0=xc[:, tsl], in1=PSM1[:, :])
                return v.tensor_mul(
                    out=xn16[:, fc * T + tsl.start:fc * T + tsl.stop],
                    in0=xc[:, tsl], in1=PSM2[:, :])
            st.v(ap_)

    for h in range(H):
        hs = h * 64
        for tc in range(4):
            tsl = slice(tc * 512, (tc + 1) * 512)
            st = p.stage(f"qk{h}_{tc}")

            def mmqk(t, h=h, tsl=tsl):
                for fc in range(8):
                    xs = xn16[:, fc * T + tsl.start:fc * T + tsl.stop]
                    t.matmul(PS1[0:64, :], wq16(fc)[:, h * 64:h * 64 + 64], xs,
                             start=(fc == 0), stop=(fc == 7))
                last = None
                for fc in range(8):
                    xs = xn16[:, fc * T + tsl.start:fc * T + tsl.stop]
                    last = t.matmul(PS2[0:64, :], wk16(fc)[:, h * 64:h * 64 + 64],
                                    xs, start=(fc == 0), stop=(fc == 7))
                return last
            st.t(mmqk)
            st = p.stage(f"qke{h}_{tc}")

            def epi(s, tsl=tsl):
                s.activation(out=sqq[:, :], in_=PS1[0:64, :], func=AF.Square)
                s.activation(out=qa[0:64, tsl], in_=PS1[0:64, :],
                             func=AF.Identity, scale=-2.0)
                return s.activation(out=ka[0:64, tsl], in_=PS2[0:64, :],
                                    func=AF.Identity)
            st.s(epi)
            st = p.stage(f"qn{h}_{tc}")
            st.t(lambda t: t.matmul(PSN1[:, :], onesb[0:64, 0:1], sqq[:, :],
                                    start=True, stop=True))
            st = p.stage(f"qne{h}_{tc}")
            st.v(lambda v, tsl=tsl: v.tensor_copy(out=qa[64:65, tsl],
                                                  in_=PSN1[:, :]))
            st = p.stage(f"ksq{h}_{tc}")
            st.s(lambda s, tsl=tsl: s.activation(out=sqq[:, :],
                                                 in_=PS2[0:64, :],
                                                 func=AF.Square))
            st = p.stage(f"kn{h}_{tc}")
            st.t(lambda t: t.matmul(PSN2[:, :], onesb[0:64, 0:1], sqq[:, :],
                                    start=True, stop=True))
            st = p.stage(f"kne{h}_{tc}")

            def kne(v, tsl=tsl):
                v.tensor_copy(out=knrow[:, tsl], in_=PSN2[:, :])
                return v.tensor_copy(out=knrow16[0:1, tsl], in_=PSN2[:, :])
            st.v(kne)
        st = p.stage(f"hst{h}")

        def hstore(y, h=h):
            return [
                y.dma_start(out=o_qaug[h], in_=qa[:]),
                y.dma_start(out=o_kaug[h, 0:65, :], in_=ka[0:65, :]),
                y.dma_start(out=o_kaug[h, 65:66, :], in_=knrow16[:]),
                y.dma_start(out=o_kn32[h:h + 1, :], in_=knrow),
            ]
        st.dma(hstore, n=4)


    st = p.stage("vtones")
    st.v(lambda v: [v.memset(vtbuf[:], 0.0)][-1])
    st = p.stage("vtones2")

    def vto(v):
        last = None
        for hh in range(16):
            last = v.memset(vtbuf[:, hh * 65 + 64:hh * 65 + 65], 1.0)
        return last
    st.v(vto)
    for tt in range(16):
        for vc in range(2):
            st = p.stage(f"vt{tt}_{vc}")

            def vmm(t, tt=tt, vc=vc):
                last = None
                for fc in range(8):
                    last = t.matmul(
                        PS1[:, :],
                        xn16[:, fc * T + tt * 128:fc * T + (tt + 1) * 128],
                        wv16(fc)[:, vc * 512:(vc + 1) * 512],
                        start=(fc == 0), stop=(fc == 7))
                return last
            st.t(vmm)
            st = p.stage(f"vte{tt}_{vc}")

            def vev(v, vc=vc):
                last = None
                for hh in range(8):
                    h = vc * 8 + hh
                    last = v.tensor_copy(
                        out=vtbuf[:, h * 65:h * 65 + 64],
                        in_=PS1[:, hh * 64:(hh + 1) * 64])
                return last
            st.v(vev)
        st = p.stage(f"vts{tt}")
        st.dma(lambda y, tt=tt: y.dma_start(
            out=o_vt[tt * 128:(tt + 1) * 128, :], in_=vtbuf[:]))

    for th in range(2):
        for oc in range(6):
            st = p.stage(f"fw{th}_{oc}")

            def fwload(g, oc=oc):
                out = []
                for fc in range(8):
                    out.append(g.dma_start(
                        out=wgc[:, fc * 512:(fc + 1) * 512],
                        in_=wg_t[fc * 128:(fc + 1) * 128, oc * 512:(oc + 1) * 512]))
                    out.append(g.dma_start(
                        out=wuc[:, fc * 512:(fc + 1) * 512],
                        in_=wu_t[fc * 128:(fc + 1) * 128, oc * 512:(oc + 1) * 512]))
                return out
            st.gdma(fwload, n=16)
            for pc in range(4):
                for tc in range(2):
                    tsl = slice(th * 1024 + tc * 512, th * 1024 + tc * 512 + 512)
                    csl = slice(tc * 512, tc * 512 + 512)
                    st = p.stage(f"fg{th}_{oc}_{pc}_{tc}")

                    def mmg(t, pc=pc, tsl=tsl):
                        for fc in range(8):
                            xs = xn16[:, fc * T + tsl.start:fc * T + tsl.stop]
                            t.matmul(PS1[:, :],
                                     wgc[:, fc * 512 + pc * 128:fc * 512 + pc * 128 + 128],
                                     xs, start=(fc == 0), stop=(fc == 7))
                        last = None
                        for fc in range(8):
                            xs = xn16[:, fc * T + tsl.start:fc * T + tsl.stop]
                            last = t.matmul(
                                PS2[:, :],
                                wuc[:, fc * 512 + pc * 128:fc * 512 + pc * 128 + 128],
                                xs, start=(fc == 0), stop=(fc == 7))
                        return last
                    st.t(mmg)
                    st = p.stage(f"fs{th}_{oc}_{pc}_{tc}")
                    st.s(lambda s: s.activation(out=sg[:], in_=PS1[:, :],
                                                func=AF.Silu, scale=1.0 / 127))
                    st = p.stage(f"fh{th}_{oc}_{pc}_{tc}")
                    st.v(lambda v, oc=oc, pc=pc, csl=csl: v.scalar_tensor_tensor(
                        out=hsl(oc * 4 + pc, csl), in0=sg[:], scalar=1.0 / 127,
                        in1=PS2[:, :], op0=ALU.mult, op1=ALU.mult))

        for op_ in range(8):
            st = p.stage(f"wdl{th}_{op_}")

            def wdload(g, op_=op_):
                return [g.dma_start(
                    out=wdbuf[:, hc * 128:(hc + 1) * 128],
                    in_=wd_t[hc * 128:(hc + 1) * 128, op_ * 128:(op_ + 1) * 128])
                    for hc in range(24)]
            st.gdma(wdload, n=24)
            for tc in range(2):
                tsl = slice(th * 1024 + tc * 512, th * 1024 + tc * 512 + 512)
                csl = slice(tc * 512, tc * 512 + 512)
                st = p.stage(f"gd{th}_{op_}_{tc}")

                def mmd(t, csl=csl):
                    last = None
                    for hc in range(24):
                        last = t.matmul(PS1[:, :],
                                        wdbuf[:, hc * 128:(hc + 1) * 128],
                                        hsl(hc, csl), start=(hc == 0),
                                        stop=(hc == 23))
                    return last
                st.t(mmd)
                st = p.stage(f"ge{th}_{op_}_{tc}")
                st.s(lambda s: s.activation(out=gout[:], in_=PS1[:, :],
                                            func=AF.Silu, scale=1.0 / 127))
                st = p.stage(f"gst{th}_{op_}_{tc}")
                st.dma(lambda y, op_=op_, tsl=tsl: y.dma_start(
                    out=o_g[op_ * 128:(op_ + 1) * 128, tsl], in_=gout[:]))



def _t(a):
    return np.ascontiguousarray(a.T)


def stage_a(inputs):
    x = inputs["L0_attn_x"]
    cvec = np.zeros((128, 4), np.float32)
    cvec[:, 0] = 1e-5
    shared = {
        "wq_t": _t(inputs["L0_attn_Wq"]), "wk_t": _t(inputs["L0_attn_Wk"]),
        "wv_t": _t(inputs["L0_attn_Wv"]),
        "wg_t": _t(inputs["L0_ffn_w_gate"]), "wu_t": _t(inputs["L0_ffn_w_up"]),
        "wd_t": _t(inputs["L0_ffn_w_down"]), "cvec": cvec,
    }
    maps = []
    for c in range(NC):
        b, r = c // 2, c % 2
        m = dict(shared)
        m["x_fm"] = _t(x[b, r * T:(r + 1) * T, :])
        maps.append(m)
    return maps


def kernel(**inputs):
    raise NotImplementedError("under construction")


def emit_b(p, D, exact=False, nheads=8, nqc=8, trunc=None):
    assert not exact, "exact variant is build_launch_bx"
    """L0 hyperbolic attention middle (per core: 8 heads, all 4096 tokens).

    In : qaug_g [8,66,S] bf16 rows [-2q; qn; ones]
         kaug_g [8,66,S] bf16 rows [k; ones; kn]
         v_g    [8,S,65] bf16 (token-major, col 64 = 1.0)
         knt    [8,128,32] f32 (kn tiled: [p, kt] = kn[kt*128+p])
         qnt    [8,128,32] f32 (same for qn; used by exact path)
         scal   [1,2] f32 = [log_c, log_beta]
         masks  [128,4*512] f32 additive causal masks (0 / +1e30),
                block mi at cols [mi*512,(mi+1)*512): mask for kk-tile offset mi*128
         cvecb  [128,8] f32: c0=1e-8 c1=1.0 c2=1.5576e-8 c3=thr(4.0001)
    Out: o16 [8,64,S] bf16 (normalized attention out, feature-major)
         flag [1,1] f32 (#kk-partitions with min z below threshold; fast only)

    Fast path: P = exp(-beta/2 * ln(z+eps) + (beta*c/4)(knmean - kn)), exact
    for the ed>2 branch; detection of any z < 4 via running min -> flag.
    Exact path: full 3-branch dist, same dropped per-q factor class (global
    shift only), overwrite o16.
    """
    i_q = D("qag", [2 * 1056, T], BF16, "ExternalInput")
    i_k = D("kag", [2 * 1056, T], BF16, "ExternalInput")
    i_v = D("vag", [2 * T, 16 * 65], BF16, "ExternalInput")
    i_kn = D("knag", [32, T], F32, "ExternalInput")
    i_hsel = D("hsel", [1, 1], I32, "ExternalInput")
    i_id32 = D("id32", [32, 32], F32, "ExternalInput")
    i_scal = D("scal", [1, 2], F32, "ExternalInput")
    i_masks = D("masks", [128, 4 * 512], F32, "ExternalInput")
    i_cv = D("cvecb", [128, 8], F32, "ExternalInput")
    o_o = D("o16", [8, 64, S], BF16, "ExternalOutput")
    o_flag = D("flag", [1, 1], F32, "ExternalOutput")

    cv = p.sbuf("cv", [128, 8], F32)
    ka = p.sbuf("ka", [66, S], BF16)
    qa = p.sbuf("qa", [66, S], BF16)
    vsb = p.sbuf("vsb", [128, 32 * 65], BF16)
    masks = p.sbuf("maskssb", [128, 4 * 512], F32)
    ptile = p.sbuf("ptile", [128, 4 * 512], BF16)
    lz = p.sbuf("lz", [128, 4 * 512], F32)
    knt = p.sbuf("kntsb", [128, 32], F32)
    knrows = p.sbuf("knrows", [32, 128], F32)
    hselb = p.sbuf("hselb", [1, 1], I32)
    id32b = p.sbuf("id32b", [32, 32], F32)
    bias = p.sbuf("biassb", [128, 32], F32)
    dacc = p.sbuf("dacc", [128, 1], F32)
    rmin = p.sbuf("rmin", [128, 1], F32)
    sc = p.sbuf("sc", [33, 8], F32)
    scb = p.sbuf("scb", [128, 4], F32)
    onesb = p.sbuf("onesb2", [128, 1], BF16)
    ones128 = p.sbuf("ones128b", [1, 128], F32)
    floor16 = p.sbuf("floor16", [1, 66], BF16)
    fval = p.sbuf("fval", [1, 512], BF16)
    rrow = p.sbuf("rrow", [1, 512], F32)
    otile = p.sbuf("otile", [64, 512], F32)
    obuf = p.sbuf("obuf", [64, 512], BF16)
    m128 = p.sbuf("m128", [128, 1], F32)
    m16 = p.sbuf("m16", [128, 1], BF16)
    flagrow = p.sbuf("flagrow", [1, 1], F32)

    PSZ = [p.psum(f"PSZ{i}", [128, 512], F32) for i in range(4)]
    PSO = p.psum("PSO", [65, 512], F32)
    PSB = p.psum("PSB", [64, 512], F32)
    PSN = p.psum("PSN", [1, 512], F32)

    st = p.stage("b_consts")

    def cload(y):
        return [y.dma_start(out=cv[:], in_=i_cv),
                y.dma_start(out=sc[0:1, 0:2], in_=i_scal),
                y.dma_start(out=masks[:], in_=i_masks),
                y.dma_start(out=hselb[:], in_=i_hsel),
                y.dma_start(out=id32b[:], in_=i_id32)]
    st.dma(cload, n=5)
    st = p.stage("b_init")

    def binit(v):
        v.memset(onesb[:], 1.0)
        v.memset(ones128[:], 1.0)
        v.memset(dacc[:], 1e30)
        v.memset(floor16[:], 0.0)
        v.memset(fval[:], 1e-30)
        return v.memset(flagrow[:], 0.0)
    st.v(binit)
    st = p.stage("b_init2")
    st.v(lambda v: v.memset(floor16[0:1, 64:65], 1.0))

    st = p.stage("b_sp1")
    st.s(lambda s: s.activation(out=sc[0:1, 2:4], in_=sc[0:1, 0:2], func=AF.Abs))
    st = p.stage("b_sp2")
    st.s(lambda s: s.activation(out=sc[0:1, 2:4], in_=sc[0:1, 2:4], func=AF.Exp,
                                scale=-1.0))
    st = p.stage("b_sp3")
    st.s(lambda s: s.activation(out=sc[0:1, 2:4], in_=sc[0:1, 2:4], func=AF.Ln,
                                bias=cv[0:1, 1:2]))
    st = p.stage("b_sp3b")
    st.s(lambda s: s.activation(out=sc[0:1, 4:6], in_=sc[0:1, 0:2],
                                func=AF.Relu))
    st = p.stage("b_sp4")
    st.v(lambda v: v.tensor_add(out=sc[0:1, 4:6], in0=sc[0:1, 4:6],
                                in1=sc[0:1, 2:4]))
    st = p.stage("b_sp5")
    st.v(lambda v: v.tensor_scalar_add(out=sc[0:1, 5:6], in0=sc[0:1, 5:6],
                                       scalar1=0.5))
    st = p.stage("b_sp6")
    st.v(lambda v: v.tensor_scalar_mul(out=sc[0:1, 6:7], in0=sc[0:1, 5:6],
                                       scalar1=-0.5))
    st = p.stage("b_sp7")
    st.v(lambda v: v.tensor_mul(out=sc[0:1, 7:8], in0=sc[0:1, 4:5],
                                in1=sc[0:1, 5:6]))
    st = p.stage("b_sp8")
    st.v(lambda v: v.tensor_scalar_mul(out=sc[0:1, 7:8], in0=sc[0:1, 7:8],
                                       scalar1=-0.25))
    st = p.stage("b_spb")

    def spb(t):
        t.matmul(PSZ[0][:, 0:1], ones128[0:1, :], sc[0:1, 6:7],
                 start=True, stop=True)
        return t.matmul(PSZ[0][:, 1:2], ones128[0:1, :], sc[0:1, 7:8],
                        start=True, stop=True)
    st.t(spb)
    st = p.stage("b_spb2")

    def spb2(v):
        v.tensor_copy(out=scb[:, 0:1], in_=PSZ[0][:, 0:1])
        return v.tensor_copy(out=scb[:, 1:2], in_=PSZ[0][:, 1:2])
    st.v(spb2)

    for h in range(nheads):
        st = p.stage(f"bh{h}_load")

        def hload(y, h=h):
            insts = []

            def emit_half(hg):
                gh = hg * 8 + h
                for rh in range(2):
                    insts.append(y.dma_start(
                        out=ka[:, rh * T:(rh + 1) * T],
                        in_=i_k[rh * 1056 + gh * 66:rh * 1056 + (gh + 1) * 66, :]))
                    insts.append(y.dma_start(
                        out=qa[:, rh * T:(rh + 1) * T],
                        in_=i_q[rh * 1056 + gh * 66:rh * 1056 + (gh + 1) * 66, :]))
                    insts.append(y.dma_start(
                        out=vsb[:].rearrange("p (kt d) -> p kt d", d=65)[
                            :, rh * 16:(rh + 1) * 16, :],
                        in_=i_v[rh * T:(rh + 1) * T, :].rearrange(
                            "(kt p) c -> p kt c", p=128)[
                            :, :, gh * 65:(gh + 1) * 65]))
                    insts.append(y.dma_start(
                        out=knrows[rh * 16:(rh + 1) * 16, :],
                        in_=i_kn[rh * 16 + gh:rh * 16 + gh + 1, :].rearrange(
                            "one (a b) -> (one a) b", b=128)))
            with y.register() as rg:
                y.reg_load(rg, hselb[0:1, 0:1])
                with y.If_ne(rg, 0):
                    emit_half(1)
                with y.Else():
                    emit_half(0)
                y.end_ifs()
            return insts
        st.dma(hload, n=8)
        st = p.stage(f"bh{h}_kt")
        st.t(lambda t: t.transpose(PSZ[0][:, 0:32], knrows[:, :], id32b[:, :]))
        st = p.stage(f"bh{h}_kte")
        st.v(lambda v: v.tensor_copy(out=knt[:], in_=PSZ[0][:, 0:32]))
        st = p.stage(f"bh{h}_b1")
        st.v(lambda v: v.reduce_sum(out=rmin[:], in_=knt[:], axis=AX))
        st = p.stage(f"bh{h}_b2")
        st.v(lambda v: v.tensor_copy(out=m16[:], in_=rmin[:]))
        st = p.stage(f"bh{h}_b3")
        st.t(lambda t: t.matmul(PSN[0:1, 0:1], onesb[:, 0:1], m16[:, 0:1],
                                start=True, stop=True))
        st = p.stage(f"bh{h}_b4")
        st.v(lambda v: v.tensor_scalar_mul(out=sc[0:1, 1:2], in0=PSN[0:1, 0:1],
                                           scalar1=1.0 / 4096))
        st = p.stage(f"bh{h}_b5")
        st.t(lambda t: t.matmul(PSZ[0][:, 0:1], ones128[0:1, :], sc[0:1, 1:2],
                                start=True, stop=True))
        st = p.stage(f"bh{h}_b5b")
        st.v(lambda v: v.tensor_copy(out=m128[:, 0:1], in_=PSZ[0][:, 0:1]))
        st = p.stage(f"bh{h}_b6")
        st.v(lambda v: v.tensor_scalar(out=bias[:], in0=knt[:],
                                       scalar1=m128[:, 0:1],
                                       scalar2=scb[:, 1:2], op0=ALU.subtract,
                                       op1=ALU.mult))

        for qc in range(nqc):
            qsl = slice(qc * 512, (qc + 1) * 512)
            ngr = qc + 1
            def zmm(t, kts, qsl=qsl):
                out = None
                for j, kt in enumerate(kts):
                    ksl = slice(kt * 128, (kt + 1) * 128)
                    out = t.matmul(PSZ[j][:, :], ka[0:66, ksl],
                                   qa[0:66, qsl], start=True, stop=True)
                    if exact:
                        out = t.matmul(PSZ[j][:, :],
                                       ka[64:66, ksl], qa[64:66, qsl],
                                       start=False, stop=True)
                return out

            st = p.stage(f"bh{h}_z{qc}")
            st.t(lambda t, zmm=zmm: zmm(t, [0, 1, 2, 3]))
            for gidx in range(ngr):
                kts = [gidx * 4 + j for j in range(4)]
                st = p.stage(f"bh{h}_e{qc}_{gidx}")

                def ew(v, h=h, qc=qc, kts=kts):
                    last = None
                    for j, kt in enumerate(kts):
                        v.tensor_reduce(out=rmin[:], in_=PSZ[j][:, :], axis=AX,
                                        op=ALU.min)
                        last = v.tensor_tensor(out=dacc[:], in0=dacc[:],
                                               in1=rmin[:], op=ALU.min)
                    return last
                st.v(ew)
                st = p.stage(f"bh{h}_ea{qc}_{gidx}")

                def ew_act(s, qc=qc, kts=kts):
                    last = None
                    for j, kt in enumerate(kts):
                        psl = slice(j * 512, (j + 1) * 512)
                        last = s.activation(out=lz[:, psl], in_=PSZ[j][:, :],
                                            func=AF.Ln, bias=cv[:, 0:1])
                    return last
                st.s(ew_act)
                if any(0 <= kt * 128 - qc * 512 < 512 for kt in kts):
                    st = p.stage(f"bh{h}_m{qc}_{gidx}")

                    def ewm(v, qc=qc, kts=kts):
                        last = None
                        for j, kt in enumerate(kts):
                            d = kt * 128 - qc * 512
                            if 0 <= d < 512:
                                mi = d // 128
                                psl = slice(j * 512, (j + 1) * 512)
                                last = v.tensor_add(
                                    out=lz[:, psl], in0=lz[:, psl],
                                    in1=masks[:, mi * 512:(mi + 1) * 512])
                        return last
                    st.v(ewm)
                st = p.stage(f"bh{h}_x{qc}_{gidx}")

                def ewx(s, kts=kts):
                    last = None
                    for j, kt in enumerate(kts):
                        psl = slice(j * 512, (j + 1) * 512)
                        last = s.activation(out=ptile[:, psl], in_=lz[:, psl],
                                            func=AF.Exp, scale=scb[:, 0:1],
                                            bias=bias[:, kt:kt + 1])
                    return last
                st.s(ewx)
                st = p.stage(f"bh{h}_pv{qc}_{gidx}")

                def pv(t, qc=qc, gidx=gidx, kts=kts, zmm=zmm):
                    last = None
                    for j, kt in enumerate(kts):
                        psl = slice(j * 512, (j + 1) * 512)
                        last = t.matmul(PSO[:, :],
                                        vsb[:, kt * 65:(kt + 1) * 65],
                                        ptile[:, psl],
                                        start=(gidx == 0 and j == 0), stop=False)
                    if gidx == qc:
                        last = t.matmul(PSO[:, :], floor16[0:1, 0:65],
                                        fval[0:1, :], start=False, stop=True)
                    else:
                        last = zmm(t, [(gidx + 1) * 4 + j for j in range(4)])
                    return last
                st.t(pv)
            st = p.stage(f"bh{h}_n{qc}")
            st.v(lambda v: v.reciprocal(out=rrow[:], in_=PSO[64:65, :]))
            st = p.stage(f"bh{h}_ne{qc}")
            st.s(lambda s: s.activation(out=otile[:], in_=PSO[0:64, :],
                                        func=AF.Identity))
            st = p.stage(f"bh{h}_nb{qc}")
            st.t(lambda t: t.matmul(PSB[:, :], ones128[0:1, 0:64], rrow[0:1, :],
                                    start=True, stop=True))
            st = p.stage(f"bh{h}_nm{qc}")
            st.v(lambda v: v.tensor_mul(out=obuf[:], in0=otile[:], in1=PSB[:, :]))
            st = p.stage(f"bh{h}_st{qc}")
            st.dma(lambda y, h=h, qsl=qsl: y.dma_start(out=o_o[h][:, qsl],
                                                       in_=obuf[:]))

    st = p.stage("b_flag1")
    st.v(lambda v: v.tensor_scalar(out=m128[:], in0=dacc[:],
                                   scalar1=cv[:, 3:4], scalar2=0.0,
                                   op0=ALU.is_lt, op1=ALU.add))
    st = p.stage("b_flag2")
    st.v(lambda v: v.tensor_copy(out=m16[:], in_=m128[:]))
    st = p.stage("b_flag3")
    st.t(lambda t: t.matmul(PSN[0:1, 0:1], onesb[:, 0:1], m16[:, 0:1],
                            start=True, stop=True))
    st = p.stage("b_flag4")
    st.v(lambda v: v.tensor_copy(out=flagrow[:], in_=PSN[0:1, 0:1]))
    st = p.stage("b_flag5")
    st.dma(lambda y: y.dma_start(out=o_flag, in_=flagrow[:]))
    if trunc is not None:
        p.stages = p.stages[:trunc]


def _causal_masks_b():
    out = np.zeros((128, 4 * 512), np.float32)
    for mi in range(4):
        kk = np.arange(128)[:, None] + mi * 128
        q = np.arange(512)[None, :]
        out[:, mi * 512:(mi + 1) * 512] = np.where(kk > q, 1e30, 0.0).astype(np.float32)
    return out


def stage_b(inputs, a_res):
    masks = _causal_masks_b()
    cvb = np.zeros((128, 8), np.float32)
    cvb[:, 0] = 1e-8
    cvb[:, 1] = 1.0
    cvb[:, 3] = 4.0001
    scal = np.array([[float(inputs["L0_attn_log_c"]),
                      float(inputs["L0_attn_log_beta"])]], np.float32)
    id32 = np.eye(32, dtype=np.float32)
    maps = []
    for c in range(NC):
        b, hg = c // 2, c % 2
        ra, rb = a_res[2 * b], a_res[2 * b + 1]
        qag = np.concatenate([ra["qaug"].reshape(1056, T),
                              rb["qaug"].reshape(1056, T)], axis=0)
        kag = np.concatenate([ra["kaug"].reshape(1056, T),
                              rb["kaug"].reshape(1056, T)], axis=0)
        vag = np.concatenate([ra["vtok"], rb["vtok"]], axis=0)
        knag = np.concatenate([ra["kn32"], rb["kn32"]], axis=0)
        maps.append({
            "qag": np.ascontiguousarray(qag),
            "kag": np.ascontiguousarray(kag),
            "vag": np.ascontiguousarray(vag),
            "knag": np.ascontiguousarray(knag),
            "hsel": np.array([[hg]], np.int32), "id32": id32,
            "scal": scal, "masks": masks, "cvecb": cvb,
        })
    return maps


def emit_c(p, D):
    x_fm = D("x_fm", [1024, T], F32, "ExternalInput")
    i_oag = D("oag", [2 * 512, S], BF16, "ExternalInput")
    i_tsel = D("tsel", [1, 1], I32, "ExternalInput")
    wo_t = D("wo_t", [1024, 1024], F32, "ExternalInput")
    g_in = D("g_in", [1024, T], F32, "ExternalInput")
    w_qkv = D("w_qkv", [1024, 3 * 1024], F32, "ExternalInput")
    cosrep = D("cosrep", [128, T], F32, "ExternalInput")
    sinsgn = D("sinsgn", [128, T], F32, "ExternalInput")
    cvec = D("cvec", [128, 4], F32, "ExternalInput")
    o_x1 = D("x1", [1024, T], F32, "ExternalOutput")
    o_xn1 = D("xn1f", [1024, T], F32, "ExternalOutput")
    o_xt = D("xn1tail", [1024, 258], F32, "ExternalOutput")
    o_q = D("q1r", [16, 64, T], BF16, "ExternalOutput")
    o_k = D("k1r", [16, 64, T], BF16, "ExternalOutput")
    o_vt = D("v1tok", [T, 16 * 65], BF16, "ExternalOutput")

    cv = p.sbuf("cv", [128, 4], F32)
    x1 = p.sbuf("x1b", [128, 8 * T], F32)
    xn16 = p.sbuf("xn16", [128, 8 * T], BF16)
    osb = p.sbuf("osb", [128, 8 * T], BF16)
    tselb = p.sbuf("tselb", [1, 1], I32)
    vtbuf = p.sbuf("vtbufc", [128, 16 * 65], BF16)
    wbuf = p.sbuf("wbuf", [128, 8 * 1024], BF16)
    gch = p.sbuf("gch", [128, T], F32)
    cosb = p.sbuf("cosb", [128, T], F32)
    sinb = p.sbuf("sinb", [128, T], F32)
    rowsA = p.sbuf("rowsA", [65, T], F32)
    rowsB = p.sbuf("rowsB", [33, T], F32)
    ones128 = p.sbuf("ones128", [33, 128], F32)
    onesb = p.sbuf("onesb", [128, 1], BF16)
    sq16 = p.sbuf("sq16", [128, T], BF16)
    xc16 = p.sbuf("xc16", [128, T], BF16)
    tmp = p.sbuf("tmpc", [128, 512], F32)
    tmq = p.sbuf("tmqc", [128, 512], F32)
    qtile = p.sbuf("qtile", [128, T], BF16)
    sums = rowsA[0:1, :]
    sqs = rowsA[32:33, :]
    mean = rowsB[0:1, :]
    rstd = rowsB[32:33, :]

    PS1 = p.psum("PS1", [128, 512], F32)
    PSN1 = p.psum("PSN1", [1, 512], F32)
    PSN2 = p.psum("PSN2", [1, 512], F32)
    PSM1 = p.psum("PSM1", [128, 512], F32)
    PSM2 = p.psum("PSM2", [128, 512], F32)

    st = p.stage("c_consts")

    def cl(y):
        return [y.dma_start(out=cv[:], in_=cvec),
                y.dma_start(out=cosb[:], in_=cosrep),
                y.dma_start(out=sinb[:], in_=sinsgn),
                y.dma_start(out=tselb[:], in_=i_tsel)]
    st.dma(cl, n=4)
    st = p.stage("c_init")

    def ci(v):
        v.memset(ones128[:], 1.0)
        v.memset(onesb[:], 1.0)
        v.memset(sums, 0.0)
        return v.memset(sqs, 0.0)
    st.v(ci)
    st = p.stage("c_wload")

    def cw(g):
        out = []
        for fc in range(8):
            out.append(g.dma_start(out=wbuf[:, fc * 1024:(fc + 1) * 1024],
                                   in_=wo_t[fc * 128:(fc + 1) * 128, :]))
        with g.register() as rg:
            g.reg_load(rg, tselb[0:1, 0:1])
            with g.If_ne(rg, 0):
                for fc in range(8):
                    out.append(g.dma_start(
                        out=osb[:, fc * T:(fc + 1) * T],
                        in_=i_oag[fc * 128:(fc + 1) * 128, T:2 * T]))
            with g.Else():
                for fc in range(8):
                    out.append(g.dma_start(
                        out=osb[:, fc * T:(fc + 1) * T],
                        in_=i_oag[fc * 128:(fc + 1) * 128, 0:T]))
            g.end_ifs()
        return out
    st.gdma(cw, n=16)

    for op_ in range(8):
        st = p.stage(f"cx{op_}")

        def cx(y, op_=op_):
            return [y.dma_start(out=gch[:], in_=g_in[op_ * 128:(op_ + 1) * 128, :]),
                    y.dma_start(out=x1[:, op_ * T:(op_ + 1) * T],
                                in_=x_fm[op_ * 128:(op_ + 1) * 128, :])]
        st.dma(cx, n=2)
        for tc in range(4):
            tsl = slice(tc * 512, (tc + 1) * 512)
            st = p.stage(f"cs{op_}_{tc}")

            def mms(t, op_=op_, tsl=tsl):
                last = None
                for fc in range(8):
                    last = t.matmul(
                        PS1[:, :],
                        wbuf[:, fc * 1024 + op_ * 128:fc * 1024 + op_ * 128 + 128],
                        osb[:, fc * T + tsl.start:fc * T + tsl.stop],
                        start=(fc == 0), stop=(fc == 7))
                return last
            st.t(mms)
            st = p.stage(f"ce{op_}_{tc}")

            def ep(v, op_=op_, tsl=tsl):
                xs = x1[:, op_ * T + tsl.start:op_ * T + tsl.stop]
                v.tensor_mul(out=tmp[:], in0=gch[:, tsl], in1=PS1[:, :])
                return v.tensor_add(out=xs, in0=xs, in1=tmp[:])
            st.v(ep)
        st = p.stage(f"cxs{op_}")
        st.dma(lambda y, op_=op_: y.dma_start(
            out=o_x1[op_ * 128:(op_ + 1) * 128, :],
            in_=x1[:, op_ * T:(op_ + 1) * T]))
        st = p.stage(f"cq{op_}")

        def csq(s, op_=op_):
            return s.activation(out=sq16[:], in_=x1[:, op_ * T:(op_ + 1) * T],
                                func=AF.Square)
        st.s(csq)
        st.v(lambda v, op_=op_: v.tensor_copy(
            out=xc16[:], in_=x1[:, op_ * T:(op_ + 1) * T]))
        for tc in range(4):
            tsl = slice(tc * 512, (tc + 1) * 512)
            st = p.stage(f"cm{op_}_{tc}")

            def mm(t, tsl=tsl):
                t.matmul(PSN1[:, :], onesb[:, 0:1], xc16[:, tsl],
                         start=True, stop=True)
                return t.matmul(PSN2[:, :], onesb[:, 0:1], sq16[:, tsl],
                                start=True, stop=True)
            st.t(mm)
            st = p.stage(f"cme{op_}_{tc}")

            def ev(v, tsl=tsl):
                v.tensor_add(out=sums[:, tsl], in0=sums[:, tsl], in1=PSN1[:, :])
                return v.tensor_add(out=sqs[:, tsl], in0=sqs[:, tsl],
                                    in1=PSN2[:, :])
            st.v(ev)

    st = p.stage("cfin")

    def fin(v):
        v.tensor_scalar_mul(out=mean, in0=sums, scalar1=1.0 / 1024)
        v.tensor_scalar_mul(out=sqs, in0=sqs, scalar1=1.0 / 1024)
        v.tensor_mul(out=rstd, in0=mean, in1=mean)
        return v.tensor_sub(out=sqs, in0=sqs, in1=rstd)
    st.v(fin)
    st = p.stage("csqrt")
    st.s(lambda s: s.activation(out=sqs, in_=sqs, func=AF.Sqrt,
                                bias=cv[0:1, 0:1]))
    st = p.stage("crec")
    st.v(lambda v: v.reciprocal(out=rstd, in_=sqs))

    for fc in range(8):
        for tc in range(4):
            tsl = slice(tc * 512, (tc + 1) * 512)
            st = p.stage(f"cb{fc}_{tc}")

            def bc(t, tsl=tsl):
                t.matmul(PSM1[:, :], ones128[0:1, :], mean[:, tsl],
                         start=True, stop=True)
                return t.matmul(PSM2[:, :], ones128[32:33, :], rstd[:, tsl],
                                start=True, stop=True)
            st.t(bc)
            st = p.stage(f"ca{fc}_{tc}")

            def ap_(v, fc=fc, tsl=tsl):
                xs = x1[:, fc * T + tsl.start:fc * T + tsl.stop]
                v.tensor_sub(out=tmp[:], in0=xs, in1=PSM1[:, :])
                v.tensor_mul(out=tmp[:], in0=tmp[:], in1=PSM2[:, :])
                return v.tensor_copy(
                    out=xn16[:, fc * T + tsl.start:fc * T + tsl.stop],
                    in_=tmp[:])
            st.v(ap_)
            st = p.stage(f"cao{fc}_{tc}")
            st.dma(lambda y, fc=fc, tsl=tsl: y.dma_start(
                out=o_xn1[fc * 128:(fc + 1) * 128, tsl], in_=tmp[:]))

    for fc in range(8):
        st = p.stage(f"ctl{fc}")
        st.dma(lambda y, fc=fc: y.dma_start(
            out=gch[:, 0:258], in_=o_xn1[fc * 128:(fc + 1) * 128, T - 258:T]))
        st = p.stage(f"cts{fc}")
        st.dma(lambda y, fc=fc: y.dma_start(
            out=o_xt[fc * 128:(fc + 1) * 128, :], in_=gch[:, 0:258]))

    for oc in range(16):
        part = oc // 8
        if oc % 8 == 0:
            st = p.stage(f"cwq{oc}")

            def cwq(g, oc=oc):
                return [g.dma_start(
                    out=wbuf[:, fc * 1024:(fc + 1) * 1024],
                    in_=w_qkv[fc * 128:(fc + 1) * 128,
                              (oc // 8) * 1024:(oc // 8 + 1) * 1024])
                    for fc in range(8)]
            st.gdma(cwq, n=8)
        for tc in range(4):
            tsl = slice(tc * 512, (tc + 1) * 512)
            st = p.stage(f"cp{oc}_{tc}")

            def mmp(t, oc=oc, tsl=tsl):
                last = None
                for fc in range(8):
                    last = t.matmul(
                        PS1[:, :],
                        wbuf[:, fc * 1024 + (oc % 8) * 128:
                             fc * 1024 + (oc % 8) * 128 + 128],
                        xn16[:, fc * T + tsl.start:fc * T + tsl.stop],
                        start=(fc == 0), stop=(fc == 7))
                return last
            st.t(mmp)
            if part < 2:
                st = p.stage(f"cr{oc}_{tc}")

                def rot(v, tsl=tsl):
                    v.tensor_copy(out=tmq[0:32, :], in_=PS1[32:64, :])
                    v.tensor_copy(out=tmq[32:64, :], in_=PS1[0:32, :])
                    v.tensor_copy(out=tmq[64:96, :], in_=PS1[96:128, :])
                    return v.tensor_copy(out=tmq[96:128, :], in_=PS1[64:96, :])
                st.v(rot)
                st = p.stage(f"cr2{oc}_{tc}")

                def rope(v, tsl=tsl):
                    v.tensor_mul(out=tmp[:], in0=cosb[:, tsl], in1=PS1[:, :])
                    v.tensor_mul(out=tmq[:, :], in0=tmq[:, :], in1=sinb[:, tsl])
                    return v.tensor_add(out=tmp[:], in0=tmp[:], in1=tmq[:, :])
                st.v(rope)
                st = p.stage(f"cr3{oc}_{tc}")
                st.v(lambda v, tsl=tsl: v.tensor_copy(out=qtile[:, tsl],
                                                      in_=tmp[:]))
        st = p.stage(f"cst{oc}")
        dst = (o_q, o_k)[part]
        h0 = (oc % 8) * 2

        def stq(y, dst=dst, h0=h0):
            return [y.dma_start(out=dst[h0], in_=qtile[0:64, :]),
                    y.dma_start(out=dst[h0 + 1], in_=qtile[64:128, :])]
        st.dma(stq, n=2)

    st = p.stage("cvw")
    st.gdma(lambda g: [g.dma_start(
        out=wbuf[:, fc * 1024:(fc + 1) * 1024],
        in_=w_qkv[fc * 128:(fc + 1) * 128, 2048:3072]) for fc in range(8)], n=8)
    st = p.stage("cvones")

    def cvo(v):
        v.memset(vtbuf[:], 0.0)
        last = None
        for hh in range(16):
            last = v.memset(vtbuf[:, hh * 65 + 64:hh * 65 + 65], 1.0)
        return last
    st.v(cvo)
    for tt in range(16):
        for vc in range(2):
            st = p.stage(f"cvt{tt}_{vc}")

            def cvmm(t, tt=tt, vc=vc):
                last = None
                for fc in range(8):
                    last = t.matmul(
                        PS1[:, :],
                        xn16[:, fc * T + tt * 128:fc * T + (tt + 1) * 128],
                        wbuf[:, fc * 1024 + vc * 512:fc * 1024 + (vc + 1) * 512],
                        start=(fc == 0), stop=(fc == 7))
                return last
            st.t(cvmm)
            st = p.stage(f"cve{tt}_{vc}")

            def cvev(v, vc=vc):
                last = None
                for hh in range(8):
                    h = vc * 8 + hh
                    last = v.tensor_copy(
                        out=vtbuf[:, h * 65:h * 65 + 64],
                        in_=PS1[:, hh * 64:(hh + 1) * 64])
                return last
            st.v(cvev)
        st = p.stage(f"cvs{tt}")
        st.dma(lambda y, tt=tt: y.dma_start(
            out=o_vt[tt * 128:(tt + 1) * 128, :], in_=vtbuf[:]))


def emit_d(p, D, nheads=8, nqc=8):
    i_q = D("qag1", [2 * 1024, T], BF16, "ExternalInput")
    i_k = D("kag1", [2 * 1024, T], BF16, "ExternalInput")
    i_v = D("vag1", [2 * T, 16 * 65], BF16, "ExternalInput")
    i_hsel = D("hseld", [1, 1], I32, "ExternalInput")
    i_masks = D("maskd", [128, 4 * 512], F32, "ExternalInput")
    o_o = D("o16", [8, 64, S], BF16, "ExternalOutput")

    ka = p.sbuf("ka", [64, S], BF16)
    qa = p.sbuf("qa", [64, S], BF16)
    vsb = p.sbuf("vsb", [128, 32 * 65], BF16)
    hselb = p.sbuf("hselbd", [1, 1], I32)
    masks = p.sbuf("maskssb", [128, 4 * 512], F32)
    ptile = p.sbuf("ptile", [128, 4 * 512], BF16)
    lzd = p.sbuf("lzd", [128, 512], F32)
    ones128 = p.sbuf("ones128", [1, 128], F32)
    floor16 = p.sbuf("floor16", [1, 66], BF16)
    fval = p.sbuf("fval", [1, 512], BF16)
    rrow = p.sbuf("rrow", [1, 512], F32)
    otile = p.sbuf("otile", [64, 512], F32)
    obuf = p.sbuf("obuf", [64, 512], BF16)

    PSZ = [p.psum(f"PSZ{i}", [128, 512], F32) for i in range(4)]
    PSO = p.psum("PSO", [65, 512], F32)
    PSB = p.psum("PSB", [64, 512], F32)

    st = p.stage("d_consts")
    st.dma(lambda y: [y.dma_start(out=masks[:], in_=i_masks),
                      y.dma_start(out=hselb[:], in_=i_hsel)], n=2)
    st = p.stage("d_init")

    def di(v):
        v.memset(ones128[:], 1.0)
        v.memset(floor16[:], 0.0)
        return v.memset(fval[:], 1e-30)
    st.v(di)
    st = p.stage("d_init2")
    st.v(lambda v: v.memset(floor16[0:1, 64:65], 1.0))

    for h in range(nheads):
        st = p.stage(f"dh{h}_load")

        def hload(y, h=h):
            insts = []

            def emit_half(hg):
                gh = hg * 8 + h
                for rh in range(2):
                    insts.append(y.dma_start(
                        out=ka[:, rh * T:(rh + 1) * T],
                        in_=i_k[rh * 1024 + gh * 64:rh * 1024 + (gh + 1) * 64, :]))
                    insts.append(y.dma_start(
                        out=qa[:, rh * T:(rh + 1) * T],
                        in_=i_q[rh * 1024 + gh * 64:rh * 1024 + (gh + 1) * 64, :]))
                    insts.append(y.dma_start(
                        out=vsb[:].rearrange("p (kt d) -> p kt d", d=65)[
                            :, rh * 16:(rh + 1) * 16, :],
                        in_=i_v[rh * T:(rh + 1) * T, :].rearrange(
                            "(kt p) c -> p kt c", p=128)[
                            :, :, gh * 65:(gh + 1) * 65]))
            with y.register() as rg:
                y.reg_load(rg, hselb[0:1, 0:1])
                with y.If_ne(rg, 0):
                    emit_half(1)
                with y.Else():
                    emit_half(0)
                y.end_ifs()
            return insts
        st.dma(hload, n=6)
        for qc in range(nqc):
            qsl = slice(qc * 512, (qc + 1) * 512)
            def zmm(t, kts, qsl=qsl):
                out = None
                for j, kt in enumerate(kts):
                    ksl = slice(kt * 128, (kt + 1) * 128)
                    out = t.matmul(PSZ[j][:, :], ka[0:64, ksl],
                                   qa[0:64, qsl], start=True, stop=True)
                return out

            st = p.stage(f"dh{h}_z{qc}")
            st.t(lambda t, zmm=zmm: zmm(t, [0, 1, 2, 3]))
            for gidx in range(qc + 1):
                kts = [gidx * 4 + j for j in range(4)]
                diag = [(j, kt) for j, kt in enumerate(kts)
                        if 0 <= kt * 128 - qc * 512 < 512]
                if diag:
                    for j, kt in diag:
                        st = p.stage(f"dh{h}_m{qc}_{gidx}_{j}")
                        mi = (kt * 128 - qc * 512) // 128

                        def ewm(v, j=j, mi=mi):
                            return v.tensor_add(
                                out=lzd[:], in0=PSZ[j][:, :],
                                in1=masks[:, mi * 512:(mi + 1) * 512])
                        st.v(ewm)
                        st = p.stage(f"dh{h}_mx{qc}_{gidx}_{j}")
                        st.s(lambda s, j=j: s.activation(
                            out=ptile[:, j * 512:(j + 1) * 512], in_=lzd[:],
                            func=AF.Exp, scale=0.125))
                ndiag = [(j, kt) for j, kt in enumerate(kts)
                         if not (0 <= kt * 128 - qc * 512 < 512)]
                if ndiag:
                    st = p.stage(f"dh{h}_x{qc}_{gidx}")

                    def ewx(s, ndiag=ndiag):
                        last = None
                        for j, kt in ndiag:
                            last = s.activation(
                                out=ptile[:, j * 512:(j + 1) * 512],
                                in_=PSZ[j][:, :], func=AF.Exp, scale=0.125)
                        return last
                    st.s(ewx)
                st = p.stage(f"dh{h}_pv{qc}_{gidx}")

                def pv(t, qc=qc, gidx=gidx, kts=kts, zmm=zmm):
                    last = None
                    for j, kt in enumerate(kts):
                        last = t.matmul(PSO[:, :],
                                        vsb[:, kt * 65:(kt + 1) * 65],
                                        ptile[:, j * 512:(j + 1) * 512],
                                        start=(gidx == 0 and j == 0),
                                        stop=False)
                    if gidx == qc:
                        last = t.matmul(PSO[:, :], floor16[0:1, 0:65],
                                        fval[0:1, :], start=False, stop=True)
                    else:
                        last = zmm(t, [(gidx + 1) * 4 + j for j in range(4)])
                    return last
                st.t(pv)
            st = p.stage(f"dh{h}_n{qc}")
            st.v(lambda v: v.reciprocal(out=rrow[:], in_=PSO[64:65, :]))
            st = p.stage(f"dh{h}_ne{qc}")
            st.s(lambda s: s.activation(out=otile[:], in_=PSO[0:64, :],
                                        func=AF.Identity))
            st = p.stage(f"dh{h}_nb{qc}")
            st.t(lambda t: t.matmul(PSB[:, :], ones128[0:1, 0:64], rrow[0:1, :],
                                    start=True, stop=True))
            st = p.stage(f"dh{h}_nm{qc}")
            st.v(lambda v: v.tensor_mul(out=obuf[:], in0=otile[:], in1=PSB[:, :]))
            st = p.stage(f"dh{h}_st{qc}")
            st.dma(lambda y, h=h, qsl=qsl: y.dma_start(out=o_o[h][:, qsl],
                                                       in_=obuf[:]))


def _rope_tables_c(r):
    inv_freq = 1.0 / (10000.0 ** (np.arange(0, 64, 2, dtype=np.float32) / 64))
    t = np.arange(r * T, (r + 1) * T, dtype=np.float32)
    fr = t[:, None] * inv_freq[None, :]
    cos = np.cos(fr).T.astype(np.float32)
    sin = np.sin(fr).T.astype(np.float32)
    cosrep = np.tile(cos, (4, 1))
    sinsgn = np.concatenate([-sin, sin, -sin, sin], axis=0)
    return np.ascontiguousarray(cosrep), np.ascontiguousarray(sinsgn)


def stage_c(inputs, a_res, b_res):
    cvec = np.zeros((128, 4), np.float32)
    cvec[:, 0] = 1e-5
    wo_t = _t(inputs["L0_attn_Wo"])
    w_qkv = np.ascontiguousarray(inputs["L1_attn_w_qkv"])
    maps = []
    for c in range(NC):
        b, r = c // 2, c % 2
        tsl = slice(r * T, (r + 1) * T)
        oag = np.concatenate([b_res[2 * b]["o16"].reshape(512, S),
                              b_res[2 * b + 1]["o16"].reshape(512, S)], axis=0)
        cosrep, sinsgn = _rope_tables_c(r)
        maps.append({
            "x_fm": _t(inputs["L0_attn_x"][b, tsl, :]),
            "oag": np.ascontiguousarray(oag), "tsel": np.array([[r]], np.int32),
            "wo_t": wo_t, "g_in": a_res[c]["g"],
            "w_qkv": w_qkv, "cosrep": cosrep, "sinsgn": sinsgn, "cvec": cvec,
        })
    return maps


def _causal_masks_d():
    out = np.zeros((128, 4 * 512), np.float32)
    for mi in range(4):
        kk = np.arange(128)[:, None] + mi * 128
        q = np.arange(512)[None, :]
        out[:, mi * 512:(mi + 1) * 512] = np.where(kk > q, -1e30, 0.0)
    return out


def stage_d(c_res):
    maskd = _causal_masks_d()
    maps = []
    for c in range(NC):
        b, hg = c // 2, c % 2
        ra, rb = c_res[2 * b], c_res[2 * b + 1]
        qag1 = np.concatenate([ra["q1r"].reshape(1024, T),
                               rb["q1r"].reshape(1024, T)], axis=0)
        kag1 = np.concatenate([ra["k1r"].reshape(1024, T),
                               rb["k1r"].reshape(1024, T)], axis=0)
        vag1 = np.concatenate([ra["v1tok"], rb["v1tok"]], axis=0)
        maps.append({"qag1": np.ascontiguousarray(qag1),
                     "kag1": np.ascontiguousarray(kag1),
                     "vag1": np.ascontiguousarray(vag1),
                     "hseld": np.array([[hg]], np.int32), "maskd": maskd})
    return maps


EXT = 258 + T
NW = T // 256


def emit_e(p, D):
    i_yo = D("y_own", [1024, T], F32, "ExternalInput")
    i_yh = D("y_halo", [2 * 1024, 258], F32, "ExternalInput")
    i_wqkv = D("wqkv", [1024, 3072], F32, "ExternalInput")
    i_wo = D("wo", [1024, 1024], F32, "ExternalInput")
    i_w1a = D("w1a", [1024, 2730], F32, "ExternalInput")
    i_w1g = D("w1g", [1024, 2730], F32, "ExternalInput")
    i_w2 = D("w2", [2730, 1024], F32, "ExternalInput")
    i_b1a = D("b1a", [128, 22], F32, "ExternalInput")
    i_b1g = D("b1g", [128, 22], F32, "ExternalInput")
    i_b2 = D("b2c", [128, 8], F32, "ExternalInput")
    i_lng = D("lng", [128, 8], F32, "ExternalInput")
    i_lnb = D("lnb", [128, 8], F32, "ExternalInput")
    i_qsA = D("qsA", [64, EXT], F32, "ExternalInput")
    i_qsB = D("qsB", [64, EXT], F32, "ExternalInput")
    i_ksA = D("ksA", [64, EXT], F32, "ExternalInput")
    i_ksB = D("ksB", [64, EXT], F32, "ExternalInput")
    i_qsc = D("qscol", [64, 2], F32, "ExternalInput")
    i_ksc = D("kscol", [64, 2], F32, "ExternalInput")
    i_mw = D("maskw", [128, 2 * 1024], F32, "ExternalInput")
    i_zg = D("zgate", [1, 1], F32, "ExternalInput")
    i_cv = D("cvec", [128, 4], F32, "ExternalInput")
    o_y = D("ynew", [1024, T], F32, "ExternalOutput")
    o_yt = D("ytail", [1024, 258], F32, "ExternalOutput")

    ysb = p.sbuf("ysb", [128, 8 * EXT], F32)
    xs16 = p.sbuf("xs16", [128, 8 * EXT], BF16)
    oacc = p.sbuf("oaccE", [128, 4 * T], BF16)
    vsb = p.sbuf("vsbE", [128, 8 * 21 * 65], BF16)
    qk16 = p.sbuf("qk16", [128, 2 * EXT], BF16)
    wbuf = p.sbuf("wbufE", [128, 8 * 512], BF16)
    tabs = p.sbuf("tabsE", [64, 4 * EXT], BF16)
    masks = p.sbuf("masksE", [128, 2 * 1024], BF16)
    qhat = p.sbuf("qhatE", [128, 1024], F32)
    tmr = p.sbuf("tmrE", [64, 512], F32)
    tmp = p.sbuf("tmpE", [128, 512], F32)
    ptile = p.sbuf("ptileE", [128, 4 * 256], BF16)
    rows = p.sbuf("rowsE", [65, 528], F32)
    rowsB = p.sbuf("rowsBE", [33, 512], F32)
    cv = p.sbuf("cvE", [128, 4], F32)
    epsT = p.sbuf("epsTE", [65, 512], BF16)
    rows2 = p.sbuf("rows2E", [33, 512], F32)
    ones128 = p.sbuf("ones128E", [65, 128], F32)
    onesb = p.sbuf("onesbE", [128, 1], BF16)
    floor16 = p.sbuf("floor16E", [1, 66], BF16)
    fval = p.sbuf("fvalE", [1, 256], BF16)
    otile = p.sbuf("otileE", [64, 256], F32)
    obuf = p.sbuf("obufE", [64, 256], BF16)
    qsc = p.sbuf("qscE", [64, 2], F32)
    ksc = p.sbuf("kscE", [64, 2], F32)
    zg = p.sbuf("zgE", [1, 1], F32)
    bcols = p.sbuf("bcolsE", [128, 68], F32)
    obnd16 = p.sbuf("obnd16", [128, 8], BF16)
    sq16 = p.sbuf("sq16E", [128, 512], BF16)
    xt16 = p.sbuf("xt16E", [128, 512], BF16)
    hh21 = p.sbuf("hh21E", [64, 512], BF16)

    PS1 = p.psum("PS1", [128, 512], F32)
    PS2 = p.psum("PS2", [128, 512], F32)
    PSM1 = p.psum("PSM1", [128, 512], F32)
    PSM2 = p.psum("PSM2", [128, 512], F32)
    PSN1 = p.psum("PSN1", [1, 512], F32)
    PSN2 = p.psum("PSN2", [1, 512], F32)
    PSB = p.psum("PSBE", [64, 512], F32)
    PSO = p.psum("PSOE", [65, 256], F32)
    PSOfull = p.psum("PSOE", [65, 512], F32)

    rrow = rows2[0:1, 0:256]
    b1a_c, b1g_c = bcols[:, 0:22], bcols[:, 22:44]
    b2_c, lng_c, lnb_c = bcols[:, 44:52], bcols[:, 52:60], bcols[:, 60:68]
    sums, sqs = rowsB[0:1, :], rowsB[32:33, :]
    meanr = rows2[0:1, :]
    msq = rows2[32:33, :]
    rstdr = rowsB[32:33, :]
    nrmr = rows[64:65, 0:512]

    PROJ_SLICES = [slice(1, 513), slice(513, 1025), slice(1025, 1537),
                   slice(1537, 2049), slice(2049, EXT)]
    VTILES = [(2 + 128 * k, 130 + 128 * k) for k in range(18)] + \
             [(1, 129), (129, 257), (257, 258)]

    def yv(fc, a, b):
        return ysb[:, fc * EXT + a:fc * EXT + b]

    def xv(fc, a, b, shifted=True):
        if shifted and fc >= 4:
            a, b = a - 1, b - 1
        return xs16[:, fc * EXT + a:fc * EXT + b]

    def vtile(hh, vt):
        base = (hh * 21 + vt) * 65
        return vsb[:, base:base + 65]

    def hhloc(i):
        if i < 16:
            return oacc[:, i * 512:(i + 1) * 512]
        if i < 21:
            return qk16[:, (i - 16) * 512:(i - 15) * 512]
        return hh21[:, :]

    def xn2loc(fc):
        return vsb[:, fc * 512:(fc + 1) * 512]

    st = p.stage("e_consts")

    def ecl(y):
        return [y.dma_start(out=cv[:], in_=i_cv),
                y.dma_start(out=qsc[:], in_=i_qsc),
                y.dma_start(out=ksc[:], in_=i_ksc),
                y.dma_start(out=zg[:], in_=i_zg),
                y.dma_start(out=bcols[:, 0:22], in_=i_b1a),
                y.dma_start(out=bcols[:, 22:44], in_=i_b1g),
                y.dma_start(out=bcols[:, 44:52], in_=i_b2),
                y.dma_start(out=bcols[:, 52:60], in_=i_lng),
                y.dma_start(out=bcols[:, 60:68], in_=i_lnb)]
    st.dma(ecl, n=9)
    st.gdma(lambda g: g.dma_start(out=masks[:], in_=i_mw), n=1)
    st = p.stage("e_init")

    def ei(v):
        v.memset(ones128[:], 1.0)
        v.memset(onesb[:], 1.0)
        v.memset(floor16[:], 0.0)
        v.memset(fval[:], 1e-30)
        v.memset(epsT[0:1, :], 1e-12)
        v.memset(epsT[32:33, :], 1e-12)
        return v.memset(epsT[64:65, :], 1e-12)
    st.v(ei)
    st = p.stage("e_init2")
    st.v(lambda v: v.memset(floor16[0:1, 64:65], 1.0))
    st = p.stage("e_zg0")
    st.t(lambda t: t.matmul(PSM2[:, 0:1], ones128[0:1, :], zg[0:1, 0:1],
                            start=True, stop=True))
    st = p.stage("e_zg0c")
    st.v(lambda v: v.tensor_copy(out=qhat[:, 0:1], in_=PSM2[:, 0:1]))

    for ti, (src, sc_, col) in enumerate(
            ((i_qsA, 'q', 0), (i_qsB, 'q', 1), (i_ksA, 'k', 0), (i_ksB, 'k', 1))):
        st = p.stage(f"e_tl{ti}")
        st.dma(lambda y, src=src: y.dma_start(out=ysb[0:64, 0:EXT], in_=src))
        st = p.stage(f"e_tm{ti}")
        scap = qsc if sc_ == 'q' else ksc
        st.v(lambda v, ti=ti, scap=scap, col=col: v.tensor_scalar_mul(
            out=tabs[:, ti * EXT:(ti + 1) * EXT], in0=ysb[0:64, 0:EXT],
            scalar1=scap[:, col:col + 1]))

    for fc in range(8):
        st = p.stage(f"e_yl{fc}")

        def yl(y, fc=fc):
            return [y.dma_start(out=yv(fc, 0, 258),
                                in_=i_yh[fc * 128:(fc + 1) * 128, :]),
                    y.dma_start(out=yv(fc, 258, EXT),
                                in_=i_yo[fc * 128:(fc + 1) * 128, :])]
        st.dma(yl, n=2)
        st = p.stage(f"e_yg{fc}")
        st.v(lambda v, fc=fc: v.tensor_scalar_mul(out=yv(fc, 0, 258),
                                                  in0=yv(fc, 0, 258),
                                                  scalar1=qhat[:, 0:1]))
        st = p.stage(f"e_yc{fc}")
        st.v(lambda v, fc=fc: v.tensor_copy(out=xs16[:, fc * EXT:(fc + 1) * EXT],
                                            in_=yv(fc, 0, EXT)))

    for hg in range(2):
        st = p.stage(f"e_vw{hg}")
        st.gdma(lambda g, hg=hg: g.dma_start(
            out=wbuf[:].rearrange("p (fc o) -> p fc o", o=512),
            in_=i_wqkv.rearrange("(fc p) o -> p fc o", p=128)[
                :, :, 2048 + hg * 512:2048 + (hg + 1) * 512]), n=1)
        for vt, (a, b) in enumerate(VTILES):
            n = b - a
            st = p.stage(f"e_vp{hg}_{vt}")

            def vmm(t, a=a, b=b, n=n):
                last = None
                for fc in range(8):
                    last = t.matmul(PS1[0:n, :], xv(fc, a, b),
                                    wbuf[:, fc * 512:(fc + 1) * 512],
                                    start=(fc == 0), stop=(fc == 7))
                return last
            st.t(vmm)
            st = p.stage(f"e_ve{hg}_{vt}")

            def vev(v, vt=vt, n=n):
                last = None
                for hh in range(8):
                    last = v.tensor_copy(out=vtile(hh, vt)[0:n, 0:64],
                                         in_=PS1[0:n, hh * 64:(hh + 1) * 64])
                return last
            st.v(vev)
        st = p.stage(f"e_vones{hg}")

        def von(v):
            last = None
            for hh in range(8):
                for vt in range(21):
                    last = v.memset(vtile(hh, vt)[:, 64:65], 1.0)
            return last
        st.v(von)

        for hp in range(4):
            st = p.stage(f"e_qw{hg}_{hp}")

            def qw(g, hg=hg, hp=hp):
                c0 = hg * 512 + hp * 128
                return [
                    g.dma_start(
                        out=wbuf[:].rearrange("p (fc o) -> p fc o", o=512)[
                            :, :, 0:128],
                        in_=i_wqkv.rearrange("(fc p) o -> p fc o", p=128)[
                            :, :, c0:c0 + 128]),
                    g.dma_start(
                        out=wbuf[:].rearrange("p (fc o) -> p fc o", o=512)[
                            :, :, 128:256],
                        in_=i_wqkv.rearrange("(fc p) o -> p fc o", p=128)[
                            :, :, 1024 + c0:1024 + c0 + 128]),
                ]
            st.gdma(qw, n=2)
            for ts_i, tsl in enumerate(PROJ_SLICES):
                n = tsl.stop - tsl.start
                st = p.stage(f"e_qp{hg}_{hp}_{ts_i}")

                def qmm(t, tsl=tsl, n=n):
                    for fc in range(8):
                        t.matmul(PS1[:, 0:n],
                                 wbuf[:, fc * 512:fc * 512 + 128],
                                 xv(fc, tsl.start, tsl.stop),
                                 start=(fc == 0), stop=(fc == 7))
                    last = None
                    for fc in range(8):
                        last = t.matmul(PS2[:, 0:n],
                                        wbuf[:, fc * 512 + 128:fc * 512 + 256],
                                        xv(fc, tsl.start, tsl.stop),
                                        start=(fc == 0), stop=(fc == 7))
                    return last
                st.t(qmm)
                sfx = f"{hg}_{hp}_{ts_i}"
                SLAB = [(PS1, 0, sq16, 0), (PS2, 0, sq16, 64),
                        (PS1, 64, xt16, 0), (PS2, 64, xt16, 64)]
                NR = [rows[64:65, 0:512], rows2[0:1, 0:512],
                      rows[0:1, 16:528], rows2[32:33, 0:512]]
                NEPS = [epsT[64:65, :], epsT[0:1, :], epsT[0:1, :],
                        epsT[32:33, :]]
                NONE_ = [ones128[64:65, 0:64], ones128[0:1, 0:64],
                         ones128[0:1, 0:64], ones128[32:33, 0:64]]
                st = p.stage(f"e_mn1{sfx}")

                def mn1(sct, n=n):
                    last = None
                    for (PS, rs, sb, ro) in SLAB:
                        last = sct.activation(out=sb[ro:ro + 64, 0:n],
                                              in_=PS[rs:rs + 64, 0:n],
                                              func=AF.Square)
                    return last
                st.s(mn1)
                st = p.stage(f"e_mn2{sfx}")
                NPS = [PSN1, PSN2, PSOfull, PSB]

                def mn2(t, n=n):
                    last = None
                    for ci, (PS, rs, sb, ro) in enumerate(SLAB):
                        last = t.matmul(NPS[ci][0:1, 0:n],
                                        onesb[ro:ro + 64, 0:1],
                                        sb[ro:ro + 64, 0:n],
                                        start=True, stop=True)
                    return last
                st.t(mn2)
                st = p.stage(f"e_mn3{sfx}")

                def mn3(sct, n=n):
                    last = None
                    for ci in range(4):
                        last = sct.activation(out=NR[ci][:, 0:n],
                                              in_=NPS[ci][0:1, 0:n],
                                              func=AF.Sqrt)
                    return last
                st.s(mn3)
                st = p.stage(f"e_mn4{sfx}")

                def mn4(v, n=n):
                    last = None
                    for ci in range(4):
                        v.tensor_max(out=NR[ci][:, 0:n], in0=NR[ci][:, 0:n],
                                     in1=NEPS[ci][:, 0:n])
                        last = v.reciprocal(out=NR[ci][:, 0:n],
                                            in_=NR[ci][:, 0:n])
                    return last
                st.v(mn4)
                for lh in range(2):
                    rs = lh * 64
                    st = p.stage(f"e_mn5{sfx}_{lh}")

                    def mn5(t, lh=lh, n=n):
                        t.matmul(PSB[0:64, 0:n], NONE_[2 * lh], NR[2 * lh][:, 0:n],
                                 start=True, stop=True)
                        return t.matmul(PSM1[0:64, 0:n], NONE_[2 * lh + 1],
                                        NR[2 * lh + 1][:, 0:n],
                                        start=True, stop=True)
                    st.t(mn5)
                    st = p.stage(f"e_mn6a{sfx}_{lh}")

                    def mn6a(sct, rs=rs, n=n):
                        sct.activation(out=qhat[0:64, 0:n],
                                       in_=PS1[rs:rs + 64, 0:n],
                                       func=AF.Identity)
                        return sct.activation(out=tmp[0:64, 0:n],
                                              in_=PS2[rs:rs + 64, 0:n],
                                              func=AF.Identity)
                    st.s(mn6a)
                    st = p.stage(f"e_mn6{sfx}_{lh}")

                    def mn6(v, n=n):
                        v.tensor_mul(out=qhat[0:64, 0:n], in0=qhat[0:64, 0:n],
                                     in1=PSB[0:64, 0:n])
                        return v.tensor_mul(out=tmp[0:64, 0:n],
                                            in0=tmp[0:64, 0:n],
                                            in1=PSM1[0:64, 0:n])
                    st.v(mn6)
                    st = p.stage(f"e_mr1{sfx}_{lh}")

                    def mr1(v, n=n):
                        v.tensor_copy(out=sq16[0:32, 0:n], in_=qhat[32:64, 0:n])
                        v.tensor_copy(out=sq16[32:64, 0:n], in_=qhat[0:32, 0:n])
                        v.tensor_copy(out=xt16[0:32, 0:n], in_=tmp[32:64, 0:n])
                        return v.tensor_copy(out=xt16[32:64, 0:n],
                                             in_=tmp[0:32, 0:n])
                    st.v(mr1)
                    st = p.stage(f"e_mr2{sfx}_{lh}")

                    def mr2(v, tsl=tsl, n=n):
                        a, b = tsl.start, tsl.stop
                        v.tensor_mul(out=qhat[0:64, 0:n], in0=qhat[0:64, 0:n],
                                     in1=tabs[:, a:b])
                        v.tensor_mul(out=sq16[0:64, 0:n], in0=sq16[0:64, 0:n],
                                     in1=tabs[:, EXT + a:EXT + b])
                        v.tensor_add(out=qhat[0:64, 0:n], in0=qhat[0:64, 0:n],
                                     in1=sq16[0:64, 0:n])
                        v.tensor_mul(out=tmp[0:64, 0:n], in0=tmp[0:64, 0:n],
                                     in1=tabs[:, 2 * EXT + a:2 * EXT + b])
                        v.tensor_mul(out=xt16[0:64, 0:n], in0=xt16[0:64, 0:n],
                                     in1=tabs[:, 3 * EXT + a:3 * EXT + b])
                        return v.tensor_add(out=tmp[0:64, 0:n],
                                            in0=tmp[0:64, 0:n],
                                            in1=xt16[0:64, 0:n])
                    st.v(mr2)
                    st = p.stage(f"e_mr3{sfx}_{lh}")

                    def mr3(v, lh=lh, tsl=tsl, n=n):
                        v.tensor_copy(
                            out=qk16[lh * 64:lh * 64 + 64,
                                     tsl.start:tsl.stop],
                            in_=qhat[0:64, 0:n])
                        return v.tensor_copy(
                            out=qk16[lh * 64:lh * 64 + 64,
                                     EXT + tsl.start:EXT + tsl.stop],
                            in_=tmp[0:64, 0:n])
                    st.v(mr3)
            for lh in range(2):
                hh = hp * 2 + lh
                qrow = slice(lh * 64, lh * 64 + 64)
                PSL = [PS1, PS2, PSM1, PSM2]
                def az_w(t, w, qrow):
                    qa_, qb_ = 258 + w * 256, 258 + (w + 1) * 256
                    ka_ = w * 256 + 2
                    out = None
                    for j in range(4):
                        ks = ka_ + j * 128
                        out = t.matmul(PSL[j][:, 0:256],
                                       qk16[qrow, EXT + ks:EXT + ks + 128],
                                       qk16[qrow, qa_:qb_],
                                       start=True, stop=True)
                    return out
                st = p.stage(f"e_az{hg}_{hp}_{lh}")
                st.t(lambda t, qrow=qrow: az_w(t, 0, qrow))
                def am_w(v, w, PSL=PSL):
                    mvar = 0 if w == 0 else 1024
                    last = None
                    for j in range(4):
                        last = v.scalar_tensor_tensor(
                            out=qhat[:, j * 256:(j + 1) * 256],
                            in0=PSL[j][:, 0:256], scalar=QK_SCALE,
                            in1=masks[:, mvar + j * 256:mvar + (j + 1) * 256],
                            op0=ALU.mult, op1=ALU.add)
                    return last

                def ax_w(s):
                    last = None
                    for j in range(4):
                        last = s.activation(
                            out=ptile[:, j * 256:(j + 1) * 256],
                            in_=qhat[:, j * 256:(j + 1) * 256], func=AF.Exp)
                    return last

                def apv_w(t, w, hh=hh):
                    for j in range(4):
                        t.matmul(PSO[:, :], vtile(hh, 2 * w + j),
                                 ptile[:, j * 256:(j + 1) * 256],
                                 start=(j == 0), stop=False)
                    return t.matmul(PSO[:, :], floor16[0:1, 0:65],
                                    fval[0:1, :], start=False, stop=True)

                def ane_w(s):
                    return s.activation(out=otile[:], in_=PSO[0:64, :],
                                        func=AF.Identity)

                def anb_w(t):
                    return t.matmul(PSB[:, 0:256], ones128[0:1, 0:64],
                                    rrow[0:1, :], start=True, stop=True)

                def anm_w(v, w, hh=hh):
                    v.tensor_mul(out=obuf[:], in0=otile[:], in1=PSB[:, 0:256])
                    oc_, ro = hh // 2, (hh % 2) * 64
                    return v.tensor_copy(
                        out=oacc[ro:ro + 64,
                                 oc_ * T + w * 256:oc_ * T + (w + 1) * 256],
                        in_=obuf[:])

                for w in range(NW):
                    sfx = f"{hg}_{hp}_{lh}_{w}"
                    st = p.stage(f"e_p0{sfx}")
                    st.v(lambda v, w=w, f=am_w: f(v, w))
                    if w > 0:
                        st.t(anb_w)
                    st = p.stage(f"e_p1{sfx}")
                    if w > 0:
                        st.s(lambda s, f1=ax_w, f2=ane_w: (f1(s), f2(s))[1])
                    else:
                        st.s(ax_w)
                    if w + 1 < NW:
                        st.t(lambda t, w=w, qrow=qrow: az_w(t, w + 1, qrow))
                    st = p.stage(f"e_p2{sfx}")
                    st.t(lambda t, w=w, f=apv_w: f(t, w))
                    if w > 0:
                        st.v(lambda v, w=w, f=anm_w: f(v, w - 1))
                    st = p.stage(f"e_p3{sfx}")
                    st.v(lambda v: v.reciprocal(out=rrow[:], in_=PSO[64:65, :]))
                st = p.stage(f"e_pe1{hg}_{hp}_{lh}")
                st.s(ane_w)
                st.t(anb_w)
                st = p.stage(f"e_pe2{hg}_{hp}_{lh}")
                st.v(lambda v, f=anm_w: f(v, NW - 1))
                sfx = f"b{hg}_{hp}_{lh}"
                st = p.stage(f"e_bz{sfx}")

                def bz(t, qrow=qrow):
                    t.matmul(PS1[:, 0:1], qk16[qrow, EXT + 1:EXT + 129],
                             qk16[qrow, 257:258], start=True, stop=True)
                    t.matmul(PS2[:, 0:1], qk16[qrow, EXT + 129:EXT + 257],
                             qk16[qrow, 257:258], start=True, stop=True)
                    return t.matmul(PSM1[0:1, 0:1],
                                    qk16[qrow, EXT + 257:EXT + 258],
                                    qk16[qrow, 257:258], start=True, stop=True)
                st.t(bz)
                st = p.stage(f"e_bm{sfx}")

                def bm(v):
                    v.tensor_scalar_mul(out=qhat[:, 0:1], in0=PS1[:, 0:1],
                                        scalar1=QK_SCALE)
                    v.tensor_scalar_mul(out=qhat[:, 1:2], in0=PS2[:, 0:1],
                                        scalar1=QK_SCALE)
                    return v.tensor_scalar_mul(out=qhat[0:1, 2:3],
                                               in0=PSM1[0:1, 0:1],
                                               scalar1=QK_SCALE)
                st.v(bm)
                st = p.stage(f"e_bx{sfx}")
                st.s(lambda s: s.activation(out=ptile[:, 0:3],
                                            in_=qhat[:, 0:3], func=AF.Exp))
                st = p.stage(f"e_bpv{sfx}")

                def bpv(t, hh=hh):
                    t.matmul(PSO[:, 0:1], vtile(hh, 18), ptile[:, 0:1],
                             start=True, stop=False)
                    t.matmul(PSO[:, 0:1], vtile(hh, 19), ptile[:, 1:2],
                             start=False, stop=False)
                    t.matmul(PSO[:, 0:1], vtile(hh, 20)[0:1, :],
                             ptile[0:1, 2:3], start=False, stop=False)
                    return t.matmul(PSO[:, 0:1], floor16[0:1, 0:65],
                                    fval[0:1, 0:1], start=False, stop=True)
                st.t(bpv)
                st = p.stage(f"e_bn{sfx}")
                st.v(lambda v: v.reciprocal(out=rrow[:, 0:1],
                                            in_=PSO[64:65, 0:1]))
                st = p.stage(f"e_bne{sfx}")
                st.s(lambda s: s.activation(out=otile[:, 0:1],
                                            in_=PSO[0:64, 0:1],
                                            func=AF.Identity))
                st = p.stage(f"e_bnb{sfx}")
                st.t(lambda t: t.matmul(PSB[:, 0:1], ones128[0:1, 0:64],
                                        rrow[0:1, 0:1], start=True, stop=True))
                st = p.stage(f"e_bnm{sfx}")

                def bnm(v, hg=hg, hh=hh):
                    v.tensor_mul(out=obuf[:, 0:1], in0=otile[:, 0:1],
                                 in1=PSB[:, 0:1])
                    return v.tensor_copy(out=rows[0:64, hg * 8 + hh:
                                                  hg * 8 + hh + 1],
                                         in_=obuf[:, 0:1])
                st.v(bnm)

        st = p.stage(f"e_obc{hg}")

        def obc(v, hg=hg):
            last = None
            for fc in range(4):
                h0 = hg * 8 + 2 * fc
                last = v.tensor_copy(out=obnd16[0:64, fc:fc + 1],
                                     in_=rows[0:64, h0:h0 + 1])
                last = v.tensor_copy(out=obnd16[64:128, fc:fc + 1],
                                     in_=rows[0:64, h0 + 1:h0 + 2])
            return last
        st.v(obc)
        for op_ in range(8):
            st = p.stage(f"e_ow{hg}_{op_}")
            st.gdma(lambda g, hg=hg, op_=op_: g.dma_start(
                out=wbuf[:].rearrange("p (fc o) -> p fc o", o=512)[:, 0:4, 0:128],
                in_=i_wo.rearrange("(fc p) o -> p fc o", p=128)[
                    :, hg * 4:(hg + 1) * 4, op_ * 128:(op_ + 1) * 128]), n=1)
            for tc in range(4):
                tsl = slice(tc * 512, (tc + 1) * 512)
                st = p.stage(f"e_op{hg}_{op_}_{tc}")

                def omm(t, tsl=tsl):
                    last = None
                    for fc in range(4):
                        last = t.matmul(
                            PS1[:, :], wbuf[:, fc * 512:fc * 512 + 128],
                            oacc[:, fc * T + tsl.start:fc * T + tsl.stop],
                            start=(fc == 0), stop=(fc == 3))
                    return last
                st.t(omm)
                st = p.stage(f"e_oe{hg}_{op_}_{tc}")

                def oe(v, op_=op_, tsl=tsl):
                    ys = yv(op_, 258 + tsl.start, 258 + tsl.stop)
                    return v.tensor_add(out=ys, in0=ys, in1=PS1[:, :])
                st.v(oe)
            st = p.stage(f"e_obm{hg}_{op_}")

            def obm(t):
                last = None
                for fc in range(4):
                    last = t.matmul(PS2[:, 0:1],
                                    wbuf[:, fc * 512:fc * 512 + 128],
                                    obnd16[:, fc:fc + 1], start=(fc == 0),
                                    stop=(fc == 3))
                return last
            st.t(obm)
            st = p.stage(f"e_obe{hg}_{op_}")

            def obe(v, op_=op_):
                ys = yv(op_, 257, 258)
                return v.tensor_add(out=ys, in0=ys, in1=PS2[:, 0:1])
            st.v(obe)

    st = p.stage("e_zgb")
    st.t(lambda t: t.matmul(PSM2[:, 0:1], ones128[0:1, :], zg[0:1, 0:1],
                            start=True, stop=True))
    st = p.stage("e_zgc")
    st.v(lambda v: v.tensor_copy(out=qhat[:, 0:1], in_=PSM2[:, 0:1]))
    for op_ in range(8):
        st = p.stage(f"e_zg{op_}")
        st.v(lambda v, op_=op_: v.tensor_mul(out=yv(op_, 257, 258),
                                             in0=yv(op_, 257, 258),
                                             in1=qhat[:, 0:1]))

    for tq in range(4):
        ha = 258 + tq * 512
        st = p.stage(f"e_f0{tq}")

        def f0(v):
            v.memset(sums, 0.0)
            return v.memset(sqs, 0.0)
        st.v(f0)
        for fc in range(8):
            sh = 1 if fc >= 4 else 0
            st = p.stage(f"e_fs{tq}_{fc}")
            st.s(lambda s, fc=fc, ha=ha, sh=sh: s.activation(
                out=sq16[:, :], in_=yv(fc, ha - sh, ha - sh + 512),
                func=AF.Square))
            st.v(lambda v, fc=fc, ha=ha, sh=sh: v.tensor_copy(
                out=xt16[:, :], in_=yv(fc, ha - sh, ha - sh + 512)))
            st = p.stage(f"e_fm{tq}_{fc}")

            def fmm(t):
                t.matmul(PSN1[:, :], onesb[:, 0:1], xt16[:, :],
                         start=True, stop=True)
                return t.matmul(PSN2[:, :], onesb[:, 0:1], sq16[:, :],
                                start=True, stop=True)
            st.t(fmm)
            st = p.stage(f"e_fe{tq}_{fc}")

            def fev(v):
                v.tensor_add(out=sums, in0=sums, in1=PSN1[:, :])
                return v.tensor_add(out=sqs, in0=sqs, in1=PSN2[:, :])
            st.v(fev)
        st = p.stage(f"e_ff{tq}")

        def ffin(v):
            v.tensor_scalar_mul(out=meanr, in0=sums, scalar1=1.0 / 1024)
            v.tensor_scalar_mul(out=sqs, in0=sqs, scalar1=1.0 / 1024)
            v.tensor_mul(out=msq, in0=meanr, in1=meanr)
            return v.tensor_sub(out=sqs, in0=sqs, in1=msq)
        st.v(ffin)
        st = p.stage(f"e_fq{tq}")
        st.s(lambda s: s.activation(out=sqs, in_=sqs, func=AF.Sqrt,
                                    bias=cv[32:33, 0:1]))
        st = p.stage(f"e_fr{tq}")
        st.v(lambda v: v.reciprocal(out=rstdr, in_=sqs))
        st = p.stage(f"e_fb{tq}")

        def fbc(t):
            t.matmul(PSM1[:, :], ones128[0:1, :], meanr[:, :],
                     start=True, stop=True)
            return t.matmul(PSM2[:, :], ones128[32:33, :], rstdr[:, :],
                            start=True, stop=True)
        st.t(fbc)
        for fc in range(8):
            sh = 1 if fc >= 4 else 0
            st = p.stage(f"e_fa{tq}_{fc}")

            def fap(v, fc=fc, ha=ha, sh=sh):
                ys = yv(fc, ha - sh, ha - sh + 512)
                v.tensor_sub(out=tmp[:], in0=ys, in1=PSM1[:, :])
                v.tensor_mul(out=tmp[:], in0=tmp[:], in1=PSM2[:, :])
                return v.tensor_scalar(
                    out=xn2loc(fc)[:, :], in0=tmp[:],
                    scalar1=lng_c[:, fc:fc + 1], scalar2=lnb_c[:, fc:fc + 1],
                    op0=ALU.mult, op1=ALU.add)
            st.v(fap)
        def w1load(g, i, n):
            po = (i % 2) * 256
            return [
                g.dma_start(
                    out=wbuf[:].rearrange("p (fc o) -> p fc o", o=512)[
                        :, :, po:po + n],
                    in_=i_w1a.rearrange("(fc p) o -> p fc o", p=128)[
                        :, :, i * 128:i * 128 + n]),
                g.dma_start(
                    out=wbuf[:].rearrange("p (fc o) -> p fc o", o=512)[
                        :, :, po + 128:po + 128 + n],
                    in_=i_w1g.rearrange("(fc p) o -> p fc o", p=128)[
                        :, :, i * 128:i * 128 + n]),
            ]
        st = p.stage(f"e_w1l{tq}")
        st.gdma(lambda g: w1load(g, 0, 128), n=2)
        for i in range(22):
            n = 128 if i < 21 else 42
            po = (i % 2) * 256
            st = p.stage(f"e_w1m{tq}_{i}")
            if i < 21:
                n2 = 128 if i + 1 < 21 else 42
                st.gdma(lambda g, i=i, n2=n2: w1load(g, i + 1, n2), n=2)

            def w1m(t, i=i, n=n, po=po):
                for fc in range(8):
                    t.matmul(PS1[0:n, :],
                             wbuf[:, fc * 512 + po:fc * 512 + po + n],
                             xn2loc(fc)[:, :], start=(fc == 0), stop=(fc == 7))
                last = None
                for fc in range(8):
                    last = t.matmul(
                        PS2[0:n, :],
                        wbuf[:, fc * 512 + po + 128:fc * 512 + po + 128 + n],
                        xn2loc(fc)[:, :], start=(fc == 0), stop=(fc == 7))
                return last
            st.t(w1m)
            st = p.stage(f"e_w1g{tq}_{i}")
            st.s(lambda s, i=i, n=n: s.activation(
                out=tmp[0:n, :], in_=PS2[0:n, :], func=AF.Gelu,
                bias=b1g_c[0:n, i:i + 1]))
            st = p.stage(f"e_w1h{tq}_{i}")

            def w1h(v, i=i, n=n):
                v.tensor_scalar(out=qhat[0:n, 0:512], in0=PS1[0:n, :],
                                scalar1=b1a_c[0:n, i:i + 1], scalar2=0.0,
                                op0=ALU.add, op1=ALU.add)
                return v.tensor_mul(out=hhloc(i)[0:n, 0:512],
                                    in0=qhat[0:n, 0:512], in1=tmp[0:n, :])
            st.v(w1h)
        for op_ in range(8):
            st = p.stage(f"e_w2l{tq}_{op_}")

            def w2l(g, op_=op_):
                return [
                    g.dma_start(
                        out=wbuf[:, 0:2688].rearrange("p (i o) -> p i o", o=128),
                        in_=i_w2[0:2688, :].rearrange("(i p) o -> p i o", p=128)[
                            :, :, op_ * 128:(op_ + 1) * 128]),
                    g.dma_start(
                        out=wbuf[0:42, 2688:2816],
                        in_=i_w2[2688:2730, op_ * 128:(op_ + 1) * 128]),
                ]
            st.gdma(w2l, n=2)
            st = p.stage(f"e_w2m{tq}_{op_}")

            def w2m(t):
                last = None
                for i in range(22):
                    n = 128 if i < 21 else 42
                    last = t.matmul(PS1[:, :], wbuf[0:n, i * 128:i * 128 + 128],
                                    hhloc(i)[0:n, 0:512], start=(i == 0),
                                    stop=(i == 21))
                return last
            st.t(w2m)
            st = p.stage(f"e_w2e{tq}_{op_}")

            def w2e(v, op_=op_, ha=ha):
                v.tensor_scalar(out=tmp[:], in0=PS1[:, :],
                                scalar1=b2_c[:, op_:op_ + 1], scalar2=0.0,
                                op0=ALU.add, op1=ALU.add)
                ys = yv(op_, ha, ha + 512)
                return v.tensor_add(out=ys, in0=ys, in1=tmp[:])
            st.v(w2e)
            st = p.stage(f"e_w2s{tq}_{op_}")
            st.dma(lambda y, op_=op_, tq=tq, ha=ha: y.dma_start(
                out=o_y[op_ * 128:(op_ + 1) * 128, tq * 512:(tq + 1) * 512],
                in_=yv(op_, ha, ha + 512)))
    for op_ in range(8):
        st = p.stage(f"e_tail{op_}")
        st.dma(lambda y, op_=op_: y.dma_start(
            out=o_yt[op_ * 128:(op_ + 1) * 128, :],
            in_=yv(op_, EXT - 258, EXT)))


def _xpos_ext_tables(r):
    half = np.arange(0, 64, 2, dtype=np.float64)
    inv_freq = 1.0 / 10000.0 ** (half / 64)
    t = np.arange(r * T - 258, r * T + T, dtype=np.float64)
    freqs = t[:, None] * inv_freq[None, :]
    freqs = np.concatenate([freqs, freqs], -1)
    base_scale = (half + 0.4 * 64) / (1.4 * 64)
    power = (t - S // 2) / (256 // 2)
    scale = base_scale[None, :] ** power[:, None]
    scale = np.concatenate([scale, scale], -1)
    cos, sin = np.cos(freqs), np.sin(freqs)
    sgn = np.concatenate([-np.ones(32), np.ones(32)])[None, :]
    qsA = np.ascontiguousarray((cos * scale).T.astype(np.float32))
    qsB = np.ascontiguousarray((sgn * sin * scale).T.astype(np.float32))
    ksA = np.ascontiguousarray((cos / scale).T.astype(np.float32))
    ksB = np.ascontiguousarray((sgn * sin / scale).T.astype(np.float32))
    return qsA, qsB, ksA, ksB


def _maskw(r):
    kk = np.arange(128)[:, None]
    q = np.arange(256)[None, :]
    out = np.zeros((128, 2048), np.float32)
    for mi in range(4):
        key_rel = mi * 128 + kk - 256
        ok = (key_rel <= q) & (key_rel >= q - 256)
        normal = np.where(ok, 0.0, -1e30).astype(np.float32)
        w0 = normal.copy()
        if r == 0:
            w0 = np.where(key_rel < 0, -1e30, w0).astype(np.float32)
        out[:, 1024 + mi * 256:1024 + (mi + 1) * 256] = normal
        out[:, mi * 256:(mi + 1) * 256] = w0
    return out


def _colpack(vec, ncol):
    out = np.zeros((128, ncol), np.float32)
    v = np.asarray(vec, np.float32)
    for j in range(ncol):
        seg = v[j * 128:(j + 1) * 128]
        out[:len(seg), j] = seg
    return out


def stage_e(inputs, li, y_fm_per_core):
    cvec = np.zeros((128, 4), np.float32)
    cvec[:, 0] = 1e-5
    W1 = inputs["L1_ffn_W1"][li]
    b1 = inputs["L1_ffn_b1"][li]
    shared = {
        "wqkv": np.ascontiguousarray(inputs["L1_ffn_Wqkv"][li]),
        "wo": np.ascontiguousarray(inputs["L1_ffn_Wo"][li]),
        "w1a": np.ascontiguousarray(W1[:, 0:2730]),
        "w1g": np.ascontiguousarray(W1[:, 2730:]),
        "w2": np.ascontiguousarray(inputs["L1_ffn_W2"][li]),
        "b1a": _colpack(b1[0:2730], 22), "b1g": _colpack(b1[2730:], 22),
        "b2c": _colpack(inputs["L1_ffn_b2"][li], 8),
        "lng": _colpack(inputs["L1_ffn_ln_g"][li], 8),
        "lnb": _colpack(inputs["L1_ffn_ln_b"][li], 8),
        "cvec": cvec,
    }
    qs = np.asarray(inputs["L1_ffn_q_scale"][li], np.float32)
    ks = np.asarray(inputs["L1_ffn_k_scale"][li], np.float32)
    shared["qscol"] = np.stack([qs, np.concatenate([qs[32:], qs[:32]])], axis=1)
    shared["kscol"] = np.stack([ks, np.concatenate([ks[32:], ks[:32]])], axis=1)
    maps = []
    for c in range(NC):
        b, r = c // 2, c % 2
        m = dict(shared)
        qsA, qsB, ksA, ksB = _xpos_ext_tables(r)
        m.update(qsA=qsA, qsB=qsB, ksA=ksA, ksB=ksB, maskw=_maskw(r),
                 zgate=np.array([[0.0 if r == 0 else 1.0]], np.float32))
        m["y_own"] = np.ascontiguousarray(y_fm_per_core[c])
        yh = np.zeros((2 * 1024, 258), np.float32)
        yh[0:1024] = y_fm_per_core[2 * b][:, T - 258:]
        yh[1024:] = y_fm_per_core[2 * b + 1][:, T - 258:]
        m["y_halo"] = yh
        maps.append(m)
    return maps


def emit_f(p, D):
    i_x1 = D("x1", [1024, T], F32, "ExternalInput")
    i_oag = D("oag1", [2 * 512, S], BF16, "ExternalInput")
    i_tsel = D("tself", [1, 1], I32, "ExternalInput")
    i_w = D("wout", [1024, 1024], F32, "ExternalInput")
    i_y6 = D("y6", [1024, T], F32, "ExternalInput")
    o_x = D("xout", [1024, T], F32, "ExternalOutput")

    osb = p.sbuf("osbF", [128, 8 * T], BF16)
    tselb = p.sbuf("tselbF", [1, 1], I32)
    wsb = p.sbuf("wsbF", [128, 8 * 1024], BF16)
    x1c = p.sbuf("x1cF", [128, T], F32)
    y6c = p.sbuf("y6cF", [128, T], F32)
    gg = p.sbuf("ggF", [128, 512], F32)
    tt = p.sbuf("ttF", [128, 512], F32)
    PS1 = p.psum("PS1", [128, 512], F32)

    st = p.stage("f_load")

    def fl(g):
        out = [g.dma_start(out=wsb[:].rearrange("p (fc o) -> p fc o", o=1024),
                           in_=i_w.rearrange("(fc p) o -> p fc o", p=128)),
               g.dma_start(out=tselb[:], in_=i_tsel)]
        with g.register() as rg:
            g.reg_load(rg, tselb[0:1, 0:1])
            with g.If_ne(rg, 0):
                for fc in range(8):
                    out.append(g.dma_start(
                        out=osb[:, fc * T:(fc + 1) * T],
                        in_=i_oag[fc * 128:(fc + 1) * 128, T:2 * T]))
            with g.Else():
                for fc in range(8):
                    out.append(g.dma_start(
                        out=osb[:, fc * T:(fc + 1) * T],
                        in_=i_oag[fc * 128:(fc + 1) * 128, 0:T]))
            g.end_ifs()
        return out
    st.gdma(fl, n=10)

    for op_ in range(8):
        st = p.stage(f"f_x{op_}")

        def fx(y, op_=op_):
            return [y.dma_start(out=x1c[:], in_=i_x1[op_ * 128:(op_ + 1) * 128, :]),
                    y.dma_start(out=y6c[:], in_=i_y6[op_ * 128:(op_ + 1) * 128, :])]
        st.dma(fx, n=2)
        for tc in range(4):
            tsl = slice(tc * 512, (tc + 1) * 512)
            st = p.stage(f"f_m{op_}_{tc}")

            def fm(t, op_=op_, tsl=tsl):
                last = None
                for fc in range(8):
                    last = t.matmul(
                        PS1[:, :],
                        wsb[:, fc * 1024 + op_ * 128:fc * 1024 + op_ * 128 + 128],
                        osb[:, fc * T + tsl.start:fc * T + tsl.stop],
                        start=(fc == 0), stop=(fc == 7))
                return last
            st.t(fm)
            st = p.stage(f"f_g{op_}_{tc}")
            st.s(lambda s, tsl=tsl: s.activation(out=gg[:], in_=y6c[:, tsl],
                                                 func=AF.Silu))
            st = p.stage(f"f_e{op_}_{tc}")

            def fe(v, tsl=tsl):
                v.tensor_mul(out=tt[:], in0=gg[:], in1=PS1[:, :])
                return v.tensor_add(out=tt[:], in0=tt[:], in1=x1c[:, tsl])
            st.v(fe)
            st = p.stage(f"f_s{op_}_{tc}")
            st.dma(lambda y, op_=op_, tsl=tsl: y.dma_start(
                out=o_x[op_ * 128:(op_ + 1) * 128, tsl], in_=tt[:]))


def stage_f(inputs, c_res, d_res, e_res):
    wout = np.ascontiguousarray(inputs["L1_attn_w_out"])
    maps = []
    for c in range(NC):
        b, r = c // 2, c % 2
        oag1 = np.concatenate([d_res[2 * b]["o16"].reshape(512, S),
                               d_res[2 * b + 1]["o16"].reshape(512, S)], axis=0)
        maps.append({
            "x1": c_res[c]["x1"], "oag1": np.ascontiguousarray(oag1),
            "tself": np.array([[r]], np.int32),
            "wout": wout, "y6": e_res[c]["ynew"],
        })
    return maps


_NC_CACHE = {}


def _get(name, builder):
    if name not in _NC_CACHE:
        _NC_CACHE[name] = builder()
    return _NC_CACHE[name]


def kernel_unfused(**inputs):
    inputs = {k: np.asarray(v) for k, v in inputs.items()}
    cores = list(range(NC))
    resA = run_bass_kernel_spmd(_get("A", build_launch_a), stage_a(inputs),
                                core_ids=cores)
    resB = run_bass_kernel_spmd(_get("B", build_launch_b),
                                stage_b(inputs, resA.results), core_ids=cores)
    if any(float(r["flag"][0, 0]) > 0 for r in resB.results):
        raise NotImplementedError(
            "L0 hyperbolic attention hit a non-asymptotic branch; exact "
            "fixup path required")
    resC = run_bass_kernel_spmd(_get("C", build_launch_c),
                                stage_c(inputs, resA.results, resB.results),
                                core_ids=cores)
    resD = run_bass_kernel_spmd(_get("D", build_launch_d),
                                stage_d(resC.results), core_ids=cores)
    y_fm = [resC.results[c]["xn1f"] for c in range(NC)]
    ncE = _get("E", build_launch_e)
    for li in range(6):
        resE = run_bass_kernel_spmd(ncE, stage_e(inputs, li, y_fm),
                                    core_ids=cores)
        y_fm = [resE.results[c]["ynew"] for c in range(NC)]
    resF = run_bass_kernel_spmd(_get("F", build_launch_f),
                                stage_f(inputs, resC.results, resD.results,
                                        {c: {"ynew": y_fm[c]} for c in range(NC)}),
                                core_ids=cores)
    out = np.zeros((B, S, D), np.float32)
    for c in range(NC):
        b, r = c // 2, c % 2
        out[b, r * T:(r + 1) * T, :] = resF.results[c]["xout"].T
    return out


def _mk_builder(emit, name, **kw):
    def build():
        p = Prog(name)
        p.begin_section("")

        def D(nm, shape, dt, kind):
            return p.nc.dram_tensor(nm, shape, dt, kind=kind).ap()
        emit(p, D, **kw)
        return p.finalize()
    return build


build_launch_a = _mk_builder(emit_a, "A")
build_launch_b = _mk_builder(emit_b, "B")
build_launch_c = _mk_builder(emit_c, "C")
build_launch_d = _mk_builder(emit_d, "D")
build_launch_e = _mk_builder(emit_e, "E")
build_launch_f = _mk_builder(emit_f, "F")


PAIRS = [[0, 1], [2, 3], [4, 5], [6, 7]]

_E_PER_LI = {"wqkv", "wo", "w1a", "w1g", "w2", "b1a", "b1g", "b2c", "lng",
             "lnb", "qscol", "kscol", "y_own", "y_halo", "ynew", "ytail"}


def build_fused():
    p = Prog("FU")
    nc = p.nc
    tensors = {}

    def getD(prefix, internal_names, rename=None):
        rename = rename or {}

        def D(nm, shape, dt, kind):
            key = rename.get(nm, prefix + nm)
            if key in tensors:
                return tensors[key]
            k = "Internal" if nm in internal_names else kind
            if k == "ExternalOutput" and nm not in internal_names:
                ap = nc.dram_tensor(key, shape, dt, kind="ExternalOutput").ap()
            else:
                ap = nc.dram_tensor(key, shape, dt, kind=k).ap()
            tensors[key] = ap
            return ap
        return D

    def ag(name, in_ap, out_ap):
        st = p.stage("ag_" + name)
        st.ops["gpsimd"].append(("cc", lambda g, i=in_ap, o=out_ap:
                                 g.collective_compute(
                                     "AllGather", ALU.bypass,
                                     replica_groups=PAIRS,
                                     ins=[i.opt()], outs=[o.opt()]), 1))

    def mk(key, shape, dt):
        ap = nc.dram_tensor(key, shape, dt, kind="Internal").ap()
        tensors[key] = ap
        return ap

    p.begin_section("A_")
    emit_a(p, getD("A_", {"qaug", "kaug", "vtok", "kn32", "g"}))
    qag = mk("qag", [2 * 1056, T], BF16)
    kag = mk("kag", [2 * 1056, T], BF16)
    vag = mk("vag", [2 * T, 16 * 65], BF16)
    knag = mk("knag", [32, T], F32)
    ag("qaug", tensors["A_qaug"].rearrange("h r t -> (h r) t"), qag)
    ag("kaug", tensors["A_kaug"].rearrange("h r t -> (h r) t"), kag)
    ag("vtok", tensors["A_vtok"], vag)
    ag("kn32", tensors["A_kn32"], knag)
    p.begin_section("B_")
    emit_b(p, getD("B_", {"o16"},
                   rename={"qag": "qag", "kag": "kag", "vag": "vag",
                           "knag": "knag"}))
    oag = mk("oag", [2 * 512, S], BF16)
    ag("o16", tensors["B_o16"].rearrange("h d s -> (h d) s"), oag)
    p.begin_section("C_")
    emit_c(p, getD("C_", {"x1", "xn1f", "xn1tail", "q1r", "k1r", "v1tok"},
                   rename={"oag": "oag", "g_in": "A_g"}))
    qag1 = mk("qag1", [2 * 1024, T], BF16)
    kag1 = mk("kag1", [2 * 1024, T], BF16)
    vag1 = mk("vag1", [2 * T, 16 * 65], BF16)
    yh0 = mk("yh0", [2 * 1024, 258], F32)
    ag("q1r", tensors["C_q1r"].rearrange("h d t -> (h d) t"), qag1)
    ag("k1r", tensors["C_k1r"].rearrange("h d t -> (h d) t"), kag1)
    ag("v1tok", tensors["C_v1tok"], vag1)
    ag("yh0", tensors["C_xn1tail"], yh0)
    p.begin_section("D_")
    emit_d(p, getD("D_", {"o16"},
                   rename={"qag1": "qag1", "kag1": "kag1", "vag1": "vag1"}))
    oag1 = mk("oag1", [2 * 512, S], BF16)
    ag("o16d", tensors["D_o16"].rearrange("h d s -> (h d) s"), oag1)
    yprev = tensors["C_xn1f"]
    yhalo = yh0
    for li in range(6):
        pre = f"E{li}_"

        def D_e(nm, shape, dt, kind, li=li, yprev=yprev, yhalo=yhalo):
            if nm == "y_own":
                return yprev
            if nm == "y_halo":
                return yhalo
            key = (f"E{li}_" + nm) if nm in _E_PER_LI else ("E_" + nm)
            if key in tensors:
                return tensors[key]
            k = "Internal" if nm in ("ynew", "ytail") else kind
            ap = nc.dram_tensor(key, shape, dt, kind=k).ap()
            tensors[key] = ap
            return ap
        p.begin_section(pre)
        emit_e(p, D_e)
        ynew = tensors[f"E{li}_ynew"]
        if li < 5:
            yh = mk(f"yh{li + 1}", [2 * 1024, 258], F32)
            ag(f"yh{li + 1}", tensors[f"E{li}_ytail"], yh)
            yhalo = yh
        yprev = ynew
    p.begin_section("F_")
    emit_f(p, getD("F_", set(),
                   rename={"oag1": "oag1", "x1": "C_x1", "y6": "E5_ynew"}))
    return p.finalize()


def stage_fused(inputs):
    cvec4 = np.zeros((128, 4), np.float32)
    cvec4[:, 0] = 1e-5
    cvb = np.zeros((128, 8), np.float32)
    cvb[:, 0] = 1e-8
    cvb[:, 1] = 1.0
    cvb[:, 3] = 4.0001
    scal = np.array([[float(inputs["L0_attn_log_c"]),
                      float(inputs["L0_attn_log_beta"])]], np.float32)
    id32 = np.eye(32, dtype=np.float32)
    masksb = _causal_masks_b()
    maskd = _causal_masks_d()
    wo_t = _t(inputs["L0_attn_Wo"])
    w_qkv = np.ascontiguousarray(inputs["L1_attn_w_qkv"])
    wout = np.ascontiguousarray(inputs["L1_attn_w_out"])
    shared_a = {
        "A_wq_t": _t(inputs["L0_attn_Wq"]), "A_wk_t": _t(inputs["L0_attn_Wk"]),
        "A_wv_t": _t(inputs["L0_attn_Wv"]),
        "A_wg_t": _t(inputs["L0_ffn_w_gate"]),
        "A_wu_t": _t(inputs["L0_ffn_w_up"]),
        "A_wd_t": _t(inputs["L0_ffn_w_down"]), "A_cvec": cvec4,
    }
    e_shared = {}
    e_perli = {}
    for li in range(6):
        m = stage_e(inputs, li, [np.zeros((1024, T), np.float32)] * NC)
        sm = m[0]
        for nm in ("wqkv", "wo", "w1a", "w1g", "w2", "b1a", "b1g", "b2c",
                   "lng", "lnb", "qscol", "kscol"):
            e_perli[f"E{li}_{nm}"] = sm[nm]
    maps = []
    for c in range(NC):
        b, r = c // 2, c % 2
        tsl = slice(r * T, (r + 1) * T)
        m = dict(shared_a)
        m["A_x_fm"] = _t(inputs["L0_attn_x"][b, tsl, :])
        m.update({"B_hsel": np.array([[r]], np.int32), "B_id32": id32,
                  "B_scal": scal, "B_masks": masksb, "B_cvecb": cvb})
        m.update({"C_x_fm": m["A_x_fm"], "C_tsel": np.array([[r]], np.int32),
                  "C_wo_t": wo_t, "C_w_qkv": w_qkv, "C_cvec": cvec4})
        cosrep, sinsgn = _rope_tables_c(r)
        m["C_cosrep"] = cosrep
        m["C_sinsgn"] = sinsgn
        m.update({"D_hseld": np.array([[r]], np.int32), "D_maskd": maskd})
        qsA, qsB, ksA, ksB = _xpos_ext_tables(r)
        m.update({"E_qsA": qsA, "E_qsB": qsB, "E_ksA": ksA, "E_ksB": ksB,
                  "E_maskw": _maskw(r),
                  "E_zgate": np.array([[0.0 if r == 0 else 1.0]], np.float32),
                  "E_cvec": cvec4})
        m.update(e_perli)
        m.update({"F_wout": wout, "F_tself": np.array([[r]], np.int32)})
        maps.append(m)
    return maps


def kernel_fused(**inputs):
    inputs = {k: np.asarray(v) for k, v in inputs.items()}
    nc = _get("FU", build_fused)
    res = run_bass_kernel_spmd(nc, stage_fused(inputs),
                               core_ids=list(range(NC)))
    if any(float(r["B_flag"][0, 0]) > 0 for r in res.results):
        raise NotImplementedError(
            "L0 hyperbolic attention hit a non-asymptotic branch; exact "
            "fixup path required")
    out = np.zeros((B, S, D), np.float32)
    for c in range(NC):
        b, r = c // 2, c % 2
        out[b, r * T:(r + 1) * T, :] = res.results[c]["F_xout"].T
    return out


kernel = kernel_fused
